# version 02
# baseline (speedup 1.0000x reference)
import numpy as np
import ml_dtypes
from contextlib import ExitStack

import concourse.bass as bass
import concourse.mybir as mybir
from concourse.bass_utils import run_bass_kernel_spmd

F32 = mybir.dt.float32
BF16 = mybir.dt.bfloat16
AF = mybir.ActivationFunctionType
ALU = mybir.AluOpType
AX = mybir.AxisListType
BF = ml_dtypes.bfloat16

T = 4096
D = 1024
CHT = 2048
NCORES = 8
import os as _os
DBG_PAIRS = int(_os.environ.get("DBG_PAIRS", "16"))
DBG_ATT = int(_os.environ.get("DBG_ATT", "1"))
DBG_HEADS = int(_os.environ.get("DBG_HEADS", "1"))
DBG_VT = int(_os.environ.get("DBG_VT", "1"))
DBG_VUP = int(_os.environ.get("DBG_VUP", "1"))
DBG_VT_MS = int(_os.environ.get("DBG_VT_MS", "1"))
DBG_VT_T = int(_os.environ.get("DBG_VT_T", "1"))
DBG_VT_EV = int(_os.environ.get("DBG_VT_EV", "1"))
DBG_LN0 = int(_os.environ.get("DBG_LN0", "1"))
DBG_VT_N = int(_os.environ.get("DBG_VT_N", "99"))


class Sem:
    __slots__ = ("h", "n")

    def __init__(self, ctx, nc, name):
        self.h = ctx.enter_context(nc.semaphore(name))
        self.n = 0

    def inc(self, instr, k=1):
        instr.then_inc(self.h, k)
        self.n += k
        return self.n


class BankGate:

    def __init__(self):
        self.rel = {}

    def acquire(self, eng, ids):
        need = {}
        for b in ids:
            for s, cnt in self.rel.get(b, ()):
                if need.get(s, 0) < cnt:
                    need[s] = cnt
        for s, cnt in need.items():
            eng.wait_ge(s.h, cnt)
        for b in ids:
            self.rel[b] = []

    def release(self, ids, sem):
        for b in ids:
            self.rel.setdefault(b, []).append((sem, sem.n))


class LaunchBase:
    def __init__(self, name):
        self.nc = bass.Bass("TRN2", target_bir_lowering=False)
        self.ctx = ExitStack()
        self.ctx.__enter__()
        self.name = name
        self.M = {}

    def din(self, name, shape, dt=F32):
        return self.nc.dram_tensor(name, shape, dt, kind="ExternalInput").ap()

    def dout(self, name, shape, dt=F32):
        return self.nc.dram_tensor(name, shape, dt, kind="ExternalOutput").ap()

    def sb(self, name, shape, dt):
        return self.ctx.enter_context(self.nc.sbuf_tensor(name, shape, dt))

    def psum_(self, name, shape, dt=F32):
        return self.ctx.enter_context(self.nc.psum_tensor(name, shape, dt))

    def sem(self, name):
        return Sem(self.ctx, self.nc, name)

    def dma(self, eng, sem, dst, srcap, **kw):
        bb = self.nc.cur_bb.bb
        n0 = len(bb.instructions)
        ins = eng.dma_start(dst, srcap, **kw)
        nsub = sum(1 for i in bb.instructions[n0:]
                   if type(i).__name__ == "InstDMACopy")
        ins.then_inc(sem.h, 16)
        sem.n += 16 * max(1, nsub)
        return ins

    def finish(self):
        self.ctx.__exit__(None, None, None)
        return self.nc


def _host_consts():
    c = {}
    c["c_ident"] = np.eye(128).astype(BF)
    c["c_identf"] = np.eye(128, dtype=np.float32)
    c["c_negones"] = np.full((1, 128), -1.0, np.float32).astype(BF)
    c["c_ones1"] = np.ones((1, 128), np.float32).astype(BF)
    c["c_ones512"] = np.ones((1, 512), np.float32).astype(BF)
    o64 = np.zeros((128, 2), np.float32)
    o64[0:64, 0] = 1.0
    o64[64:128, 1] = 1.0
    c["c_ones64"] = o64.astype(BF)
    DHE = 32
    inv_freq = 1.0 / (10000.0 ** (np.arange(0, DHE, 2, dtype=np.float64) / DHE))
    freqs = np.arange(T, dtype=np.float64)[:, None] * inv_freq[None, :]
    emb = np.concatenate([freqs, freqs], axis=-1)
    cos = np.cos(emb)
    sinp = np.sin(emb) * np.where(np.arange(DHE) % 2 == 0, -1.0, 1.0)[None, :]
    csn = np.zeros((64, 2 * T), np.float64)
    csn[32:64, 0:T] = cos.T
    csn[32:64, T:2 * T] = sinp.T
    c["c_csn"] = csn.astype(BF)
    m = np.zeros((128, 4 * 512), np.float32)
    ii = np.arange(128)[:, None]
    jj = np.arange(512)[None, :]
    for r in range(4):
        m[:, r * 512:(r + 1) * 512] = (jj >= ii + 128 * r).astype(np.float32)
    c["c_mask"] = m.astype(BF)
    return c


def _host_prep(inputs):
    p = {}
    f32 = np.float32

    def T2(a):
        return np.ascontiguousarray(np.asarray(a, f32).T)

    p["l0_wdq"] = np.ascontiguousarray(np.asarray(inputs["L0_attn_Wq_down"], f32))
    p["l0_wdk"] = np.ascontiguousarray(np.asarray(inputs["L0_attn_Wk_down"], f32))
    p["l0_wdv"] = np.ascontiguousarray(np.asarray(inputs["L0_attn_Wv_down"], f32))

    def rep(a):
        a = np.asarray(a, f32)
        return np.concatenate([a, a], axis=0)

    def swap_cols(a):
        a = np.asarray(a, f32)
        b = np.empty_like(a)
        b[:, 0::2] = a[:, 1::2]
        b[:, 1::2] = a[:, 0::2]
        return b

    up = np.zeros((128, 256), f32)
    up[:, 0:32] = rep(inputs["L0_attn_Wq_up_c"])
    up[:, 32:64] = rep(inputs["L0_attn_Wq_up_e"])
    up[:, 64:96] = rep(swap_cols(inputs["L0_attn_Wq_up_e"]))
    up[:, 96:128] = rep(inputs["L0_attn_Wk_up_c"])
    up[:, 128:160] = rep(inputs["L0_attn_Wk_up_e"])
    up[:, 160:192] = rep(swap_cols(inputs["L0_attn_Wk_up_e"]))
    up[:, 192:256] = rep(inputs["L0_attn_Wv_up"])
    p["l0_up"] = np.ascontiguousarray(up)

    p["l0_wc"] = np.ascontiguousarray(np.asarray(inputs["L0_attn_Wc"], f32))
    p["l0_gate_wT"] = T2(inputs["L0_ffn_gate_w"])
    p["l0_eguT"] = np.ascontiguousarray(
        np.transpose(np.asarray(inputs["L0_ffn_experts_gate_up"], f32), (0, 2, 1)))
    p["l0_edownT"] = np.ascontiguousarray(
        np.transpose(np.asarray(inputs["L0_ffn_experts_down"], f32), (0, 2, 1)))
    p["l0_sgT"] = T2(inputs["L0_ffn_shared_gate_w"])
    p["l0_suT"] = T2(inputs["L0_ffn_shared_up_w"])
    p["l0_sdT"] = T2(inputs["L0_ffn_shared_down_w"])
    p["l0_segT"] = T2(inputs["L0_ffn_shared_expert_gate_w"])

    p["l1_wtq"] = T2(inputs["L1_attn_Wq"])
    p["l1_wtk"] = T2(inputs["L1_attn_Wk"])
    p["l1_wtv"] = T2(inputs["L1_attn_Wv"])
    p["l1_wto"] = T2(inputs["L1_attn_Wo"])

    def fm8(b):
        return np.ascontiguousarray(np.asarray(b, f32).reshape(8, 128).T)

    p["l1_bq"] = fm8(inputs["L1_attn_bq"])
    p["l1_bk"] = fm8(inputs["L1_attn_bk"])
    p["l1_bv"] = fm8(inputs["L1_attn_bv"])
    p["l1_bo"] = fm8(inputs["L1_attn_bo"])

    p["l1_router_w"] = np.ascontiguousarray(np.asarray(inputs["L1_ffn_router_w"], f32))
    p["l1_router_b"] = np.ascontiguousarray(np.asarray(inputs["L1_ffn_router_b"], f32).reshape(1, 8))
    p["l1_w1"] = np.ascontiguousarray(np.asarray(inputs["L1_ffn_W1"], f32))
    p["l1_w2"] = np.ascontiguousarray(np.asarray(inputs["L1_ffn_W2"], f32))
    p["l1_b1"] = np.ascontiguousarray(
        np.asarray(inputs["L1_ffn_b1"], f32).reshape(8, 32, 128).transpose(0, 2, 1))
    p["l1_b2"] = np.ascontiguousarray(np.asarray(inputs["L1_ffn_b2"], f32))
    return p



class AttnBuilder(LaunchBase):
    def __init__(self, L):
        super().__init__(f"attn{L}")
        self.L = L
        self.build()

    def bk(self, i, r0=0, r1=128):
        return self.ps[r0:r1, i * 512:(i + 1) * 512]

    def bkb(self, i, c0, c1, r0=0, r1=128):
        return self.psb[r0:r1, i * 1024 + c0:i * 1024 + c1]

    def mview(self, hh, a, b):
        off = hh * CHT if self.L == 1 else 0
        return self.mrow[0:1, off + a:off + b]

    def build(self):
        L = self.L
        B = self

        self.ident_in = B.din("c_ident", [128, 128], BF16)
        self.negones_in = B.din("c_negones", [1, 128], BF16)
        self.ones64_in = B.din("c_ones64", [128, 2], BF16)
        if L == 0:
            self.x_in = B.din("x", [T, D], F32)
            self.wq_in = B.din("wdq", [D, D], F32)
            self.wk_in = B.din("wdk", [D, D], F32)
            self.wv_in = B.din("wdv", [D, D], F32)
            self.up_in = B.din("up", [128, 256], F32)
            self.csn_in = B.din("c_csn", [64, 2 * T], BF16)
            self.mask_in = B.din("c_mask", [128, 2048], BF16)
            self.xn_out = B.dout("xn_fm_o", [128, 8 * T], BF16)
        else:
            self.xn_in = B.din("xn_fm", [128, 8 * T], BF16)
            self.wq_in = B.din("wtq", [D, D], F32)
            self.wk_in = B.din("wtk", [D, D], F32)
            self.wv_in = B.din("wtv", [D, D], F32)
            self.bq_in = B.din("bq", [128, 8], F32)
            self.bk_in = B.din("bk", [128, 8], F32)
            self.bv_in = B.din("bv", [128, 8], F32)
        self.y_out = B.dout("y_tm", [T, D], BF16)

        self.xn = B.sb("xn", [128, 8 * T], BF16)
        self.y_tm = B.sb("y_tm_s", [128, 16 * D], BF16)
        self.ident = B.sb("ident", [128, 128], BF16)
        self.negones = B.sb("negones", [1, 128], BF16)
        self.ones64 = B.sb("ones64", [128, 2], BF16)
        self.wbf = B.sb("wbf", [128, 2 * 3 * D], BF16)
        if L == 0:
            self.xst = B.sb("xst", [128, 2 * D], F32)
            self.wst = self.xst
        else:
            self.wst = B.sb("wst", [128, 2 * D], F32)
        if L == 0:
            self.xsq = B.sb("xsq", [128, 2 * D], F32)
            self.xnt = B.sb("xnt", [128, 2 * D], BF16)
            self.lst = B.sb("lst", [128, 24], F32)
            self.csn = B.sb("csn", [64, 2 * T], BF16)
            self.maskb = B.sb("maskb", [128, 2048], BF16)
            self.upf = B.sb("upf", [128, 256], F32)
            self.upb = B.sb("upb", [128, 256], BF16)
            self.qs = B.sb("qs", [128, CHT], BF16)
            self.ks = B.sb("ks", [128, T], BF16)
            self.vs = B.sb("vs", [128, T], BF16)
            self.q_h = B.sb("q_h", [64, CHT], BF16)
            self.k_h = B.sb("k_h", [64, T], BF16)
            self.rtmp = B.sb("rtmp", [64, 1024], BF16)
            self.v2f = B.sb("v2f", [128, T], BF16)
        else:
            self.bfm = B.sb("bfm", [128, 24], F32)
            self.qs = B.sb("qs", [128, CHT], BF16)
            self.ks = B.sb("ks", [128, T], BF16)
            self.v2f = B.sb("v2f", [128, T], BF16)
        if L == 0:
            self.sq = self.xsq.bitcast(BF16)
        else:
            self.sq = B.sb("sqb", [128, T], BF16)
        self.v_tm = B.sb("v_tm", [128, 32 * 130], BF16)
        self.PT = B.sb("PT", [128, 2 * 1024], BF16)
        self.yev = B.sb("yev", [65, 512], BF16)
        if L == 0:
            self.mrow = self.xnt
        else:
            self.mrow = B.sb("mrow", [1, 2 * CHT], BF16)
        self.kst = B.sb("kst", [128, 16], F32)

        self.ps = B.psum_("ps", [128, 6 * 512], F32)
        self.psb = self.ps.bitcast(BF16)
        self.G = BankGate()

        for nm in ("s_c s_x s_gq s_ln s_xe s_vm s_sm s_w s_wc s_pj s_pe s_ua "
                   "s_vt s_ve s_s s_ex s_mk s_pv s_ye s_yd s_nr s_sq").split():
            setattr(self, nm, B.sem(nm))
        self.s_xp = [B.sem("s_xp0"), B.sem("s_xp1")]
        self.s_wp = [B.sem("s_wp0"), B.sem("s_wp1")]
        self.s_ys = B.sem("s_ys")
        self.s_vo = B.sem("s_vo")

        self.blk = self.ctx.enter_context(self.nc.Block())
        self.phase0()
        if L == 0:
            self.ln0()
        npair = 0
        for c in range(2):
            for p in range(8):
                if npair < DBG_PAIRS:
                    self.pair(c, p)
                    npair += 1
            if npair >= (c + 1) * 8:
                self.chunk_tail(c)
        self.epilogue()
        self.finish()

    def phase0(self):
        L, blk = self.L, self.blk
        s_c, s_x = self.s_c, self.s_x

        @blk.sync
        def _(sync):
            self.dma(sync, s_c, self.ident[:, :], self.ident_in)
            self.dma(sync, s_c, self.negones[:, :], self.negones_in)
            self.dma(sync, s_c, self.ones64[:, :], self.ones64_in)
            if L == 0:
                self.dma(sync, s_c, self.csn[:, :], self.csn_in)
                self.dma(sync, s_c, self.maskb[:, :], self.mask_in)
                self.dma(sync, s_c, self.upf[:, :], self.up_in)
            else:
                for i, b in enumerate((self.bq_in, self.bk_in, self.bv_in)):
                    self.dma(sync, s_c, self.bfm[:, 8 * i:8 * i + 8], b)
                for i in range(8):
                    self.dma(sync, s_x, self.xn[:, i * T:(i + 1) * T],
                             self.xn_in[:, i * T:(i + 1) * T])
        self.CONSTS = s_c.n
        self.XN0 = s_x.n
        self.UPC = 0

        if L == 0:
            s_wc = self.s_wc

            @blk.gpsimd
            def _(gp):
                gp.wait_ge(s_c.h, self.CONSTS)
                s_wc.inc(gp.tensor_copy(self.upb[:, :], self.upf[:, :]))
            self.UPC = s_wc.n

    def ln0(self):
        blk = self.blk
        if not DBG_LN0:
            self.s_xe.n = 32
            self.s_ln.n = 32

            @blk.scalar
            def _(sc):
                sc.sem_inc(self.s_xe.h, 32)
                sc.sem_inc(self.s_ln.h, 32)
            return
        s_x, s_gq, s_vm, s_sm, s_ln, s_vt, s_xe = (
            self.s_x, self.s_gq, self.s_vm, self.s_sm, self.s_ln, self.s_vt, self.s_xe)
        XN0 = self.XN0
        xst, xsq, xnt, lst, xn = self.xst, self.xsq, self.xnt, self.lst, self.xn
        VM0 = s_vm.n

        xm = []

        @blk.sync
        def _(sync):
            for tt in range(32):
                par = tt % 2
                if tt >= 2:
                    sync.wait_ge(s_ln.h, tt - 1)
                self.dma(sync, self.s_xp[par], xst[:, par * D:(par + 1) * D],
                         self.x_in[tt * 128:(tt + 1) * 128, :])
                xm.append(self.s_xp[par].n)

        @blk.gpsimd
        def _(gp):
            for tt in range(32):
                par = tt % 2
                gp.wait_ge(self.s_xp[tt % 2].h, xm[tt])
                if tt >= 2:
                    gp.wait_ge(s_vm.h, VM0 + 8 * (tt - 2) + 1)
                s_gq.inc(gp.tensor_tensor(xsq[:, par * D:(par + 1) * D],
                                          xst[:, par * D:(par + 1) * D],
                                          xst[:, par * D:(par + 1) * D], ALU.mult))
            gp.wait_ge(s_vm.h, VM0 + 8 * 31 + 1)
            s_gq.inc(gp.memset(self.sq[64:128, :], 0.0))

        @blk.vector
        def _(v):
            for tt in range(32):
                par = tt % 2
                o = par * 12
                xv = xst[:, par * D:(par + 1) * D]
                qv = xsq[:, par * D:(par + 1) * D]
                v.wait_ge(self.s_xp[tt % 2].h, xm[tt])
                v.wait_ge(s_gq.h, tt + 1)
                v.reduce_sum(lst[:, o:o + 1], xv, axis=AX.X)
                v.wait_ge(s_vm.h, s_vm.inc(v.reduce_sum(lst[:, o + 1:o + 2], qv, axis=AX.X)))
                v.wait_ge(s_vm.h, s_vm.inc(v.tensor_scalar(
                    lst[:, o + 2:o + 3], lst[:, o:o + 1], 1.0 / D, None, ALU.mult)))
                v.tensor_tensor(lst[:, o + 3:o + 4], lst[:, o + 2:o + 3],
                                lst[:, o + 2:o + 3], ALU.mult)
                v.wait_ge(s_vm.h, s_vm.inc(v.tensor_scalar(
                    lst[:, o + 4:o + 5], lst[:, o + 1:o + 2], 1.0 / D, None, ALU.mult)))
                v.wait_ge(s_vm.h, s_vm.inc(v.tensor_tensor(
                    lst[:, o + 5:o + 6], lst[:, o + 4:o + 5], lst[:, o + 3:o + 4], ALU.subtract)))
                v.wait_ge(s_vm.h, s_vm.inc(v.tensor_scalar(
                    lst[:, o + 6:o + 7], lst[:, o + 5:o + 6], 1.0, 1e-5, ALU.mult, ALU.add)))
                s_vm.inc(v.reciprocal(lst[:, o + 7:o + 8], lst[:, o + 6:o + 7]))
                v.wait_ge(s_sm.h, tt + 1)
                v.wait_ge(s_vm.h, s_vm.inc(v.tensor_tensor(
                    lst[:, o + 9:o + 10], lst[:, o + 2:o + 3], lst[:, o + 8:o + 9], ALU.mult)))
                v.wait_ge(s_vm.h, s_vm.inc(v.tensor_scalar(
                    lst[:, o + 10:o + 11], lst[:, o + 9:o + 10], -1.0, None, ALU.mult)))
        NV = 8

        @blk.scalar
        def _(sc):
            for tt in range(32):
                par = tt % 2
                o = par * 12
                sc.wait_ge(s_vm.h, VM0 + NV * tt + 6)
                s_sm.inc(sc.activation(lst[:, o + 8:o + 9], lst[:, o + 7:o + 8], AF.Sqrt))
                sc.wait_ge(s_sm.h, tt + 1)
                sc.wait_ge(s_vm.h, VM0 + NV * tt + 8)
                if tt >= 2:
                    sc.wait_ge(s_vt.h, tt - 1)
                s_ln.inc(sc.activation(xnt[:, par * D:(par + 1) * D],
                                       xst[:, par * D:(par + 1) * D], AF.Identity,
                                       bias=lst[:, o + 10:o + 11], scale=lst[:, o + 8:o + 9]))
                if tt >= 1:
                    sc.wait_ge(s_vt.h, tt)
                    src = self.bkb(4 + (tt - 1) % 2, 0, 1024)
                    dst = xn[:, :].rearrange("p (c t) -> p c t", c=8)[:, :, (tt - 1) * 128:tt * 128]
                    s_xe.inc(sc.copy(dst, src))
            sc.wait_ge(s_vt.h, 32)
            src = self.bkb(4 + 31 % 2, 0, 1024)
            dst = xn[:, :].rearrange("p (c t) -> p c t", c=8)[:, :, 31 * 128:32 * 128]
            s_xe.inc(sc.copy(dst, src))

        @blk.tensor
        def _(t):
            t.wait_ge(self.s_c.h, self.CONSTS)
            for tt in range(32):
                par = tt % 2
                t.wait_ge(s_ln.h, tt + 1)
                if tt >= 2:
                    t.wait_ge(s_xe.h, tt - 1)
                for cch in range(8):
                    ins = t.transpose(self.bkb(4 + tt % 2, cch * 128, (cch + 1) * 128),
                                      xnt[:, par * D + cch * 128:par * D + (cch + 1) * 128],
                                      self.ident[:, :])
                s_vt.inc(ins)

        @blk.scalar
        def _(sc):
            sc.wait_ge(s_xe.h, 32)
            for i in range(8):
                self.dma(sc, self.s_yd, self.xn_out[:, i * T:(i + 1) * T],
                         xn[:, i * T:(i + 1) * T])
        self.XNDMA = self.s_yd.n

    def pair(self, c, p):
        L, blk, G, M = self.L, self.blk, self.G, self.M
        kend = 2048 if (L == 0 and c == 0) else 4096
        nkt = kend // 128
        qbase = c * CHT
        it = c * 8 + p
        par = it % 2
        xn = self.xn
        s_w, s_wc, s_pj, s_pe = self.s_w, self.s_wc, self.s_pj, self.s_pe

        @blk.sync
        def _(sync):
            if L == 0 and it == 0:
                sync.wait_ge(self.s_ln.h, 32)
            wc0 = self.UPC if L == 0 else 0
            for mi, w in enumerate((self.wk_in, self.wv_in, self.wq_in)):
                gmi = it * 3 + mi
                sp = gmi % 2
                if gmi >= 2:
                    sync.wait_ge(s_wc.h, wc0 + gmi - 1)
                for kc in range(8):
                    self.dma(sync, self.s_wp[sp],
                             self.wst[:, sp * D + kc * 128:sp * D + (kc + 1) * 128],
                             w[kc * 128:(kc + 1) * 128, p * 128:(p + 1) * 128])
                M[("wdma", gmi)] = self.s_wp[sp].n

        @blk.gpsimd
        def _(gp):
            for mi in range(3):
                gmi = it * 3 + mi
                sp = gmi % 2
                gp.wait_ge(self.s_wp[gmi % 2].h, M[("wdma", gmi)])
                if it >= 2:
                    gp.wait_ge(s_pj.h, M[("prjdone", it - 2)])
                s_wc.inc(gp.tensor_copy(
                    self.wbf[:, (par * 3 + mi) * D:(par * 3 + mi + 1) * D],
                    self.wst[:, sp * D:(sp + 1) * D]))
                M[("wcv", gmi)] = s_wc.n

        mats = [("k", kend), ("v", kend), ("q", CHT)]

        @blk.tensor
        def _(t):
            t.wait_ge(self.s_c.h, self.CONSTS)
            if L == 0:
                t.wait_ge(self.s_xe.h, 32)
                t.wait_ge(s_wc.h, self.UPC)
            else:
                t.wait_ge(self.s_x.h, self.XN0)
            gi = 0
            pe0 = s_pe.n
            for mi, (nm, ntok) in enumerate(mats):
                gmi = it * 3 + mi
                t.wait_ge(s_wc.h, M[("wcv", gmi)])
                absb = qbase if nm == "q" else 0
                for g in range(ntok // 1024):
                    bks = [2 * (gi % 2), 2 * (gi % 2) + 1]
                    if gi >= 2:
                        t.wait_ge(s_pe.h, pe0 + 2 * (gi - 1))
                    else:
                        G.acquire(t, bks)
                    for kc in range(8):
                        for t2 in range(2):
                            ins = t.matmul(
                                self.bk(bks[t2]),
                                self.wbf[:, (par * 3 + mi) * D + kc * 128:
                                         (par * 3 + mi) * D + (kc + 1) * 128],
                                xn[:, kc * T + absb + g * 1024 + t2 * 512:
                                   kc * T + absb + g * 1024 + (t2 + 1) * 512],
                                start=(kc == 0), stop=(kc == 7))
                            if kc == 7:
                                s_pj.inc(ins)
                    M[("prj", it, mi, g)] = s_pj.n
                    gi += 1
            M[("prjdone", it)] = s_pj.n

        @blk.scalar
        def _(sc):
            gi = 0
            for mi, (nm, ntok) in enumerate(mats):
                for g in range(ntok // 1024):
                    bks = [2 * (gi % 2), 2 * (gi % 2) + 1]
                    sc.wait_ge(s_pj.h, M[("prj", it, mi, g)])
                    for t2 in range(2):
                        col = g * 1024 + t2 * 512
                        if L == 0:
                            dst = {"k": self.ks, "v": self.vs, "q": self.qs}[nm]
                            ins = sc.activation(dst[:, col:col + 512], self.bk(bks[t2]), AF.Copy)
                        else:
                            dst = {"k": self.ks, "v": self.v2f, "q": self.qs}[nm]
                            bcol = {"q": 0, "k": 8, "v": 16}[nm] + p
                            ins = sc.activation(dst[:, col:col + 512], self.bk(bks[t2]),
                                                AF.Identity, bias=self.bfm[:, bcol:bcol + 1])
                        s_pe.inc(ins)
                    G.release(bks, s_pe)
                    gi += 1
            M[("prjevac", it)] = s_pe.n

        if L == 0 and DBG_VUP:
            @blk.tensor
            def _(t):
                t.wait_ge(s_pe.h, M[("prjevac", it)])
                pe0v = s_pe.n
                for g in range(nkt // 4):
                    bb = 2 * (g % 2)
                    if g >= 2:
                        t.wait_ge(s_pe.h, pe0v + (g - 1))
                    else:
                        G.acquire(t, [bb])
                    i1 = t.matmul(self.bk(bb, 0, 64), self.upb[0:64, 192:256],
                                  self.vs[0:64, g * 512:(g + 1) * 512], start=True, stop=True)
                    s_pj.inc(i1)
                    i2 = t.matmul(self.bk(bb, 64, 128), self.upb[64:128, 192:256],
                                  self.vs[64:128, g * 512:(g + 1) * 512],
                                  start=True, stop=True, tile_position=(64, 64))
                    s_pj.inc(i2)
                    M[("vup", it, g)] = s_pj.n

            @blk.scalar
            def _(sc):
                for g in range(nkt // 4):
                    bb = 2 * (g % 2)
                    sc.wait_ge(s_pj.h, M[("vup", it, g)])
                    s_pe.inc(sc.activation(self.v2f[:, g * 512:(g + 1) * 512],
                                           self.bk(bb), AF.Copy))
                    G.release([bb], s_pe)
                M[("vupe", it)] = s_pe.n

        s_vt, s_ve = self.s_vt, self.s_ve
        if not DBG_VT:
            M[("vtmdone", it)] = self.s_ve.n
            M[("pairpv", it)] = self.s_pv.n
            if DBG_HEADS:
                for hh in range(2):
                    self.head(c, p, hh, kend, nkt, qbase, it)
            M[("pairpv", it)] = self.s_pv.n
            return

        @blk.gpsimd
        def _(gp):
            if it >= 1:
                gp.wait_ge(self.s_pv.h, M[("pairpv", it - 1)])
            if DBG_VT_MS:
                self.s_vo.inc(gp.memset(self.v_tm[:, 0:nkt * 130], 1.0))
            else:
                self.s_vo.inc(gp.engine_nop())
            M[("vones", it)] = self.s_vo.n

        @blk.tensor
        def _(t):
            vsrc = self.v2f
            wait_m = M[("vupe", it)] if (L == 0 and DBG_VUP) else M[("prjevac", it)]
            t.wait_ge(s_pe.h, wait_m)
            pe0v2 = s_pe.n
            for g4 in range(nkt // 4):
                M[("vtm", it, g4)] = pe0v2 + 2 * (g4 + 1)
            for g4 in range(nkt // 4):
                half = g4 % 2
                if g4 >= 2:
                    t.wait_ge(s_pe.h, M[("vtm", it, g4 - 2)])
                else:
                    G.acquire(t, [4 + half])
                if DBG_VT_T:
                    for j in range(4):
                        ins = t.transpose(
                            self.bkb(4 + half, j * 128, (j + 1) * 128),
                            vsrc[:, (g4 * 4 + j) * 128:(g4 * 4 + j + 1) * 128],
                            self.ident[:, :])
                else:
                    ins = t.engine_nop()
                s_vt.inc(ins)
                M[("vT", it, g4)] = s_vt.n

        @blk.scalar
        def _(sc):
            sc.wait_ge(self.s_vo.h, M[("vones", it)])
            for g4 in range(nkt // 4):
                half = g4 % 2
                sc.wait_ge(s_vt.h, M[("vT", it, g4)])
                base = (4 + half) * 1024
                for hh in range(2):
                    if g4 < DBG_VT_N:
                        for j in range(3, -1, -1):
                            kt = g4 * 4 + j
                            ins = sc.copy(
                                self.v_tm[:, kt * 130 + hh * 65:kt * 130 + hh * 65 + 64],
                                self.psb[:, base + j * 128 + hh * 64:base + j * 128 + (hh + 1) * 64])
                    else:
                        ins = sc.copy(self.kst[:, 0:1], self.kst[:, 1:2])
                    s_pe.inc(ins)
                assert M[("vtm", it, g4)] == s_pe.n
            M[("vtmdone", it)] = s_pe.n
        G.release([4, 5], s_pe)

        for hh in range(2 if DBG_HEADS else 0):
            self.head(c, p, hh, kend, nkt, qbase, it)

        M[("pairpv", it)] = self.s_pv.n

    def head(self, c, p, hh, kend, nkt, qbase, it):
        L, blk, G, M = self.L, self.blk, self.G, self.M
        h = 2 * p + hh
        hr = 64 * hh
        s_pj, s_pe, s_ua, s_sq, s_nr = self.s_pj, self.s_pe, self.s_ua, self.s_sq, self.s_nr
        s_vm, s_sm = self.s_vm, self.s_sm

        if L == 0:
            for kind, lat, dst, ntok, absb, ubase in (
                    ("k", self.ks, self.k_h, kend, 0, 96),
                    ("q", self.qs, self.q_h, CHT, qbase, 0)):
                ntt = ntok // 512

                @blk.tensor
                def _(t, kind=kind, lat=lat, ntt=ntt, absb=absb, ubase=ubase):
                    if kind == "k":
                        t.wait_ge(s_pe.h, M[("prjevac", it)])
                        if hh == 1:
                            t.wait_ge(self.s_s.h, M[("attdone", it, 0)])
                    pe0u = s_pe.n
                    ua0 = s_ua.n
                    for g in range(ntt):
                        b0 = 2 * (g % 2)
                        b1 = b0 + 1
                        if g >= 2:
                            t.wait_ge(s_pe.h, pe0u + (g - 1))
                            t.wait_ge(s_ua.h, ua0 + 2 * (g - 1))
                        else:
                            G.acquire(t, [b0, b1])
                        rhs = lat[hr:hr + 64, g * 512:(g + 1) * 512]
                        i1 = t.matmul(self.bk(b0, 0, 32), self.upb[hr:hr + 64, ubase:ubase + 32],
                                      rhs, start=True, stop=True)
                        s_pj.inc(i1)
                        i2 = t.matmul(self.bk(b0, 32, 64),
                                      self.upb[hr:hr + 64, ubase + 32:ubase + 64], rhs,
                                      start=True, stop=True, tile_position=(hr, 32))
                        s_pj.inc(i2)
                        i3 = t.matmul(self.bk(b1, 32, 64),
                                      self.upb[hr:hr + 64, ubase + 64:ubase + 96], rhs,
                                      start=True, stop=True, tile_position=(hr, 32))
                        s_pj.inc(i3)
                        M[("up", it, hh, kind, g)] = s_pj.n

                @blk.scalar
                def _(sc, kind=kind, dst=dst, ntt=ntt):
                    for g in range(ntt):
                        b0 = 2 * (g % 2)
                        sc.wait_ge(s_pj.h, M[("up", it, hh, kind, g)])
                        s_pe.inc(sc.activation(dst[0:32, g * 512:(g + 1) * 512],
                                               self.bk(b0, 0, 32), AF.Copy))
                        G.release([b0], s_pe)
                        M[("upc", it, hh, kind, g)] = s_pe.n

                @blk.vector
                def _(v, kind=kind, dst=dst, ntt=ntt, absb=absb):
                    ua0v = s_ua.n
                    for g in range(ntt):
                        b0 = 2 * (g % 2)
                        b1 = b0 + 1
                        v.wait_ge(s_pj.h, M[("up", it, hh, kind, g)])
                        v.wait_ge(s_ua.h, ua0v + 2 * g)
                        ac = absb + g * 512
                        v.tensor_mul(self.rtmp[32:64, 0:512], self.bk(b0, 32, 64),
                                     self.csn[32:64, ac:ac + 512])
                        mm = s_ua.inc(v.tensor_mul(self.rtmp[32:64, 512:1024],
                                                   self.bk(b1, 32, 64),
                                                   self.csn[32:64, T + ac:T + ac + 512]))
                        v.wait_ge(s_ua.h, mm)
                        s_ua.inc(v.tensor_add(dst[32:64, g * 512:(g + 1) * 512],
                                              self.rtmp[32:64, 0:512],
                                              self.rtmp[32:64, 512:1024]))
                        G.release([b0, b1], s_ua)
                        M[("uprope", it, hh, kind, g)] = s_ua.n
                M[(kind + "done", it, hh)] = (s_pe.n, s_ua.n)

            qsrc, ksrc, qr = self.q_h, self.k_h, 0
        else:
            qsrc, ksrc, qr = self.qs, self.ks, hr

        if L == 0:
            self.norms_k(it, hh, kend, ksrc)
            self.norms_q(it, hh, qsrc)
        else:
            if hh == 0:
                self.norms_k(it, 0, kend, ksrc)
                self.norms_k(it, 1, kend, ksrc)
                self.norms_q(it, 0, qsrc)
                self.norms_q(it, 1, qsrc)

        s_s, s_ex, s_mk, s_pv, s_ye = self.s_s, self.s_ex, self.s_mk, self.s_pv, self.s_ye

        for qg in range(4 if DBG_ATT else 0):
            qa = qbase + qg * 512
            nkt_q = min(nkt, (qa + 512) // 128) if L == 0 else nkt
            ngrp = (nkt_q + 1) // 2

            if L == 0:
                if not hasattr(self, "mk_pred"):
                    self.mk_pred = 0
                for j in range(4):
                    M[("mk", it, hh, qg, qa // 128 + j)] = self.mk_pred + j + 1
                self.mk_pred += 4
            if not hasattr(self, "ex_pred"):
                self.ex_pred = 0
            for g in range(ngrp):
                M[("exp", it, hh, qg, g)] = self.ex_pred + g + 1
            self.ex_pred += ngrp

            M[("pvprev", it, hh, qg)] = self.s_pv.n

            @blk.tensor
            def _(t, qg=qg, qa=qa, nkt_q=nkt_q, ngrp=ngrp):
                t.wait_ge(s_pe.h, M[("mrow", it, hh)])
                t.wait_ge(s_pe.h, M[("vtmdone", it)])
                for g in range(ngrp):
                    sgb = [2 * (g % 2), 2 * (g % 2) + 1]
                    if g >= 2:
                        t.wait_ge(self.s_ex.h, M[("exp", it, hh, qg, g - 2)])
                    else:
                        G.acquire(t, sgb)
                    for kt in (2 * g, 2 * g + 1):
                        if kt >= nkt_q:
                            continue
                        sub = kt % 2
                        vw = self.bk(sgb[sub])
                        t.matmul(vw, ksrc[qr:qr + 64, kt * 128:(kt + 1) * 128],
                                 qsrc[qr:qr + 64, qg * 512:(qg + 1) * 512],
                                 start=True, stop=False)
                        s_s.inc(t.matmul(vw, self.negones[0:1, :],
                                         self.mview(hh, qg * 512, (qg + 1) * 512),
                                         start=False, stop=True))
                    M[("sgrp", it, hh, qg, g)] = s_s.n
                    if g >= 1:
                        self._pv_group(t, it, hh, qg, g - 1, nkt_q, qa)
                self._pv_group(t, it, hh, qg, ngrp - 1, nkt_q, qa)
                M[("pvdone", it, hh, qg)] = s_pv.n

            @blk.scalar
            def _(sc, qg=qg, qa=qa, nkt_q=nkt_q, ngrp=ngrp):
                for g in range(ngrp):
                    sgb = [2 * (g % 2), 2 * (g % 2) + 1]
                    n_in_g = 1 if (2 * g + 1 >= nkt_q) else 2
                    sc.wait_ge(s_s.h, M[("sgrp", it, hh, qg, g)])
                    if g >= 2:
                        sc.wait_ge(s_pv.h, M[("pvg", it, hh, qg, g - 2)])
                    elif ("pvprev", it, hh, qg) in M:
                        sc.wait_ge(s_pv.h, M[("pvprev", it, hh, qg)])
                    s_ex.inc(sc.activation(self.PT[:, (g % 2) * 1024:(g % 2) * 1024 + n_in_g * 512],
                                           self.ps[:, sgb[0] * 512:sgb[0] * 512 + n_in_g * 512],
                                           AF.Exp, scale=0.125))
                    assert M[("exp", it, hh, qg, g)] == s_ex.n
                    G.release(sgb, s_ex)

            if L == 0:
                @blk.gpsimd
                def _(gp, qg=qg, qa=qa, nkt_q=nkt_q):
                    for kt in range(nkt_q):
                        ktb = kt * 128
                        if ktb < qa:
                            continue
                        r = (ktb - qa) // 128
                        g = kt // 2
                        gp.wait_ge(s_ex.h, M[("exp", it, hh, qg, g)])
                        pcol = (g % 2) * 1024 + (kt % 2) * 512
                        s_mk.inc(gp.tensor_mul(self.PT[:, pcol:pcol + 512],
                                               self.PT[:, pcol:pcol + 512],
                                               self.maskb[:, r * 512:(r + 1) * 512]))
                        assert M[("mk", it, hh, qg, kt)] == s_mk.n

            @blk.scalar
            def _(sc, qg=qg):
                sc.wait_ge(s_pv.h, M[("pvdone", it, hh, qg)])
                s_ye.inc(sc.activation(self.yev[:, :], self.bk(4, 0, 65), AF.Copy))
                M[("yev", it, hh, qg)] = s_ye.n
                G.release([4], s_ye)

            ys0 = self.s_ys.n
            for j in range(4):
                M[("ystj", it, hh, qg, j)] = ys0 + j + 1
            M[("yst", it, hh, qg)] = ys0 + 4
            vt0y = self.s_vt.n

            @blk.tensor
            def _(t, qg=qg):
                t.wait_ge(s_ye.h, M[("yev", it, hh, qg)])
                for j in range(4):
                    if j < 2:
                        G.acquire(t, [4 + j % 2])
                    else:
                        t.wait_ge(self.s_ys.h, M[("ystj", it, hh, qg, j - 2)])
                    ins = t.transpose(self.bkb(4 + j % 2, 0, 65, 0, 128),
                                      self.yev[:, j * 128:(j + 1) * 128],
                                      self.ident[0:65, 0:65])
                    self.s_vt.inc(ins)
                M[("yT", it, hh, qg)] = self.s_vt.n

            @blk.vector
            def _(v, qg=qg):
                if c == 1:
                    v.wait_ge(self.s_yd.h, M[("ydma", 0)])
                for j in range(4):
                    v.wait_ge(self.s_vt.h, vt0y + j + 1)
                    lv = self.kst[:, 12 + (j % 2):13 + (j % 2)]
                    mm = s_vm.inc(v.reciprocal(lv, self.bkb(4 + j % 2, 64, 65, 0, 128)))
                    v.wait_ge(s_vm.h, mm)
                    tt128 = qg * 4 + j
                    self.s_ys.inc(v.tensor_scalar(
                        self.y_tm[:, tt128 * D + h * 64:tt128 * D + (h + 1) * 64],
                        self.bkb(4 + j % 2, 0, 64, 0, 128),
                        lv, None, ALU.mult))
                    assert M[("ystj", it, hh, qg, j)] == self.s_ys.n
                G.release([4, 5], self.s_ys)

        M[("attdone", it, hh)] = self.s_s.n

    def norms_k(self, it, hh, kend, ksrc):
        L, blk, G, M = self.L, self.blk, self.G, self.M
        s_pj, s_pe, s_ua, s_sq, s_nr = self.s_pj, self.s_pe, self.s_ua, self.s_sq, self.s_nr

        @blk.scalar
        def _(sc):
            if L == 0:
                pe_m, ua_m = M[("kdone", it, hh)]
                sc.wait_ge(s_pe.h, pe_m)
                sc.wait_ge(s_ua.h, ua_m)
                s_sq.inc(sc.activation(self.sq[0:64, 0:kend], ksrc[0:64, 0:kend], AF.Square))
            elif hh == 0:
                sc.wait_ge(s_pe.h, M[("prjevac", it)])
                s_sq.inc(sc.activation(self.sq[:, 0:kend], ksrc[:, 0:kend], AF.Square))
            M[("ksq", it, hh)] = s_sq.n

        o64col = self.ones64[:, (hh if L == 1 else 0):(hh if L == 1 else 0) + 1]

        @blk.tensor
        def _(t):
            if L == 0:
                t.wait_ge(self.s_gq.h, 33)
            t.wait_ge(s_sq.h, M[("ksq", it, hh)])
            nr0 = self.s_nr.n
            for g in range(kend // 512):
                bb = 4 if g % 2 == 0 else 5
                if g >= 2:
                    t.wait_ge(self.s_nr.h, nr0 + (g - 1))
                else:
                    G.acquire(t, [bb])
                s_pj.inc(t.matmul(self.bk(bb, 0, 1), o64col,
                                  self.sq[0:128, g * 512:(g + 1) * 512],
                                  start=True, stop=True))
                M[("knmm", it, hh, g)] = s_pj.n

        @blk.vector
        def _(v):
            for g in range(kend // 512):
                bb = 4 if g % 2 == 0 else 5
                v.wait_ge(s_pj.h, M[("knmm", it, hh, g)])
                s_nr.inc(v.reduce_max(self.kst[0:1, g:g + 1], self.bk(bb, 0, 1), axis=AX.X))
                G.release([bb], s_nr)
            v.wait_ge(s_nr.h, s_nr.n)
            mm = s_nr.inc(v.reduce_max(self.kst[0:1, 8 + hh:9 + hh],
                                       self.kst[0:1, 0:kend // 512], axis=AX.X))
            v.wait_ge(s_nr.h, mm)
            s_nr.inc(v.tensor_scalar(self.kst[0:1, 10 + hh:11 + hh],
                                     self.kst[0:1, 8 + hh:9 + hh], 1.1025, None, ALU.mult))
            M[("khat", it, hh)] = s_nr.n

    def norms_q(self, it, hh, qsrc):
        L, blk, G, M = self.L, self.blk, self.G, self.M
        s_pj, s_pe, s_ua, s_sq, s_nr = self.s_pj, self.s_pe, self.s_ua, self.s_sq, self.s_nr

        @blk.scalar
        def _(sc):
            if L == 0:
                pe_m, ua_m = M[("qdone", it, hh)]
                sc.wait_ge(s_pe.h, pe_m)
                sc.wait_ge(s_ua.h, ua_m)
                sc.wait_ge(s_nr.h, M[("khat", it, hh)])
                s_sq.inc(sc.activation(self.sq[0:64, 0:CHT], qsrc[0:64, 0:CHT], AF.Square))
            elif hh == 0:
                sc.wait_ge(s_nr.h, M[("khat", it, 1)])
                s_sq.inc(sc.activation(self.sq[:, 0:CHT], qsrc[:, 0:CHT], AF.Square))
            M[("qsq", it, hh)] = s_sq.n

        o64col = self.ones64[:, (hh if L == 1 else 0):(hh if L == 1 else 0) + 1]

        @blk.tensor
        def _(t):
            t.wait_ge(s_sq.h, M[("qsq", it, hh)])
            pe0n = s_pe.n
            for g in range(4):
                bb = 4 if g % 2 == 0 else 5
                if g >= 2:
                    t.wait_ge(s_pe.h, pe0n + (g - 1))
                else:
                    G.acquire(t, [bb])
                s_pj.inc(t.matmul(self.bk(bb, 0, 1), o64col,
                                  self.sq[0:128, g * 512:(g + 1) * 512],
                                  start=True, stop=True))
                M[("qnmm", it, hh, g)] = s_pj.n

        @blk.scalar
        def _(sc):
            sc.wait_ge(s_nr.h, M[("khat", it, hh)])
            for g in range(4):
                bb = 4 if g % 2 == 0 else 5
                sc.wait_ge(s_pj.h, M[("qnmm", it, hh, g)])
                s_pe.inc(sc.activation(self.mview(hh, g * 512, (g + 1) * 512),
                                       self.bk(bb, 0, 1), AF.Sqrt,
                                       scale=self.kst[0:1, 10 + hh:11 + hh]))
                G.release([bb], s_pe)
            M[("mrow", it, hh)] = s_pe.n

    def _pv_group(self, t, it, hh, qg, g, nkt_q, qa):
        L, G, M = self.L, self.G, self.M
        s_pv = self.s_pv
        for kt in (2 * g, 2 * g + 1):
            if kt >= nkt_q:
                continue
            if kt == 0:
                G.acquire(t, [4])
            if L == 0 and kt * 128 >= qa:
                t.wait_ge(self.s_mk.h, M[("mk", it, hh, qg, kt)])
            else:
                t.wait_ge(self.s_ex.h, M[("exp", it, hh, qg, g)])
            pcol = (g % 2) * 1024 + (kt % 2) * 512
            ins = t.matmul(self.bk(4, 0, 65),
                           self.v_tm[:, kt * 130 + hh * 65:kt * 130 + (hh + 1) * 65],
                           self.PT[:, pcol:pcol + 512],
                           start=(kt == 0), stop=(kt == nkt_q - 1))
            s_pv.inc(ins)
        M[("pvg", it, hh, qg, g)] = s_pv.n

    def chunk_tail(self, c):
        blk, M = self.blk, self.M
        it_last = c * 8 + 7
        snap = [M[("yst", c * 8 + p, hh, qg)] for p in range(8) for hh in range(2) for qg in range(4)]
        need = max(snap)

        @blk.scalar
        def _(sc):
            sc.wait_ge(self.s_ys.h, need)
            for tt in range(16):
                self.dma(sc, self.s_yd,
                         self.y_out[c * CHT + tt * 128:c * CHT + (tt + 1) * 128, :],
                         self.y_tm[:, tt * D:(tt + 1) * D])
        M[("ydma", c)] = self.s_yd.n

    def epilogue(self):
        blk = self.blk
        total = self.s_yd.n

        @blk.sync
        def _(sync):
            sync.wait_ge(self.s_yd.h, total)


def build_attn(L):
    b = AttnBuilder(L)
    return b.nc



class MixBuilder(LaunchBase):
    def __init__(self, L):
        super().__init__(f"mix{L}")
        self.L = L
        self.build()

    def bk(self, i, r0=0, r1=128):
        return self.ps[r0:r1, i * 512:(i + 1) * 512]

    def bk2(self, i):
        return self.ps[:, i * 512:(i + 2) * 512]

    def bkb(self, i, c0, c1, r0=0, r1=128):
        return self.psb[r0:r1, i * 1024 + c0:i * 1024 + c1]

    def build(self):
        L = self.L
        B = self

        self.ident_in = B.din("c_ident", [128, 128], BF16)
        self.identf_in = B.din("c_identf", [128, 128], F32)
        self.ones1_in = B.din("c_ones1", [1, 128], BF16)
        self.ones512_in = B.din("c_ones512", [1, 512], BF16)
        self.xn_in = B.din("xn_fm", [128, 8 * T], BF16)
        self.y_in = B.din("y_tm", [T, D], BF16)
        self.res_in = B.din("res", [T, D], F32)
        if L == 0:
            self.wc_in = B.din("wc", [D, D], F32)
            self.gate_in = B.din("gate_wT", [D, 8], F32)
            self.egu_in = B.din("eguT", [8, D, 2816], F32)
            self.edn_in = B.din("edownT", [8, 1408, D], F32)
            self.sg_in = B.din("sgT", [D, 2816], F32)
            self.su_in = B.din("suT", [D, 2816], F32)
            self.sd_in = B.din("sdT", [2816, D], F32)
            self.seg_in = B.din("segT", [D, 1], F32)
            self.x1_out = B.dout("x1", [T, D], F32)
            self.xn1_out = B.dout("xn1_fm", [128, 8 * T], BF16)
        else:
            self.wc_in = B.din("wto", [D, D], F32)
            self.bo_in = B.din("bo", [128, 8], F32)
            self.rw_in = B.din("router_w", [D, 8], F32)
            self.rb_in = B.din("router_b", [1, 8], F32)
            self.w1_in = B.din("w1", [8, D, 4096], F32)
            self.b1_in = B.din("b1", [8, 128, 32], F32)
            self.w2_in = B.din("w2", [8, 4096, D], F32)
            self.b2_in = B.din("b2", [8, D], F32)
            self.x1_out = B.dout("out", [T, D], F32)

        self.xn = B.sb("xn", [128, 8 * CHT], BF16)
        self.acc = B.sb("acc", [128, 8 * CHT], F32)
        self.ident = B.sb("ident", [128, 128], BF16)
        self.identf = B.sb("identf", [128, 128], F32)
        self.ones1 = B.sb("ones1", [1, 128], BF16)
        self.ones512 = B.sb("ones512", [1, 512], BF16)
        if L == 0:
            self.h_sb = B.sb("h_sb", [128, 11 * CHT], BF16)
            hf = self.h_sb.bitcast(F32)
            self.yst = self.h_sb[:, 0:2048]
            self.yfm = self.h_sb[:, 2048:6144]
            self.wcbf = self.h_sb[:, 6144:14336]
            self.xnt = self.h_sb[:, 0:2048]
            self.xne = self.h_sb[:, 2048:4096]
            self.x1t = hf[:, 7168:9216]
            self.xrst = hf[:, 9216:11264]
            self.wcst = B.sb("wcst", [128, 2 * D], F32)
            self.gust = self.wcst
            self.gubf = B.sb("gubf", [128, 2 * 8 * 256], BF16)
            self.dnst = B.sb("dnst", [128, 11 * 256], F32)
            self.dnbf = B.sb("dnbf", [128, 2 * 11 * 256], BF16)
            self.gwbf = B.sb("gwbf", [128, 72], BF16)
            self.gwst = hf[:, 0:72]
            self.lst = B.sb("lst", [128, 24], F32)
        else:
            self.bof = B.sb("bof", [128, 8], F32)
            self.h_sb = B.sb("h_sb", [128, 4 * CHT], BF16)
            hf = self.h_sb.bitcast(F32)
            self.yst = self.h_sb[:, 0:2048]
            self.yfm = self.h_sb[:, 2048:6144]
            self.x1t = hf[:, 0:2048]
            self.xrst = hf[:, 2048:4096]
            self.wcbf = B.sb("wcbf", [128, 8 * D], BF16)
            self.gust = B.sb("gust", [128, 8 * 512], F32)
            self.wcst = self.gust[:, 0:2048]
            self.gubf = B.sb("gubf", [128, 2 * 8 * 512], BF16)
            self.dnbf = B.sb("dnbf", [128, 2 * 4 * 1024], BF16)
            self.gwbf = B.sb("gwbf", [128, 72], BF16)
            self.gwst = B.sb("gwst", [128, 72], F32)
            self.b1f = B.sb("b1f", [128, 32], F32)
            self.b2r = B.sb("b2r", [1, 1024], BF16)
            self.b2st = B.sb("b2st", [1, 1024], F32)
            self.rbr = B.sb("rbr", [1, 8], BF16)
            self.rbst = B.sb("rbst", [1, 8], F32)
        self.sgev = B.sb("sgev", [128, 2048], BF16)
        self.hmid = B.sb("hmid", [128, 1024], BF16)
        self.w_all = B.sb("w_all", [128, 160], BF16)
        self.wrow = self.sgev[0:1, 0:CHT]
        self.wb_sb = B.sb("wb_sb", [128, CHT], BF16)
        self.ls = B.sb("ls", [128, 64], F32)
        self.ps = B.psum_("ps", [128, 8 * 512], F32)
        self.psb = self.ps.bitcast(BF16)
        self.G = BankGate()

        for nm in ("s_c s_x s_y s_w s_wc s_pj s_pe s_ex s_ve s_vm s_sm s_gt "
                   "s_vt s_acc s_xr s_x1 s_ln s_od s_gq s_hh").split():
            setattr(self, nm, B.sem(nm))
        self.s_wp = [B.sem("s_wp0"), B.sem("s_wp1")]
        self.s_xp = [B.sem("s_xp0"), B.sem("s_xp1")]
        self.s_yp = [B.sem("s_yp0"), B.sem("s_yp1")]
        self.s_wd = B.sem("s_wd")
        self.s_x1p = [B.sem("s_x1p0"), B.sem("s_x1p1")]
        self.s_odp = [B.sem("s_odp0"), B.sem("s_odp1")]

        self.blk = self.ctx.enter_context(self.nc.Block())
        self.phase0()
        for c in range(2):
            self.wc_phase(c)
            self.gating(c)
            nexp = 10 if L == 0 else 8
            for e in range(nexp):
                self.expert(c, e)
            self.tail(c)
        self.epilogue()
        self.finish()

    def phase0(self):
        L, blk = self.L, self.blk
        s_c, s_x = self.s_c, self.s_x

        @blk.sync
        def _(sync):
            self.dma(sync, s_c, self.ident[:, :], self.ident_in)
            self.dma(sync, s_c, self.identf[:, :], self.identf_in)
            self.dma(sync, s_c, self.ones1[:, :], self.ones1_in)
            self.dma(sync, s_c, self.ones512[:, :], self.ones512_in)
            if L == 0:
                for kc in range(8):
                    self.dma(sync, s_c, self.gwst[:, kc * 8:kc * 8 + 8],
                             self.gate_in[kc * 128:(kc + 1) * 128, :])
                    self.dma(sync, s_c, self.gwst[:, 64 + kc:65 + kc],
                             self.seg_in[kc * 128:(kc + 1) * 128, :])
            else:
                for kc in range(8):
                    self.dma(sync, s_c, self.gwst[:, kc * 8:kc * 8 + 8],
                             self.rw_in[kc * 128:(kc + 1) * 128, :])
                self.dma(sync, s_c, self.rbst[:, :], self.rb_in)
                self.dma(sync, s_c, self.bof[:, :], self.bo_in)
            for i in range(8):
                self.dma(sync, s_x, self.xn[:, i * CHT:(i + 1) * CHT],
                         self.xn_in[:, i * T:i * T + CHT])
        self.CONSTS = s_c.n
        self.XN0 = s_x.n

        @blk.gpsimd
        def _(gp):
            gp.wait_ge(s_c.h, self.CONSTS)
            ncol = 72 if L == 0 else 64
            self.s_gq.inc(gp.tensor_copy(self.gwbf[:, 0:ncol], self.gwst[:, 0:ncol]))
            if L == 1:
                self.s_gq.inc(gp.tensor_copy(self.rbr[:, :], self.rbst[:, :]))
        self.GC = self.s_gq.n

    def wc_phase(self, c):
        L, blk, G, M = self.L, self.blk, self.G, self.M
        s_y, s_wc, s_pj, s_pe, s_vt = (self.s_y, self.s_wc, self.s_pj,
                                       self.s_pe, self.s_vt)
        vt0 = s_vt.n
        pe0 = s_pe.n
        wc0 = s_wc.n
        for gt in range(16):
            M[("yT", c, gt)] = vt0 + gt + 1
            M[("yev", c, gt)] = pe0 + gt + 1
        for oc in range(8):
            M[("wcv", c, oc)] = wc0 + oc + 1

        @blk.sync
        def _(sync, c=c):
            for oc in range(8):
                sp = oc % 2
                if oc >= 2:
                    sync.wait_ge(s_wc.h, M[("wcv", c, oc - 2)])
                elif c == 1:
                    if L == 0:
                        sync.wait_ge(s_wc.h, M[("gucv_last", 9)])
                    else:
                        sync.wait_ge(s_wc.h, M[("w2cv", 63)])
                for kc in range(8):
                    self.dma(sync, self.s_wp[sp],
                             self.wcst[:, sp * D + kc * 128:sp * D + (kc + 1) * 128],
                             self.wc_in[kc * 128:(kc + 1) * 128, oc * 128:(oc + 1) * 128])
                M[("wcd", c, oc)] = self.s_wp[sp].n

        @blk.gpsimd
        def _(gp, c=c):
            gp.wait_ge(self.s_gq.h, self.GC)
            for oc in range(8):
                sp = oc % 2
                gp.wait_ge(self.s_wp[sp].h, M[("wcd", c, oc)])
                if c == 1 and oc < 2:
                    gp.wait_ge(s_pj.h, M[("wcmm", 0)])
                self.s_wc.inc(gp.tensor_copy(
                    self.wcbf[:, oc * D:(oc + 1) * D],
                    self.wcst[:, sp * D:(sp + 1) * D]))
                assert M[("wcv", c, oc)] == self.s_wc.n

        for tg in range(4):
            @blk.sync
            def _(sync, c=c, tg=tg):
                for tt in range(4):
                    gt = tg * 4 + tt
                    par = gt % 2
                    if c == 0 and gt == 0 and L == 0:
                        sync.wait_ge(self.s_gq.h, self.GC)
                    if c == 1 and gt == 0:
                        sync.wait_ge(self.s_x1p[1].h, M[("x1", 0, 15)])
                        sync.wait_ge(self.s_x1p[0].h, M[("x1", 0, 14)])
                        sync.wait_ge(self.s_vm.h, M[("x1vd", 0)])
                        if L == 0:
                            sync.wait_ge(self.s_odp[1].h, M[("xnd", 0, 15)])
                            sync.wait_ge(self.s_odp[0].h, M[("xnd", 0, 14)])
                    if gt >= 2:
                        sync.wait_ge(s_vt.h, M[("yT", c, gt - 2)])
                    self.dma(sync, self.s_yp[par],
                             self.yst[:, par * D:(par + 1) * D],
                             self.y_in[c * CHT + gt * 128:c * CHT + (gt + 1) * 128, :])
                    M[("ydma", c, gt)] = self.s_yp[par].n

            @blk.tensor
            def _(t, c=c, tg=tg):
                t.wait_ge(self.s_c.h, self.CONSTS)
                for tt in range(4):
                    gt = tg * 4 + tt
                    par = gt % 2
                    t.wait_ge(self.s_yp[gt % 2].h, M[("ydma", c, gt)])
                    bb = 6 + gt % 2
                    if gt >= 2:
                        t.wait_ge(s_pe.h, M[("yev", c, gt - 2)])
                    else:
                        G.acquire(t, [bb])
                    for kc in range(8):
                        ins = t.transpose(self.bkb(bb, kc * 128, (kc + 1) * 128),
                                          self.yst[:, par * D + kc * 128:par * D + (kc + 1) * 128],
                                          self.ident[:, :])
                    s_vt.inc(ins)
                    assert M[("yT", c, gt)] == s_vt.n

            @blk.scalar
            def _(sc, c=c, tg=tg):
                for tt in range(4):
                    gt = tg * 4 + tt
                    bb = 6 + gt % 2
                    sc.wait_ge(s_vt.h, M[("yT", c, gt)])
                    dst = self.yfm[:, :].rearrange("p (k t) -> p k t", k=8)[:, :, tt * 128:(tt + 1) * 128]
                    src = self.bkb(bb, 0, 1024).rearrange("p (k t) -> p k t", k=8)
                    s_pe.inc(sc.copy(dst, src))
                    G.release([bb], s_pe)
                    assert M[("yev", c, gt)] == s_pe.n
                M[("yfm", c, tg)] = s_pe.n

            @blk.tensor
            def _(t, c=c, tg=tg):
                t.wait_ge(s_pe.h, M[("yfm", c, tg)])
                for ocp in range(4):
                    gidx = tg * 4 + ocp
                    bks = [2 * (gidx % 2), 2 * (gidx % 2) + 1]
                    if gidx >= 2:
                        t.wait_ge(self.s_acc.h, M[("wacc", c, gidx - 2)])
                    else:
                        G.acquire(t, bks)
                    for oc2 in range(2):
                        oc = ocp * 2 + oc2
                        t.wait_ge(s_wc.h, M[("wcv", c, oc)])
                        for kc in range(8):
                            ins = t.matmul(
                                self.bk(bks[oc2]),
                                self.wcbf[:, oc * D + kc * 128:oc * D + (kc + 1) * 128],
                                self.yfm[:, kc * 512:(kc + 1) * 512],
                                start=(kc == 0), stop=(kc == 7))
                        s_pj.inc(ins)
                    M[("wcg", c, gidx)] = s_pj.n
                    M[("wacc", c, gidx)] = self.acc_pred(2)

            @blk.scalar
            def _(sc, c=c, tg=tg):
                for ocp in range(4):
                    gidx = tg * 4 + ocp
                    bks = [2 * (gidx % 2), 2 * (gidx % 2) + 1]
                    sc.wait_ge(s_pj.h, M[("wcg", c, gidx)])
                    if c == 1 and gidx == 0:
                        sc.wait_ge(s_vt.h, M[("tailT", 0)])
                    for oc2 in range(2):
                        oc = ocp * 2 + oc2
                        dv = self.acc[:, oc * CHT + tg * 512:oc * CHT + (tg + 1) * 512]
                        if L == 1:
                            ins = sc.activation(dv, self.bk(bks[oc2]), AF.Identity,
                                                bias=self.bof[:, oc:oc + 1])
                        else:
                            ins = sc.activation(dv, self.bk(bks[oc2]), AF.Copy)
                        self.s_acc.inc(ins)
                    G.release(bks, self.s_acc)
                    assert M[("wacc", c, gidx)] == self.s_acc.n
        M[("wcmm", c)] = self.s_pj.n
        M[("wcdone", c)] = self.s_acc.n

    def acc_pred(self, k):
        if not hasattr(self, "_accp"):
            self._accp = 0
        self._accp += k
        return self._accp

    def gating(self, c):
        L, blk, G, M = self.L, self.blk, self.G, self.M
        s_pj, s_vm = self.s_pj, self.s_vm
        if c == 1:
            @blk.sync
            def _(sync):
                if L == 0:
                    sync.wait_ge(s_pj.h, M[("gumm_last", 9)])
                else:
                    sync.wait_ge(s_pj.h, M[("hmm", 63)])
                for i in range(8):
                    self.dma(sync, self.s_x, self.xn[:, i * CHT:(i + 1) * CHT],
                             self.xn_in[:, i * T + CHT:(i + 1) * T])
            self.XN0 = self.s_x.n
        vm0 = s_vm.n
        ex0p = self.s_ex.n
        perx = 2 if L == 0 else 1
        for tt in range(16):
            M[("lgv", c, tt)] = vm0 + 12 * tt + 4
            M[("lgw", c, tt)] = vm0 + 12 * (tt + 1)
            M[("lge", c, tt)] = ex0p + perx * (tt + 1)

        @blk.tensor
        def _(t, c=c):
            t.wait_ge(self.s_gq.h, self.GC)
            t.wait_ge(self.s_x.h, self.XN0)
            for tt in range(16):
                bb = 4 + tt % 2
                if tt >= 2:
                    t.wait_ge(s_vm.h, M[("lgv", c, tt - 2)])
                    if L == 0:
                        t.wait_ge(self.s_ex.h, M[("lge", c, tt - 2)])
                else:
                    G.acquire(t, [bb])
                for kc in range(8):
                    ins = t.matmul(self.bk(bb)[:, 0:8],
                                   self.xn[:, kc * CHT + tt * 128:
                                           kc * CHT + (tt + 1) * 128],
                                   self.gwbf[:, kc * 8:kc * 8 + 8],
                                   start=(kc == 0), stop=(kc == 7 and L == 0))
                if L == 1:
                    ins = t.matmul(self.bk(bb)[:, 0:8], self.ones1[0:1, :],
                                   self.rbr[0:1, :], start=False, stop=True)
                s_pj.inc(ins)
                if L == 0:
                    for kc in range(8):
                        ins = t.matmul(self.bk(bb)[:, 8:9],
                                       self.xn[:, kc * CHT + tt * 128:
                                               kc * CHT + (tt + 1) * 128],
                                       self.gwbf[:, 64 + kc:65 + kc],
                                       start=(kc == 0), stop=(kc == 7))
                    s_pj.inc(ins)
                M[("lgmm", c, tt)] = s_pj.n

        @blk.vector
        def _(v, c=c):
            for tt in range(16):
                bb = 4 + tt % 2
                o = (tt % 2) * 32
                v.wait_ge(s_pj.h, M[("lgmm", c, tt)])
                lsv = self.ls[:, o:o + 8]
                mm = s_vm.inc(v.tensor_copy(lsv, self.bk(bb)[:, 0:8]))
                v.wait_ge(s_vm.h, mm)
                mm = s_vm.inc(v.reduce_max(self.ls[:, o + 8:o + 9], lsv, axis=AX.X))
                v.wait_ge(s_vm.h, mm)
                mm = s_vm.inc(v.tensor_scalar(self.ls[:, o + 9:o + 10],
                                              self.ls[:, o + 8:o + 9], -1.0, None, ALU.mult))
                v.wait_ge(s_vm.h, mm)
                mm = s_vm.inc(v.tensor_scalar(self.ls[:, o + 10:o + 18], lsv,
                                              self.ls[:, o + 9:o + 10], None, ALU.add))
                assert M[("lgv", c, tt)] == mm
                if c == 1 and tt < 2:
                    v.wait_ge(self.s_vt.h, M[("wrTend", 0)])
                v.wait_ge(self.s_ex.h, M[("lge", c, tt)])
                ev = self.ls[:, o + 10:o + 18]
                t8 = self.ls[:, o + 18:o + 26]
                mm = s_vm.inc(v.tensor_scalar(t8, lsv, self.ls[:, o + 8:o + 9],
                                              -1e30, ALU.is_ge, ALU.mult))
                v.wait_ge(s_vm.h, mm)
                mm = s_vm.inc(v.tensor_tensor(t8, t8, lsv, ALU.add))
                v.wait_ge(s_vm.h, mm)
                mm = s_vm.inc(v.reduce_max(self.ls[:, o + 26:o + 27], t8, axis=AX.X))
                v.wait_ge(s_vm.h, mm)
                mm = s_vm.inc(v.tensor_scalar(t8, lsv, self.ls[:, o + 26:o + 27],
                                              None, ALU.is_ge))
                v.wait_ge(s_vm.h, mm)
                if L == 0:
                    mm = s_vm.inc(v.reduce_sum(self.ls[:, o + 27:o + 28], ev, axis=AX.X))
                    v.wait_ge(s_vm.h, mm)
                    mm = s_vm.inc(v.reciprocal(self.ls[:, o + 28:o + 29],
                                               self.ls[:, o + 27:o + 28]))
                    v.wait_ge(s_vm.h, mm)
                    mm = s_vm.inc(v.tensor_scalar(ev, ev, self.ls[:, o + 28:o + 29],
                                                  None, ALU.mult))
                    v.wait_ge(s_vm.h, mm)
                    mm = s_vm.inc(v.tensor_tensor(self.w_all[:, tt * 8:(tt + 1) * 8],
                                                  ev, t8, ALU.mult))
                else:
                    mm = s_vm.inc(v.tensor_tensor(ev, ev, t8, ALU.mult))
                    v.wait_ge(s_vm.h, mm)
                    mm = s_vm.inc(v.reduce_sum(self.ls[:, o + 27:o + 28], ev, axis=AX.X))
                    v.wait_ge(s_vm.h, mm)
                    mm = s_vm.inc(v.reciprocal(self.ls[:, o + 28:o + 29],
                                               self.ls[:, o + 27:o + 28]))
                    v.wait_ge(s_vm.h, mm)
                    mm = s_vm.inc(v.tensor_scalar(self.w_all[:, tt * 8:(tt + 1) * 8],
                                                  ev, self.ls[:, o + 28:o + 29],
                                                  None, ALU.mult))
                assert M[("lgw", c, tt)] == mm
            M[("gatedone", c)] = s_vm.n

        @blk.scalar
        def _(sc, c=c):
            for tt in range(16):
                o = (tt % 2) * 32
                sc.wait_ge(s_vm.h, M[("lgv", c, tt)])
                self.s_ex.inc(sc.activation(self.ls[:, o + 10:o + 18],
                                            self.ls[:, o + 10:o + 18], AF.Exp))
                if L == 0:
                    bb = 4 + tt % 2
                    self.s_ex.inc(sc.activation(self.w_all[:, 128 + tt:129 + tt],
                                                self.bk(bb)[:, 8:9], AF.Sigmoid))
                assert M[("lge", c, tt)] == self.s_ex.n

    def expert(self, c, e):
        L, blk, G, M = self.L, self.blk, self.G, self.M
        s_pj, s_vt, s_w, s_wc = self.s_pj, self.s_vt, self.s_w, self.s_wc
        s_ex, s_ve, s_vm = self.s_ex, self.s_ve, self.s_vm
        ei = c * (10 if L == 0 else 8) + e

        shared = (L == 0 and e >= 8)
        if not (shared and e == 9):
            @blk.tensor
            def _(t, c=c, e=e):
                t.wait_ge(s_vm.h, M[("gatedone", c)])
                for half in range(2):
                    bb = 4 + half
                    G.acquire(t, [bb])
                    for tt8 in range(8):
                        tt = half * 8 + tt8
                        col = (128 + tt) if shared else (tt * 8 + e)
                        ins = t.transpose(self.bkb(bb, tt8 * 128, (tt8 + 1) * 128, 0, 1),
                                          self.w_all[:, col:col + 1],
                                          self.ident[:, :])
                    s_vt.inc(ins)
                    M[("wrT", ei, half)] = s_vt.n

            @blk.vector
            def _(v, c=c, e=e):
                for half in range(2):
                    bb = 4 + half
                    v.wait_ge(s_vt.h, M[("wrT", ei, half)])
                    s_ve.inc(v.tensor_copy(self.wrow[0:1, half * 1024:(half + 1) * 1024],
                                           self.bkb(bb, 0, 1024, 0, 1)))
                    G.release([bb], s_ve)
                M[("wrow", ei)] = s_ve.n

            @blk.tensor
            def _(t, c=c, e=e):
                t.wait_ge(s_ve.h, M[("wrow", ei)])
                ve0w = s_ve.n
                for tg in range(4):
                    M[("wbcp", ei, tg)] = ve0w + tg + 1
                for tg in range(4):
                    bb = 4 + tg % 2
                    if tg >= 2:
                        t.wait_ge(s_ve.h, M[("wbcp", ei, tg - 2)])
                    else:
                        G.acquire(t, [bb])
                    s_pj.inc(t.matmul(self.bk(bb), self.ones1[0:1, :],
                                      self.wrow[0:1, tg * 512:(tg + 1) * 512],
                                      start=True, stop=True))
                    M[("wbmm", ei, tg)] = s_pj.n

            @blk.vector
            def _(v, c=c, e=e):
                for tg in range(4):
                    bb = 4 + tg % 2
                    v.wait_ge(s_pj.h, M[("wbmm", ei, tg)])
                    s_ve.inc(v.tensor_copy(self.wb_sb[:, tg * 512:(tg + 1) * 512],
                                           self.bk(bb)))
                    G.release([bb], s_ve)
                    assert M[("wbcp", ei, tg)] == s_ve.n
                M[("wb", ei)] = s_ve.n
        else:
            M[("wb", ei)] = M[("wb", ei - 1)]

        M[("wrTend", c)] = s_vt.n

        if L == 0:
            self.expert_l0(c, e, ei, shared)
        else:
            self.expert_l1(c, e, ei)

    def expert_l0(self, c, e, ei, shared):
        blk, G, M = self.blk, self.G, self.M
        s_pj, s_w, s_wc = self.s_pj, self.s_w, self.s_wc
        s_ex, s_ve = self.s_ex, self.s_ve
        sh = e - 8 if shared else 0
        wcb = s_wc.n
        for j in range(11):
            M[("gucv", ei, j)] = wcb + j + 1
        M[("gucv_last", ei)] = wcb + 11
        for ob in range(4):
            M[("dncv", ei, ob)] = wcb + 11 + ob + 1
        M[("dncv_last", ei)] = wcb + 15
        pjb = s_pj.n
        for git in range(22):
            M[("gumm2", ei, git)] = pjb + git + 1
        for j in range(11):
            M[("gumm", ei, j)] = pjb + 2 * (j + 1)
        M[("gumm_last", ei)] = pjb + 22
        pjb2 = pjb + 22
        for dit in range(32):
            M[("dnmm2", ei, dit)] = pjb2 + dit + 1
        for ob in range(4):
            M[("dnmm", ei, ob)] = pjb2 + 8 * (ob + 1)
        M[("dnmm_last", ei)] = pjb2 + 32

        @blk.sync
        def _(sync, c=c, e=e):
            for j in range(11):
                if j == 0 and e == 0:
                    sync.wait_ge(s_wc.h, M[("wcv", c, 7)])
                if (ei, j) != (0, 0):
                    sync.wait_ge(s_wc.h, M[("gucv", ei, j - 1) if j > 0 else ("gucv_last", ei - 1)])
                for kc in range(8):
                    if shared:
                        gsrc = self.sg_in[kc * 128:(kc + 1) * 128,
                                          sh * 1408 + j * 128:sh * 1408 + (j + 1) * 128]
                        usrc = self.su_in[kc * 128:(kc + 1) * 128,
                                          sh * 1408 + j * 128:sh * 1408 + (j + 1) * 128]
                    else:
                        gsrc = self.egu_in[e, kc * 128:(kc + 1) * 128, j * 128:(j + 1) * 128]
                        usrc = self.egu_in[e, kc * 128:(kc + 1) * 128,
                                           1408 + j * 128:1408 + (j + 1) * 128]
                    self.dma(sync, s_w, self.gust[:, kc * 256:kc * 256 + 128], gsrc)
                    self.dma(sync, s_w, self.gust[:, kc * 256 + 128:(kc + 1) * 256], usrc)
                M[("gud", ei, j)] = s_w.n

        @blk.gpsimd
        def _(gp, c=c, e=e):
            for j in range(11):
                par = j % 2
                gp.wait_ge(s_w.h, M[("gud", ei, j)])
                if j >= 2:
                    gp.wait_ge(s_pj.h, M[("gumm", ei, j - 2)])
                elif ei > 0:
                    gp.wait_ge(s_pj.h, M[("gumm_last", ei - 1)])
                s_wc.inc(gp.tensor_copy(self.gubf[:, par * 2048:(par + 1) * 2048],
                                        self.gust[:, :]))
                assert M[("gucv", ei, j)] == s_wc.n

        @blk.tensor
        def _(t, c=c, e=e):
            ve0 = s_ve.n
            for git in range(22):
                M[("hfold", ei, git)] = ve0 + git + 1
            git = 0
            for j in range(11):
                par = j % 2
                t.wait_ge(s_wc.h, M[("gucv", ei, j)])
                if j == 0:
                    t.wait_ge(s_ve.h, M[("wb", ei)])
                for tg2 in range(2):
                    q = git % 2
                    bks = [4 * q, 4 * q + 1, 4 * q + 2, 4 * q + 3]
                    if git >= 2:
                        t.wait_ge(s_ve.h, M[("hfold", ei, git - 2)])
                    else:
                        G.acquire(t, bks)
                    for kc in range(8):
                        for t2 in range(2):
                            ins = t.matmul(
                                self.bk(bks[t2]),
                                self.gubf[:, par * 2048 + kc * 256:par * 2048 + kc * 256 + 128],
                                self.xn[:, kc * CHT + tg2 * 1024 + t2 * 512:
                                        kc * CHT + tg2 * 1024 + (t2 + 1) * 512],
                                start=(kc == 0), stop=(kc == 7))
                        for t2 in range(2):
                            ins = t.matmul(
                                self.bk(bks[2 + t2]),
                                self.gubf[:, par * 2048 + kc * 256 + 128:par * 2048 + (kc + 1) * 256],
                                self.xn[:, kc * CHT + tg2 * 1024 + t2 * 512:
                                        kc * CHT + tg2 * 1024 + (t2 + 1) * 512],
                                start=(kc == 0), stop=(kc == 7))
                    s_pj.inc(ins)
                    assert M[("gumm2", ei, git)] == s_pj.n
                    git += 1

        @blk.scalar
        def _(sc, c=c, e=e):
            git = 0
            for j in range(11):
                for tg2 in range(2):
                    q = git % 2
                    sc.wait_ge(s_pj.h, M[("gumm2", ei, git)])
                    if ei > 0 and git == 0:
                        sc.wait_ge(s_pj.h, M[("dnmm_last", ei - 1)])
                    if git >= 2:
                        sc.wait_ge(s_ve.h, M[("hfold", ei, git - 2)])
                    s_ex.inc(sc.activation(self.sgev[:, q * 1024:(q + 1) * 1024],
                                           self.ps[:, 4 * q * 512:(4 * q + 2) * 512],
                                           AF.Silu))
                    M[("silu", ei, git)] = s_ex.n
                    git += 1

        @blk.vector
        def _(v, c=c, e=e):
            git = 0
            for j in range(11):
                for tg2 in range(2):
                    q = git % 2
                    v.wait_ge(s_ex.h, M[("silu", ei, git)])
                    if git >= 1:
                        v.wait_ge(s_ve.h, M[("hfold", ei, git - 1)])
                    mm = self.s_vm.inc(v.tensor_mul(
                        self.hmid[:, :], self.sgev[:, q * 1024:(q + 1) * 1024],
                        self.ps[:, (4 * q + 2) * 512:(4 * q + 4) * 512]))
                    v.wait_ge(self.s_vm.h, mm)
                    hv = self.h_sb[:, :].rearrange("p (j t) -> p j t", j=11)[:, j, tg2 * 1024:(tg2 + 1) * 1024]
                    ins = v.tensor_mul(hv, self.hmid[:, :],
                                       self.wb_sb[:, tg2 * 1024:(tg2 + 1) * 1024])
                    s_ve.inc(ins)
                    assert M[("hfold", ei, git)] == s_ve.n
                    git += 1
            M[("hdone", ei)] = s_ve.n

        @blk.sync
        def _(sync, c=c, e=e):
            for ob in range(4):
                if (ei, ob) != (0, 0):
                    sync.wait_ge(s_wc.h, M[("dncv", ei, ob - 1) if ob > 0 else ("dncv_last", ei - 1)])
                for ic in range(11):
                    if shared:
                        src = self.sd_in[sh * 1408 + ic * 128:sh * 1408 + (ic + 1) * 128,
                                         ob * 256:(ob + 1) * 256]
                    else:
                        src = self.edn_in[e, ic * 128:(ic + 1) * 128, ob * 256:(ob + 1) * 256]
                    self.dma(sync, self.s_wd, self.dnst[:, ic * 256:(ic + 1) * 256], src)
                M[("dnd", ei, ob)] = self.s_wd.n

        @blk.gpsimd
        def _(gp, c=c, e=e):
            for ob in range(4):
                par = ob % 2
                gp.wait_ge(self.s_wd.h, M[("dnd", ei, ob)])
                if ob >= 2:
                    gp.wait_ge(s_pj.h, M[("dnmm", ei, ob - 2)])
                elif ei > 0:
                    gp.wait_ge(s_pj.h, M[("dnmm_last", ei - 1)])
                s_wc.inc(gp.tensor_copy(self.dnbf[:, par * 2816:(par + 1) * 2816],
                                        self.dnst[:, :]))
                assert M[("dncv", ei, ob)] == s_wc.n

        @blk.tensor
        def _(t, c=c, e=e):
            t.wait_ge(s_ve.h, M[("hdone", ei)])
            ve0d = s_ve.n
            for dit in range(32):
                M[("dacc", ei, dit)] = ve0d + dit + 1
            dit = 0
            for ob in range(4):
                par = ob % 2
                t.wait_ge(s_wc.h, M[("dncv", ei, ob)])
                for oc2 in range(2):
                    for tg in range(4):
                        bb = dit % 4
                        if dit >= 4:
                            t.wait_ge(s_ve.h, M[("dacc", ei, dit - 4)])
                        else:
                            G.acquire(t, [bb])
                        for ic in range(11):
                            ins = t.matmul(
                                self.bk(bb),
                                self.dnbf[:, par * 2816 + ic * 256 + oc2 * 128:
                                          par * 2816 + ic * 256 + (oc2 + 1) * 128],
                                self.h_sb[:, :].rearrange("p (j t) -> p j t", j=11)[:, ic, tg * 512:(tg + 1) * 512],
                                start=(ic == 0), stop=(ic == 10))
                        s_pj.inc(ins)
                        assert M[("dnmm2", ei, dit)] == s_pj.n
                        dit += 1

        @blk.vector
        def _(v, c=c, e=e):
            v.wait_ge(self.s_acc.h, M[("wcdone", c)])
            dit = 0
            for ob in range(4):
                for oc2 in range(2):
                    oc = ob * 2 + oc2
                    for tg in range(4):
                        bb = dit % 4
                        v.wait_ge(s_pj.h, M[("dnmm2", ei, dit)])
                        av = self.acc[:, oc * CHT + tg * 512:oc * CHT + (tg + 1) * 512]
                        ins = v.tensor_add(av, av, self.bk(bb))
                        self.s_ve.inc(ins)
                        assert M[("dacc", ei, dit)] == self.s_ve.n
                        dit += 1
            M[("eadd", ei)] = self.s_ve.n
        G.release([0, 1, 2, 3], self.s_ve)


    def expert_l1(self, c, e, ei):
        blk, G, M = self.blk, self.G, self.M
        s_pj, s_w, s_wc = self.s_pj, self.s_w, self.s_wc
        s_ex, s_ve = self.s_ex, self.s_ve

        @blk.sync
        def _(sync, c=c, e=e):
            if ei > 0:
                sync.wait_ge(self.s_ex.h, M[("hrelu_last", ei - 1)])
            self.dma(sync, self.s_wd, self.b1f[:, :], self.b1_in[e])
            self.dma(sync, self.s_wd, self.b2st[0:1, :], self.b2_in[e:e + 1, :])
            M[("bd", ei)] = self.s_wd.n

        @blk.gpsimd
        def _(gp, c=c, e=e):
            gp.wait_ge(self.s_wd.h, M[("bd", ei)])
            if ei > 0:
                gp.wait_ge(s_pj.h, M[("w2mm_last", ei - 1)])
            s_wc.inc(gp.tensor_copy(self.b2r[:, :], self.b2st[:, :]))
            M[("b2c", ei)] = s_wc.n

        for fcg in range(8):
            fi = ei * 8 + fcg

            @blk.sync
            def _(sync, c=c, e=e, fcg=fcg, fi=fi):
                if fi % 64 == 0:
                    sync.wait_ge(s_wc.h, M[("wcv", c, 7)])
                if fi > 0:
                    sync.wait_ge(s_wc.h, M[("w2cv", fi - 1)])
                for kc in range(8):
                    self.dma(sync, s_w, self.gust[:, kc * 512:(kc + 1) * 512],
                             self.w1_in[e, kc * 128:(kc + 1) * 128,
                                        fcg * 512:(fcg + 1) * 512])
                M[("w1d", fi)] = s_w.n

            @blk.gpsimd
            def _(gp, c=c, e=e, fcg=fcg, fi=fi):
                gp.wait_ge(s_w.h, M[("w1d", fi)])
                if fi >= 2:
                    gp.wait_ge(s_pj.h, M[("hmm", fi - 2)])
                s_wc.inc(gp.tensor_copy(self.gubf[:, (fi % 2) * 4096:(fi % 2 + 1) * 4096],
                                        self.gust[:, :]))
                M[("w1cv", fi)] = s_wc.n

            @blk.tensor
            def _(t, c=c, e=e, fcg=fcg, fi=fi):
                ex0 = s_ex.n
                ve0 = s_ve.n
                for hit in range(16):
                    M[("hrelu", fi, hit)] = ex0 + hit + 1
                    M[("hcf", fi, hit)] = ve0 + hit + 1
                t.wait_ge(s_wc.h, M[("w1cv", fi)])
                if fcg == 0:
                    t.wait_ge(s_ve.h, M[("wb", ei)])
                hit = 0
                for fc in range(4):
                    for tg in range(4):
                        bb = hit % 4
                        if hit >= 4:
                            t.wait_ge(s_ve.h, M[("hcf", fi, hit - 4)])
                        else:
                            G.acquire(t, [bb])
                        for kc in range(8):
                            ins = t.matmul(
                                self.bk(bb),
                                self.gubf[:, (fi % 2) * 4096 + kc * 512 + fc * 128:
                                          (fi % 2) * 4096 + kc * 512 + (fc + 1) * 128],
                                self.xn[:, kc * CHT + tg * 512:
                                        kc * CHT + (tg + 1) * 512],
                                start=(kc == 0), stop=(kc == 7))
                        s_pj.inc(ins)
                        M[("hmm2", fi, hit)] = s_pj.n
                        hit += 1
                M[("hmm", fi)] = s_pj.n

            @blk.scalar
            def _(sc, c=c, e=e, fcg=fcg, fi=fi):
                hit = 0
                for fc in range(4):
                    for tg in range(4):
                        bb = hit % 4
                        sc.wait_ge(s_pj.h, M[("hmm2", fi, hit)])
                        if fcg == 0 and hit == 0 and ei > 0:
                            sc.wait_ge(s_pj.h, M[("w2mm_last", ei - 1)])
                        if fcg > 0 and hit == 0:
                            sc.wait_ge(s_pj.h, M[("w2mm", ei, fcg - 1)])
                        hv = self.h_sb[:, :].rearrange("p (f t) -> p f t", f=4)[:, fc, tg * 512:(tg + 1) * 512]
                        s_ex.inc(sc.activation(hv, self.bk(bb), AF.Relu,
                                               bias=self.b1f[:, fcg * 4 + fc:fcg * 4 + fc + 1]))
                        assert M[("hrelu", fi, hit)] == s_ex.n
                        hit += 1
                M[("hrelu_last", ei)] = s_ex.n

            @blk.vector
            def _(v, c=c, e=e, fcg=fcg, fi=fi):
                hit = 0
                for fc in range(4):
                    for tg in range(4):
                        v.wait_ge(s_ex.h, M[("hrelu", fi, hit)])
                        hv = self.h_sb[:, :].rearrange("p (f t) -> p f t", f=4)[:, fc, tg * 512:(tg + 1) * 512]
                        s_ve.inc(v.tensor_mul(hv, hv, self.wb_sb[:, tg * 512:(tg + 1) * 512]))
                        assert M[("hcf", fi, hit)] == s_ve.n
                        hit += 1
                M[("hcfd", fi)] = s_ve.n

            @blk.sync
            def _(sync, c=c, e=e, fcg=fcg, fi=fi):
                sync.wait_ge(s_wc.h, M[("w1cv", fi)])
                for fkc in range(4):
                    self.dma(sync, s_w, self.gust[:, fkc * 1024:(fkc + 1) * 1024],
                             self.w2_in[e, fcg * 512 + fkc * 128:fcg * 512 + (fkc + 1) * 128, :])
                M[("w2d", fi)] = s_w.n

            @blk.gpsimd
            def _(gp, c=c, e=e, fcg=fcg, fi=fi):
                gp.wait_ge(s_w.h, M[("w2d", fi)])
                if fi >= 2:
                    gp.wait_ge(s_pj.h, M[("w2mmf", fi - 2)])
                s_wc.inc(gp.tensor_copy(self.dnbf[:, (fi % 2) * 4096:(fi % 2 + 1) * 4096],
                                        self.gust[:, :]))
                M[("w2cv", fi)] = s_wc.n

            @blk.tensor
            def _(t, c=c, e=e, fcg=fcg, fi=fi):
                ve0d = s_ve.n
                for dit in range(32):
                    M[("w2acc", fi, dit)] = ve0d + dit + 1
                t.wait_ge(s_wc.h, M[("w2cv", fi)])
                t.wait_ge(s_ve.h, M[("hcfd", fi)])
                if ei > 0 and fcg == 0:
                    t.wait_ge(s_wc.h, M[("b2c", ei)])
                dit = 0
                for oc in range(8):
                    for tg in range(4):
                        bb = dit % 4
                        if dit >= 4:
                            t.wait_ge(s_ve.h, M[("w2acc", fi, dit - 4)])
                        else:
                            G.acquire(t, [bb])
                        for fkc in range(4):
                            ins = t.matmul(
                                self.bk(bb),
                                self.dnbf[:, (fi % 2) * 4096 + fkc * 1024 + oc * 128:
                                          (fi % 2) * 4096 + fkc * 1024 + (oc + 1) * 128],
                                self.h_sb[:, :].rearrange("p (f t) -> p f t", f=4)[:, fkc, tg * 512:(tg + 1) * 512],
                                start=(fkc == 0), stop=(fkc == 3 and fcg > 0))
                        if fcg == 0:
                            ins = t.matmul(self.bk(bb), self.b2r[0:1, oc * 128:(oc + 1) * 128],
                                           self.ones512[0:1, :], start=False, stop=True)
                        s_pj.inc(ins)
                        M[("w2mm2", fi, dit)] = s_pj.n
                        dit += 1
                M[("w2mmf", fi)] = s_pj.n
                M[("w2mm", ei, fcg)] = s_pj.n
                if fcg == 7:
                    M[("w2mm_last", ei)] = s_pj.n

            @blk.vector
            def _(v, c=c, e=e, fcg=fcg, fi=fi):
                v.wait_ge(self.s_acc.h, M[("wcdone", c)])
                dit = 0
                for oc in range(8):
                    for tg in range(4):
                        bb = dit % 4
                        v.wait_ge(s_pj.h, M[("w2mm2", fi, dit)])
                        av = self.acc[:, oc * CHT + tg * 512:oc * CHT + (tg + 1) * 512]
                        s_ve.inc(v.tensor_add(av, av, self.bk(bb)))
                        assert M[("w2acc", fi, dit)] == s_ve.n
                        dit += 1
            G.release([0, 1, 2, 3], s_ve)
        M[("eadd", ei)] = s_ve.n

    def tail(self, c):
        L, blk, G, M = self.L, self.blk, self.G, self.M
        s_pj, s_vt, s_vm, s_sm = self.s_pj, self.s_vt, self.s_vm, self.s_sm
        s_xr, s_x1, s_ln, s_od = self.s_xr, self.s_x1, self.s_ln, self.s_od
        nexp = 10 if L == 0 else 8
        ADD_DONE = M[("eadd", c * nexp + nexp - 1)]

        x1_0p = [self.s_x1p[0].n, self.s_x1p[1].n]
        gq0 = self.s_gq.n
        sm0 = self.s_sm.n
        vt0 = s_vt.n
        ln0 = self.s_ln.n
        for tt in range(16):
            M[("x1", c, tt)] = x1_0p[tt % 2] + 16 * (tt // 2 + 1)
            M[("sq", c, tt)] = gq0 + tt + 1
            M[("lnsq", c, tt)] = sm0 + 2 * tt + 1
            M[("lnnorm", c, tt)] = sm0 + 2 * tt + 2
            if L == 0:
                M[("accT", c, tt)] = vt0 + (1 if tt == 0 else 2 * tt)
                M[("xnT", c, tt)] = vt0 + (2 * tt + 3 if tt <= 13 else (31 if tt == 14 else 32))
            else:
                M[("accT", c, tt)] = vt0 + tt + 1
            M[("xne", c, tt)] = ln0 + tt + 1
        vm0 = s_vm.n
        if L == 0:
            NVT = 9
            for tt in range(16):
                M[("x1v", c, tt)] = vm0 + NVT * tt + 1
                M[("lnr1", c, tt)] = vm0 + NVT * tt + 2
                M[("lnv", c, tt)] = vm0 + NVT * (tt + 1)
        else:
            for tt in range(16):
                M[("x1v", c, tt)] = vm0 + tt + 1
        xrm = []

        @blk.sync
        def _(sync, c=c):
            for tt in range(16):
                par = tt % 2
                if tt == 0:
                    key = ("dnmm_last", c * 10 + 9) if L == 0 else ("w2mm_last", c * 8 + 7)
                    sync.wait_ge(s_pj.h, M[key])
                if tt >= 2:
                    if L == 0:
                        sync.wait_ge(s_vm.h, M[("lnr1", c, tt - 2)])
                    else:
                        sync.wait_ge(s_vm.h, M[("x1v", c, tt - 2)])
                self.dma(sync, self.s_xp[par], self.xrst[:, par * D:(par + 1) * D],
                         self.res_in[c * CHT + tt * 128:c * CHT + (tt + 1) * 128, :])
                xrm.append(self.s_xp[par].n)

        def emit_xnT(t, c, tt):
            par = tt % 2
            bb = tt % 2
            t.wait_ge(self.s_sm.h, M[("lnnorm", c, tt)])
            if tt < 2:
                G.acquire(t, [bb])
            else:
                t.wait_ge(self.s_ln.h, M[("xne", c, tt - 2)])
            for cch in range(8):
                ins = t.transpose(self.bkb(bb, cch * 128, (cch + 1) * 128),
                                  self.xnt[:, par * D + cch * 128:par * D + (cch + 1) * 128],
                                  self.ident[:, :])
            s_vt.inc(ins)
            assert M[("xnT", c, tt)] == s_vt.n

        @blk.tensor
        def _(t, c=c):
            for tt in range(16):
                if tt == 0:
                    t.wait_ge(self.s_ve.h, ADD_DONE)
                bp = 4 + 2 * (tt % 2)
                if tt >= 2:
                    t.wait_ge(s_vm.h, M[("x1v", c, tt - 2)])
                else:
                    G.acquire(t, [bp, bp + 1])
                for oc in range(8):
                    ins = t.transpose(
                        self.ps[:, bp * 512 + oc * 128:bp * 512 + (oc + 1) * 128],
                        self.acc[:, oc * CHT + tt * 128:oc * CHT + (tt + 1) * 128],
                        self.identf[:, :])
                s_vt.inc(ins)
                assert M[("accT", c, tt)] == s_vt.n
                if L == 0 and tt >= 1:
                    emit_xnT(t, c, tt - 1)
            if L == 0:
                emit_xnT(t, c, 15)

        @blk.vector
        def _(v, c=c):
            for tt in range(16):
                par = tt % 2
                bp = 4 + 2 * (tt % 2)
                v.wait_ge(s_vt.h, M[("accT", c, tt)])
                v.wait_ge(self.s_xp[par].h, xrm[tt])
                if tt >= 2:
                    v.wait_ge(self.s_x1p[tt % 2].h, M[("x1", c, tt - 2)])
                    if L == 0:
                        v.wait_ge(self.s_gq.h, M[("sq", c, tt - 2)])
                        v.wait_ge(self.s_sm.h, M[("lnnorm", c, tt - 2)])
                mm = s_vm.inc(v.tensor_add(
                    self.x1t[:, par * D:(par + 1) * D],
                    self.xrst[:, par * D:(par + 1) * D],
                    self.ps[:, bp * 512:(bp + 2) * 512]))
                assert M[("x1v", c, tt)] == mm
                if L == 0:
                    self.tail_ln_vec(v, c, tt)
            M[("x1vd", c)] = s_vm.n
        G.release([4, 5, 6, 7], s_vm)
        M[("tailT", c)] = self.s_vt.n

        if L == 0:
            self.tail_ln(c)
        else:
            @blk.scalar
            def _(sc, c=c):
                for tt in range(16):
                    par = tt % 2
                    sc.wait_ge(s_vm.h, M[("x1v", c, tt)])
                    self.dma(sc, self.s_x1p[par],
                             self.x1_out[c * CHT + tt * 128:c * CHT + (tt + 1) * 128, :],
                             self.x1t[:, par * D:(par + 1) * D])
                    assert M[("x1", c, tt)] == self.s_x1p[par].n

    def tail_ln_vec(self, v, c, tt):
        M, lst = self.M, self.lst
        s_vm = self.s_vm
        par = tt % 2
        o = par * 12
        xv = self.x1t[:, par * D:(par + 1) * D]
        qv = self.xrst[:, par * D:(par + 1) * D]
        v.wait_ge(self.s_gq.h, M[("sq", c, tt)])
        v.reduce_sum(lst[:, o:o + 1], xv, axis=AX.X)
        mm = s_vm.inc(v.reduce_sum(lst[:, o + 1:o + 2], qv, axis=AX.X))
        assert M[("lnr1", c, tt)] == mm
        v.wait_ge(s_vm.h, mm)
        v.wait_ge(s_vm.h, s_vm.inc(v.tensor_scalar(
            lst[:, o + 2:o + 3], lst[:, o:o + 1], 1.0 / D, None, ALU.mult)))
        v.tensor_tensor(lst[:, o + 3:o + 4], lst[:, o + 2:o + 3],
                        lst[:, o + 2:o + 3], ALU.mult)
        v.wait_ge(s_vm.h, s_vm.inc(v.tensor_scalar(
            lst[:, o + 4:o + 5], lst[:, o + 1:o + 2], 1.0 / D, None, ALU.mult)))
        v.wait_ge(s_vm.h, s_vm.inc(v.tensor_tensor(
            lst[:, o + 5:o + 6], lst[:, o + 4:o + 5], lst[:, o + 3:o + 4], ALU.subtract)))
        v.wait_ge(s_vm.h, s_vm.inc(v.tensor_scalar(
            lst[:, o + 6:o + 7], lst[:, o + 5:o + 6], 1.0, 1e-5, ALU.mult, ALU.add)))
        s_vm.inc(v.reciprocal(lst[:, o + 7:o + 8], lst[:, o + 6:o + 7]))
        v.wait_ge(self.s_sm.h, M[("lnsq", c, tt)])
        v.wait_ge(s_vm.h, s_vm.inc(v.tensor_tensor(
            lst[:, o + 9:o + 10], lst[:, o + 2:o + 3], lst[:, o + 8:o + 9], ALU.mult)))
        mm = s_vm.inc(v.tensor_scalar(
            lst[:, o + 10:o + 11], lst[:, o + 9:o + 10], -1.0, None, ALU.mult))
        assert M[("lnv", c, tt)] == mm

    def tail_ln(self, c):
        blk, G, M = self.blk, self.G, self.M
        s_vm, s_sm, s_ln = self.s_vm, self.s_sm, self.s_ln
        lst = self.lst

        @blk.gpsimd
        def _(gp, c=c):
            for tt in range(16):
                par = tt % 2
                gp.wait_ge(s_vm.h, M[("x1v", c, tt)])
                if tt >= 2:
                    gp.wait_ge(s_vm.h, M[("lnr1", c, tt - 2)])
                self.s_gq.inc(gp.tensor_tensor(self.xrst[:, par * D:(par + 1) * D],
                                               self.x1t[:, par * D:(par + 1) * D],
                                               self.x1t[:, par * D:(par + 1) * D], ALU.mult))
                assert M[("sq", c, tt)] == self.s_gq.n

        def emit_xne(sc, c, tt):
            par = tt % 2
            bb = tt % 2
            sc.wait_ge(self.s_vt.h, M[("xnT", c, tt)])
            if tt >= 2:
                sc.wait_ge(self.s_odp[tt % 2].h, M[("xnd", c, tt - 2)])
            self.s_ln.inc(sc.copy(self.xne[:, par * 1024:(par + 1) * 1024],
                                  self.bkb(bb, 0, 1024)))
            assert M[("xne", c, tt)] == self.s_ln.n
            sc.wait_ge(self.s_ln.h, self.s_ln.n)
            gt = c * 16 + tt
            dst = self.xn1_out[:, :].rearrange("p (k t) -> p k t", k=8)[:, :, gt * 128:(gt + 1) * 128]
            srcv = self.xne[:, par * 1024:(par + 1) * 1024].rearrange("p (k t) -> p k t", k=8)
            self.dma(sc, self.s_odp[tt % 2], dst, srcv)
            M[("xnd", c, tt)] = self.s_odp[tt % 2].n

        @blk.scalar
        def _(sc, c=c):
            s_x1 = self.s_x1
            for tt in range(16):
                par = tt % 2
                o = par * 12
                sc.wait_ge(s_vm.h, M[("x1v", c, tt)])
                self.dma(sc, self.s_x1p[par],
                         self.x1_out[c * CHT + tt * 128:c * CHT + (tt + 1) * 128, :],
                         self.x1t[:, par * D:(par + 1) * D])
                assert M[("x1", c, tt)] == self.s_x1p[par].n
                sc.wait_ge(s_vm.h, M[("lnr1", c, tt)] + 5)
                s1 = self.s_sm.inc(sc.activation(lst[:, o + 8:o + 9],
                                                 lst[:, o + 7:o + 8], AF.Sqrt))
                assert M[("lnsq", c, tt)] == s1
                sc.wait_ge(self.s_sm.h, s1)
                sc.wait_ge(s_vm.h, M[("lnv", c, tt)])
                if tt >= 2:
                    sc.wait_ge(self.s_vt.h, M[("xnT", c, tt - 2)])
                s2 = self.s_sm.inc(sc.activation(self.xnt[:, par * D:(par + 1) * D],
                                                 self.x1t[:, par * D:(par + 1) * D], AF.Identity,
                                                 bias=lst[:, o + 10:o + 11],
                                                 scale=lst[:, o + 8:o + 9]))
                assert M[("lnnorm", c, tt)] == s2
                if tt >= 1:
                    emit_xne(sc, c, tt - 1)
            emit_xne(sc, c, 15)
        G.release([0, 1], self.s_ln)

    def epilogue(self):
        blk = self.blk
        t0p = self.s_x1p[0].n
        t1p = self.s_x1p[1].n
        o0 = self.s_odp[0].n
        o1 = self.s_odp[1].n

        @blk.sync
        def _(sync):
            sync.wait_ge(self.s_x1p[0].h, t0p)
            sync.wait_ge(self.s_x1p[1].h, t1p)
            if o0:
                sync.wait_ge(self.s_odp[0].h, o0)
            if o1:
                sync.wait_ge(self.s_odp[1].h, o1)


def build_mix(L):
    b = MixBuilder(L)
    return b.nc



_NC_CACHE = {}


def _get_nc(kind, L):
    key = (kind, L)
    if key not in _NC_CACHE:
        _NC_CACHE[key] = build_attn(L) if kind == "attn" else build_mix(L)
    return _NC_CACHE[key]


def kernel(**inputs):
    consts = _host_consts()
    prep = _host_prep(inputs)
    x = np.ascontiguousarray(np.asarray(inputs["L0_attn_x"], np.float32))

    cores = list(range(NCORES))

    nc = _get_nc("attn", 0)
    maps = []
    for ci in cores:
        m = {k: consts[k] for k in ("c_ident", "c_negones", "c_ones64", "c_csn", "c_mask")}
        m.update({"x": x[ci], "wdq": prep["l0_wdq"], "wdk": prep["l0_wdk"],
                  "wdv": prep["l0_wdv"], "up": prep["l0_up"]})
        maps.append(m)
    r1 = run_bass_kernel_spmd(nc, maps, core_ids=cores)
    xn0 = [r["xn_fm_o"] for r in r1.results]
    y0 = [r["y_tm"] for r in r1.results]

    nc = _get_nc("mix", 0)
    maps = []
    for ci in cores:
        m = {k: consts[k] for k in ("c_ident", "c_identf", "c_ones1", "c_ones512")}
        m.update({"xn_fm": xn0[ci], "y_tm": y0[ci], "res": x[ci],
                  "wc": prep["l0_wc"], "gate_wT": prep["l0_gate_wT"],
                  "eguT": prep["l0_eguT"], "edownT": prep["l0_edownT"],
                  "sgT": prep["l0_sgT"], "suT": prep["l0_suT"],
                  "sdT": prep["l0_sdT"], "segT": prep["l0_segT"]})
        maps.append(m)
    r2 = run_bass_kernel_spmd(nc, maps, core_ids=cores)
    x1 = [r["x1"] for r in r2.results]
    xn1 = [r["xn1_fm"] for r in r2.results]

    nc = _get_nc("attn", 1)
    maps = []
    for ci in cores:
        m = {k: consts[k] for k in ("c_ident", "c_negones", "c_ones64")}
        m.update({"xn_fm": xn1[ci], "wtq": prep["l1_wtq"], "wtk": prep["l1_wtk"],
                  "wtv": prep["l1_wtv"], "bq": prep["l1_bq"], "bk": prep["l1_bk"],
                  "bv": prep["l1_bv"]})
        maps.append(m)
    r3 = run_bass_kernel_spmd(nc, maps, core_ids=cores)
    y1 = [r["y_tm"] for r in r3.results]

    nc = _get_nc("mix", 1)
    maps = []
    for ci in cores:
        m = {k: consts[k] for k in ("c_ident", "c_identf", "c_ones1", "c_ones512")}
        m.update({"xn_fm": xn1[ci], "y_tm": y1[ci], "res": x1[ci],
                  "wto": prep["l1_wto"], "bo": prep["l1_bo"],
                  "router_w": prep["l1_router_w"], "router_b": prep["l1_router_b"],
                  "w1": prep["l1_w1"], "b1": prep["l1_b1"],
                  "w2": prep["l1_w2"], "b2": prep["l1_b2"]})
        maps.append(m)
    r4 = run_bass_kernel_spmd(nc, maps, core_ids=cores)
    out = np.stack([r["out"] for r in r4.results], axis=0)
    return out.astype(np.float32)


# version 03
# speedup vs baseline: 1.1587x; 1.1587x over previous
import numpy as np
import ml_dtypes
from contextlib import ExitStack

import concourse.bass as bass
import concourse.mybir as mybir
from concourse.bass_utils import run_bass_kernel_spmd

F32 = mybir.dt.float32
BF16 = mybir.dt.bfloat16
AF = mybir.ActivationFunctionType
ALU = mybir.AluOpType
AX = mybir.AxisListType
BF = ml_dtypes.bfloat16

T = 4096
D = 1024
CHT = 2048
NCORES = 8
import os as _os
DBG_PAIRS = int(_os.environ.get("DBG_PAIRS", "16"))
DBG_ATT = int(_os.environ.get("DBG_ATT", "1"))
DBG_HEADS = int(_os.environ.get("DBG_HEADS", "1"))
DBG_VT = int(_os.environ.get("DBG_VT", "1"))
DBG_VUP = int(_os.environ.get("DBG_VUP", "1"))
DBG_VT_MS = int(_os.environ.get("DBG_VT_MS", "1"))
DBG_VT_T = int(_os.environ.get("DBG_VT_T", "1"))
DBG_VT_EV = int(_os.environ.get("DBG_VT_EV", "1"))
DBG_LN0 = int(_os.environ.get("DBG_LN0", "1"))
DBG_VT_N = int(_os.environ.get("DBG_VT_N", "99"))


class Sem:
    __slots__ = ("h", "n")

    def __init__(self, ctx, nc, name):
        self.h = ctx.enter_context(nc.semaphore(name))
        self.n = 0

    def inc(self, instr, k=1):
        instr.then_inc(self.h, k)
        self.n += k
        return self.n


class BankGate:

    def __init__(self):
        self.rel = {}

    def acquire(self, eng, ids):
        need = {}
        for b in ids:
            for s, cnt in self.rel.get(b, ()):
                if need.get(s, 0) < cnt:
                    need[s] = cnt
        for s, cnt in need.items():
            eng.wait_ge(s.h, cnt)
        for b in ids:
            self.rel[b] = []

    def release(self, ids, sem):
        for b in ids:
            self.rel.setdefault(b, []).append((sem, sem.n))


class LaunchBase:
    def __init__(self, name, parent=None, io=None, entry=None):
        self.parent = parent
        self.io = io or {}
        self.entry = entry or []
        if parent is None:
            self.nc = bass.Bass("TRN2", target_bir_lowering=False)
            self.sempool = {}
            self.semctx = None
        else:
            self.nc = parent.nc
            self.sempool = parent.sempool
            self.semctx = parent.ctx
        self.ctx = ExitStack()
        self.ctx.__enter__()
        if parent is None:
            self.semctx = self.ctx
        self.name = name
        self.M = {}

    def din(self, name, shape, dt=F32):
        if name in self.io:
            return self.io[name]
        return self.nc.dram_tensor(name, shape, dt, kind="ExternalInput").ap()

    def dout(self, name, shape, dt=F32):
        if name in self.io:
            return self.io[name]
        return self.nc.dram_tensor(name, shape, dt, kind="ExternalOutput").ap()

    def sb(self, name, shape, dt):
        return self.ctx.enter_context(self.nc.sbuf_tensor(self.name + "_" + name, shape, dt))

    def psum_(self, name, shape, dt=F32):
        return self.ctx.enter_context(self.nc.psum_tensor(self.name + "_" + name, shape, dt))

    def sem(self, name):
        if name not in self.sempool:
            self.sempool[name] = Sem(self.semctx, self.nc, name)
        return self.sempool[name]

    def emit_entry(self, eng):
        for s, cnt in self.entry:
            eng.wait_ge(s.h, cnt)

    def dma(self, eng, sem, dst, srcap, **kw):
        bb = self.nc.cur_bb.bb
        n0 = len(bb.instructions)
        ins = eng.dma_start(dst, srcap, **kw)
        nsub = sum(1 for i in bb.instructions[n0:]
                   if type(i).__name__ == "InstDMACopy")
        ins.then_inc(sem.h, 16)
        sem.n += 16 * max(1, nsub)
        return ins

    def finish(self):
        self.ctx.__exit__(None, None, None)
        return self.nc


def _host_consts():
    c = {}
    c["c_ident"] = np.eye(128).astype(BF)
    c["c_identf"] = np.eye(128, dtype=np.float32)
    c["c_negones"] = np.full((1, 128), -1.0, np.float32).astype(BF)
    c["c_ones1"] = np.ones((1, 128), np.float32).astype(BF)
    c["c_ones512"] = np.ones((1, 512), np.float32).astype(BF)
    o64 = np.zeros((128, 2), np.float32)
    o64[0:64, 0] = 1.0
    o64[64:128, 1] = 1.0
    c["c_ones64"] = o64.astype(BF)
    DHE = 32
    inv_freq = 1.0 / (10000.0 ** (np.arange(0, DHE, 2, dtype=np.float64) / DHE))
    freqs = np.arange(T, dtype=np.float64)[:, None] * inv_freq[None, :]
    emb = np.concatenate([freqs, freqs], axis=-1)
    cos = np.cos(emb)
    sinp = np.sin(emb) * np.where(np.arange(DHE) % 2 == 0, -1.0, 1.0)[None, :]
    csn = np.zeros((64, 2 * T), np.float64)
    csn[32:64, 0:T] = cos.T
    csn[32:64, T:2 * T] = sinp.T
    c["c_csn"] = csn.astype(BF)
    m = np.zeros((128, 4 * 512), np.float32)
    ii = np.arange(128)[:, None]
    jj = np.arange(512)[None, :]
    for r in range(4):
        m[:, r * 512:(r + 1) * 512] = (jj >= ii + 128 * r).astype(np.float32)
    c["c_mask"] = m.astype(BF)
    return c


def _host_prep(inputs):
    p = {}
    f32 = np.float32

    def T2(a):
        return np.ascontiguousarray(np.asarray(a, f32).T)

    p["l0_wdq"] = np.ascontiguousarray(np.asarray(inputs["L0_attn_Wq_down"], f32))
    p["l0_wdk"] = np.ascontiguousarray(np.asarray(inputs["L0_attn_Wk_down"], f32))
    p["l0_wdv"] = np.ascontiguousarray(np.asarray(inputs["L0_attn_Wv_down"], f32))

    def rep(a):
        a = np.asarray(a, f32)
        return np.concatenate([a, a], axis=0)

    def swap_cols(a):
        a = np.asarray(a, f32)
        b = np.empty_like(a)
        b[:, 0::2] = a[:, 1::2]
        b[:, 1::2] = a[:, 0::2]
        return b

    up = np.zeros((128, 256), f32)
    up[:, 0:32] = rep(inputs["L0_attn_Wq_up_c"])
    up[:, 32:64] = rep(inputs["L0_attn_Wq_up_e"])
    up[:, 64:96] = rep(swap_cols(inputs["L0_attn_Wq_up_e"]))
    up[:, 96:128] = rep(inputs["L0_attn_Wk_up_c"])
    up[:, 128:160] = rep(inputs["L0_attn_Wk_up_e"])
    up[:, 160:192] = rep(swap_cols(inputs["L0_attn_Wk_up_e"]))
    up[:, 192:256] = rep(inputs["L0_attn_Wv_up"])
    p["l0_up"] = np.ascontiguousarray(up)

    p["l0_wc"] = np.ascontiguousarray(np.asarray(inputs["L0_attn_Wc"], f32))
    p["l0_gate_wT"] = T2(inputs["L0_ffn_gate_w"])
    p["l0_eguT"] = np.ascontiguousarray(
        np.transpose(np.asarray(inputs["L0_ffn_experts_gate_up"], f32), (0, 2, 1)))
    p["l0_edownT"] = np.ascontiguousarray(
        np.transpose(np.asarray(inputs["L0_ffn_experts_down"], f32), (0, 2, 1)))
    p["l0_sgT"] = T2(inputs["L0_ffn_shared_gate_w"])
    p["l0_suT"] = T2(inputs["L0_ffn_shared_up_w"])
    p["l0_sdT"] = T2(inputs["L0_ffn_shared_down_w"])
    p["l0_segT"] = T2(inputs["L0_ffn_shared_expert_gate_w"])

    p["l1_wtq"] = T2(inputs["L1_attn_Wq"])
    p["l1_wtk"] = T2(inputs["L1_attn_Wk"])
    p["l1_wtv"] = T2(inputs["L1_attn_Wv"])
    p["l1_wto"] = T2(inputs["L1_attn_Wo"])

    def fm8(b):
        return np.ascontiguousarray(np.asarray(b, f32).reshape(8, 128).T)

    p["l1_bq"] = fm8(inputs["L1_attn_bq"])
    p["l1_bk"] = fm8(inputs["L1_attn_bk"])
    p["l1_bv"] = fm8(inputs["L1_attn_bv"])
    p["l1_bo"] = fm8(inputs["L1_attn_bo"])

    p["l1_router_w"] = np.ascontiguousarray(np.asarray(inputs["L1_ffn_router_w"], f32))
    p["l1_router_b"] = np.ascontiguousarray(np.asarray(inputs["L1_ffn_router_b"], f32).reshape(1, 8))
    p["l1_w1"] = np.ascontiguousarray(np.asarray(inputs["L1_ffn_W1"], f32))
    p["l1_w2"] = np.ascontiguousarray(np.asarray(inputs["L1_ffn_W2"], f32))
    p["l1_b1"] = np.ascontiguousarray(
        np.asarray(inputs["L1_ffn_b1"], f32).reshape(8, 32, 128).transpose(0, 2, 1))
    p["l1_b2"] = np.ascontiguousarray(np.asarray(inputs["L1_ffn_b2"], f32))
    return p



class AttnBuilder(LaunchBase):
    def __init__(self, L, parent=None, io=None, entry=None):
        super().__init__(f"attn{L}", parent=parent, io=io, entry=entry)
        self.L = L
        self.build()

    def bk(self, i, r0=0, r1=128):
        return self.ps[r0:r1, i * 512:(i + 1) * 512]

    def bkb(self, i, c0, c1, r0=0, r1=128):
        return self.psb[r0:r1, i * 1024 + c0:i * 1024 + c1]

    def mview(self, hh, a, b):
        off = hh * CHT if self.L == 1 else 0
        return self.mrow[0:1, off + a:off + b]

    def build(self):
        L = self.L
        B = self

        self.ident_in = B.din("c_ident", [128, 128], BF16)
        self.negones_in = B.din("c_negones", [1, 128], BF16)
        self.ones64_in = B.din("c_ones64", [128, 2], BF16)
        if L == 0:
            self.x_in = B.din("x", [T, D], F32)
            self.wq_in = B.din("wdq", [D, D], F32)
            self.wk_in = B.din("wdk", [D, D], F32)
            self.wv_in = B.din("wdv", [D, D], F32)
            self.up_in = B.din("up", [128, 256], F32)
            self.csn_in = B.din("c_csn", [64, 2 * T], BF16)
            self.mask_in = B.din("c_mask", [128, 2048], BF16)
            self.xn_out = B.dout("xn_fm_o", [128, 8 * T], BF16)
        else:
            self.xn_in = B.din("xn_fm", [128, 8 * T], BF16)
            self.wq_in = B.din("wtq", [D, D], F32)
            self.wk_in = B.din("wtk", [D, D], F32)
            self.wv_in = B.din("wtv", [D, D], F32)
            self.bq_in = B.din("bq", [128, 8], F32)
            self.bk_in = B.din("bk", [128, 8], F32)
            self.bv_in = B.din("bv", [128, 8], F32)
        self.y_out = B.dout("y_tm", [T, D], BF16)

        self.xn = B.sb("xn", [128, 8 * T], BF16)
        self.y_tm = B.sb("y_tm_s", [128, 16 * D], BF16)
        self.ident = B.sb("ident", [128, 128], BF16)
        self.negones = B.sb("negones", [1, 128], BF16)
        self.ones64 = B.sb("ones64", [128, 2], BF16)
        self.wbf = B.sb("wbf", [128, 2 * 3 * D], BF16)
        if L == 0:
            self.xst = B.sb("xst", [128, 2 * D], F32)
            self.wst = self.xst
        else:
            self.wst = B.sb("wst", [128, 2 * D], F32)
        if L == 0:
            self.xsq = B.sb("xsq", [128, 2 * D], F32)
            self.xnt = B.sb("xnt", [128, 2 * D], BF16)
            self.lst = B.sb("lst", [128, 24], F32)
            self.csn = B.sb("csn", [64, 2 * T], BF16)
            self.maskb = B.sb("maskb", [128, 2048], BF16)
            self.upf = B.sb("upf", [128, 256], F32)
            self.upb = B.sb("upb", [128, 256], BF16)
            self.qs = B.sb("qs", [128, CHT], BF16)
            self.ks = B.sb("ks", [128, T], BF16)
            self.vs = B.sb("vs", [128, T], BF16)
            self.q_h = B.sb("q_h", [64, CHT], BF16)
            self.k_h = B.sb("k_h", [64, T], BF16)
            self.rtmp = B.sb("rtmp", [64, 1024], BF16)
            self.v2f = B.sb("v2f", [128, T], BF16)
        else:
            self.bfm = B.sb("bfm", [128, 24], F32)
            self.qs = B.sb("qs", [128, CHT], BF16)
            self.ks = B.sb("ks", [128, T], BF16)
            self.v2f = B.sb("v2f", [128, T], BF16)
        if L == 0:
            self.sq = self.xsq.bitcast(BF16)
        else:
            self.sq = B.sb("sqb", [128, T], BF16)
        self.v_tm = B.sb("v_tm", [128, 32 * 130], BF16)
        self.PT = B.sb("PT", [128, 2 * 1024], BF16)
        self.yev = B.sb("yev", [65, 512], BF16)
        if L == 0:
            self.mrow = self.xnt
        else:
            self.mrow = B.sb("mrow", [1, 2 * CHT], BF16)
        self.kst = B.sb("kst", [128, 16], F32)

        self.ps = B.psum_("ps", [128, 6 * 512], F32)
        self.psb = self.ps.bitcast(BF16)
        self.G = BankGate()

        for nm in ("s_c s_x s_gq s_ln s_xe s_vm s_sm s_w s_wc s_pj s_pe s_ua "
                   "s_vt s_ve s_s s_ex s_mk s_pv s_ye s_yd s_nr s_sq").split():
            setattr(self, nm, B.sem(nm))
        self.s_xp = [B.sem("s_xp0"), B.sem("s_xp1")]
        self.s_wp = [B.sem("s_wp0"), B.sem("s_wp1")]
        self.s_ys = B.sem("s_ys")
        self.s_vo = B.sem("s_vo")

        if self.parent is None:
            self.blk = self.ctx.enter_context(self.nc.Block())
        else:
            self.blk = self.parent.blk
        self.phase0()
        if L == 0:
            self.ln0()
        npair = 0
        for c in range(2):
            for p in range(8):
                if npair < DBG_PAIRS:
                    self.pair(c, p)
                    npair += 1
            if npair >= (c + 1) * 8:
                self.chunk_tail(c)
        self.epilogue()
        self.finish()

    def phase0(self):
        L, blk = self.L, self.blk
        s_c, s_x = self.s_c, self.s_x

        @blk.sync
        def _(sync):
            self.dma(sync, s_c, self.ident[:, :], self.ident_in)
            self.dma(sync, s_c, self.negones[:, :], self.negones_in)
            self.dma(sync, s_c, self.ones64[:, :], self.ones64_in)
            if L == 0:
                self.dma(sync, s_c, self.csn[:, :], self.csn_in)
                self.dma(sync, s_c, self.maskb[:, :], self.mask_in)
                self.dma(sync, s_c, self.upf[:, :], self.up_in)
            else:
                for i, b in enumerate((self.bq_in, self.bk_in, self.bv_in)):
                    self.dma(sync, s_c, self.bfm[:, 8 * i:8 * i + 8], b)
                for i in range(8):
                    self.dma(sync, s_x, self.xn[:, i * T:(i + 1) * T],
                             self.xn_in[:, i * T:(i + 1) * T])
        self.CONSTS = s_c.n
        self.XN0 = s_x.n
        self.UPC = self.s_wc.n

        if L == 0:
            s_wc = self.s_wc

            @blk.gpsimd
            def _(gp):
                gp.wait_ge(s_c.h, self.CONSTS)
                s_wc.inc(gp.tensor_copy(self.upb[:, :], self.upf[:, :]))
            self.UPC = s_wc.n

    def ln0(self):
        blk = self.blk
        self.LNB = self.s_ln.n
        self.XEB = self.s_xe.n
        GQ0 = self.s_gq.n
        SM0 = self.s_sm.n
        VT0 = self.s_vt.n
        if not DBG_LN0:
            self.s_xe.n += 32
            self.s_ln.n += 32

            @blk.scalar
            def _(sc):
                sc.sem_inc(self.s_xe.h, 32)
                sc.sem_inc(self.s_ln.h, 32)

            @blk.gpsimd
            def _(gp):
                self.s_gq.inc(gp.memset(self.sq[64:128, :], 0.0))
            self.SQZ = self.s_gq.n
            return
        s_x, s_gq, s_vm, s_sm, s_ln, s_vt, s_xe = (
            self.s_x, self.s_gq, self.s_vm, self.s_sm, self.s_ln, self.s_vt, self.s_xe)
        XN0 = self.XN0
        xst, xsq, xnt, lst, xn = self.xst, self.xsq, self.xnt, self.lst, self.xn
        VM0 = s_vm.n

        xm = []

        @blk.sync
        def _(sync):
            for tt in range(32):
                par = tt % 2
                if tt >= 2:
                    sync.wait_ge(s_ln.h, self.LNB + tt - 1)
                self.dma(sync, self.s_xp[par], xst[:, par * D:(par + 1) * D],
                         self.x_in[tt * 128:(tt + 1) * 128, :])
                xm.append(self.s_xp[par].n)

        @blk.gpsimd
        def _(gp):
            for tt in range(32):
                par = tt % 2
                gp.wait_ge(self.s_xp[tt % 2].h, xm[tt])
                if tt >= 2:
                    gp.wait_ge(s_vm.h, VM0 + 8 * (tt - 2) + 1)
                s_gq.inc(gp.tensor_tensor(xsq[:, par * D:(par + 1) * D],
                                          xst[:, par * D:(par + 1) * D],
                                          xst[:, par * D:(par + 1) * D], ALU.mult))
            gp.wait_ge(s_vm.h, VM0 + 8 * 31 + 1)
            s_gq.inc(gp.memset(self.sq[64:128, :], 0.0))
            self.SQZ = s_gq.n

        @blk.vector
        def _(v):
            for tt in range(32):
                par = tt % 2
                o = par * 12
                xv = xst[:, par * D:(par + 1) * D]
                qv = xsq[:, par * D:(par + 1) * D]
                v.wait_ge(self.s_xp[tt % 2].h, xm[tt])
                v.wait_ge(s_gq.h, GQ0 + tt + 1)
                v.reduce_sum(lst[:, o:o + 1], xv, axis=AX.X)
                v.wait_ge(s_vm.h, s_vm.inc(v.reduce_sum(lst[:, o + 1:o + 2], qv, axis=AX.X)))
                v.wait_ge(s_vm.h, s_vm.inc(v.tensor_scalar(
                    lst[:, o + 2:o + 3], lst[:, o:o + 1], 1.0 / D, None, ALU.mult)))
                v.tensor_tensor(lst[:, o + 3:o + 4], lst[:, o + 2:o + 3],
                                lst[:, o + 2:o + 3], ALU.mult)
                v.wait_ge(s_vm.h, s_vm.inc(v.tensor_scalar(
                    lst[:, o + 4:o + 5], lst[:, o + 1:o + 2], 1.0 / D, None, ALU.mult)))
                v.wait_ge(s_vm.h, s_vm.inc(v.tensor_tensor(
                    lst[:, o + 5:o + 6], lst[:, o + 4:o + 5], lst[:, o + 3:o + 4], ALU.subtract)))
                v.wait_ge(s_vm.h, s_vm.inc(v.tensor_scalar(
                    lst[:, o + 6:o + 7], lst[:, o + 5:o + 6], 1.0, 1e-5, ALU.mult, ALU.add)))
                s_vm.inc(v.reciprocal(lst[:, o + 7:o + 8], lst[:, o + 6:o + 7]))
                v.wait_ge(s_sm.h, SM0 + tt + 1)
                v.wait_ge(s_vm.h, s_vm.inc(v.tensor_tensor(
                    lst[:, o + 9:o + 10], lst[:, o + 2:o + 3], lst[:, o + 8:o + 9], ALU.mult)))
                v.wait_ge(s_vm.h, s_vm.inc(v.tensor_scalar(
                    lst[:, o + 10:o + 11], lst[:, o + 9:o + 10], -1.0, None, ALU.mult)))
        NV = 8

        @blk.scalar
        def _(sc):
            for tt in range(32):
                par = tt % 2
                o = par * 12
                sc.wait_ge(s_vm.h, VM0 + NV * tt + 6)
                s_sm.inc(sc.activation(lst[:, o + 8:o + 9], lst[:, o + 7:o + 8], AF.Sqrt))
                sc.wait_ge(s_sm.h, s_sm.n)
                sc.wait_ge(s_vm.h, VM0 + NV * tt + 8)
                if tt >= 2:
                    sc.wait_ge(s_vt.h, VT0 + tt - 1)
                s_ln.inc(sc.activation(xnt[:, par * D:(par + 1) * D],
                                       xst[:, par * D:(par + 1) * D], AF.Identity,
                                       bias=lst[:, o + 10:o + 11], scale=lst[:, o + 8:o + 9]))
                if tt >= 1:
                    sc.wait_ge(s_vt.h, VT0 + tt)
                    src = self.bkb(4 + (tt - 1) % 2, 0, 1024)
                    dst = xn[:, :].rearrange("p (c t) -> p c t", c=8)[:, :, (tt - 1) * 128:tt * 128]
                    s_xe.inc(sc.copy(dst, src))
            sc.wait_ge(s_vt.h, VT0 + 32)
            src = self.bkb(4 + 31 % 2, 0, 1024)
            dst = xn[:, :].rearrange("p (c t) -> p c t", c=8)[:, :, 31 * 128:32 * 128]
            s_xe.inc(sc.copy(dst, src))

        @blk.tensor
        def _(t):
            t.wait_ge(self.s_c.h, self.CONSTS)
            for tt in range(32):
                par = tt % 2
                t.wait_ge(s_ln.h, self.LNB + tt + 1)
                if tt >= 2:
                    t.wait_ge(s_xe.h, self.XEB + tt - 1)
                for cch in range(8):
                    ins = t.transpose(self.bkb(4 + tt % 2, cch * 128, (cch + 1) * 128),
                                      xnt[:, par * D + cch * 128:par * D + (cch + 1) * 128],
                                      self.ident[:, :])
                s_vt.inc(ins)

        @blk.scalar
        def _(sc):
            sc.wait_ge(s_xe.h, self.XEB + 32)
            for i in range(8):
                self.dma(sc, self.s_yd, self.xn_out[:, i * T:(i + 1) * T],
                         xn[:, i * T:(i + 1) * T])
        self.XNDMA = self.s_yd.n

    def pair(self, c, p):
        L, blk, G, M = self.L, self.blk, self.G, self.M
        kend = 2048 if (L == 0 and c == 0) else 4096
        nkt = kend // 128
        qbase = c * CHT
        it = c * 8 + p
        par = it % 2
        xn = self.xn
        s_w, s_wc, s_pj, s_pe = self.s_w, self.s_wc, self.s_pj, self.s_pe

        @blk.sync
        def _(sync):
            if L == 0 and it == 0:
                sync.wait_ge(self.s_ln.h, self.LNB + 32)
            wc0 = self.UPC
            for mi, w in enumerate((self.wk_in, self.wv_in, self.wq_in)):
                gmi = it * 3 + mi
                sp = gmi % 2
                if gmi >= 2:
                    sync.wait_ge(s_wc.h, wc0 + gmi - 1)
                for kc in range(8):
                    self.dma(sync, self.s_wp[sp],
                             self.wst[:, sp * D + kc * 128:sp * D + (kc + 1) * 128],
                             w[kc * 128:(kc + 1) * 128, p * 128:(p + 1) * 128])
                M[("wdma", gmi)] = self.s_wp[sp].n

        @blk.gpsimd
        def _(gp):
            for mi in range(3):
                gmi = it * 3 + mi
                sp = gmi % 2
                gp.wait_ge(self.s_wp[gmi % 2].h, M[("wdma", gmi)])
                if it >= 2:
                    gp.wait_ge(s_pj.h, M[("prjdone", it - 2)])
                s_wc.inc(gp.tensor_copy(
                    self.wbf[:, (par * 3 + mi) * D:(par * 3 + mi + 1) * D],
                    self.wst[:, sp * D:(sp + 1) * D]))
                M[("wcv", gmi)] = s_wc.n

        mats = [("k", kend), ("v", kend), ("q", CHT)]

        @blk.tensor
        def _(t):
            t.wait_ge(self.s_c.h, self.CONSTS)
            if L == 0:
                t.wait_ge(self.s_xe.h, self.XEB + 32)
                t.wait_ge(s_wc.h, self.UPC)
            else:
                t.wait_ge(self.s_x.h, self.XN0)
            gi = 0
            pe0 = s_pe.n
            for mi, (nm, ntok) in enumerate(mats):
                gmi = it * 3 + mi
                t.wait_ge(s_wc.h, M[("wcv", gmi)])
                absb = qbase if nm == "q" else 0
                for g in range(ntok // 1024):
                    bks = [2 * (gi % 2), 2 * (gi % 2) + 1]
                    if gi >= 2:
                        t.wait_ge(s_pe.h, pe0 + 2 * (gi - 1))
                    else:
                        G.acquire(t, bks)
                    for kc in range(8):
                        for t2 in range(2):
                            ins = t.matmul(
                                self.bk(bks[t2]),
                                self.wbf[:, (par * 3 + mi) * D + kc * 128:
                                         (par * 3 + mi) * D + (kc + 1) * 128],
                                xn[:, kc * T + absb + g * 1024 + t2 * 512:
                                   kc * T + absb + g * 1024 + (t2 + 1) * 512],
                                start=(kc == 0), stop=(kc == 7))
                            if kc == 7:
                                s_pj.inc(ins)
                    M[("prj", it, mi, g)] = s_pj.n
                    gi += 1
            M[("prjdone", it)] = s_pj.n

        @blk.scalar
        def _(sc):
            gi = 0
            for mi, (nm, ntok) in enumerate(mats):
                for g in range(ntok // 1024):
                    bks = [2 * (gi % 2), 2 * (gi % 2) + 1]
                    sc.wait_ge(s_pj.h, M[("prj", it, mi, g)])
                    for t2 in range(2):
                        col = g * 1024 + t2 * 512
                        if L == 0:
                            dst = {"k": self.ks, "v": self.vs, "q": self.qs}[nm]
                            ins = sc.activation(dst[:, col:col + 512], self.bk(bks[t2]), AF.Copy)
                        else:
                            dst = {"k": self.ks, "v": self.v2f, "q": self.qs}[nm]
                            bcol = {"q": 0, "k": 8, "v": 16}[nm] + p
                            ins = sc.activation(dst[:, col:col + 512], self.bk(bks[t2]),
                                                AF.Identity, bias=self.bfm[:, bcol:bcol + 1])
                        s_pe.inc(ins)
                    G.release(bks, s_pe)
                    gi += 1
            M[("prjevac", it)] = s_pe.n

        if L == 0 and DBG_VUP:
            @blk.tensor
            def _(t):
                t.wait_ge(s_pe.h, M[("prjevac", it)])
                pe0v = s_pe.n
                for g in range(nkt // 4):
                    bb = 2 * (g % 2)
                    if g >= 2:
                        t.wait_ge(s_pe.h, pe0v + (g - 1))
                    else:
                        G.acquire(t, [bb])
                    i1 = t.matmul(self.bk(bb, 0, 64), self.upb[0:64, 192:256],
                                  self.vs[0:64, g * 512:(g + 1) * 512], start=True, stop=True)
                    s_pj.inc(i1)
                    i2 = t.matmul(self.bk(bb, 64, 128), self.upb[64:128, 192:256],
                                  self.vs[64:128, g * 512:(g + 1) * 512],
                                  start=True, stop=True, tile_position=(64, 64))
                    s_pj.inc(i2)
                    M[("vup", it, g)] = s_pj.n

            @blk.scalar
            def _(sc):
                for g in range(nkt // 4):
                    bb = 2 * (g % 2)
                    sc.wait_ge(s_pj.h, M[("vup", it, g)])
                    s_pe.inc(sc.activation(self.v2f[:, g * 512:(g + 1) * 512],
                                           self.bk(bb), AF.Copy))
                    G.release([bb], s_pe)
                M[("vupe", it)] = s_pe.n

        s_vt, s_ve = self.s_vt, self.s_ve
        if not DBG_VT:
            M[("vtmdone", it)] = self.s_ve.n
            M[("pairpv", it)] = self.s_pv.n
            if DBG_HEADS:
                for hh in range(2):
                    self.head(c, p, hh, kend, nkt, qbase, it)
            M[("pairpv", it)] = self.s_pv.n
            return

        @blk.gpsimd
        def _(gp):
            if it >= 1:
                gp.wait_ge(self.s_pv.h, M[("pairpv", it - 1)])
            if DBG_VT_MS:
                self.s_vo.inc(gp.memset(self.v_tm[:, 0:nkt * 130], 1.0))
            else:
                self.s_vo.inc(gp.engine_nop())
            M[("vones", it)] = self.s_vo.n

        @blk.tensor
        def _(t):
            vsrc = self.v2f
            wait_m = M[("vupe", it)] if (L == 0 and DBG_VUP) else M[("prjevac", it)]
            t.wait_ge(s_pe.h, wait_m)
            pe0v2 = s_pe.n
            for g4 in range(nkt // 4):
                M[("vtm", it, g4)] = pe0v2 + 2 * (g4 + 1)
            for g4 in range(nkt // 4):
                half = g4 % 2
                if g4 >= 2:
                    t.wait_ge(s_pe.h, M[("vtm", it, g4 - 2)])
                else:
                    G.acquire(t, [4 + half])
                if DBG_VT_T:
                    for j in range(4):
                        ins = t.transpose(
                            self.bkb(4 + half, j * 128, (j + 1) * 128),
                            vsrc[:, (g4 * 4 + j) * 128:(g4 * 4 + j + 1) * 128],
                            self.ident[:, :])
                else:
                    ins = t.engine_nop()
                s_vt.inc(ins)
                M[("vT", it, g4)] = s_vt.n

        @blk.scalar
        def _(sc):
            sc.wait_ge(self.s_vo.h, M[("vones", it)])
            for g4 in range(nkt // 4):
                half = g4 % 2
                sc.wait_ge(s_vt.h, M[("vT", it, g4)])
                base = (4 + half) * 1024
                for hh in range(2):
                    if g4 < DBG_VT_N:
                        for j in range(3, -1, -1):
                            kt = g4 * 4 + j
                            ins = sc.copy(
                                self.v_tm[:, kt * 130 + hh * 65:kt * 130 + hh * 65 + 64],
                                self.psb[:, base + j * 128 + hh * 64:base + j * 128 + (hh + 1) * 64])
                    else:
                        ins = sc.copy(self.kst[:, 0:1], self.kst[:, 1:2])
                    s_pe.inc(ins)
                assert M[("vtm", it, g4)] == s_pe.n
            M[("vtmdone", it)] = s_pe.n
        G.release([4, 5], s_pe)

        for hh in range(2 if DBG_HEADS else 0):
            self.head(c, p, hh, kend, nkt, qbase, it)

        M[("pairpv", it)] = self.s_pv.n

    def head(self, c, p, hh, kend, nkt, qbase, it):
        L, blk, G, M = self.L, self.blk, self.G, self.M
        h = 2 * p + hh
        hr = 64 * hh
        s_pj, s_pe, s_ua, s_sq, s_nr = self.s_pj, self.s_pe, self.s_ua, self.s_sq, self.s_nr
        s_vm, s_sm = self.s_vm, self.s_sm

        if L == 0:
            for kind, lat, dst, ntok, absb, ubase in (
                    ("k", self.ks, self.k_h, kend, 0, 96),
                    ("q", self.qs, self.q_h, CHT, qbase, 0)):
                ntt = ntok // 512

                @blk.tensor
                def _(t, kind=kind, lat=lat, ntt=ntt, absb=absb, ubase=ubase):
                    if kind == "k":
                        t.wait_ge(s_pe.h, M[("prjevac", it)])
                        if hh == 1:
                            t.wait_ge(self.s_s.h, M[("attdone", it, 0)])
                    pe0u = s_pe.n
                    ua0 = s_ua.n
                    for g in range(ntt):
                        b0 = 2 * (g % 2)
                        b1 = b0 + 1
                        if g >= 2:
                            t.wait_ge(s_pe.h, pe0u + (g - 1))
                            t.wait_ge(s_ua.h, ua0 + 2 * (g - 1))
                        else:
                            G.acquire(t, [b0, b1])
                        rhs = lat[hr:hr + 64, g * 512:(g + 1) * 512]
                        i1 = t.matmul(self.bk(b0, 0, 32), self.upb[hr:hr + 64, ubase:ubase + 32],
                                      rhs, start=True, stop=True)
                        s_pj.inc(i1)
                        i2 = t.matmul(self.bk(b0, 32, 64),
                                      self.upb[hr:hr + 64, ubase + 32:ubase + 64], rhs,
                                      start=True, stop=True, tile_position=(hr, 32))
                        s_pj.inc(i2)
                        i3 = t.matmul(self.bk(b1, 32, 64),
                                      self.upb[hr:hr + 64, ubase + 64:ubase + 96], rhs,
                                      start=True, stop=True, tile_position=(hr, 32))
                        s_pj.inc(i3)
                        M[("up", it, hh, kind, g)] = s_pj.n

                @blk.scalar
                def _(sc, kind=kind, dst=dst, ntt=ntt):
                    for g in range(ntt):
                        b0 = 2 * (g % 2)
                        sc.wait_ge(s_pj.h, M[("up", it, hh, kind, g)])
                        s_pe.inc(sc.activation(dst[0:32, g * 512:(g + 1) * 512],
                                               self.bk(b0, 0, 32), AF.Copy))
                        G.release([b0], s_pe)
                        M[("upc", it, hh, kind, g)] = s_pe.n

                @blk.vector
                def _(v, kind=kind, dst=dst, ntt=ntt, absb=absb):
                    ua0v = s_ua.n
                    for g in range(ntt):
                        b0 = 2 * (g % 2)
                        b1 = b0 + 1
                        v.wait_ge(s_pj.h, M[("up", it, hh, kind, g)])
                        v.wait_ge(s_ua.h, ua0v + 2 * g)
                        ac = absb + g * 512
                        v.tensor_mul(self.rtmp[32:64, 0:512], self.bk(b0, 32, 64),
                                     self.csn[32:64, ac:ac + 512])
                        mm = s_ua.inc(v.tensor_mul(self.rtmp[32:64, 512:1024],
                                                   self.bk(b1, 32, 64),
                                                   self.csn[32:64, T + ac:T + ac + 512]))
                        v.wait_ge(s_ua.h, mm)
                        s_ua.inc(v.tensor_add(dst[32:64, g * 512:(g + 1) * 512],
                                              self.rtmp[32:64, 0:512],
                                              self.rtmp[32:64, 512:1024]))
                        G.release([b0, b1], s_ua)
                        M[("uprope", it, hh, kind, g)] = s_ua.n
                M[(kind + "done", it, hh)] = (s_pe.n, s_ua.n)

            qsrc, ksrc, qr = self.q_h, self.k_h, 0
        else:
            qsrc, ksrc, qr = self.qs, self.ks, hr

        if L == 0:
            self.norms_k(it, hh, kend, ksrc)
            self.norms_q(it, hh, qsrc)
        else:
            if hh == 0:
                self.norms_k(it, 0, kend, ksrc)
                self.norms_k(it, 1, kend, ksrc)
                self.norms_q(it, 0, qsrc)
                self.norms_q(it, 1, qsrc)

        s_s, s_ex, s_mk, s_pv, s_ye = self.s_s, self.s_ex, self.s_mk, self.s_pv, self.s_ye

        for qg in range(4 if DBG_ATT else 0):
            qa = qbase + qg * 512
            nkt_q = min(nkt, (qa + 512) // 128) if L == 0 else nkt
            ngrp = (nkt_q + 1) // 2

            if L == 0:
                if not hasattr(self, "mk_pred"):
                    self.mk_pred = self.s_mk.n
                for j in range(4):
                    M[("mk", it, hh, qg, qa // 128 + j)] = self.mk_pred + j + 1
                self.mk_pred += 4
            if not hasattr(self, "ex_pred"):
                self.ex_pred = self.s_ex.n
            for g in range(ngrp):
                M[("exp", it, hh, qg, g)] = self.ex_pred + g + 1
            self.ex_pred += ngrp

            M[("pvprev", it, hh, qg)] = self.s_pv.n

            @blk.tensor
            def _(t, qg=qg, qa=qa, nkt_q=nkt_q, ngrp=ngrp):
                t.wait_ge(s_pe.h, M[("mrow", it, hh)])
                t.wait_ge(s_pe.h, M[("vtmdone", it)])
                for g in range(ngrp):
                    sgb = [2 * (g % 2), 2 * (g % 2) + 1]
                    if g >= 2:
                        t.wait_ge(self.s_ex.h, M[("exp", it, hh, qg, g - 2)])
                    else:
                        G.acquire(t, sgb)
                    for kt in (2 * g, 2 * g + 1):
                        if kt >= nkt_q:
                            continue
                        sub = kt % 2
                        vw = self.bk(sgb[sub])
                        t.matmul(vw, ksrc[qr:qr + 64, kt * 128:(kt + 1) * 128],
                                 qsrc[qr:qr + 64, qg * 512:(qg + 1) * 512],
                                 start=True, stop=False)
                        s_s.inc(t.matmul(vw, self.negones[0:1, :],
                                         self.mview(hh, qg * 512, (qg + 1) * 512),
                                         start=False, stop=True))
                    M[("sgrp", it, hh, qg, g)] = s_s.n
                    if g >= 1:
                        self._pv_group(t, it, hh, qg, g - 1, nkt_q, qa)
                self._pv_group(t, it, hh, qg, ngrp - 1, nkt_q, qa)
                M[("pvdone", it, hh, qg)] = s_pv.n

            @blk.scalar
            def _(sc, qg=qg, qa=qa, nkt_q=nkt_q, ngrp=ngrp):
                for g in range(ngrp):
                    sgb = [2 * (g % 2), 2 * (g % 2) + 1]
                    n_in_g = 1 if (2 * g + 1 >= nkt_q) else 2
                    sc.wait_ge(s_s.h, M[("sgrp", it, hh, qg, g)])
                    if g >= 2:
                        sc.wait_ge(s_pv.h, M[("pvg", it, hh, qg, g - 2)])
                    elif ("pvprev", it, hh, qg) in M:
                        sc.wait_ge(s_pv.h, M[("pvprev", it, hh, qg)])
                    s_ex.inc(sc.activation(self.PT[:, (g % 2) * 1024:(g % 2) * 1024 + n_in_g * 512],
                                           self.ps[:, sgb[0] * 512:sgb[0] * 512 + n_in_g * 512],
                                           AF.Exp, scale=0.125))
                    assert M[("exp", it, hh, qg, g)] == s_ex.n
                    G.release(sgb, s_ex)

            if L == 0:
                @blk.gpsimd
                def _(gp, qg=qg, qa=qa, nkt_q=nkt_q):
                    for kt in range(nkt_q):
                        ktb = kt * 128
                        if ktb < qa:
                            continue
                        r = (ktb - qa) // 128
                        g = kt // 2
                        gp.wait_ge(s_ex.h, M[("exp", it, hh, qg, g)])
                        pcol = (g % 2) * 1024 + (kt % 2) * 512
                        s_mk.inc(gp.tensor_mul(self.PT[:, pcol:pcol + 512],
                                               self.PT[:, pcol:pcol + 512],
                                               self.maskb[:, r * 512:(r + 1) * 512]))
                        assert M[("mk", it, hh, qg, kt)] == s_mk.n

            @blk.scalar
            def _(sc, qg=qg):
                sc.wait_ge(s_pv.h, M[("pvdone", it, hh, qg)])
                s_ye.inc(sc.activation(self.yev[:, :], self.bk(4, 0, 65), AF.Copy))
                M[("yev", it, hh, qg)] = s_ye.n
                G.release([4], s_ye)

            ys0 = self.s_ys.n
            for j in range(4):
                M[("ystj", it, hh, qg, j)] = ys0 + j + 1
            M[("yst", it, hh, qg)] = ys0 + 4
            vt0y = self.s_vt.n

            @blk.tensor
            def _(t, qg=qg):
                t.wait_ge(s_ye.h, M[("yev", it, hh, qg)])
                for j in range(4):
                    if j < 2:
                        G.acquire(t, [4 + j % 2])
                    else:
                        t.wait_ge(self.s_ys.h, M[("ystj", it, hh, qg, j - 2)])
                    ins = t.transpose(self.bkb(4 + j % 2, 0, 65, 0, 128),
                                      self.yev[:, j * 128:(j + 1) * 128],
                                      self.ident[0:65, 0:65])
                    self.s_vt.inc(ins)
                M[("yT", it, hh, qg)] = self.s_vt.n

            @blk.vector
            def _(v, qg=qg):
                if c == 1:
                    v.wait_ge(self.s_yd.h, M[("ydma", 0)])
                for j in range(4):
                    v.wait_ge(self.s_vt.h, vt0y + j + 1)
                    lv = self.kst[:, 12 + (j % 2):13 + (j % 2)]
                    mm = s_vm.inc(v.reciprocal(lv, self.bkb(4 + j % 2, 64, 65, 0, 128)))
                    v.wait_ge(s_vm.h, mm)
                    tt128 = qg * 4 + j
                    self.s_ys.inc(v.tensor_scalar(
                        self.y_tm[:, tt128 * D + h * 64:tt128 * D + (h + 1) * 64],
                        self.bkb(4 + j % 2, 0, 64, 0, 128),
                        lv, None, ALU.mult))
                    assert M[("ystj", it, hh, qg, j)] == self.s_ys.n
                G.release([4, 5], self.s_ys)

        M[("attdone", it, hh)] = self.s_s.n

    def norms_k(self, it, hh, kend, ksrc):
        L, blk, G, M = self.L, self.blk, self.G, self.M
        s_pj, s_pe, s_ua, s_sq, s_nr = self.s_pj, self.s_pe, self.s_ua, self.s_sq, self.s_nr

        @blk.scalar
        def _(sc):
            if L == 0:
                pe_m, ua_m = M[("kdone", it, hh)]
                sc.wait_ge(s_pe.h, pe_m)
                sc.wait_ge(s_ua.h, ua_m)
                s_sq.inc(sc.activation(self.sq[0:64, 0:kend], ksrc[0:64, 0:kend], AF.Square))
            elif hh == 0:
                sc.wait_ge(s_pe.h, M[("prjevac", it)])
                s_sq.inc(sc.activation(self.sq[:, 0:kend], ksrc[:, 0:kend], AF.Square))
            M[("ksq", it, hh)] = s_sq.n

        o64col = self.ones64[:, (hh if L == 1 else 0):(hh if L == 1 else 0) + 1]

        @blk.tensor
        def _(t):
            if L == 0:
                t.wait_ge(self.s_gq.h, self.SQZ)
            t.wait_ge(s_sq.h, M[("ksq", it, hh)])
            nr0 = self.s_nr.n
            for g in range(kend // 512):
                bb = 4 if g % 2 == 0 else 5
                if g >= 2:
                    t.wait_ge(self.s_nr.h, nr0 + (g - 1))
                else:
                    G.acquire(t, [bb])
                s_pj.inc(t.matmul(self.bk(bb, 0, 1), o64col,
                                  self.sq[0:128, g * 512:(g + 1) * 512],
                                  start=True, stop=True))
                M[("knmm", it, hh, g)] = s_pj.n

        @blk.vector
        def _(v):
            for g in range(kend // 512):
                bb = 4 if g % 2 == 0 else 5
                v.wait_ge(s_pj.h, M[("knmm", it, hh, g)])
                s_nr.inc(v.reduce_max(self.kst[0:1, g:g + 1], self.bk(bb, 0, 1), axis=AX.X))
                G.release([bb], s_nr)
            v.wait_ge(s_nr.h, s_nr.n)
            mm = s_nr.inc(v.reduce_max(self.kst[0:1, 8 + hh:9 + hh],
                                       self.kst[0:1, 0:kend // 512], axis=AX.X))
            v.wait_ge(s_nr.h, mm)
            s_nr.inc(v.tensor_scalar(self.kst[0:1, 10 + hh:11 + hh],
                                     self.kst[0:1, 8 + hh:9 + hh], 1.1025, None, ALU.mult))
            M[("khat", it, hh)] = s_nr.n

    def norms_q(self, it, hh, qsrc):
        L, blk, G, M = self.L, self.blk, self.G, self.M
        s_pj, s_pe, s_ua, s_sq, s_nr = self.s_pj, self.s_pe, self.s_ua, self.s_sq, self.s_nr

        @blk.scalar
        def _(sc):
            if L == 0:
                pe_m, ua_m = M[("qdone", it, hh)]
                sc.wait_ge(s_pe.h, pe_m)
                sc.wait_ge(s_ua.h, ua_m)
                sc.wait_ge(s_nr.h, M[("khat", it, hh)])
                s_sq.inc(sc.activation(self.sq[0:64, 0:CHT], qsrc[0:64, 0:CHT], AF.Square))
            elif hh == 0:
                sc.wait_ge(s_nr.h, M[("khat", it, 1)])
                s_sq.inc(sc.activation(self.sq[:, 0:CHT], qsrc[:, 0:CHT], AF.Square))
            M[("qsq", it, hh)] = s_sq.n

        o64col = self.ones64[:, (hh if L == 1 else 0):(hh if L == 1 else 0) + 1]

        @blk.tensor
        def _(t):
            t.wait_ge(s_sq.h, M[("qsq", it, hh)])
            pe0n = s_pe.n
            for g in range(4):
                bb = 4 if g % 2 == 0 else 5
                if g >= 2:
                    t.wait_ge(s_pe.h, pe0n + (g - 1))
                else:
                    G.acquire(t, [bb])
                s_pj.inc(t.matmul(self.bk(bb, 0, 1), o64col,
                                  self.sq[0:128, g * 512:(g + 1) * 512],
                                  start=True, stop=True))
                M[("qnmm", it, hh, g)] = s_pj.n

        @blk.scalar
        def _(sc):
            sc.wait_ge(s_nr.h, M[("khat", it, hh)])
            for g in range(4):
                bb = 4 if g % 2 == 0 else 5
                sc.wait_ge(s_pj.h, M[("qnmm", it, hh, g)])
                s_pe.inc(sc.activation(self.mview(hh, g * 512, (g + 1) * 512),
                                       self.bk(bb, 0, 1), AF.Sqrt,
                                       scale=self.kst[0:1, 10 + hh:11 + hh]))
                G.release([bb], s_pe)
            M[("mrow", it, hh)] = s_pe.n

    def _pv_group(self, t, it, hh, qg, g, nkt_q, qa):
        L, G, M = self.L, self.G, self.M
        s_pv = self.s_pv
        for kt in (2 * g, 2 * g + 1):
            if kt >= nkt_q:
                continue
            if kt == 0:
                G.acquire(t, [4])
            if L == 0 and kt * 128 >= qa:
                t.wait_ge(self.s_mk.h, M[("mk", it, hh, qg, kt)])
            else:
                t.wait_ge(self.s_ex.h, M[("exp", it, hh, qg, g)])
            pcol = (g % 2) * 1024 + (kt % 2) * 512
            ins = t.matmul(self.bk(4, 0, 65),
                           self.v_tm[:, kt * 130 + hh * 65:kt * 130 + (hh + 1) * 65],
                           self.PT[:, pcol:pcol + 512],
                           start=(kt == 0), stop=(kt == nkt_q - 1))
            s_pv.inc(ins)
        M[("pvg", it, hh, qg, g)] = s_pv.n

    def chunk_tail(self, c):
        blk, M = self.blk, self.M
        it_last = c * 8 + 7
        snap = [M[("yst", c * 8 + p, hh, qg)] for p in range(8) for hh in range(2) for qg in range(4)]
        need = max(snap)

        @blk.scalar
        def _(sc):
            sc.wait_ge(self.s_ys.h, need)
            for tt in range(16):
                self.dma(sc, self.s_yd,
                         self.y_out[c * CHT + tt * 128:c * CHT + (tt + 1) * 128, :],
                         self.y_tm[:, tt * D:(tt + 1) * D])
        M[("ydma", c)] = self.s_yd.n

    def epilogue(self):
        blk = self.blk
        total = self.s_yd.n

        @blk.sync
        def _(sync):
            sync.wait_ge(self.s_yd.h, total)


def build_attn(L):
    b = AttnBuilder(L)
    return b.nc



class MixBuilder(LaunchBase):
    def __init__(self, L, parent=None, io=None, entry=None):
        super().__init__(f"mix{L}", parent=parent, io=io, entry=entry)
        self.L = L
        self.build()

    def bk(self, i, r0=0, r1=128):
        return self.ps[r0:r1, i * 512:(i + 1) * 512]

    def bk2(self, i):
        return self.ps[:, i * 512:(i + 2) * 512]

    def bkb(self, i, c0, c1, r0=0, r1=128):
        return self.psb[r0:r1, i * 1024 + c0:i * 1024 + c1]

    def build(self):
        L = self.L
        B = self

        self.ident_in = B.din("c_ident", [128, 128], BF16)
        self.identf_in = B.din("c_identf", [128, 128], F32)
        self.ones1_in = B.din("c_ones1", [1, 128], BF16)
        self.ones512_in = B.din("c_ones512", [1, 512], BF16)
        self.xn_in = B.din("xn_fm", [128, 8 * T], BF16)
        self.y_in = B.din("y_tm", [T, D], BF16)
        self.res_in = B.din("res", [T, D], F32)
        if L == 0:
            self.wc_in = B.din("wc", [D, D], F32)
            self.gate_in = B.din("gate_wT", [D, 8], F32)
            self.egu_in = B.din("eguT", [8, D, 2816], F32)
            self.edn_in = B.din("edownT", [8, 1408, D], F32)
            self.sg_in = B.din("sgT", [D, 2816], F32)
            self.su_in = B.din("suT", [D, 2816], F32)
            self.sd_in = B.din("sdT", [2816, D], F32)
            self.seg_in = B.din("segT", [D, 1], F32)
            self.x1_out = B.dout("x1", [T, D], F32)
            self.xn1_out = B.dout("xn1_fm", [128, 8 * T], BF16)
        else:
            self.wc_in = B.din("wto", [D, D], F32)
            self.bo_in = B.din("bo", [128, 8], F32)
            self.rw_in = B.din("router_w", [D, 8], F32)
            self.rb_in = B.din("router_b", [1, 8], F32)
            self.w1_in = B.din("w1", [8, D, 4096], F32)
            self.b1_in = B.din("b1", [8, 128, 32], F32)
            self.w2_in = B.din("w2", [8, 4096, D], F32)
            self.b2_in = B.din("b2", [8, D], F32)
            self.x1_out = B.dout("out", [T, D], F32)

        self.xn = B.sb("xn", [128, 8 * CHT], BF16)
        self.acc = B.sb("acc", [128, 8 * CHT], F32)
        self.ident = B.sb("ident", [128, 128], BF16)
        self.identf = B.sb("identf", [128, 128], F32)
        self.ones1 = B.sb("ones1", [1, 128], BF16)
        self.ones512 = B.sb("ones512", [1, 512], BF16)
        if L == 0:
            self.h_sb = B.sb("h_sb", [128, 11 * CHT], BF16)
            hf = self.h_sb.bitcast(F32)
            self.yst = self.h_sb[:, 0:2048]
            self.yfm = self.h_sb[:, 2048:6144]
            self.wcbf = self.h_sb[:, 6144:14336]
            self.xnt = self.h_sb[:, 0:2048]
            self.xne = self.h_sb[:, 2048:4096]
            self.x1t = hf[:, 7168:9216]
            self.xrst = hf[:, 9216:11264]
            self.wcst = B.sb("wcst", [128, 2 * D], F32)
            self.gust = self.wcst
            self.gubf = B.sb("gubf", [128, 2 * 8 * 256], BF16)
            self.dnst = B.sb("dnst", [128, 11 * 256], F32)
            self.dnbf = B.sb("dnbf", [128, 2 * 11 * 256], BF16)
            self.gwbf = B.sb("gwbf", [128, 72], BF16)
            self.gwst = hf[:, 0:72]
            self.lst = B.sb("lst", [128, 24], F32)
        else:
            self.bof = B.sb("bof", [128, 8], F32)
            self.h_sb = B.sb("h_sb", [128, 4 * CHT], BF16)
            hf = self.h_sb.bitcast(F32)
            self.yst = self.h_sb[:, 0:2048]
            self.yfm = self.h_sb[:, 2048:6144]
            self.x1t = hf[:, 0:2048]
            self.xrst = hf[:, 2048:4096]
            self.wcbf = B.sb("wcbf", [128, 8 * D], BF16)
            self.gust = B.sb("gust", [128, 8 * 512], F32)
            self.wcst = self.gust[:, 0:2048]
            self.gubf = B.sb("gubf", [128, 2 * 8 * 512], BF16)
            self.dnbf = B.sb("dnbf", [128, 2 * 4 * 1024], BF16)
            self.gwbf = B.sb("gwbf", [128, 72], BF16)
            self.gwst = B.sb("gwst", [128, 72], F32)
            self.b1f = B.sb("b1f", [128, 32], F32)
            self.b2r = B.sb("b2r", [1, 1024], BF16)
            self.b2st = B.sb("b2st", [1, 1024], F32)
            self.rbr = B.sb("rbr", [1, 8], BF16)
            self.rbst = B.sb("rbst", [1, 8], F32)
        self.sgev = B.sb("sgev", [128, 2048], BF16)
        self.hmid = B.sb("hmid", [128, 1024], BF16)
        self.w_all = B.sb("w_all", [128, 160], BF16)
        self.wrow = self.sgev[0:1, 0:CHT]
        self.wb_sb = B.sb("wb_sb", [128, CHT], BF16)
        self.ls = B.sb("ls", [128, 64], F32)
        self.ps = B.psum_("ps", [128, 8 * 512], F32)
        self.psb = self.ps.bitcast(BF16)
        self.G = BankGate()

        for nm in ("s_c s_x s_y s_w s_wc s_pj s_pe s_ex s_ve s_vm s_sm s_gt "
                   "s_vt s_acc s_xr s_x1 s_ln s_od s_gq s_hh").split():
            setattr(self, nm, B.sem(nm))
        self.s_wp = [B.sem("s_wp0"), B.sem("s_wp1")]
        self.s_xp = [B.sem("s_xp0"), B.sem("s_xp1")]
        self.s_yp = [B.sem("s_yp0"), B.sem("s_yp1")]
        self.s_wd = B.sem("s_wd")
        self.s_x1p = [B.sem("s_x1p0"), B.sem("s_x1p1")]
        self.s_odp = [B.sem("s_odp0"), B.sem("s_odp1")]

        if self.parent is None:
            self.blk = self.ctx.enter_context(self.nc.Block())
        else:
            self.blk = self.parent.blk
        self.phase0()
        for c in range(2):
            self.wc_phase(c)
            self.gating(c)
            nexp = 10 if L == 0 else 8
            for e in range(nexp):
                self.expert(c, e)
            self.tail(c)
        self.epilogue()
        self.finish()

    def phase0(self):
        L, blk = self.L, self.blk
        s_c, s_x = self.s_c, self.s_x

        @blk.sync
        def _(sync):
            self.emit_entry(sync)
            self.dma(sync, s_c, self.ident[:, :], self.ident_in)
            self.dma(sync, s_c, self.identf[:, :], self.identf_in)
            self.dma(sync, s_c, self.ones1[:, :], self.ones1_in)
            self.dma(sync, s_c, self.ones512[:, :], self.ones512_in)
            if L == 0:
                for kc in range(8):
                    self.dma(sync, s_c, self.gwst[:, kc * 8:kc * 8 + 8],
                             self.gate_in[kc * 128:(kc + 1) * 128, :])
                    self.dma(sync, s_c, self.gwst[:, 64 + kc:65 + kc],
                             self.seg_in[kc * 128:(kc + 1) * 128, :])
            else:
                for kc in range(8):
                    self.dma(sync, s_c, self.gwst[:, kc * 8:kc * 8 + 8],
                             self.rw_in[kc * 128:(kc + 1) * 128, :])
                self.dma(sync, s_c, self.rbst[:, :], self.rb_in)
                self.dma(sync, s_c, self.bof[:, :], self.bo_in)
            for i in range(8):
                self.dma(sync, s_x, self.xn[:, i * CHT:(i + 1) * CHT],
                         self.xn_in[:, i * T:i * T + CHT])
        self.CONSTS = s_c.n
        self.XN0 = s_x.n

        @blk.gpsimd
        def _(gp):
            gp.wait_ge(s_c.h, self.CONSTS)
            ncol = 72 if L == 0 else 64
            self.s_gq.inc(gp.tensor_copy(self.gwbf[:, 0:ncol], self.gwst[:, 0:ncol]))
            if L == 1:
                self.s_gq.inc(gp.tensor_copy(self.rbr[:, :], self.rbst[:, :]))
        self.GC = self.s_gq.n

    def wc_phase(self, c):
        L, blk, G, M = self.L, self.blk, self.G, self.M
        s_y, s_wc, s_pj, s_pe, s_vt = (self.s_y, self.s_wc, self.s_pj,
                                       self.s_pe, self.s_vt)
        vt0 = s_vt.n
        pe0 = s_pe.n
        wc0 = s_wc.n
        for gt in range(16):
            M[("yT", c, gt)] = vt0 + gt + 1
            M[("yev", c, gt)] = pe0 + gt + 1
        for oc in range(8):
            M[("wcv", c, oc)] = wc0 + oc + 1

        @blk.sync
        def _(sync, c=c):
            for oc in range(8):
                sp = oc % 2
                if oc >= 2:
                    sync.wait_ge(s_wc.h, M[("wcv", c, oc - 2)])
                elif c == 1:
                    if L == 0:
                        sync.wait_ge(s_wc.h, M[("gucv_last", 9)])
                    else:
                        sync.wait_ge(s_wc.h, M[("w2cv", 63)])
                for kc in range(8):
                    self.dma(sync, self.s_wp[sp],
                             self.wcst[:, sp * D + kc * 128:sp * D + (kc + 1) * 128],
                             self.wc_in[kc * 128:(kc + 1) * 128, oc * 128:(oc + 1) * 128])
                M[("wcd", c, oc)] = self.s_wp[sp].n

        @blk.gpsimd
        def _(gp, c=c):
            gp.wait_ge(self.s_gq.h, self.GC)
            for oc in range(8):
                sp = oc % 2
                gp.wait_ge(self.s_wp[sp].h, M[("wcd", c, oc)])
                if c == 1 and oc < 2:
                    gp.wait_ge(s_pj.h, M[("wcmm", 0)])
                self.s_wc.inc(gp.tensor_copy(
                    self.wcbf[:, oc * D:(oc + 1) * D],
                    self.wcst[:, sp * D:(sp + 1) * D]))
                assert M[("wcv", c, oc)] == self.s_wc.n

        for tg in range(4):
            @blk.sync
            def _(sync, c=c, tg=tg):
                for tt in range(4):
                    gt = tg * 4 + tt
                    par = gt % 2
                    if c == 0 and gt == 0 and L == 0:
                        sync.wait_ge(self.s_gq.h, self.GC)
                    if c == 1 and gt == 0:
                        sync.wait_ge(self.s_x1p[1].h, M[("x1", 0, 15)])
                        sync.wait_ge(self.s_x1p[0].h, M[("x1", 0, 14)])
                        sync.wait_ge(self.s_vm.h, M[("x1vd", 0)])
                        if L == 0:
                            sync.wait_ge(self.s_odp[1].h, M[("xnd", 0, 15)])
                            sync.wait_ge(self.s_odp[0].h, M[("xnd", 0, 14)])
                    if gt >= 2:
                        sync.wait_ge(s_vt.h, M[("yT", c, gt - 2)])
                    self.dma(sync, self.s_yp[par],
                             self.yst[:, par * D:(par + 1) * D],
                             self.y_in[c * CHT + gt * 128:c * CHT + (gt + 1) * 128, :])
                    M[("ydma", c, gt)] = self.s_yp[par].n

            @blk.tensor
            def _(t, c=c, tg=tg):
                t.wait_ge(self.s_c.h, self.CONSTS)
                for tt in range(4):
                    gt = tg * 4 + tt
                    par = gt % 2
                    t.wait_ge(self.s_yp[gt % 2].h, M[("ydma", c, gt)])
                    bb = 6 + gt % 2
                    if gt >= 2:
                        t.wait_ge(s_pe.h, M[("yev", c, gt - 2)])
                    else:
                        G.acquire(t, [bb])
                    for kc in range(8):
                        ins = t.transpose(self.bkb(bb, kc * 128, (kc + 1) * 128),
                                          self.yst[:, par * D + kc * 128:par * D + (kc + 1) * 128],
                                          self.ident[:, :])
                    s_vt.inc(ins)
                    assert M[("yT", c, gt)] == s_vt.n

            @blk.scalar
            def _(sc, c=c, tg=tg):
                for tt in range(4):
                    gt = tg * 4 + tt
                    bb = 6 + gt % 2
                    sc.wait_ge(s_vt.h, M[("yT", c, gt)])
                    dst = self.yfm[:, :].rearrange("p (k t) -> p k t", k=8)[:, :, tt * 128:(tt + 1) * 128]
                    src = self.bkb(bb, 0, 1024).rearrange("p (k t) -> p k t", k=8)
                    s_pe.inc(sc.copy(dst, src))
                    G.release([bb], s_pe)
                    assert M[("yev", c, gt)] == s_pe.n
                M[("yfm", c, tg)] = s_pe.n

            @blk.tensor
            def _(t, c=c, tg=tg):
                t.wait_ge(s_pe.h, M[("yfm", c, tg)])
                for ocp in range(4):
                    gidx = tg * 4 + ocp
                    bks = [2 * (gidx % 2), 2 * (gidx % 2) + 1]
                    if gidx >= 2:
                        t.wait_ge(self.s_acc.h, M[("wacc", c, gidx - 2)])
                    else:
                        G.acquire(t, bks)
                    for oc2 in range(2):
                        oc = ocp * 2 + oc2
                        t.wait_ge(s_wc.h, M[("wcv", c, oc)])
                        for kc in range(8):
                            ins = t.matmul(
                                self.bk(bks[oc2]),
                                self.wcbf[:, oc * D + kc * 128:oc * D + (kc + 1) * 128],
                                self.yfm[:, kc * 512:(kc + 1) * 512],
                                start=(kc == 0), stop=(kc == 7))
                        s_pj.inc(ins)
                    M[("wcg", c, gidx)] = s_pj.n
                    M[("wacc", c, gidx)] = self.acc_pred(2)

            @blk.scalar
            def _(sc, c=c, tg=tg):
                for ocp in range(4):
                    gidx = tg * 4 + ocp
                    bks = [2 * (gidx % 2), 2 * (gidx % 2) + 1]
                    sc.wait_ge(s_pj.h, M[("wcg", c, gidx)])
                    if c == 1 and gidx == 0:
                        sc.wait_ge(s_vt.h, M[("tailT", 0)])
                    for oc2 in range(2):
                        oc = ocp * 2 + oc2
                        dv = self.acc[:, oc * CHT + tg * 512:oc * CHT + (tg + 1) * 512]
                        if L == 1:
                            ins = sc.activation(dv, self.bk(bks[oc2]), AF.Identity,
                                                bias=self.bof[:, oc:oc + 1])
                        else:
                            ins = sc.activation(dv, self.bk(bks[oc2]), AF.Copy)
                        self.s_acc.inc(ins)
                    G.release(bks, self.s_acc)
                    assert M[("wacc", c, gidx)] == self.s_acc.n
        M[("wcmm", c)] = self.s_pj.n
        M[("wcdone", c)] = self.s_acc.n

    def acc_pred(self, k):
        if not hasattr(self, "_accp"):
            self._accp = self.s_acc.n
        self._accp += k
        return self._accp

    def gating(self, c):
        L, blk, G, M = self.L, self.blk, self.G, self.M
        s_pj, s_vm = self.s_pj, self.s_vm
        if c == 1:
            @blk.sync
            def _(sync):
                if L == 0:
                    sync.wait_ge(s_pj.h, M[("gumm_last", 9)])
                else:
                    sync.wait_ge(s_pj.h, M[("hmm", 63)])
                for i in range(8):
                    self.dma(sync, self.s_x, self.xn[:, i * CHT:(i + 1) * CHT],
                             self.xn_in[:, i * T + CHT:(i + 1) * T])
            self.XN0 = self.s_x.n
        vm0 = s_vm.n
        ex0p = self.s_ex.n
        perx = 2 if L == 0 else 1
        for tt in range(16):
            M[("lgv", c, tt)] = vm0 + 12 * tt + 4
            M[("lgw", c, tt)] = vm0 + 12 * (tt + 1)
            M[("lge", c, tt)] = ex0p + perx * (tt + 1)

        @blk.tensor
        def _(t, c=c):
            t.wait_ge(self.s_gq.h, self.GC)
            t.wait_ge(self.s_x.h, self.XN0)
            for tt in range(16):
                bb = 4 + tt % 2
                if tt >= 2:
                    t.wait_ge(s_vm.h, M[("lgv", c, tt - 2)])
                    if L == 0:
                        t.wait_ge(self.s_ex.h, M[("lge", c, tt - 2)])
                else:
                    G.acquire(t, [bb])
                for kc in range(8):
                    ins = t.matmul(self.bk(bb)[:, 0:8],
                                   self.xn[:, kc * CHT + tt * 128:
                                           kc * CHT + (tt + 1) * 128],
                                   self.gwbf[:, kc * 8:kc * 8 + 8],
                                   start=(kc == 0), stop=(kc == 7 and L == 0))
                if L == 1:
                    ins = t.matmul(self.bk(bb)[:, 0:8], self.ones1[0:1, :],
                                   self.rbr[0:1, :], start=False, stop=True)
                s_pj.inc(ins)
                if L == 0:
                    for kc in range(8):
                        ins = t.matmul(self.bk(bb)[:, 8:9],
                                       self.xn[:, kc * CHT + tt * 128:
                                               kc * CHT + (tt + 1) * 128],
                                       self.gwbf[:, 64 + kc:65 + kc],
                                       start=(kc == 0), stop=(kc == 7))
                    s_pj.inc(ins)
                M[("lgmm", c, tt)] = s_pj.n

        @blk.vector
        def _(v, c=c):
            for tt in range(16):
                bb = 4 + tt % 2
                o = (tt % 2) * 32
                v.wait_ge(s_pj.h, M[("lgmm", c, tt)])
                lsv = self.ls[:, o:o + 8]
                mm = s_vm.inc(v.tensor_copy(lsv, self.bk(bb)[:, 0:8]))
                v.wait_ge(s_vm.h, mm)
                mm = s_vm.inc(v.reduce_max(self.ls[:, o + 8:o + 9], lsv, axis=AX.X))
                v.wait_ge(s_vm.h, mm)
                mm = s_vm.inc(v.tensor_scalar(self.ls[:, o + 9:o + 10],
                                              self.ls[:, o + 8:o + 9], -1.0, None, ALU.mult))
                v.wait_ge(s_vm.h, mm)
                mm = s_vm.inc(v.tensor_scalar(self.ls[:, o + 10:o + 18], lsv,
                                              self.ls[:, o + 9:o + 10], None, ALU.add))
                assert M[("lgv", c, tt)] == mm
                if c == 1 and tt < 2:
                    v.wait_ge(self.s_vt.h, M[("wrTend", 0)])
                v.wait_ge(self.s_ex.h, M[("lge", c, tt)])
                ev = self.ls[:, o + 10:o + 18]
                t8 = self.ls[:, o + 18:o + 26]
                mm = s_vm.inc(v.tensor_scalar(t8, lsv, self.ls[:, o + 8:o + 9],
                                              -1e30, ALU.is_ge, ALU.mult))
                v.wait_ge(s_vm.h, mm)
                mm = s_vm.inc(v.tensor_tensor(t8, t8, lsv, ALU.add))
                v.wait_ge(s_vm.h, mm)
                mm = s_vm.inc(v.reduce_max(self.ls[:, o + 26:o + 27], t8, axis=AX.X))
                v.wait_ge(s_vm.h, mm)
                mm = s_vm.inc(v.tensor_scalar(t8, lsv, self.ls[:, o + 26:o + 27],
                                              None, ALU.is_ge))
                v.wait_ge(s_vm.h, mm)
                if L == 0:
                    mm = s_vm.inc(v.reduce_sum(self.ls[:, o + 27:o + 28], ev, axis=AX.X))
                    v.wait_ge(s_vm.h, mm)
                    mm = s_vm.inc(v.reciprocal(self.ls[:, o + 28:o + 29],
                                               self.ls[:, o + 27:o + 28]))
                    v.wait_ge(s_vm.h, mm)
                    mm = s_vm.inc(v.tensor_scalar(ev, ev, self.ls[:, o + 28:o + 29],
                                                  None, ALU.mult))
                    v.wait_ge(s_vm.h, mm)
                    mm = s_vm.inc(v.tensor_tensor(self.w_all[:, tt * 8:(tt + 1) * 8],
                                                  ev, t8, ALU.mult))
                else:
                    mm = s_vm.inc(v.tensor_tensor(ev, ev, t8, ALU.mult))
                    v.wait_ge(s_vm.h, mm)
                    mm = s_vm.inc(v.reduce_sum(self.ls[:, o + 27:o + 28], ev, axis=AX.X))
                    v.wait_ge(s_vm.h, mm)
                    mm = s_vm.inc(v.reciprocal(self.ls[:, o + 28:o + 29],
                                               self.ls[:, o + 27:o + 28]))
                    v.wait_ge(s_vm.h, mm)
                    mm = s_vm.inc(v.tensor_scalar(self.w_all[:, tt * 8:(tt + 1) * 8],
                                                  ev, self.ls[:, o + 28:o + 29],
                                                  None, ALU.mult))
                assert M[("lgw", c, tt)] == mm
            M[("gatedone", c)] = s_vm.n

        @blk.scalar
        def _(sc, c=c):
            for tt in range(16):
                o = (tt % 2) * 32
                sc.wait_ge(s_vm.h, M[("lgv", c, tt)])
                self.s_ex.inc(sc.activation(self.ls[:, o + 10:o + 18],
                                            self.ls[:, o + 10:o + 18], AF.Exp))
                if L == 0:
                    bb = 4 + tt % 2
                    self.s_ex.inc(sc.activation(self.w_all[:, 128 + tt:129 + tt],
                                                self.bk(bb)[:, 8:9], AF.Sigmoid))
                assert M[("lge", c, tt)] == self.s_ex.n

    def expert(self, c, e):
        L, blk, G, M = self.L, self.blk, self.G, self.M
        s_pj, s_vt, s_w, s_wc = self.s_pj, self.s_vt, self.s_w, self.s_wc
        s_ex, s_ve, s_vm = self.s_ex, self.s_ve, self.s_vm
        ei = c * (10 if L == 0 else 8) + e

        shared = (L == 0 and e >= 8)
        if not (shared and e == 9):
            @blk.tensor
            def _(t, c=c, e=e):
                t.wait_ge(s_vm.h, M[("gatedone", c)])
                for half in range(2):
                    bb = 4 + half
                    G.acquire(t, [bb])
                    for tt8 in range(8):
                        tt = half * 8 + tt8
                        col = (128 + tt) if shared else (tt * 8 + e)
                        ins = t.transpose(self.bkb(bb, tt8 * 128, (tt8 + 1) * 128, 0, 1),
                                          self.w_all[:, col:col + 1],
                                          self.ident[:, :])
                    s_vt.inc(ins)
                    M[("wrT", ei, half)] = s_vt.n

            @blk.vector
            def _(v, c=c, e=e):
                for half in range(2):
                    bb = 4 + half
                    v.wait_ge(s_vt.h, M[("wrT", ei, half)])
                    s_ve.inc(v.tensor_copy(self.wrow[0:1, half * 1024:(half + 1) * 1024],
                                           self.bkb(bb, 0, 1024, 0, 1)))
                    G.release([bb], s_ve)
                M[("wrow", ei)] = s_ve.n

            @blk.tensor
            def _(t, c=c, e=e):
                t.wait_ge(s_ve.h, M[("wrow", ei)])
                ve0w = s_ve.n
                for tg in range(4):
                    M[("wbcp", ei, tg)] = ve0w + tg + 1
                for tg in range(4):
                    bb = 4 + tg % 2
                    if tg >= 2:
                        t.wait_ge(s_ve.h, M[("wbcp", ei, tg - 2)])
                    else:
                        G.acquire(t, [bb])
                    s_pj.inc(t.matmul(self.bk(bb), self.ones1[0:1, :],
                                      self.wrow[0:1, tg * 512:(tg + 1) * 512],
                                      start=True, stop=True))
                    M[("wbmm", ei, tg)] = s_pj.n

            @blk.vector
            def _(v, c=c, e=e):
                for tg in range(4):
                    bb = 4 + tg % 2
                    v.wait_ge(s_pj.h, M[("wbmm", ei, tg)])
                    s_ve.inc(v.tensor_copy(self.wb_sb[:, tg * 512:(tg + 1) * 512],
                                           self.bk(bb)))
                    G.release([bb], s_ve)
                    assert M[("wbcp", ei, tg)] == s_ve.n
                M[("wb", ei)] = s_ve.n
        else:
            M[("wb", ei)] = M[("wb", ei - 1)]

        M[("wrTend", c)] = s_vt.n

        if L == 0:
            self.expert_l0(c, e, ei, shared)
        else:
            self.expert_l1(c, e, ei)

    def expert_l0(self, c, e, ei, shared):
        blk, G, M = self.blk, self.G, self.M
        s_pj, s_w, s_wc = self.s_pj, self.s_w, self.s_wc
        s_ex, s_ve = self.s_ex, self.s_ve
        sh = e - 8 if shared else 0
        wcb = s_wc.n
        for j in range(11):
            M[("gucv", ei, j)] = wcb + j + 1
        M[("gucv_last", ei)] = wcb + 11
        for ob in range(4):
            M[("dncv", ei, ob)] = wcb + 11 + ob + 1
        M[("dncv_last", ei)] = wcb + 15
        pjb = s_pj.n
        for git in range(22):
            M[("gumm2", ei, git)] = pjb + git + 1
        for j in range(11):
            M[("gumm", ei, j)] = pjb + 2 * (j + 1)
        M[("gumm_last", ei)] = pjb + 22
        pjb2 = pjb + 22
        for dit in range(32):
            M[("dnmm2", ei, dit)] = pjb2 + dit + 1
        for ob in range(4):
            M[("dnmm", ei, ob)] = pjb2 + 8 * (ob + 1)
        M[("dnmm_last", ei)] = pjb2 + 32

        @blk.sync
        def _(sync, c=c, e=e):
            for j in range(11):
                if j == 0 and e == 0:
                    sync.wait_ge(s_wc.h, M[("wcv", c, 7)])
                if (ei, j) != (0, 0):
                    sync.wait_ge(s_wc.h, M[("gucv", ei, j - 1) if j > 0 else ("gucv_last", ei - 1)])
                for kc in range(8):
                    if shared:
                        gsrc = self.sg_in[kc * 128:(kc + 1) * 128,
                                          sh * 1408 + j * 128:sh * 1408 + (j + 1) * 128]
                        usrc = self.su_in[kc * 128:(kc + 1) * 128,
                                          sh * 1408 + j * 128:sh * 1408 + (j + 1) * 128]
                    else:
                        gsrc = self.egu_in[e, kc * 128:(kc + 1) * 128, j * 128:(j + 1) * 128]
                        usrc = self.egu_in[e, kc * 128:(kc + 1) * 128,
                                           1408 + j * 128:1408 + (j + 1) * 128]
                    self.dma(sync, s_w, self.gust[:, kc * 256:kc * 256 + 128], gsrc)
                    self.dma(sync, s_w, self.gust[:, kc * 256 + 128:(kc + 1) * 256], usrc)
                M[("gud", ei, j)] = s_w.n

        @blk.gpsimd
        def _(gp, c=c, e=e):
            for j in range(11):
                par = j % 2
                gp.wait_ge(s_w.h, M[("gud", ei, j)])
                if j >= 2:
                    gp.wait_ge(s_pj.h, M[("gumm", ei, j - 2)])
                elif ei > 0:
                    gp.wait_ge(s_pj.h, M[("gumm_last", ei - 1)])
                s_wc.inc(gp.tensor_copy(self.gubf[:, par * 2048:(par + 1) * 2048],
                                        self.gust[:, :]))
                assert M[("gucv", ei, j)] == s_wc.n

        @blk.tensor
        def _(t, c=c, e=e):
            ve0 = s_ve.n
            for git in range(22):
                M[("hfold", ei, git)] = ve0 + git + 1
            git = 0
            for j in range(11):
                par = j % 2
                t.wait_ge(s_wc.h, M[("gucv", ei, j)])
                if j == 0:
                    t.wait_ge(s_ve.h, M[("wb", ei)])
                for tg2 in range(2):
                    q = git % 2
                    bks = [4 * q, 4 * q + 1, 4 * q + 2, 4 * q + 3]
                    if git >= 2:
                        t.wait_ge(s_ve.h, M[("hfold", ei, git - 2)])
                    else:
                        G.acquire(t, bks)
                    for kc in range(8):
                        for t2 in range(2):
                            ins = t.matmul(
                                self.bk(bks[t2]),
                                self.gubf[:, par * 2048 + kc * 256:par * 2048 + kc * 256 + 128],
                                self.xn[:, kc * CHT + tg2 * 1024 + t2 * 512:
                                        kc * CHT + tg2 * 1024 + (t2 + 1) * 512],
                                start=(kc == 0), stop=(kc == 7))
                        for t2 in range(2):
                            ins = t.matmul(
                                self.bk(bks[2 + t2]),
                                self.gubf[:, par * 2048 + kc * 256 + 128:par * 2048 + (kc + 1) * 256],
                                self.xn[:, kc * CHT + tg2 * 1024 + t2 * 512:
                                        kc * CHT + tg2 * 1024 + (t2 + 1) * 512],
                                start=(kc == 0), stop=(kc == 7))
                    s_pj.inc(ins)
                    assert M[("gumm2", ei, git)] == s_pj.n
                    git += 1

        @blk.scalar
        def _(sc, c=c, e=e):
            git = 0
            for j in range(11):
                for tg2 in range(2):
                    q = git % 2
                    sc.wait_ge(s_pj.h, M[("gumm2", ei, git)])
                    if ei > 0 and git == 0:
                        sc.wait_ge(s_pj.h, M[("dnmm_last", ei - 1)])
                    if git >= 2:
                        sc.wait_ge(s_ve.h, M[("hfold", ei, git - 2)])
                    s_ex.inc(sc.activation(self.sgev[:, q * 1024:(q + 1) * 1024],
                                           self.ps[:, 4 * q * 512:(4 * q + 2) * 512],
                                           AF.Silu))
                    M[("silu", ei, git)] = s_ex.n
                    git += 1

        @blk.vector
        def _(v, c=c, e=e):
            git = 0
            for j in range(11):
                for tg2 in range(2):
                    q = git % 2
                    v.wait_ge(s_ex.h, M[("silu", ei, git)])
                    if git >= 1:
                        v.wait_ge(s_ve.h, M[("hfold", ei, git - 1)])
                    mm = self.s_vm.inc(v.tensor_mul(
                        self.hmid[:, :], self.sgev[:, q * 1024:(q + 1) * 1024],
                        self.ps[:, (4 * q + 2) * 512:(4 * q + 4) * 512]))
                    v.wait_ge(self.s_vm.h, mm)
                    hv = self.h_sb[:, :].rearrange("p (j t) -> p j t", j=11)[:, j, tg2 * 1024:(tg2 + 1) * 1024]
                    ins = v.tensor_mul(hv, self.hmid[:, :],
                                       self.wb_sb[:, tg2 * 1024:(tg2 + 1) * 1024])
                    s_ve.inc(ins)
                    assert M[("hfold", ei, git)] == s_ve.n
                    git += 1
            M[("hdone", ei)] = s_ve.n

        @blk.sync
        def _(sync, c=c, e=e):
            for ob in range(4):
                if (ei, ob) != (0, 0):
                    sync.wait_ge(s_wc.h, M[("dncv", ei, ob - 1) if ob > 0 else ("dncv_last", ei - 1)])
                for ic in range(11):
                    if shared:
                        src = self.sd_in[sh * 1408 + ic * 128:sh * 1408 + (ic + 1) * 128,
                                         ob * 256:(ob + 1) * 256]
                    else:
                        src = self.edn_in[e, ic * 128:(ic + 1) * 128, ob * 256:(ob + 1) * 256]
                    self.dma(sync, self.s_wd, self.dnst[:, ic * 256:(ic + 1) * 256], src)
                M[("dnd", ei, ob)] = self.s_wd.n

        @blk.gpsimd
        def _(gp, c=c, e=e):
            for ob in range(4):
                par = ob % 2
                gp.wait_ge(self.s_wd.h, M[("dnd", ei, ob)])
                if ob >= 2:
                    gp.wait_ge(s_pj.h, M[("dnmm", ei, ob - 2)])
                elif ei > 0:
                    gp.wait_ge(s_pj.h, M[("dnmm_last", ei - 1)])
                s_wc.inc(gp.tensor_copy(self.dnbf[:, par * 2816:(par + 1) * 2816],
                                        self.dnst[:, :]))
                assert M[("dncv", ei, ob)] == s_wc.n

        @blk.tensor
        def _(t, c=c, e=e):
            t.wait_ge(s_ve.h, M[("hdone", ei)])
            ve0d = s_ve.n
            for dit in range(32):
                M[("dacc", ei, dit)] = ve0d + dit + 1
            dit = 0
            for ob in range(4):
                par = ob % 2
                t.wait_ge(s_wc.h, M[("dncv", ei, ob)])
                for oc2 in range(2):
                    for tg in range(4):
                        bb = dit % 4
                        if dit >= 4:
                            t.wait_ge(s_ve.h, M[("dacc", ei, dit - 4)])
                        else:
                            G.acquire(t, [bb])
                        for ic in range(11):
                            ins = t.matmul(
                                self.bk(bb),
                                self.dnbf[:, par * 2816 + ic * 256 + oc2 * 128:
                                          par * 2816 + ic * 256 + (oc2 + 1) * 128],
                                self.h_sb[:, :].rearrange("p (j t) -> p j t", j=11)[:, ic, tg * 512:(tg + 1) * 512],
                                start=(ic == 0), stop=(ic == 10))
                        s_pj.inc(ins)
                        assert M[("dnmm2", ei, dit)] == s_pj.n
                        dit += 1

        @blk.vector
        def _(v, c=c, e=e):
            v.wait_ge(self.s_acc.h, M[("wcdone", c)])
            dit = 0
            for ob in range(4):
                for oc2 in range(2):
                    oc = ob * 2 + oc2
                    for tg in range(4):
                        bb = dit % 4
                        v.wait_ge(s_pj.h, M[("dnmm2", ei, dit)])
                        av = self.acc[:, oc * CHT + tg * 512:oc * CHT + (tg + 1) * 512]
                        ins = v.tensor_add(av, av, self.bk(bb))
                        self.s_ve.inc(ins)
                        assert M[("dacc", ei, dit)] == self.s_ve.n
                        dit += 1
            M[("eadd", ei)] = self.s_ve.n
        G.release([0, 1, 2, 3], self.s_ve)


    def expert_l1(self, c, e, ei):
        blk, G, M = self.blk, self.G, self.M
        s_pj, s_w, s_wc = self.s_pj, self.s_w, self.s_wc
        s_ex, s_ve = self.s_ex, self.s_ve

        @blk.sync
        def _(sync, c=c, e=e):
            if ei > 0:
                sync.wait_ge(self.s_ex.h, M[("hrelu_last", ei - 1)])
            self.dma(sync, self.s_wd, self.b1f[:, :], self.b1_in[e])
            self.dma(sync, self.s_wd, self.b2st[0:1, :], self.b2_in[e:e + 1, :])
            M[("bd", ei)] = self.s_wd.n

        @blk.gpsimd
        def _(gp, c=c, e=e):
            gp.wait_ge(self.s_wd.h, M[("bd", ei)])
            if ei > 0:
                gp.wait_ge(s_pj.h, M[("w2mm_last", ei - 1)])
            s_wc.inc(gp.tensor_copy(self.b2r[:, :], self.b2st[:, :]))
            M[("b2c", ei)] = s_wc.n

        for fcg in range(8):
            fi = ei * 8 + fcg

            @blk.sync
            def _(sync, c=c, e=e, fcg=fcg, fi=fi):
                if fi % 64 == 0:
                    sync.wait_ge(s_wc.h, M[("wcv", c, 7)])
                if fi > 0:
                    sync.wait_ge(s_wc.h, M[("w2cv", fi - 1)])
                for kc in range(8):
                    self.dma(sync, s_w, self.gust[:, kc * 512:(kc + 1) * 512],
                             self.w1_in[e, kc * 128:(kc + 1) * 128,
                                        fcg * 512:(fcg + 1) * 512])
                M[("w1d", fi)] = s_w.n

            @blk.gpsimd
            def _(gp, c=c, e=e, fcg=fcg, fi=fi):
                gp.wait_ge(s_w.h, M[("w1d", fi)])
                if fi >= 2:
                    gp.wait_ge(s_pj.h, M[("hmm", fi - 2)])
                s_wc.inc(gp.tensor_copy(self.gubf[:, (fi % 2) * 4096:(fi % 2 + 1) * 4096],
                                        self.gust[:, :]))
                M[("w1cv", fi)] = s_wc.n

            @blk.tensor
            def _(t, c=c, e=e, fcg=fcg, fi=fi):
                ex0 = s_ex.n
                ve0 = s_ve.n
                for hit in range(16):
                    M[("hrelu", fi, hit)] = ex0 + hit + 1
                    M[("hcf", fi, hit)] = ve0 + hit + 1
                t.wait_ge(s_wc.h, M[("w1cv", fi)])
                if fcg == 0:
                    t.wait_ge(s_ve.h, M[("wb", ei)])
                hit = 0
                for fc in range(4):
                    for tg in range(4):
                        bb = hit % 4
                        if hit >= 4:
                            t.wait_ge(s_ve.h, M[("hcf", fi, hit - 4)])
                        else:
                            G.acquire(t, [bb])
                        for kc in range(8):
                            ins = t.matmul(
                                self.bk(bb),
                                self.gubf[:, (fi % 2) * 4096 + kc * 512 + fc * 128:
                                          (fi % 2) * 4096 + kc * 512 + (fc + 1) * 128],
                                self.xn[:, kc * CHT + tg * 512:
                                        kc * CHT + (tg + 1) * 512],
                                start=(kc == 0), stop=(kc == 7))
                        s_pj.inc(ins)
                        M[("hmm2", fi, hit)] = s_pj.n
                        hit += 1
                M[("hmm", fi)] = s_pj.n

            @blk.scalar
            def _(sc, c=c, e=e, fcg=fcg, fi=fi):
                hit = 0
                for fc in range(4):
                    for tg in range(4):
                        bb = hit % 4
                        sc.wait_ge(s_pj.h, M[("hmm2", fi, hit)])
                        if fcg == 0 and hit == 0 and ei > 0:
                            sc.wait_ge(s_pj.h, M[("w2mm_last", ei - 1)])
                        if fcg > 0 and hit == 0:
                            sc.wait_ge(s_pj.h, M[("w2mm", ei, fcg - 1)])
                        hv = self.h_sb[:, :].rearrange("p (f t) -> p f t", f=4)[:, fc, tg * 512:(tg + 1) * 512]
                        s_ex.inc(sc.activation(hv, self.bk(bb), AF.Relu,
                                               bias=self.b1f[:, fcg * 4 + fc:fcg * 4 + fc + 1]))
                        assert M[("hrelu", fi, hit)] == s_ex.n
                        hit += 1
                M[("hrelu_last", ei)] = s_ex.n

            @blk.vector
            def _(v, c=c, e=e, fcg=fcg, fi=fi):
                hit = 0
                for fc in range(4):
                    for tg in range(4):
                        v.wait_ge(s_ex.h, M[("hrelu", fi, hit)])
                        hv = self.h_sb[:, :].rearrange("p (f t) -> p f t", f=4)[:, fc, tg * 512:(tg + 1) * 512]
                        s_ve.inc(v.tensor_mul(hv, hv, self.wb_sb[:, tg * 512:(tg + 1) * 512]))
                        assert M[("hcf", fi, hit)] == s_ve.n
                        hit += 1
                M[("hcfd", fi)] = s_ve.n

            @blk.sync
            def _(sync, c=c, e=e, fcg=fcg, fi=fi):
                sync.wait_ge(s_wc.h, M[("w1cv", fi)])
                for fkc in range(4):
                    self.dma(sync, s_w, self.gust[:, fkc * 1024:(fkc + 1) * 1024],
                             self.w2_in[e, fcg * 512 + fkc * 128:fcg * 512 + (fkc + 1) * 128, :])
                M[("w2d", fi)] = s_w.n

            @blk.gpsimd
            def _(gp, c=c, e=e, fcg=fcg, fi=fi):
                gp.wait_ge(s_w.h, M[("w2d", fi)])
                if fi >= 2:
                    gp.wait_ge(s_pj.h, M[("w2mmf", fi - 2)])
                s_wc.inc(gp.tensor_copy(self.dnbf[:, (fi % 2) * 4096:(fi % 2 + 1) * 4096],
                                        self.gust[:, :]))
                M[("w2cv", fi)] = s_wc.n

            @blk.tensor
            def _(t, c=c, e=e, fcg=fcg, fi=fi):
                ve0d = s_ve.n
                for dit in range(32):
                    M[("w2acc", fi, dit)] = ve0d + dit + 1
                t.wait_ge(s_wc.h, M[("w2cv", fi)])
                t.wait_ge(s_ve.h, M[("hcfd", fi)])
                if ei > 0 and fcg == 0:
                    t.wait_ge(s_wc.h, M[("b2c", ei)])
                dit = 0
                for oc in range(8):
                    for tg in range(4):
                        bb = dit % 4
                        if dit >= 4:
                            t.wait_ge(s_ve.h, M[("w2acc", fi, dit - 4)])
                        else:
                            G.acquire(t, [bb])
                        for fkc in range(4):
                            ins = t.matmul(
                                self.bk(bb),
                                self.dnbf[:, (fi % 2) * 4096 + fkc * 1024 + oc * 128:
                                          (fi % 2) * 4096 + fkc * 1024 + (oc + 1) * 128],
                                self.h_sb[:, :].rearrange("p (f t) -> p f t", f=4)[:, fkc, tg * 512:(tg + 1) * 512],
                                start=(fkc == 0), stop=(fkc == 3 and fcg > 0))
                        if fcg == 0:
                            ins = t.matmul(self.bk(bb), self.b2r[0:1, oc * 128:(oc + 1) * 128],
                                           self.ones512[0:1, :], start=False, stop=True)
                        s_pj.inc(ins)
                        M[("w2mm2", fi, dit)] = s_pj.n
                        dit += 1
                M[("w2mmf", fi)] = s_pj.n
                M[("w2mm", ei, fcg)] = s_pj.n
                if fcg == 7:
                    M[("w2mm_last", ei)] = s_pj.n

            @blk.vector
            def _(v, c=c, e=e, fcg=fcg, fi=fi):
                v.wait_ge(self.s_acc.h, M[("wcdone", c)])
                dit = 0
                for oc in range(8):
                    for tg in range(4):
                        bb = dit % 4
                        v.wait_ge(s_pj.h, M[("w2mm2", fi, dit)])
                        av = self.acc[:, oc * CHT + tg * 512:oc * CHT + (tg + 1) * 512]
                        s_ve.inc(v.tensor_add(av, av, self.bk(bb)))
                        assert M[("w2acc", fi, dit)] == s_ve.n
                        dit += 1
            G.release([0, 1, 2, 3], s_ve)
        M[("eadd", ei)] = s_ve.n

    def tail(self, c):
        L, blk, G, M = self.L, self.blk, self.G, self.M
        s_pj, s_vt, s_vm, s_sm = self.s_pj, self.s_vt, self.s_vm, self.s_sm
        s_xr, s_x1, s_ln, s_od = self.s_xr, self.s_x1, self.s_ln, self.s_od
        nexp = 10 if L == 0 else 8
        ADD_DONE = M[("eadd", c * nexp + nexp - 1)]

        x1_0p = [self.s_x1p[0].n, self.s_x1p[1].n]
        gq0 = self.s_gq.n
        sm0 = self.s_sm.n
        vt0 = s_vt.n
        ln0 = self.s_ln.n
        for tt in range(16):
            M[("x1", c, tt)] = x1_0p[tt % 2] + 16 * (tt // 2 + 1)
            M[("sq", c, tt)] = gq0 + tt + 1
            M[("lnsq", c, tt)] = sm0 + 2 * tt + 1
            M[("lnnorm", c, tt)] = sm0 + 2 * tt + 2
            if L == 0:
                M[("accT", c, tt)] = vt0 + (1 if tt == 0 else 2 * tt)
                M[("xnT", c, tt)] = vt0 + (2 * tt + 3 if tt <= 13 else (31 if tt == 14 else 32))
            else:
                M[("accT", c, tt)] = vt0 + tt + 1
            M[("xne", c, tt)] = ln0 + tt + 1
        vm0 = s_vm.n
        if L == 0:
            NVT = 9
            for tt in range(16):
                M[("x1v", c, tt)] = vm0 + NVT * tt + 1
                M[("lnr1", c, tt)] = vm0 + NVT * tt + 2
                M[("lnv", c, tt)] = vm0 + NVT * (tt + 1)
        else:
            for tt in range(16):
                M[("x1v", c, tt)] = vm0 + tt + 1
        xrm = []

        @blk.sync
        def _(sync, c=c):
            for tt in range(16):
                par = tt % 2
                if tt == 0:
                    key = ("dnmm_last", c * 10 + 9) if L == 0 else ("w2mm_last", c * 8 + 7)
                    sync.wait_ge(s_pj.h, M[key])
                if tt >= 2:
                    if L == 0:
                        sync.wait_ge(s_vm.h, M[("lnr1", c, tt - 2)])
                    else:
                        sync.wait_ge(s_vm.h, M[("x1v", c, tt - 2)])
                self.dma(sync, self.s_xp[par], self.xrst[:, par * D:(par + 1) * D],
                         self.res_in[c * CHT + tt * 128:c * CHT + (tt + 1) * 128, :])
                xrm.append(self.s_xp[par].n)

        def emit_xnT(t, c, tt):
            par = tt % 2
            bb = tt % 2
            t.wait_ge(self.s_sm.h, M[("lnnorm", c, tt)])
            if tt < 2:
                G.acquire(t, [bb])
            else:
                t.wait_ge(self.s_ln.h, M[("xne", c, tt - 2)])
            for cch in range(8):
                ins = t.transpose(self.bkb(bb, cch * 128, (cch + 1) * 128),
                                  self.xnt[:, par * D + cch * 128:par * D + (cch + 1) * 128],
                                  self.ident[:, :])
            s_vt.inc(ins)
            assert M[("xnT", c, tt)] == s_vt.n

        @blk.tensor
        def _(t, c=c):
            for tt in range(16):
                if tt == 0:
                    t.wait_ge(self.s_ve.h, ADD_DONE)
                bp = 4 + 2 * (tt % 2)
                if tt >= 2:
                    t.wait_ge(s_vm.h, M[("x1v", c, tt - 2)])
                else:
                    G.acquire(t, [bp, bp + 1])
                for oc in range(8):
                    ins = t.transpose(
                        self.ps[:, bp * 512 + oc * 128:bp * 512 + (oc + 1) * 128],
                        self.acc[:, oc * CHT + tt * 128:oc * CHT + (tt + 1) * 128],
                        self.identf[:, :])
                s_vt.inc(ins)
                assert M[("accT", c, tt)] == s_vt.n
                if L == 0 and tt >= 1:
                    emit_xnT(t, c, tt - 1)
            if L == 0:
                emit_xnT(t, c, 15)

        @blk.vector
        def _(v, c=c):
            for tt in range(16):
                par = tt % 2
                bp = 4 + 2 * (tt % 2)
                v.wait_ge(s_vt.h, M[("accT", c, tt)])
                v.wait_ge(self.s_xp[par].h, xrm[tt])
                if tt >= 2:
                    v.wait_ge(self.s_x1p[tt % 2].h, M[("x1", c, tt - 2)])
                    if L == 0:
                        v.wait_ge(self.s_gq.h, M[("sq", c, tt - 2)])
                        v.wait_ge(self.s_sm.h, M[("lnnorm", c, tt - 2)])
                mm = s_vm.inc(v.tensor_add(
                    self.x1t[:, par * D:(par + 1) * D],
                    self.xrst[:, par * D:(par + 1) * D],
                    self.ps[:, bp * 512:(bp + 2) * 512]))
                assert M[("x1v", c, tt)] == mm
                if L == 0:
                    self.tail_ln_vec(v, c, tt)
            M[("x1vd", c)] = s_vm.n
        G.release([4, 5, 6, 7], s_vm)
        M[("tailT", c)] = self.s_vt.n

        if L == 0:
            self.tail_ln(c)
        else:
            @blk.scalar
            def _(sc, c=c):
                for tt in range(16):
                    par = tt % 2
                    sc.wait_ge(s_vm.h, M[("x1v", c, tt)])
                    self.dma(sc, self.s_x1p[par],
                             self.x1_out[c * CHT + tt * 128:c * CHT + (tt + 1) * 128, :],
                             self.x1t[:, par * D:(par + 1) * D])
                    assert M[("x1", c, tt)] == self.s_x1p[par].n

    def tail_ln_vec(self, v, c, tt):
        M, lst = self.M, self.lst
        s_vm = self.s_vm
        par = tt % 2
        o = par * 12
        xv = self.x1t[:, par * D:(par + 1) * D]
        qv = self.xrst[:, par * D:(par + 1) * D]
        v.wait_ge(self.s_gq.h, M[("sq", c, tt)])
        v.reduce_sum(lst[:, o:o + 1], xv, axis=AX.X)
        mm = s_vm.inc(v.reduce_sum(lst[:, o + 1:o + 2], qv, axis=AX.X))
        assert M[("lnr1", c, tt)] == mm
        v.wait_ge(s_vm.h, mm)
        v.wait_ge(s_vm.h, s_vm.inc(v.tensor_scalar(
            lst[:, o + 2:o + 3], lst[:, o:o + 1], 1.0 / D, None, ALU.mult)))
        v.tensor_tensor(lst[:, o + 3:o + 4], lst[:, o + 2:o + 3],
                        lst[:, o + 2:o + 3], ALU.mult)
        v.wait_ge(s_vm.h, s_vm.inc(v.tensor_scalar(
            lst[:, o + 4:o + 5], lst[:, o + 1:o + 2], 1.0 / D, None, ALU.mult)))
        v.wait_ge(s_vm.h, s_vm.inc(v.tensor_tensor(
            lst[:, o + 5:o + 6], lst[:, o + 4:o + 5], lst[:, o + 3:o + 4], ALU.subtract)))
        v.wait_ge(s_vm.h, s_vm.inc(v.tensor_scalar(
            lst[:, o + 6:o + 7], lst[:, o + 5:o + 6], 1.0, 1e-5, ALU.mult, ALU.add)))
        s_vm.inc(v.reciprocal(lst[:, o + 7:o + 8], lst[:, o + 6:o + 7]))
        v.wait_ge(self.s_sm.h, M[("lnsq", c, tt)])
        v.wait_ge(s_vm.h, s_vm.inc(v.tensor_tensor(
            lst[:, o + 9:o + 10], lst[:, o + 2:o + 3], lst[:, o + 8:o + 9], ALU.mult)))
        mm = s_vm.inc(v.tensor_scalar(
            lst[:, o + 10:o + 11], lst[:, o + 9:o + 10], -1.0, None, ALU.mult))
        assert M[("lnv", c, tt)] == mm

    def tail_ln(self, c):
        blk, G, M = self.blk, self.G, self.M
        s_vm, s_sm, s_ln = self.s_vm, self.s_sm, self.s_ln
        lst = self.lst

        @blk.gpsimd
        def _(gp, c=c):
            for tt in range(16):
                par = tt % 2
                gp.wait_ge(s_vm.h, M[("x1v", c, tt)])
                if tt >= 2:
                    gp.wait_ge(s_vm.h, M[("lnr1", c, tt - 2)])
                self.s_gq.inc(gp.tensor_tensor(self.xrst[:, par * D:(par + 1) * D],
                                               self.x1t[:, par * D:(par + 1) * D],
                                               self.x1t[:, par * D:(par + 1) * D], ALU.mult))
                assert M[("sq", c, tt)] == self.s_gq.n

        def emit_xne(sc, c, tt):
            par = tt % 2
            bb = tt % 2
            sc.wait_ge(self.s_vt.h, M[("xnT", c, tt)])
            if tt >= 2:
                sc.wait_ge(self.s_odp[tt % 2].h, M[("xnd", c, tt - 2)])
            self.s_ln.inc(sc.copy(self.xne[:, par * 1024:(par + 1) * 1024],
                                  self.bkb(bb, 0, 1024)))
            assert M[("xne", c, tt)] == self.s_ln.n
            sc.wait_ge(self.s_ln.h, self.s_ln.n)
            gt = c * 16 + tt
            dst = self.xn1_out[:, :].rearrange("p (k t) -> p k t", k=8)[:, :, gt * 128:(gt + 1) * 128]
            srcv = self.xne[:, par * 1024:(par + 1) * 1024].rearrange("p (k t) -> p k t", k=8)
            self.dma(sc, self.s_odp[tt % 2], dst, srcv)
            M[("xnd", c, tt)] = self.s_odp[tt % 2].n

        @blk.scalar
        def _(sc, c=c):
            s_x1 = self.s_x1
            for tt in range(16):
                par = tt % 2
                o = par * 12
                sc.wait_ge(s_vm.h, M[("x1v", c, tt)])
                self.dma(sc, self.s_x1p[par],
                         self.x1_out[c * CHT + tt * 128:c * CHT + (tt + 1) * 128, :],
                         self.x1t[:, par * D:(par + 1) * D])
                assert M[("x1", c, tt)] == self.s_x1p[par].n
                sc.wait_ge(s_vm.h, M[("lnr1", c, tt)] + 5)
                s1 = self.s_sm.inc(sc.activation(lst[:, o + 8:o + 9],
                                                 lst[:, o + 7:o + 8], AF.Sqrt))
                assert M[("lnsq", c, tt)] == s1
                sc.wait_ge(self.s_sm.h, s1)
                sc.wait_ge(s_vm.h, M[("lnv", c, tt)])
                if tt >= 2:
                    sc.wait_ge(self.s_vt.h, M[("xnT", c, tt - 2)])
                s2 = self.s_sm.inc(sc.activation(self.xnt[:, par * D:(par + 1) * D],
                                                 self.x1t[:, par * D:(par + 1) * D], AF.Identity,
                                                 bias=lst[:, o + 10:o + 11],
                                                 scale=lst[:, o + 8:o + 9]))
                assert M[("lnnorm", c, tt)] == s2
                if tt >= 1:
                    emit_xne(sc, c, tt - 1)
            emit_xne(sc, c, 15)
        G.release([0, 1], self.s_ln)

    def epilogue(self):
        blk = self.blk
        t0p = self.s_x1p[0].n
        t1p = self.s_x1p[1].n
        o0 = self.s_odp[0].n
        o1 = self.s_odp[1].n

        @blk.sync
        def _(sync):
            sync.wait_ge(self.s_x1p[0].h, t0p)
            sync.wait_ge(self.s_x1p[1].h, t1p)
            if o0:
                sync.wait_ge(self.s_odp[0].h, o0)
            if o1:
                sync.wait_ge(self.s_odp[1].h, o1)


def build_mix(L):
    b = MixBuilder(L)
    return b.nc



ENGINES = ("sync", "scalar", "vector", "tensor", "gpsimd")


def _stage_gate(blk, sem, cnt):
    for e in ENGINES:
        @getattr(blk, e)
        def _(eng):
            eng.wait_ge(sem.h, cnt)


def _stage_done(blk, sem):
    for e in ENGINES:
        @getattr(blk, e)
        def _(eng):
            eng.sem_inc(sem.h, 1)
    sem.n += 5


def build_fused():
    root = LaunchBase("fused")
    nc = root.nc
    cst = {}
    for nm, shp, dt in (("c_ident", [128, 128], BF16), ("c_identf", [128, 128], F32),
                        ("c_negones", [1, 128], BF16), ("c_ones1", [1, 128], BF16),
                        ("c_ones512", [1, 512], BF16), ("c_ones64", [128, 2], BF16),
                        ("c_csn", [64, 2 * T], BF16), ("c_mask", [128, 2048], BF16)):
        cst[nm] = nc.dram_tensor(nm, shp, dt, kind="ExternalInput").ap()
    xn0 = nc.dram_tensor("i_xn0", [128, 8 * T], BF16, kind="Internal").ap()
    y0 = nc.dram_tensor("i_y0", [T, D], BF16, kind="Internal").ap()
    x1 = nc.dram_tensor("i_x1", [T, D], F32, kind="Internal").ap()
    xn1 = nc.dram_tensor("i_xn1", [128, 8 * T], BF16, kind="Internal").ap()
    y1 = nc.dram_tensor("i_y1", [T, D], BF16, kind="Internal").ap()
    root.blk = root.ctx.enter_context(nc.Block())
    fin = root.sem("s_fin")

    a0 = AttnBuilder(0, parent=root, io={
        "c_ident": cst["c_ident"], "c_negones": cst["c_negones"],
        "c_ones64": cst["c_ones64"], "c_csn": cst["c_csn"],
        "c_mask": cst["c_mask"], "xn_fm_o": xn0, "y_tm": y0})
    x_in = a0.x_in
    _stage_done(root.blk, fin)

    _stage_gate(root.blk, fin, fin.n)
    MixBuilder(0, parent=root, io={
        "c_ident": cst["c_ident"], "c_identf": cst["c_identf"],
        "c_ones1": cst["c_ones1"], "c_ones512": cst["c_ones512"],
        "xn_fm": xn0, "y_tm": y0, "res": x_in, "x1": x1, "xn1_fm": xn1})
    _stage_done(root.blk, fin)

    _stage_gate(root.blk, fin, fin.n)
    AttnBuilder(1, parent=root, io={
        "c_ident": cst["c_ident"], "c_negones": cst["c_negones"],
        "c_ones64": cst["c_ones64"], "xn_fm": xn1, "y_tm": y1})
    _stage_done(root.blk, fin)

    _stage_gate(root.blk, fin, fin.n)
    MixBuilder(1, parent=root, io={
        "c_ident": cst["c_ident"], "c_identf": cst["c_identf"],
        "c_ones1": cst["c_ones1"], "c_ones512": cst["c_ones512"],
        "xn_fm": xn1, "y_tm": y1, "res": x1})
    root.finish()
    return root.nc



_NC_CACHE = {}


def _get_nc(kind, L):
    key = (kind, L)
    if key not in _NC_CACHE:
        if kind == "fused":
            _NC_CACHE[key] = build_fused()
        else:
            _NC_CACHE[key] = build_attn(L) if kind == "attn" else build_mix(L)
    return _NC_CACHE[key]


def kernel(**inputs):
    if _os.environ.get("KERNEL_UNFUSED", "0") == "1":
        return _kernel_unfused(**inputs)
    consts = _host_consts()
    prep = _host_prep(inputs)
    x = np.ascontiguousarray(np.asarray(inputs["L0_attn_x"], np.float32))
    cores = list(range(NCORES))
    nc = _get_nc("fused", 0)
    maps = []
    for ci in cores:
        m = dict(consts)
        m.update({"x": x[ci],
                  "wdq": prep["l0_wdq"], "wdk": prep["l0_wdk"],
                  "wdv": prep["l0_wdv"], "up": prep["l0_up"],
                  "wc": prep["l0_wc"], "gate_wT": prep["l0_gate_wT"],
                  "eguT": prep["l0_eguT"], "edownT": prep["l0_edownT"],
                  "sgT": prep["l0_sgT"], "suT": prep["l0_suT"],
                  "sdT": prep["l0_sdT"], "segT": prep["l0_segT"],
                  "wtq": prep["l1_wtq"], "wtk": prep["l1_wtk"],
                  "wtv": prep["l1_wtv"], "bq": prep["l1_bq"],
                  "bk": prep["l1_bk"], "bv": prep["l1_bv"],
                  "wto": prep["l1_wto"], "bo": prep["l1_bo"],
                  "router_w": prep["l1_router_w"], "router_b": prep["l1_router_b"],
                  "w1": prep["l1_w1"], "b1": prep["l1_b1"],
                  "w2": prep["l1_w2"], "b2": prep["l1_b2"]})
        maps.append(m)
    r = run_bass_kernel_spmd(nc, maps, core_ids=cores)
    out = np.stack([rr["out"] for rr in r.results], axis=0)
    return out.astype(np.float32)


def _kernel_unfused(**inputs):
    consts = _host_consts()
    prep = _host_prep(inputs)
    x = np.ascontiguousarray(np.asarray(inputs["L0_attn_x"], np.float32))

    cores = list(range(NCORES))

    nc = _get_nc("attn", 0)
    maps = []
    for ci in cores:
        m = {k: consts[k] for k in ("c_ident", "c_negones", "c_ones64", "c_csn", "c_mask")}
        m.update({"x": x[ci], "wdq": prep["l0_wdq"], "wdk": prep["l0_wdk"],
                  "wdv": prep["l0_wdv"], "up": prep["l0_up"]})
        maps.append(m)
    r1 = run_bass_kernel_spmd(nc, maps, core_ids=cores)
    xn0 = [r["xn_fm_o"] for r in r1.results]
    y0 = [r["y_tm"] for r in r1.results]

    nc = _get_nc("mix", 0)
    maps = []
    for ci in cores:
        m = {k: consts[k] for k in ("c_ident", "c_identf", "c_ones1", "c_ones512")}
        m.update({"xn_fm": xn0[ci], "y_tm": y0[ci], "res": x[ci],
                  "wc": prep["l0_wc"], "gate_wT": prep["l0_gate_wT"],
                  "eguT": prep["l0_eguT"], "edownT": prep["l0_edownT"],
                  "sgT": prep["l0_sgT"], "suT": prep["l0_suT"],
                  "sdT": prep["l0_sdT"], "segT": prep["l0_segT"]})
        maps.append(m)
    r2 = run_bass_kernel_spmd(nc, maps, core_ids=cores)
    x1 = [r["x1"] for r in r2.results]
    xn1 = [r["xn1_fm"] for r in r2.results]

    nc = _get_nc("attn", 1)
    maps = []
    for ci in cores:
        m = {k: consts[k] for k in ("c_ident", "c_negones", "c_ones64")}
        m.update({"xn_fm": xn1[ci], "wtq": prep["l1_wtq"], "wtk": prep["l1_wtk"],
                  "wtv": prep["l1_wtv"], "bq": prep["l1_bq"], "bk": prep["l1_bk"],
                  "bv": prep["l1_bv"]})
        maps.append(m)
    r3 = run_bass_kernel_spmd(nc, maps, core_ids=cores)
    y1 = [r["y_tm"] for r in r3.results]

    nc = _get_nc("mix", 1)
    maps = []
    for ci in cores:
        m = {k: consts[k] for k in ("c_ident", "c_identf", "c_ones1", "c_ones512")}
        m.update({"xn_fm": xn1[ci], "y_tm": y1[ci], "res": x1[ci],
                  "wto": prep["l1_wto"], "bo": prep["l1_bo"],
                  "router_w": prep["l1_router_w"], "router_b": prep["l1_router_b"],
                  "w1": prep["l1_w1"], "b1": prep["l1_b1"],
                  "w2": prep["l1_w2"], "b2": prep["l1_b2"]})
        maps.append(m)
    r4 = run_bass_kernel_spmd(nc, maps, core_ids=cores)
    out = np.stack([r["out"] for r in r4.results], axis=0)
    return out.astype(np.float32)


# version 04
# speedup vs baseline: 1.3904x; 1.2000x over previous
import numpy as np
import ml_dtypes
from contextlib import ExitStack

import concourse.bass as bass
import concourse.mybir as mybir
from concourse.bass_utils import run_bass_kernel_spmd

F32 = mybir.dt.float32
BF16 = mybir.dt.bfloat16
AF = mybir.ActivationFunctionType
ALU = mybir.AluOpType
AX = mybir.AxisListType
BF = ml_dtypes.bfloat16

T = 4096
D = 1024
CHT = 2048
NCORES = 8
import os as _os
DBG_PAIRS = int(_os.environ.get("DBG_PAIRS", "16"))
DBG_ATT = int(_os.environ.get("DBG_ATT", "1"))
DBG_HEADS = int(_os.environ.get("DBG_HEADS", "1"))
DBG_VT = int(_os.environ.get("DBG_VT", "1"))
DBG_VUP = int(_os.environ.get("DBG_VUP", "1"))
DBG_VT_MS = int(_os.environ.get("DBG_VT_MS", "1"))
DBG_VT_T = int(_os.environ.get("DBG_VT_T", "1"))
DBG_VT_EV = int(_os.environ.get("DBG_VT_EV", "1"))
DBG_LN0 = int(_os.environ.get("DBG_LN0", "1"))
DBG_VT_N = int(_os.environ.get("DBG_VT_N", "99"))


class Sem:
    __slots__ = ("h", "n")

    def __init__(self, ctx, nc, name):
        self.h = ctx.enter_context(nc.semaphore(name))
        self.n = 0

    def inc(self, instr, k=1):
        instr.then_inc(self.h, k)
        self.n += k
        return self.n


class BankGate:

    def __init__(self):
        self.rel = {}

    def acquire(self, eng, ids):
        need = {}
        for b in ids:
            for s, cnt in self.rel.get(b, ()):
                if need.get(s, 0) < cnt:
                    need[s] = cnt
        for s, cnt in need.items():
            eng.wait_ge(s.h, cnt)
        for b in ids:
            self.rel[b] = []

    def release(self, ids, sem):
        for b in ids:
            self.rel.setdefault(b, []).append((sem, sem.n))


class LaunchBase:
    def __init__(self, name, parent=None, io=None, entry=None):
        self.parent = parent
        self.io = io or {}
        self.entry = entry or []
        if parent is None:
            self.nc = bass.Bass("TRN2", target_bir_lowering=False)
            self.sempool = {}
            self.semctx = None
        else:
            self.nc = parent.nc
            self.sempool = parent.sempool
            self.semctx = parent.ctx
        self.ctx = ExitStack()
        self.ctx.__enter__()
        if parent is None:
            self.semctx = self.ctx
        self.name = name
        self.M = {}

    def din(self, name, shape, dt=F32):
        if name in self.io:
            return self.io[name]
        return self.nc.dram_tensor(name, shape, dt, kind="ExternalInput").ap()

    def dout(self, name, shape, dt=F32):
        if name in self.io:
            return self.io[name]
        return self.nc.dram_tensor(name, shape, dt, kind="ExternalOutput").ap()

    def sb(self, name, shape, dt):
        return self.ctx.enter_context(self.nc.sbuf_tensor(self.name + "_" + name, shape, dt))

    def psum_(self, name, shape, dt=F32):
        return self.ctx.enter_context(self.nc.psum_tensor(self.name + "_" + name, shape, dt))

    def sem(self, name):
        if name not in self.sempool:
            self.sempool[name] = Sem(self.semctx, self.nc, name)
        return self.sempool[name]

    def emit_entry(self, eng):
        for s, cnt in self.entry:
            eng.wait_ge(s.h, cnt)

    def dma(self, eng, sem, dst, srcap, **kw):
        bb = self.nc.cur_bb.bb
        n0 = len(bb.instructions)
        ins = eng.dma_start(dst, srcap, **kw)
        nsub = sum(1 for i in bb.instructions[n0:]
                   if type(i).__name__ == "InstDMACopy")
        ins.then_inc(sem.h, 16)
        sem.n += 16 * max(1, nsub)
        return ins

    def finish(self):
        self.ctx.__exit__(None, None, None)
        return self.nc


def _host_consts():
    c = {}
    c["c_ident"] = np.eye(128).astype(BF)
    c["c_identf"] = np.eye(128, dtype=np.float32)
    c["c_negones"] = np.full((1, 128), -1.0, np.float32).astype(BF)
    c["c_ones1"] = np.ones((1, 128), np.float32).astype(BF)
    c["c_ones512"] = np.ones((1, 512), np.float32).astype(BF)
    o64 = np.zeros((128, 2), np.float32)
    o64[0:64, 0] = 1.0
    o64[64:128, 1] = 1.0
    c["c_ones64"] = o64.astype(BF)
    DHE = 32
    inv_freq = 1.0 / (10000.0 ** (np.arange(0, DHE, 2, dtype=np.float64) / DHE))
    freqs = np.arange(T, dtype=np.float64)[:, None] * inv_freq[None, :]
    emb = np.concatenate([freqs, freqs], axis=-1)
    cos = np.cos(emb)
    sinp = np.sin(emb) * np.where(np.arange(DHE) % 2 == 0, -1.0, 1.0)[None, :]
    csn = np.zeros((64, 2 * T), np.float64)
    csn[32:64, 0:T] = cos.T
    csn[32:64, T:2 * T] = sinp.T
    c["c_csn"] = csn.astype(BF)
    m = np.zeros((128, 4 * 512), np.float32)
    ii = np.arange(128)[:, None]
    jj = np.arange(512)[None, :]
    for r in range(4):
        m[:, r * 512:(r + 1) * 512] = (jj >= ii + 128 * r).astype(np.float32)
    c["c_mask"] = m.astype(BF)
    return c


def _host_prep(inputs):
    p = {}
    f32 = np.float32

    def T2(a):
        return np.ascontiguousarray(np.asarray(a, f32).T)

    p["l0_wdq"] = np.ascontiguousarray(np.asarray(inputs["L0_attn_Wq_down"], f32))
    p["l0_wdk"] = np.ascontiguousarray(np.asarray(inputs["L0_attn_Wk_down"], f32))
    p["l0_wdv"] = np.ascontiguousarray(np.asarray(inputs["L0_attn_Wv_down"], f32))

    def swap_cols(a):
        a = np.asarray(a, f32)
        b = np.empty_like(a)
        b[:, 0::2] = a[:, 1::2]
        b[:, 1::2] = a[:, 0::2]
        return b

    up = np.zeros((128, 640), f32)
    wqc = np.asarray(inputs["L0_attn_Wq_up_c"], f32)
    wqe = np.asarray(inputs["L0_attn_Wq_up_e"], f32)
    wkc = np.asarray(inputs["L0_attn_Wk_up_c"], f32)
    wke = np.asarray(inputs["L0_attn_Wk_up_e"], f32)
    wv = np.asarray(inputs["L0_attn_Wv_up"], f32)
    for ki, (wc_, we_) in enumerate(((wqc, wqe), (wkc, wke))):
        for hhp in range(2):
            b = ki * 256 + hhp * 128
            r = slice(64 * hhp, 64 * hhp + 64)
            up[r, b + 0:b + 32] = wc_
            up[r, b + 32:b + 64] = we_
            up[r, b + 96:b + 128] = swap_cols(we_)
    up[0:64, 512:576] = wv
    up[64:128, 576:640] = wv
    p["l0_up"] = np.ascontiguousarray(up)

    p["l0_wc"] = np.ascontiguousarray(np.asarray(inputs["L0_attn_Wc"], f32))
    p["l0_gate_wT"] = T2(inputs["L0_ffn_gate_w"])
    p["l0_eguT"] = np.ascontiguousarray(
        np.transpose(np.asarray(inputs["L0_ffn_experts_gate_up"], f32), (0, 2, 1)))
    p["l0_edownT"] = np.ascontiguousarray(
        np.transpose(np.asarray(inputs["L0_ffn_experts_down"], f32), (0, 2, 1)))
    p["l0_sgT"] = T2(inputs["L0_ffn_shared_gate_w"])
    p["l0_suT"] = T2(inputs["L0_ffn_shared_up_w"])
    p["l0_sdT"] = T2(inputs["L0_ffn_shared_down_w"])
    p["l0_segT"] = T2(inputs["L0_ffn_shared_expert_gate_w"])

    p["l1_wtq"] = T2(inputs["L1_attn_Wq"])
    p["l1_wtk"] = T2(inputs["L1_attn_Wk"])
    p["l1_wtv"] = T2(inputs["L1_attn_Wv"])
    p["l1_wto"] = T2(inputs["L1_attn_Wo"])

    def fm8(b):
        return np.ascontiguousarray(np.asarray(b, f32).reshape(8, 128).T)

    p["l1_bq"] = fm8(inputs["L1_attn_bq"])
    p["l1_bk"] = fm8(inputs["L1_attn_bk"])
    p["l1_bv"] = fm8(inputs["L1_attn_bv"])
    p["l1_bo"] = fm8(inputs["L1_attn_bo"])

    p["l1_router_w"] = np.ascontiguousarray(np.asarray(inputs["L1_ffn_router_w"], f32))
    p["l1_router_b"] = np.ascontiguousarray(np.asarray(inputs["L1_ffn_router_b"], f32).reshape(1, 8))
    p["l1_w1"] = np.ascontiguousarray(np.asarray(inputs["L1_ffn_W1"], f32))
    p["l1_w2"] = np.ascontiguousarray(np.asarray(inputs["L1_ffn_W2"], f32))
    p["l1_b1"] = np.ascontiguousarray(
        np.asarray(inputs["L1_ffn_b1"], f32).reshape(8, 32, 128).transpose(0, 2, 1))
    p["l1_b2"] = np.ascontiguousarray(np.asarray(inputs["L1_ffn_b2"], f32))
    return p



class AttnBuilder(LaunchBase):
    def __init__(self, L, parent=None, io=None, entry=None):
        super().__init__(f"attn{L}", parent=parent, io=io, entry=entry)
        self.L = L
        self.build()

    def bk(self, i, r0=0, r1=128):
        return self.ps[r0:r1, i * 512:(i + 1) * 512]

    def bkb(self, i, c0, c1, r0=0, r1=128):
        return self.psb[r0:r1, i * 1024 + c0:i * 1024 + c1]

    def mview(self, hh, a, b):
        if self.L == 0:
            return self.q_h[64:65, a:b]
        return self.q_hA[64:65, a:b] if hh == 0 else self.q_hB[0:1, a:b]

    def build(self):
        L = self.L
        B = self

        self.ident_in = B.din("c_ident", [128, 128], BF16)
        self.negones_in = B.din("c_negones", [1, 128], BF16)
        self.ones64_in = B.din("c_ones64", [128, 2], BF16)
        if L == 0:
            self.x_in = B.din("x", [T, D], F32)
            self.wq_in = B.din("wdq", [D, D], F32)
            self.wk_in = B.din("wdk", [D, D], F32)
            self.wv_in = B.din("wdv", [D, D], F32)
            self.up_in = B.din("up", [128, 640], F32)
            self.csn_in = B.din("c_csn", [64, 2 * T], BF16)
            self.mask_in = B.din("c_mask", [128, 2048], BF16)
            self.xn_out = B.dout("xn_fm_o", [128, 8 * T], BF16)
        else:
            self.xn_in = B.din("xn_fm", [128, 8 * T], BF16)
            self.wq_in = B.din("wtq", [D, D], F32)
            self.wk_in = B.din("wtk", [D, D], F32)
            self.wv_in = B.din("wtv", [D, D], F32)
            self.bq_in = B.din("bq", [128, 8], F32)
            self.bk_in = B.din("bk", [128, 8], F32)
            self.bv_in = B.din("bv", [128, 8], F32)
        self.y_out = B.dout("y_tm", [T, D], BF16)

        self.xn = B.sb("xn", [128, 8 * T], BF16)
        self.y_tm = B.sb("y_tm_s", [128, 16 * D], BF16)
        self.ident = B.sb("ident", [128, 128], BF16)
        self.negones = B.sb("negones", [1, 128], BF16)
        self.ones64 = B.sb("ones64", [128, 2], BF16)
        self.wbf = B.sb("wbf", [128, 2 * 3 * D], BF16)
        if L == 0:
            self.xst = B.sb("xst", [128, 2 * D], F32)
            self.wst = self.xst
        else:
            self.wst = B.sb("wst", [128, 2 * D], F32)
        if L == 0:
            self.xsq = B.sb("xsq", [128, 2 * D], F32)
            self.xnt = B.sb("xnt", [128, 2 * D], BF16)
            self.lst = B.sb("lst", [128, 24], F32)
            self.csn = B.sb("csn", [64, 2 * T], BF16)
            self.maskb = B.sb("maskb", [128, 2048], BF16)
            self.upf = B.sb("upf", [128, 640], F32)
            self.upb = B.sb("upb", [128, 640], BF16)
            self.qs = B.sb("qs", [128, CHT], BF16)
            self.ks = B.sb("ks", [128, T], BF16)
            self.vs = B.sb("vs", [128, T], BF16)
            self.q_h = B.sb("q_h", [128, CHT], BF16)
            self.k_h = B.sb("k_h", [128, T], BF16)
            self.rtmp = B.sb("rtmp", [64, 1024], BF16)
            self.v2f = B.sb("v2f", [128, T], BF16)
        else:
            self.bfm = B.sb("bfm", [128, 24], F32)
            self.q_hA = B.sb("q_hA", [128, CHT], BF16)
            self.q_hB = B.sb("q_hB", [128, CHT], BF16)
            self.k_hA = B.sb("k_hA", [128, T], BF16)
            self.k_hB = B.sb("k_hB", [128, T], BF16)
            self.v2f = B.sb("v2f", [128, T], BF16)
        if L == 0:
            self.sq = self.xsq.bitcast(BF16)
        else:
            self.sq = B.sb("sqb", [128, T], BF16)
        self.v_tm = B.sb("v_tm", [128, 32 * 130], BF16)
        self.PT = B.sb("PT", [128, 2 * 1024], BF16)
        self.yev = B.sb("yev", [128, 512], BF16)
        self.kst = B.sb("kst", [128, 16], F32)

        self.ps = B.psum_("ps", [128, 6 * 512], F32)
        self.psb = self.ps.bitcast(BF16)
        self.G = BankGate()

        for nm in ("s_c s_x s_gq s_ln s_xe s_vm s_sm s_w s_wc s_pj s_pe s_ua "
                   "s_vt s_ve s_s s_ex s_mk s_pv s_ye s_yd s_nr s_sq").split():
            setattr(self, nm, B.sem(nm))
        self.s_xp = [B.sem("s_xp0"), B.sem("s_xp1")]
        self.s_wp = [B.sem("s_wp0"), B.sem("s_wp1")]
        self.s_ys = B.sem("s_ys")
        self.s_vo = B.sem("s_vo")

        if self.parent is None:
            self.blk = self.ctx.enter_context(self.nc.Block())
        else:
            self.blk = self.parent.blk
        self.phase0()
        if L == 0:
            self.ln0()
        npair = 0
        for c in range(2):
            for p in range(8):
                if npair < DBG_PAIRS:
                    self.pair(c, p)
                    npair += 1
            if npair >= (c + 1) * 8:
                self.chunk_tail(c)
        self.epilogue()
        self.finish()

    def phase0(self):
        L, blk = self.L, self.blk
        s_c, s_x = self.s_c, self.s_x

        @blk.sync
        def _(sync):
            self.dma(sync, s_c, self.ident[:, :], self.ident_in)
            self.dma(sync, s_c, self.negones[:, :], self.negones_in)
            self.dma(sync, s_c, self.ones64[:, :], self.ones64_in)
            if L == 0:
                self.dma(sync, s_c, self.csn[:, :], self.csn_in)
                self.dma(sync, s_c, self.maskb[:, :], self.mask_in)
                self.dma(sync, s_c, self.upf[:, :], self.up_in)
            else:
                for i, b in enumerate((self.bq_in, self.bk_in, self.bv_in)):
                    self.dma(sync, s_c, self.bfm[:, 8 * i:8 * i + 8], b)
                for i in range(8):
                    self.dma(sync, s_x, self.xn[:, i * T:(i + 1) * T],
                             self.xn_in[:, i * T:(i + 1) * T])
        self.CONSTS = s_c.n
        self.XN0 = s_x.n
        self.UPC = self.s_wc.n

        if L == 0:
            s_wc = self.s_wc

            @blk.gpsimd
            def _(gp):
                gp.wait_ge(s_c.h, self.CONSTS)
                s_wc.inc(gp.tensor_copy(self.upb[:, :], self.upf[:, :]))
            self.UPC = s_wc.n

        @blk.gpsimd
        def _(gp):
            if L == 0:
                m0 = self.s_gq.inc(gp.memset(self.k_h[64:128, :], 0.0))
                gp.wait_ge(self.s_gq.h, m0)
                gp.memset(self.k_h[64:65, :], -1.0)
                gp.memset(self.q_h[64:128, :], 0.0)
            else:
                m0 = self.s_gq.inc(gp.memset(self.k_hA[64:128, :], 0.0))
                gp.wait_ge(self.s_gq.h, m0)
                gp.memset(self.k_hA[64:65, :], -1.0)
                m0 = self.s_gq.inc(gp.memset(self.k_hB[0:64, :], 0.0))
                gp.wait_ge(self.s_gq.h, m0)
                gp.memset(self.k_hB[0:1, :], -1.0)
                gp.memset(self.q_hA[64:128, :], 0.0)
                gp.memset(self.q_hB[0:64, :], 0.0)
            self.s_gq.inc(gp.memset(self.yev[64:128, :], 0.0))
        self.PADZ = self.s_gq.n

    def ln0(self):
        blk = self.blk
        self.LNB = self.s_ln.n
        self.XEB = self.s_xe.n
        GQ0 = self.s_gq.n
        SM0 = self.s_sm.n
        VT0 = self.s_vt.n
        if not DBG_LN0:
            self.s_xe.n += 32
            self.s_ln.n += 32

            @blk.scalar
            def _(sc):
                sc.sem_inc(self.s_xe.h, 32)
                sc.sem_inc(self.s_ln.h, 32)

            @blk.gpsimd
            def _(gp):
                self.s_gq.inc(gp.memset(self.sq[64:128, :], 0.0))
            self.SQZ = self.s_gq.n
            return
        s_x, s_gq, s_vm, s_sm, s_ln, s_vt, s_xe = (
            self.s_x, self.s_gq, self.s_vm, self.s_sm, self.s_ln, self.s_vt, self.s_xe)
        XN0 = self.XN0
        xst, xsq, xnt, lst, xn = self.xst, self.xsq, self.xnt, self.lst, self.xn
        VM0 = s_vm.n

        xm = []

        @blk.sync
        def _(sync):
            for tt in range(32):
                par = tt % 2
                if tt >= 2:
                    sync.wait_ge(s_ln.h, self.LNB + tt - 1)
                self.dma(sync, self.s_xp[par], xst[:, par * D:(par + 1) * D],
                         self.x_in[tt * 128:(tt + 1) * 128, :])
                xm.append(self.s_xp[par].n)

        @blk.gpsimd
        def _(gp):
            for tt in range(32):
                par = tt % 2
                gp.wait_ge(self.s_xp[tt % 2].h, xm[tt])
                if tt >= 2:
                    gp.wait_ge(s_vm.h, VM0 + 8 * (tt - 2) + 1)
                s_gq.inc(gp.tensor_tensor(xsq[:, par * D:(par + 1) * D],
                                          xst[:, par * D:(par + 1) * D],
                                          xst[:, par * D:(par + 1) * D], ALU.mult))
            gp.wait_ge(s_vm.h, VM0 + 8 * 31 + 1)
            s_gq.inc(gp.memset(self.sq[64:128, :], 0.0))
            self.SQZ = s_gq.n

        @blk.vector
        def _(v):
            for tt in range(32):
                par = tt % 2
                o = par * 12
                xv = xst[:, par * D:(par + 1) * D]
                qv = xsq[:, par * D:(par + 1) * D]
                v.wait_ge(self.s_xp[tt % 2].h, xm[tt])
                v.wait_ge(s_gq.h, GQ0 + tt + 1)
                v.reduce_sum(lst[:, o:o + 1], xv, axis=AX.X)
                v.wait_ge(s_vm.h, s_vm.inc(v.reduce_sum(lst[:, o + 1:o + 2], qv, axis=AX.X)))
                v.wait_ge(s_vm.h, s_vm.inc(v.tensor_scalar(
                    lst[:, o + 2:o + 3], lst[:, o:o + 1], 1.0 / D, None, ALU.mult)))
                v.tensor_tensor(lst[:, o + 3:o + 4], lst[:, o + 2:o + 3],
                                lst[:, o + 2:o + 3], ALU.mult)
                v.wait_ge(s_vm.h, s_vm.inc(v.tensor_scalar(
                    lst[:, o + 4:o + 5], lst[:, o + 1:o + 2], 1.0 / D, None, ALU.mult)))
                v.wait_ge(s_vm.h, s_vm.inc(v.tensor_tensor(
                    lst[:, o + 5:o + 6], lst[:, o + 4:o + 5], lst[:, o + 3:o + 4], ALU.subtract)))
                v.wait_ge(s_vm.h, s_vm.inc(v.tensor_scalar(
                    lst[:, o + 6:o + 7], lst[:, o + 5:o + 6], 1.0, 1e-5, ALU.mult, ALU.add)))
                s_vm.inc(v.reciprocal(lst[:, o + 7:o + 8], lst[:, o + 6:o + 7]))
                v.wait_ge(s_sm.h, SM0 + tt + 1)
                v.wait_ge(s_vm.h, s_vm.inc(v.tensor_tensor(
                    lst[:, o + 9:o + 10], lst[:, o + 2:o + 3], lst[:, o + 8:o + 9], ALU.mult)))
                v.wait_ge(s_vm.h, s_vm.inc(v.tensor_scalar(
                    lst[:, o + 10:o + 11], lst[:, o + 9:o + 10], -1.0, None, ALU.mult)))
        NV = 8

        @blk.scalar
        def _(sc):
            for tt in range(32):
                par = tt % 2
                o = par * 12
                sc.wait_ge(s_vm.h, VM0 + NV * tt + 6)
                s_sm.inc(sc.activation(lst[:, o + 8:o + 9], lst[:, o + 7:o + 8], AF.Sqrt))
                sc.wait_ge(s_sm.h, s_sm.n)
                sc.wait_ge(s_vm.h, VM0 + NV * tt + 8)
                if tt >= 2:
                    sc.wait_ge(s_vt.h, VT0 + tt - 1)
                s_ln.inc(sc.activation(xnt[:, par * D:(par + 1) * D],
                                       xst[:, par * D:(par + 1) * D], AF.Identity,
                                       bias=lst[:, o + 10:o + 11], scale=lst[:, o + 8:o + 9]))
                if tt >= 1:
                    sc.wait_ge(s_vt.h, VT0 + tt)
                    src = self.bkb(4 + (tt - 1) % 2, 0, 1024)
                    dst = xn[:, :].rearrange("p (c t) -> p c t", c=8)[:, :, (tt - 1) * 128:tt * 128]
                    s_xe.inc(sc.copy(dst, src))
            sc.wait_ge(s_vt.h, VT0 + 32)
            src = self.bkb(4 + 31 % 2, 0, 1024)
            dst = xn[:, :].rearrange("p (c t) -> p c t", c=8)[:, :, 31 * 128:32 * 128]
            s_xe.inc(sc.copy(dst, src))

        @blk.tensor
        def _(t):
            t.wait_ge(self.s_c.h, self.CONSTS)
            for tt in range(32):
                par = tt % 2
                t.wait_ge(s_ln.h, self.LNB + tt + 1)
                if tt >= 2:
                    t.wait_ge(s_xe.h, self.XEB + tt - 1)
                for cch in range(8):
                    ins = t.transpose(self.bkb(4 + tt % 2, cch * 128, (cch + 1) * 128),
                                      xnt[:, par * D + cch * 128:par * D + (cch + 1) * 128],
                                      self.ident[:, :])
                s_vt.inc(ins)

        @blk.scalar
        def _(sc):
            sc.wait_ge(s_xe.h, self.XEB + 32)
            for i in range(8):
                self.dma(sc, self.s_yd, self.xn_out[:, i * T:(i + 1) * T],
                         xn[:, i * T:(i + 1) * T])
        self.XNDMA = self.s_yd.n

    def pair(self, c, p):
        L, blk, G, M = self.L, self.blk, self.G, self.M
        kend = 2048 if (L == 0 and c == 0) else 4096
        nkt = kend // 128
        qbase = c * CHT
        it = c * 8 + p
        par = it % 2
        xn = self.xn
        s_w, s_wc, s_pj, s_pe = self.s_w, self.s_wc, self.s_pj, self.s_pe

        @blk.sync
        def _(sync):
            if L == 0 and it == 0:
                sync.wait_ge(self.s_ln.h, self.LNB + 32)
            wc0 = self.UPC
            for mi, w in enumerate((self.wk_in, self.wv_in, self.wq_in)):
                gmi = it * 3 + mi
                sp = gmi % 2
                if gmi >= 2:
                    sync.wait_ge(s_wc.h, wc0 + gmi - 1)
                for kc in range(8):
                    self.dma(sync, self.s_wp[sp],
                             self.wst[:, sp * D + kc * 128:sp * D + (kc + 1) * 128],
                             w[kc * 128:(kc + 1) * 128, p * 128:(p + 1) * 128])
                M[("wdma", gmi)] = self.s_wp[sp].n

        @blk.gpsimd
        def _(gp):
            for mi in range(3):
                gmi = it * 3 + mi
                sp = gmi % 2
                gp.wait_ge(self.s_wp[gmi % 2].h, M[("wdma", gmi)])
                if it >= 2:
                    gp.wait_ge(s_pj.h, M[("prjdone", it - 2)])
                s_wc.inc(gp.tensor_copy(
                    self.wbf[:, (par * 3 + mi) * D:(par * 3 + mi + 1) * D],
                    self.wst[:, sp * D:(sp + 1) * D]))
                M[("wcv", gmi)] = s_wc.n

        mats = [("k", kend), ("v", kend), ("q", CHT)]
        evn = []
        cum = 0
        for nm, ntok in mats:
            per = 2 if (L == 0 or nm == "v") else 4
            for g in range(ntok // 1024):
                cum += per
                evn.append(cum)

        @blk.tensor
        def _(t):
            t.wait_ge(self.s_c.h, self.CONSTS)
            if L == 0:
                t.wait_ge(self.s_xe.h, self.XEB + 32)
                t.wait_ge(s_wc.h, self.UPC)
            else:
                t.wait_ge(self.s_x.h, self.XN0)
            gi = 0
            pe0 = s_pe.n
            for mi, (nm, ntok) in enumerate(mats):
                gmi = it * 3 + mi
                t.wait_ge(s_wc.h, M[("wcv", gmi)])
                absb = qbase if nm == "q" else 0
                for g in range(ntok // 1024):
                    bks = [2 * (gi % 2), 2 * (gi % 2) + 1]
                    if gi >= 2:
                        t.wait_ge(s_pe.h, pe0 + evn[gi - 2])
                    else:
                        G.acquire(t, bks)
                    for kc in range(8):
                        for t2 in range(2):
                            ins = t.matmul(
                                self.bk(bks[t2]),
                                self.wbf[:, (par * 3 + mi) * D + kc * 128:
                                         (par * 3 + mi) * D + (kc + 1) * 128],
                                xn[:, kc * T + absb + g * 1024 + t2 * 512:
                                   kc * T + absb + g * 1024 + (t2 + 1) * 512],
                                start=(kc == 0), stop=(kc == 7))
                            if kc == 7:
                                s_pj.inc(ins)
                    M[("prj", it, mi, g)] = s_pj.n
                    gi += 1
            M[("prjdone", it)] = s_pj.n

        @blk.scalar
        def _(sc):
            gi = 0
            pe_base = s_pe.n
            for mi, (nm, ntok) in enumerate(mats):
                for g in range(ntok // 1024):
                    bks = [2 * (gi % 2), 2 * (gi % 2) + 1]
                    sc.wait_ge(s_pj.h, M[("prj", it, mi, g)])
                    for t2 in range(2):
                        col = g * 1024 + t2 * 512
                        if L == 0:
                            dst = {"k": self.ks, "v": self.vs, "q": self.qs}[nm]
                            s_pe.inc(sc.activation(dst[:, col:col + 512],
                                                   self.bk(bks[t2]), AF.Copy))
                        else:
                            bcol = {"q": 0, "k": 8, "v": 16}[nm] + p
                            bias = self.bfm[:, bcol:bcol + 1]
                            if nm == "v":
                                s_pe.inc(sc.activation(self.v2f[:, col:col + 512],
                                                       self.bk(bks[t2]), AF.Identity,
                                                       bias=bias))
                            else:
                                dA, dB = ((self.k_hA, self.k_hB) if nm == "k"
                                          else (self.q_hA, self.q_hB))
                                s_pe.inc(sc.activation(
                                    dA[0:64, col:col + 512], self.bk(bks[t2], 0, 64),
                                    AF.Identity, bias=self.bfm[0:64, bcol:bcol + 1]))
                                s_pe.inc(sc.activation(
                                    dB[64:128, col:col + 512], self.bk(bks[t2], 64, 128),
                                    AF.Identity, bias=self.bfm[64:128, bcol:bcol + 1]))
                    G.release(bks, s_pe)
                    assert s_pe.n == pe_base + evn[gi]
                    gi += 1
            M[("prjevac", it)] = s_pe.n

        if L == 0 and DBG_VUP:
            @blk.tensor
            def _(t):
                t.wait_ge(s_pe.h, M[("prjevac", it)])
                pe0v = s_pe.n
                for g in range(nkt // 4):
                    bb = 2 * (g % 2)
                    if g >= 2:
                        t.wait_ge(s_pe.h, pe0v + (g - 1))
                    else:
                        G.acquire(t, [bb])
                    i1 = t.matmul(self.bk(bb), self.upb[:, 512:640],
                                  self.vs[:, g * 512:(g + 1) * 512], start=True, stop=True)
                    s_pj.inc(i1)
                    M[("vup", it, g)] = s_pj.n

            @blk.scalar
            def _(sc):
                for g in range(nkt // 4):
                    bb = 2 * (g % 2)
                    sc.wait_ge(s_pj.h, M[("vup", it, g)])
                    s_pe.inc(sc.activation(self.v2f[:, g * 512:(g + 1) * 512],
                                           self.bk(bb), AF.Copy))
                    G.release([bb], s_pe)
                M[("vupe", it)] = s_pe.n

        s_vt, s_ve = self.s_vt, self.s_ve
        if not DBG_VT:
            M[("vtmdone", it)] = self.s_ve.n
            M[("pairpv", it)] = self.s_pv.n
            if DBG_HEADS:
                for hh in range(2):
                    self.head(c, p, hh, kend, nkt, qbase, it)
            M[("pairpv", it)] = self.s_pv.n
            return

        @blk.gpsimd
        def _(gp):
            if it >= 1:
                gp.wait_ge(self.s_pv.h, M[("pairpv", it - 1)])
            if DBG_VT_MS:
                self.s_vo.inc(gp.memset(self.v_tm[:, 0:nkt * 130], 1.0))
            else:
                self.s_vo.inc(gp.engine_nop())
            M[("vones", it)] = self.s_vo.n

        @blk.tensor
        def _(t):
            vsrc = self.v2f
            wait_m = M[("vupe", it)] if (L == 0 and DBG_VUP) else M[("prjevac", it)]
            t.wait_ge(s_pe.h, wait_m)
            pe0v2 = s_pe.n
            for g4 in range(nkt // 4):
                M[("vtm", it, g4)] = pe0v2 + 2 * (g4 + 1)
            for g4 in range(nkt // 4):
                half = g4 % 2
                if g4 >= 2:
                    t.wait_ge(s_pe.h, M[("vtm", it, g4 - 2)])
                else:
                    G.acquire(t, [4 + half])
                if DBG_VT_T:
                    for j in range(4):
                        ins = t.transpose(
                            self.bkb(4 + half, j * 128, (j + 1) * 128),
                            vsrc[:, (g4 * 4 + j) * 128:(g4 * 4 + j + 1) * 128],
                            self.ident[:, :])
                else:
                    ins = t.engine_nop()
                s_vt.inc(ins)
                M[("vT", it, g4)] = s_vt.n

        @blk.scalar
        def _(sc):
            sc.wait_ge(self.s_vo.h, M[("vones", it)])
            for g4 in range(nkt // 4):
                half = g4 % 2
                sc.wait_ge(s_vt.h, M[("vT", it, g4)])
                base = (4 + half) * 1024
                for hh in range(2):
                    if g4 < DBG_VT_N:
                        for j in range(3, -1, -1):
                            kt = g4 * 4 + j
                            ins = sc.copy(
                                self.v_tm[:, kt * 130 + hh * 65:kt * 130 + hh * 65 + 64],
                                self.psb[:, base + j * 128 + hh * 64:base + j * 128 + (hh + 1) * 64])
                    else:
                        ins = sc.copy(self.kst[:, 0:1], self.kst[:, 1:2])
                    s_pe.inc(ins)
                assert M[("vtm", it, g4)] == s_pe.n
            M[("vtmdone", it)] = s_pe.n
        G.release([4, 5], s_pe)

        for hh in range(2 if DBG_HEADS else 0):
            self.head(c, p, hh, kend, nkt, qbase, it)

        M[("pairpv", it)] = self.s_pv.n

    def head(self, c, p, hh, kend, nkt, qbase, it):
        L, blk, G, M = self.L, self.blk, self.G, self.M
        h = 2 * p + hh
        hr = 64 * hh
        s_pj, s_pe, s_ua, s_sq, s_nr = self.s_pj, self.s_pe, self.s_ua, self.s_sq, self.s_nr
        s_vm, s_sm = self.s_vm, self.s_sm

        if L == 0:
            for kind, lat, dst, ntok, absb, ubase in (
                    ("k", self.ks, self.k_h, kend, 0, 256 + 128 * hh),
                    ("q", self.qs, self.q_h, CHT, qbase, 0 + 128 * hh)):
                ntt = ntok // 512

                @blk.tensor
                def _(t, kind=kind, lat=lat, ntt=ntt, absb=absb, ubase=ubase):
                    if kind == "k":
                        t.wait_ge(s_pe.h, M[("prjevac", it)])
                        if hh == 1:
                            t.wait_ge(self.s_s.h, M[("attdone", it, 0)])
                    pe0u = s_pe.n
                    ua0 = s_ua.n
                    for g in range(ntt):
                        b0 = 2 * (g % 2)
                        b1 = b0 + 1
                        if g >= 2:
                            t.wait_ge(s_pe.h, pe0u + (g - 1))
                            t.wait_ge(s_ua.h, ua0 + 2 * (g - 1))
                        else:
                            G.acquire(t, [b0, b1])
                        rhs = lat[0:128, g * 512:(g + 1) * 512]
                        i1 = t.matmul(self.bk(b0, 0, 64), self.upb[:, ubase:ubase + 64],
                                      rhs, start=True, stop=True)
                        s_pj.inc(i1)
                        i3 = t.matmul(self.bk(b1, 0, 64),
                                      self.upb[:, ubase + 64:ubase + 128], rhs,
                                      start=True, stop=True)
                        s_pj.inc(i3)
                        M[("up", it, hh, kind, g)] = s_pj.n

                @blk.scalar
                def _(sc, kind=kind, dst=dst, ntt=ntt):
                    for g in range(ntt):
                        b0 = 2 * (g % 2)
                        sc.wait_ge(s_pj.h, M[("up", it, hh, kind, g)])
                        s_pe.inc(sc.activation(dst[0:32, g * 512:(g + 1) * 512],
                                               self.bk(b0, 0, 32), AF.Copy))
                        G.release([b0], s_pe)
                        M[("upc", it, hh, kind, g)] = s_pe.n

                @blk.vector
                def _(v, kind=kind, dst=dst, ntt=ntt, absb=absb):
                    ua0v = s_ua.n
                    for g in range(ntt):
                        b0 = 2 * (g % 2)
                        b1 = b0 + 1
                        v.wait_ge(s_pj.h, M[("up", it, hh, kind, g)])
                        v.wait_ge(s_ua.h, ua0v + 2 * g)
                        ac = absb + g * 512
                        v.tensor_mul(self.rtmp[32:64, 0:512], self.bk(b0, 32, 64),
                                     self.csn[32:64, ac:ac + 512])
                        mm = s_ua.inc(v.tensor_mul(self.rtmp[32:64, 512:1024],
                                                   self.bk(b1, 32, 64),
                                                   self.csn[32:64, T + ac:T + ac + 512]))
                        v.wait_ge(s_ua.h, mm)
                        s_ua.inc(v.tensor_add(dst[32:64, g * 512:(g + 1) * 512],
                                              self.rtmp[32:64, 0:512],
                                              self.rtmp[32:64, 512:1024]))
                        G.release([b0, b1], s_ua)
                        M[("uprope", it, hh, kind, g)] = s_ua.n
                M[(kind + "done", it, hh)] = (s_pe.n, s_ua.n)

            qsrc, ksrc = self.q_h, self.k_h
        else:
            qsrc, ksrc = ((self.q_hA, self.k_hA) if hh == 0
                          else (self.q_hB, self.k_hB))

        if L == 0:
            self.norms_k(it, hh, kend, ksrc)
            self.norms_q(it, hh, qsrc)
        else:
            if hh == 0:
                self.norms_k(it, 0, kend, ksrc)
                self.norms_k(it, 1, kend, ksrc)
                self.norms_q(it, 0, qsrc)
                self.norms_q(it, 1, qsrc)

        s_s, s_ex, s_mk, s_pv, s_ye = self.s_s, self.s_ex, self.s_mk, self.s_pv, self.s_ye

        for qg in range(4 if DBG_ATT else 0):
            qa = qbase + qg * 512
            nkt_q = min(nkt, (qa + 512) // 128) if L == 0 else nkt
            ngrp = (nkt_q + 1) // 2

            if L == 0:
                if not hasattr(self, "mk_pred"):
                    self.mk_pred = self.s_mk.n
                for j in range(4):
                    M[("mk", it, hh, qg, qa // 128 + j)] = self.mk_pred + j + 1
                self.mk_pred += 4
            if not hasattr(self, "ex_pred"):
                self.ex_pred = self.s_ex.n
            for g in range(ngrp):
                M[("exp", it, hh, qg, g)] = self.ex_pred + g + 1
            self.ex_pred += ngrp

            M[("pvprev", it, hh, qg)] = self.s_pv.n

            @blk.tensor
            def _(t, qg=qg, qa=qa, nkt_q=nkt_q, ngrp=ngrp):
                t.wait_ge(self.s_gq.h, self.PADZ)
                t.wait_ge(s_pe.h, M[("mrow", it, hh)])
                t.wait_ge(s_pe.h, M[("vtmdone", it)])
                for g in range(ngrp):
                    sgb = [2 * (g % 2), 2 * (g % 2) + 1]
                    if g >= 2:
                        t.wait_ge(self.s_ex.h, M[("exp", it, hh, qg, g - 2)])
                    else:
                        G.acquire(t, sgb)
                    for kt in (2 * g, 2 * g + 1):
                        if kt >= nkt_q:
                            continue
                        sub = kt % 2
                        vw = self.bk(sgb[sub])
                        s_s.inc(t.matmul(vw, ksrc[0:128, kt * 128:(kt + 1) * 128],
                                         qsrc[0:128, qg * 512:(qg + 1) * 512],
                                         start=True, stop=True))
                    M[("sgrp", it, hh, qg, g)] = s_s.n
                    if g >= 1:
                        self._pv_group(t, it, hh, qg, g - 1, nkt_q, qa)
                self._pv_group(t, it, hh, qg, ngrp - 1, nkt_q, qa)
                M[("pvdone", it, hh, qg)] = s_pv.n

            @blk.scalar
            def _(sc, qg=qg, qa=qa, nkt_q=nkt_q, ngrp=ngrp):
                for g in range(ngrp):
                    sgb = [2 * (g % 2), 2 * (g % 2) + 1]
                    n_in_g = 1 if (2 * g + 1 >= nkt_q) else 2
                    sc.wait_ge(s_s.h, M[("sgrp", it, hh, qg, g)])
                    if g >= 2:
                        sc.wait_ge(s_pv.h, M[("pvg", it, hh, qg, g - 2)])
                    elif ("pvprev", it, hh, qg) in M:
                        sc.wait_ge(s_pv.h, M[("pvprev", it, hh, qg)])
                    s_ex.inc(sc.activation(self.PT[:, (g % 2) * 1024:(g % 2) * 1024 + n_in_g * 512],
                                           self.ps[:, sgb[0] * 512:sgb[0] * 512 + n_in_g * 512],
                                           AF.Exp, scale=0.125))
                    assert M[("exp", it, hh, qg, g)] == s_ex.n
                    G.release(sgb, s_ex)

            if L == 0:
                @blk.gpsimd
                def _(gp, qg=qg, qa=qa, nkt_q=nkt_q):
                    for kt in range(nkt_q):
                        ktb = kt * 128
                        if ktb < qa:
                            continue
                        r = (ktb - qa) // 128
                        g = kt // 2
                        gp.wait_ge(s_ex.h, M[("exp", it, hh, qg, g)])
                        pcol = (g % 2) * 1024 + (kt % 2) * 512
                        s_mk.inc(gp.tensor_mul(self.PT[:, pcol:pcol + 512],
                                               self.PT[:, pcol:pcol + 512],
                                               self.maskb[:, r * 512:(r + 1) * 512]))
                        assert M[("mk", it, hh, qg, kt)] == s_mk.n

            @blk.scalar
            def _(sc, qg=qg):
                sc.wait_ge(s_pv.h, M[("pvdone", it, hh, qg)])
                s_ye.inc(sc.activation(self.yev[0:65, :], self.bk(4, 0, 65), AF.Copy))
                M[("yev", it, hh, qg)] = s_ye.n
                G.release([4], s_ye)

            ys0 = self.s_ys.n
            for j in range(4):
                M[("ystj", it, hh, qg, j)] = ys0 + j + 1
            M[("yst", it, hh, qg)] = ys0 + 4
            vt0y = self.s_vt.n

            @blk.tensor
            def _(t, qg=qg):
                t.wait_ge(s_ye.h, M[("yev", it, hh, qg)])
                for j in range(4):
                    if j < 2:
                        G.acquire(t, [4 + j % 2])
                    else:
                        t.wait_ge(self.s_ys.h, M[("ystj", it, hh, qg, j - 2)])
                    ins = t.transpose(self.bkb(4 + j % 2, 0, 128, 0, 128),
                                      self.yev[0:128, j * 128:(j + 1) * 128],
                                      self.ident[:, :])
                    self.s_vt.inc(ins)
                M[("yT", it, hh, qg)] = self.s_vt.n

            @blk.vector
            def _(v, qg=qg):
                if c == 1:
                    v.wait_ge(self.s_yd.h, M[("ydma", 0)])
                for j in range(4):
                    v.wait_ge(self.s_vt.h, vt0y + j + 1)
                    lv = self.kst[:, 12 + (j % 2):13 + (j % 2)]
                    mm = s_vm.inc(v.reciprocal(lv, self.bkb(4 + j % 2, 64, 65, 0, 128)))
                    v.wait_ge(s_vm.h, mm)
                    tt128 = qg * 4 + j
                    self.s_ys.inc(v.tensor_scalar(
                        self.y_tm[:, tt128 * D + h * 64:tt128 * D + (h + 1) * 64],
                        self.bkb(4 + j % 2, 0, 64, 0, 128),
                        lv, None, ALU.mult))
                    assert M[("ystj", it, hh, qg, j)] == self.s_ys.n
                G.release([4, 5], self.s_ys)

        M[("attdone", it, hh)] = self.s_s.n

    def norms_k(self, it, hh, kend, ksrc):
        L, blk, G, M = self.L, self.blk, self.G, self.M
        s_pj, s_pe, s_ua, s_sq, s_nr = self.s_pj, self.s_pe, self.s_ua, self.s_sq, self.s_nr

        @blk.scalar
        def _(sc):
            if L == 0:
                pe_m, ua_m = M[("kdone", it, hh)]
                sc.wait_ge(s_pe.h, pe_m)
                sc.wait_ge(s_ua.h, ua_m)
                s_sq.inc(sc.activation(self.sq[0:64, 0:kend], ksrc[0:64, 0:kend], AF.Square))
            elif hh == 0:
                sc.wait_ge(s_pe.h, M[("prjevac", it)])
                s_sq.inc(sc.activation(self.sq[0:64, 0:kend],
                                       self.k_hA[0:64, 0:kend], AF.Square))
                s_sq.inc(sc.activation(self.sq[64:128, 0:kend],
                                       self.k_hB[64:128, 0:kend], AF.Square))
            M[("ksq", it, hh)] = s_sq.n

        o64col = self.ones64[:, (hh if L == 1 else 0):(hh if L == 1 else 0) + 1]

        @blk.tensor
        def _(t):
            if L == 0:
                t.wait_ge(self.s_gq.h, self.SQZ)
            t.wait_ge(s_sq.h, M[("ksq", it, hh)])
            nr0 = self.s_nr.n
            for g in range(kend // 512):
                bb = 4 if g % 2 == 0 else 5
                if g >= 2:
                    t.wait_ge(self.s_nr.h, nr0 + (g - 1))
                else:
                    G.acquire(t, [bb])
                s_pj.inc(t.matmul(self.bk(bb, 0, 1), o64col,
                                  self.sq[0:128, g * 512:(g + 1) * 512],
                                  start=True, stop=True))
                M[("knmm", it, hh, g)] = s_pj.n

        @blk.vector
        def _(v):
            for g in range(kend // 512):
                bb = 4 if g % 2 == 0 else 5
                v.wait_ge(s_pj.h, M[("knmm", it, hh, g)])
                s_nr.inc(v.reduce_max(self.kst[0:1, g:g + 1], self.bk(bb, 0, 1), axis=AX.X))
                G.release([bb], s_nr)
            v.wait_ge(s_nr.h, s_nr.n)
            mm = s_nr.inc(v.reduce_max(self.kst[0:1, 8 + hh:9 + hh],
                                       self.kst[0:1, 0:kend // 512], axis=AX.X))
            v.wait_ge(s_nr.h, mm)
            s_nr.inc(v.tensor_scalar(self.kst[0:1, 10 + hh:11 + hh],
                                     self.kst[0:1, 8 + hh:9 + hh], 1.1025, None, ALU.mult))
            M[("khat", it, hh)] = s_nr.n

    def norms_q(self, it, hh, qsrc):
        L, blk, G, M = self.L, self.blk, self.G, self.M
        s_pj, s_pe, s_ua, s_sq, s_nr = self.s_pj, self.s_pe, self.s_ua, self.s_sq, self.s_nr

        @blk.scalar
        def _(sc):
            if L == 0:
                pe_m, ua_m = M[("qdone", it, hh)]
                sc.wait_ge(s_pe.h, pe_m)
                sc.wait_ge(s_ua.h, ua_m)
                sc.wait_ge(s_nr.h, M[("khat", it, hh)])
                s_sq.inc(sc.activation(self.sq[0:64, 0:CHT], qsrc[0:64, 0:CHT], AF.Square))
            elif hh == 0:
                sc.wait_ge(s_nr.h, M[("khat", it, 1)])
                s_sq.inc(sc.activation(self.sq[0:64, 0:CHT],
                                       self.q_hA[0:64, 0:CHT], AF.Square))
                s_sq.inc(sc.activation(self.sq[64:128, 0:CHT],
                                       self.q_hB[64:128, 0:CHT], AF.Square))
            M[("qsq", it, hh)] = s_sq.n

        o64col = self.ones64[:, (hh if L == 1 else 0):(hh if L == 1 else 0) + 1]

        @blk.tensor
        def _(t):
            t.wait_ge(s_sq.h, M[("qsq", it, hh)])
            pe0n = s_pe.n
            for g in range(4):
                bb = 4 if g % 2 == 0 else 5
                if g >= 2:
                    t.wait_ge(s_pe.h, pe0n + (g - 1))
                else:
                    G.acquire(t, [bb])
                s_pj.inc(t.matmul(self.bk(bb, 0, 1), o64col,
                                  self.sq[0:128, g * 512:(g + 1) * 512],
                                  start=True, stop=True))
                M[("qnmm", it, hh, g)] = s_pj.n

        @blk.scalar
        def _(sc):
            sc.wait_ge(s_nr.h, M[("khat", it, hh)])
            for g in range(4):
                bb = 4 if g % 2 == 0 else 5
                sc.wait_ge(s_pj.h, M[("qnmm", it, hh, g)])
                s_pe.inc(sc.activation(self.mview(hh, g * 512, (g + 1) * 512),
                                       self.bk(bb, 0, 1), AF.Sqrt,
                                       scale=self.kst[0:1, 10 + hh:11 + hh]))
                G.release([bb], s_pe)
            M[("mrow", it, hh)] = s_pe.n

    def _pv_group(self, t, it, hh, qg, g, nkt_q, qa):
        L, G, M = self.L, self.G, self.M
        s_pv = self.s_pv
        for kt in (2 * g, 2 * g + 1):
            if kt >= nkt_q:
                continue
            if kt == 0:
                G.acquire(t, [4])
            if L == 0 and kt * 128 >= qa:
                t.wait_ge(self.s_mk.h, M[("mk", it, hh, qg, kt)])
            else:
                t.wait_ge(self.s_ex.h, M[("exp", it, hh, qg, g)])
            pcol = (g % 2) * 1024 + (kt % 2) * 512
            ins = t.matmul(self.bk(4, 0, 65),
                           self.v_tm[:, kt * 130 + hh * 65:kt * 130 + (hh + 1) * 65],
                           self.PT[:, pcol:pcol + 512],
                           start=(kt == 0), stop=(kt == nkt_q - 1))
            s_pv.inc(ins)
        M[("pvg", it, hh, qg, g)] = s_pv.n

    def chunk_tail(self, c):
        blk, M = self.blk, self.M
        it_last = c * 8 + 7
        snap = [M[("yst", c * 8 + p, hh, qg)] for p in range(8) for hh in range(2) for qg in range(4)]
        need = max(snap)

        @blk.scalar
        def _(sc):
            sc.wait_ge(self.s_ys.h, need)
            for tt in range(16):
                self.dma(sc, self.s_yd,
                         self.y_out[c * CHT + tt * 128:c * CHT + (tt + 1) * 128, :],
                         self.y_tm[:, tt * D:(tt + 1) * D])
        M[("ydma", c)] = self.s_yd.n

    def epilogue(self):
        blk = self.blk
        total = self.s_yd.n

        @blk.sync
        def _(sync):
            sync.wait_ge(self.s_yd.h, total)


def build_attn(L):
    b = AttnBuilder(L)
    return b.nc



class MixBuilder(LaunchBase):
    def __init__(self, L, parent=None, io=None, entry=None):
        super().__init__(f"mix{L}", parent=parent, io=io, entry=entry)
        self.L = L
        self.build()

    def bk(self, i, r0=0, r1=128):
        return self.ps[r0:r1, i * 512:(i + 1) * 512]

    def bk2(self, i):
        return self.ps[:, i * 512:(i + 2) * 512]

    def bkb(self, i, c0, c1, r0=0, r1=128):
        return self.psb[r0:r1, i * 1024 + c0:i * 1024 + c1]

    def build(self):
        L = self.L
        B = self

        self.ident_in = B.din("c_ident", [128, 128], BF16)
        self.identf_in = B.din("c_identf", [128, 128], F32)
        self.ones1_in = B.din("c_ones1", [1, 128], BF16)
        self.ones512_in = B.din("c_ones512", [1, 512], BF16)
        self.xn_in = B.din("xn_fm", [128, 8 * T], BF16)
        self.y_in = B.din("y_tm", [T, D], BF16)
        self.res_in = B.din("res", [T, D], F32)
        if L == 0:
            self.wc_in = B.din("wc", [D, D], F32)
            self.gate_in = B.din("gate_wT", [D, 8], F32)
            self.egu_in = B.din("eguT", [8, D, 2816], F32)
            self.edn_in = B.din("edownT", [8, 1408, D], F32)
            self.sg_in = B.din("sgT", [D, 2816], F32)
            self.su_in = B.din("suT", [D, 2816], F32)
            self.sd_in = B.din("sdT", [2816, D], F32)
            self.seg_in = B.din("segT", [D, 1], F32)
            self.x1_out = B.dout("x1", [T, D], F32)
            self.xn1_out = B.dout("xn1_fm", [128, 8 * T], BF16)
        else:
            self.wc_in = B.din("wto", [D, D], F32)
            self.bo_in = B.din("bo", [128, 8], F32)
            self.rw_in = B.din("router_w", [D, 8], F32)
            self.rb_in = B.din("router_b", [1, 8], F32)
            self.w1_in = B.din("w1", [8, D, 4096], F32)
            self.b1_in = B.din("b1", [8, 128, 32], F32)
            self.w2_in = B.din("w2", [8, 4096, D], F32)
            self.b2_in = B.din("b2", [8, D], F32)
            self.x1_out = B.dout("out", [T, D], F32)

        self.xn = B.sb("xn", [128, 8 * CHT], BF16)
        self.acc = B.sb("acc", [128, 8 * CHT], F32)
        self.ident = B.sb("ident", [128, 128], BF16)
        self.identf = B.sb("identf", [128, 128], F32)
        self.ones1 = B.sb("ones1", [1, 128], BF16)
        self.ones512 = B.sb("ones512", [1, 512], BF16)
        if L == 0:
            self.h_sb = B.sb("h_sb", [128, 11 * CHT], BF16)
            hf = self.h_sb.bitcast(F32)
            self.yst = self.h_sb[:, 0:2048]
            self.yfm = self.h_sb[:, 2048:6144]
            self.wcbf = self.h_sb[:, 6144:14336]
            self.xnt = self.h_sb[:, 0:2048]
            self.xne = self.h_sb[:, 2048:4096]
            self.x1t = hf[:, 7168:9216]
            self.xrst = hf[:, 9216:11264]
            self.wcst = B.sb("wcst", [128, 2 * D], F32)
            self.gust = self.wcst
            self.gubf = B.sb("gubf", [128, 2 * 8 * 256], BF16)
            self.dnst = B.sb("dnst", [128, 11 * 256], F32)
            self.dnbf = B.sb("dnbf", [128, 2 * 11 * 256], BF16)
            self.gwbf = B.sb("gwbf", [128, 72], BF16)
            self.gwst = hf[:, 0:72]
            self.lst = B.sb("lst", [128, 24], F32)
        else:
            self.bof = B.sb("bof", [128, 8], F32)
            self.h_sb = B.sb("h_sb", [128, 4 * CHT], BF16)
            hf = self.h_sb.bitcast(F32)
            self.yst = self.h_sb[:, 0:2048]
            self.yfm = self.h_sb[:, 2048:6144]
            self.x1t = hf[:, 0:2048]
            self.xrst = hf[:, 2048:4096]
            self.wcbf = B.sb("wcbf", [128, 8 * D], BF16)
            self.gust = B.sb("gust", [128, 8 * 512], F32)
            self.wcst = self.gust[:, 0:2048]
            self.gubf = B.sb("gubf", [128, 2 * 8 * 512], BF16)
            self.dnbf = B.sb("dnbf", [128, 2 * 4 * 1024], BF16)
            self.gwbf = B.sb("gwbf", [128, 72], BF16)
            self.gwst = B.sb("gwst", [128, 72], F32)
            self.b1f = B.sb("b1f", [128, 32], F32)
            self.b2r = B.sb("b2r", [1, 1024], BF16)
            self.b2st = B.sb("b2st", [1, 1024], F32)
            self.rbr = B.sb("rbr", [1, 8], BF16)
            self.rbst = B.sb("rbst", [1, 8], F32)
        self.sgev = B.sb("sgev", [128, 2048], BF16)
        self.hmid = B.sb("hmid", [128, 1024], BF16)
        self.w_all = B.sb("w_all", [128, 160], BF16)
        self.wrow = self.sgev[0:1, 0:CHT]
        self.wb_sb = B.sb("wb_sb", [128, CHT], BF16)
        self.ls = B.sb("ls", [128, 64], F32)
        self.ps = B.psum_("ps", [128, 8 * 512], F32)
        self.psb = self.ps.bitcast(BF16)
        self.G = BankGate()

        for nm in ("s_c s_x s_y s_w s_wc s_pj s_pe s_ex s_ve s_vm s_sm s_gt "
                   "s_vt s_acc s_xr s_x1 s_ln s_od s_gq s_hh").split():
            setattr(self, nm, B.sem(nm))
        self.s_wp = [B.sem("s_wp0"), B.sem("s_wp1")]
        self.s_xp = [B.sem("s_xp0"), B.sem("s_xp1")]
        self.s_yp = [B.sem("s_yp0"), B.sem("s_yp1")]
        self.s_wd = B.sem("s_wd")
        self.s_x1p = [B.sem("s_x1p0"), B.sem("s_x1p1")]
        self.s_odp = [B.sem("s_odp0"), B.sem("s_odp1")]

        if self.parent is None:
            self.blk = self.ctx.enter_context(self.nc.Block())
        else:
            self.blk = self.parent.blk
        self.phase0()
        for c in range(2):
            self.wc_phase(c)
            self.gating(c)
            nexp = 10 if L == 0 else 8
            for e in range(nexp):
                self.expert(c, e)
            self.tail(c)
        self.epilogue()
        self.finish()

    def phase0(self):
        L, blk = self.L, self.blk
        s_c, s_x = self.s_c, self.s_x

        @blk.sync
        def _(sync):
            self.emit_entry(sync)
            self.dma(sync, s_c, self.ident[:, :], self.ident_in)
            self.dma(sync, s_c, self.identf[:, :], self.identf_in)
            self.dma(sync, s_c, self.ones1[:, :], self.ones1_in)
            self.dma(sync, s_c, self.ones512[:, :], self.ones512_in)
            if L == 0:
                for kc in range(8):
                    self.dma(sync, s_c, self.gwst[:, kc * 8:kc * 8 + 8],
                             self.gate_in[kc * 128:(kc + 1) * 128, :])
                    self.dma(sync, s_c, self.gwst[:, 64 + kc:65 + kc],
                             self.seg_in[kc * 128:(kc + 1) * 128, :])
            else:
                for kc in range(8):
                    self.dma(sync, s_c, self.gwst[:, kc * 8:kc * 8 + 8],
                             self.rw_in[kc * 128:(kc + 1) * 128, :])
                self.dma(sync, s_c, self.rbst[:, :], self.rb_in)
                self.dma(sync, s_c, self.bof[:, :], self.bo_in)
            for i in range(8):
                self.dma(sync, s_x, self.xn[:, i * CHT:(i + 1) * CHT],
                         self.xn_in[:, i * T:i * T + CHT])
        self.CONSTS = s_c.n
        self.XN0 = s_x.n

        @blk.gpsimd
        def _(gp):
            gp.wait_ge(s_c.h, self.CONSTS)
            ncol = 72 if L == 0 else 64
            self.s_gq.inc(gp.tensor_copy(self.gwbf[:, 0:ncol], self.gwst[:, 0:ncol]))
            if L == 1:
                self.s_gq.inc(gp.tensor_copy(self.rbr[:, :], self.rbst[:, :]))
        self.GC = self.s_gq.n

    def wc_phase(self, c):
        L, blk, G, M = self.L, self.blk, self.G, self.M
        s_y, s_wc, s_pj, s_pe, s_vt = (self.s_y, self.s_wc, self.s_pj,
                                       self.s_pe, self.s_vt)
        vt0 = s_vt.n
        pe0 = s_pe.n
        wc0 = s_wc.n
        for gt in range(16):
            M[("yT", c, gt)] = vt0 + gt + 1
            M[("yev", c, gt)] = pe0 + gt + 1
        for oc in range(8):
            M[("wcv", c, oc)] = wc0 + oc + 1

        @blk.sync
        def _(sync, c=c):
            for oc in range(8):
                sp = oc % 2
                if oc >= 2:
                    sync.wait_ge(s_wc.h, M[("wcv", c, oc - 2)])
                elif c == 1:
                    if L == 0:
                        sync.wait_ge(s_wc.h, M[("gucv_last", 9)])
                    else:
                        sync.wait_ge(s_wc.h, M[("w2cv", 63)])
                for kc in range(8):
                    self.dma(sync, self.s_wp[sp],
                             self.wcst[:, sp * D + kc * 128:sp * D + (kc + 1) * 128],
                             self.wc_in[kc * 128:(kc + 1) * 128, oc * 128:(oc + 1) * 128])
                M[("wcd", c, oc)] = self.s_wp[sp].n

        @blk.gpsimd
        def _(gp, c=c):
            gp.wait_ge(self.s_gq.h, self.GC)
            for oc in range(8):
                sp = oc % 2
                gp.wait_ge(self.s_wp[sp].h, M[("wcd", c, oc)])
                if c == 1 and oc < 2:
                    gp.wait_ge(s_pj.h, M[("wcmm", 0)])
                self.s_wc.inc(gp.tensor_copy(
                    self.wcbf[:, oc * D:(oc + 1) * D],
                    self.wcst[:, sp * D:(sp + 1) * D]))
                assert M[("wcv", c, oc)] == self.s_wc.n

        for tg in range(4):
            @blk.sync
            def _(sync, c=c, tg=tg):
                for tt in range(4):
                    gt = tg * 4 + tt
                    par = gt % 2
                    if c == 0 and gt == 0 and L == 0:
                        sync.wait_ge(self.s_gq.h, self.GC)
                    if c == 1 and gt == 0:
                        sync.wait_ge(self.s_x1p[1].h, M[("x1", 0, 15)])
                        sync.wait_ge(self.s_x1p[0].h, M[("x1", 0, 14)])
                        sync.wait_ge(self.s_vm.h, M[("x1vd", 0)])
                        if L == 0:
                            sync.wait_ge(self.s_odp[1].h, M[("xnd", 0, 15)])
                            sync.wait_ge(self.s_odp[0].h, M[("xnd", 0, 14)])
                    if gt >= 2:
                        sync.wait_ge(s_vt.h, M[("yT", c, gt - 2)])
                    self.dma(sync, self.s_yp[par],
                             self.yst[:, par * D:(par + 1) * D],
                             self.y_in[c * CHT + gt * 128:c * CHT + (gt + 1) * 128, :])
                    M[("ydma", c, gt)] = self.s_yp[par].n

            @blk.tensor
            def _(t, c=c, tg=tg):
                t.wait_ge(self.s_c.h, self.CONSTS)
                for tt in range(4):
                    gt = tg * 4 + tt
                    par = gt % 2
                    t.wait_ge(self.s_yp[gt % 2].h, M[("ydma", c, gt)])
                    bb = 6 + gt % 2
                    if gt >= 2:
                        t.wait_ge(s_pe.h, M[("yev", c, gt - 2)])
                    else:
                        G.acquire(t, [bb])
                    for kc in range(8):
                        ins = t.transpose(self.bkb(bb, kc * 128, (kc + 1) * 128),
                                          self.yst[:, par * D + kc * 128:par * D + (kc + 1) * 128],
                                          self.ident[:, :])
                    s_vt.inc(ins)
                    assert M[("yT", c, gt)] == s_vt.n

            @blk.scalar
            def _(sc, c=c, tg=tg):
                for tt in range(4):
                    gt = tg * 4 + tt
                    bb = 6 + gt % 2
                    sc.wait_ge(s_vt.h, M[("yT", c, gt)])
                    dst = self.yfm[:, :].rearrange("p (k t) -> p k t", k=8)[:, :, tt * 128:(tt + 1) * 128]
                    src = self.bkb(bb, 0, 1024).rearrange("p (k t) -> p k t", k=8)
                    s_pe.inc(sc.copy(dst, src))
                    G.release([bb], s_pe)
                    assert M[("yev", c, gt)] == s_pe.n
                M[("yfm", c, tg)] = s_pe.n

            @blk.tensor
            def _(t, c=c, tg=tg):
                t.wait_ge(s_pe.h, M[("yfm", c, tg)])
                for ocp in range(4):
                    gidx = tg * 4 + ocp
                    bks = [2 * (gidx % 2), 2 * (gidx % 2) + 1]
                    if gidx >= 2:
                        t.wait_ge(self.s_acc.h, M[("wacc", c, gidx - 2)])
                    else:
                        G.acquire(t, bks)
                    for oc2 in range(2):
                        oc = ocp * 2 + oc2
                        t.wait_ge(s_wc.h, M[("wcv", c, oc)])
                        for kc in range(8):
                            ins = t.matmul(
                                self.bk(bks[oc2]),
                                self.wcbf[:, oc * D + kc * 128:oc * D + (kc + 1) * 128],
                                self.yfm[:, kc * 512:(kc + 1) * 512],
                                start=(kc == 0), stop=(kc == 7))
                        s_pj.inc(ins)
                    M[("wcg", c, gidx)] = s_pj.n
                    M[("wacc", c, gidx)] = self.acc_pred(2)

            @blk.scalar
            def _(sc, c=c, tg=tg):
                for ocp in range(4):
                    gidx = tg * 4 + ocp
                    bks = [2 * (gidx % 2), 2 * (gidx % 2) + 1]
                    sc.wait_ge(s_pj.h, M[("wcg", c, gidx)])
                    if c == 1 and gidx == 0:
                        sc.wait_ge(s_vt.h, M[("tailT", 0)])
                    for oc2 in range(2):
                        oc = ocp * 2 + oc2
                        dv = self.acc[:, oc * CHT + tg * 512:oc * CHT + (tg + 1) * 512]
                        if L == 1:
                            ins = sc.activation(dv, self.bk(bks[oc2]), AF.Identity,
                                                bias=self.bof[:, oc:oc + 1])
                        else:
                            ins = sc.activation(dv, self.bk(bks[oc2]), AF.Copy)
                        self.s_acc.inc(ins)
                    G.release(bks, self.s_acc)
                    assert M[("wacc", c, gidx)] == self.s_acc.n
        M[("wcmm", c)] = self.s_pj.n
        M[("wcdone", c)] = self.s_acc.n

    def acc_pred(self, k):
        if not hasattr(self, "_accp"):
            self._accp = self.s_acc.n
        self._accp += k
        return self._accp

    def gating(self, c):
        L, blk, G, M = self.L, self.blk, self.G, self.M
        s_pj, s_vm = self.s_pj, self.s_vm
        if c == 1:
            @blk.sync
            def _(sync):
                if L == 0:
                    sync.wait_ge(s_pj.h, M[("gumm_last", 9)])
                else:
                    sync.wait_ge(s_pj.h, M[("hmm", 63)])
                for i in range(8):
                    self.dma(sync, self.s_x, self.xn[:, i * CHT:(i + 1) * CHT],
                             self.xn_in[:, i * T + CHT:(i + 1) * T])
            self.XN0 = self.s_x.n
        vm0 = s_vm.n
        ex0p = self.s_ex.n
        perx = 2 if L == 0 else 1
        for tt in range(16):
            M[("lgv", c, tt)] = vm0 + 12 * tt + 4
            M[("lgw", c, tt)] = vm0 + 12 * (tt + 1)
            M[("lge", c, tt)] = ex0p + perx * (tt + 1)

        @blk.tensor
        def _(t, c=c):
            t.wait_ge(self.s_gq.h, self.GC)
            t.wait_ge(self.s_x.h, self.XN0)
            for tt in range(16):
                bb = 4 + tt % 2
                if tt >= 2:
                    t.wait_ge(s_vm.h, M[("lgv", c, tt - 2)])
                    if L == 0:
                        t.wait_ge(self.s_ex.h, M[("lge", c, tt - 2)])
                else:
                    G.acquire(t, [bb])
                for kc in range(8):
                    ins = t.matmul(self.bk(bb)[:, 0:8],
                                   self.xn[:, kc * CHT + tt * 128:
                                           kc * CHT + (tt + 1) * 128],
                                   self.gwbf[:, kc * 8:kc * 8 + 8],
                                   start=(kc == 0), stop=(kc == 7 and L == 0))
                if L == 1:
                    ins = t.matmul(self.bk(bb)[:, 0:8], self.ones1[0:1, :],
                                   self.rbr[0:1, :], start=False, stop=True)
                s_pj.inc(ins)
                if L == 0:
                    for kc in range(8):
                        ins = t.matmul(self.bk(bb)[:, 8:9],
                                       self.xn[:, kc * CHT + tt * 128:
                                               kc * CHT + (tt + 1) * 128],
                                       self.gwbf[:, 64 + kc:65 + kc],
                                       start=(kc == 0), stop=(kc == 7))
                    s_pj.inc(ins)
                M[("lgmm", c, tt)] = s_pj.n

        @blk.vector
        def _(v, c=c):
            for tt in range(16):
                bb = 4 + tt % 2
                o = (tt % 2) * 32
                v.wait_ge(s_pj.h, M[("lgmm", c, tt)])
                lsv = self.ls[:, o:o + 8]
                mm = s_vm.inc(v.tensor_copy(lsv, self.bk(bb)[:, 0:8]))
                v.wait_ge(s_vm.h, mm)
                mm = s_vm.inc(v.reduce_max(self.ls[:, o + 8:o + 9], lsv, axis=AX.X))
                v.wait_ge(s_vm.h, mm)
                mm = s_vm.inc(v.tensor_scalar(self.ls[:, o + 9:o + 10],
                                              self.ls[:, o + 8:o + 9], -1.0, None, ALU.mult))
                v.wait_ge(s_vm.h, mm)
                mm = s_vm.inc(v.tensor_scalar(self.ls[:, o + 10:o + 18], lsv,
                                              self.ls[:, o + 9:o + 10], None, ALU.add))
                assert M[("lgv", c, tt)] == mm
                if c == 1 and tt < 2:
                    v.wait_ge(self.s_vt.h, M[("wrTend", 0)])
                v.wait_ge(self.s_ex.h, M[("lge", c, tt)])
                ev = self.ls[:, o + 10:o + 18]
                t8 = self.ls[:, o + 18:o + 26]
                mm = s_vm.inc(v.tensor_scalar(t8, lsv, self.ls[:, o + 8:o + 9],
                                              -1e30, ALU.is_ge, ALU.mult))
                v.wait_ge(s_vm.h, mm)
                mm = s_vm.inc(v.tensor_tensor(t8, t8, lsv, ALU.add))
                v.wait_ge(s_vm.h, mm)
                mm = s_vm.inc(v.reduce_max(self.ls[:, o + 26:o + 27], t8, axis=AX.X))
                v.wait_ge(s_vm.h, mm)
                mm = s_vm.inc(v.tensor_scalar(t8, lsv, self.ls[:, o + 26:o + 27],
                                              None, ALU.is_ge))
                v.wait_ge(s_vm.h, mm)
                if L == 0:
                    mm = s_vm.inc(v.reduce_sum(self.ls[:, o + 27:o + 28], ev, axis=AX.X))
                    v.wait_ge(s_vm.h, mm)
                    mm = s_vm.inc(v.reciprocal(self.ls[:, o + 28:o + 29],
                                               self.ls[:, o + 27:o + 28]))
                    v.wait_ge(s_vm.h, mm)
                    mm = s_vm.inc(v.tensor_scalar(ev, ev, self.ls[:, o + 28:o + 29],
                                                  None, ALU.mult))
                    v.wait_ge(s_vm.h, mm)
                    mm = s_vm.inc(v.tensor_tensor(self.w_all[:, tt * 8:(tt + 1) * 8],
                                                  ev, t8, ALU.mult))
                else:
                    mm = s_vm.inc(v.tensor_tensor(ev, ev, t8, ALU.mult))
                    v.wait_ge(s_vm.h, mm)
                    mm = s_vm.inc(v.reduce_sum(self.ls[:, o + 27:o + 28], ev, axis=AX.X))
                    v.wait_ge(s_vm.h, mm)
                    mm = s_vm.inc(v.reciprocal(self.ls[:, o + 28:o + 29],
                                               self.ls[:, o + 27:o + 28]))
                    v.wait_ge(s_vm.h, mm)
                    mm = s_vm.inc(v.tensor_scalar(self.w_all[:, tt * 8:(tt + 1) * 8],
                                                  ev, self.ls[:, o + 28:o + 29],
                                                  None, ALU.mult))
                assert M[("lgw", c, tt)] == mm
            M[("gatedone", c)] = s_vm.n

        @blk.scalar
        def _(sc, c=c):
            for tt in range(16):
                o = (tt % 2) * 32
                sc.wait_ge(s_vm.h, M[("lgv", c, tt)])
                self.s_ex.inc(sc.activation(self.ls[:, o + 10:o + 18],
                                            self.ls[:, o + 10:o + 18], AF.Exp))
                if L == 0:
                    bb = 4 + tt % 2
                    self.s_ex.inc(sc.activation(self.w_all[:, 128 + tt:129 + tt],
                                                self.bk(bb)[:, 8:9], AF.Sigmoid))
                assert M[("lge", c, tt)] == self.s_ex.n

    def expert(self, c, e):
        L, blk, G, M = self.L, self.blk, self.G, self.M
        s_pj, s_vt, s_w, s_wc = self.s_pj, self.s_vt, self.s_w, self.s_wc
        s_ex, s_ve, s_vm = self.s_ex, self.s_ve, self.s_vm
        ei = c * (10 if L == 0 else 8) + e

        shared = (L == 0 and e >= 8)
        if not (shared and e == 9):
            @blk.tensor
            def _(t, c=c, e=e):
                t.wait_ge(s_vm.h, M[("gatedone", c)])
                for half in range(2):
                    bb = 4 + half
                    G.acquire(t, [bb])
                    for tt8 in range(8):
                        tt = half * 8 + tt8
                        col = (128 + tt) if shared else (tt * 8 + e)
                        ins = t.transpose(self.bkb(bb, tt8 * 128, (tt8 + 1) * 128, 0, 1),
                                          self.w_all[:, col:col + 1],
                                          self.ident[:, :])
                    s_vt.inc(ins)
                    M[("wrT", ei, half)] = s_vt.n

            @blk.vector
            def _(v, c=c, e=e):
                for half in range(2):
                    bb = 4 + half
                    v.wait_ge(s_vt.h, M[("wrT", ei, half)])
                    s_ve.inc(v.tensor_copy(self.wrow[0:1, half * 1024:(half + 1) * 1024],
                                           self.bkb(bb, 0, 1024, 0, 1)))
                    G.release([bb], s_ve)
                M[("wrow", ei)] = s_ve.n

            @blk.tensor
            def _(t, c=c, e=e):
                t.wait_ge(s_ve.h, M[("wrow", ei)])
                ve0w = s_ve.n
                for tg in range(4):
                    M[("wbcp", ei, tg)] = ve0w + tg + 1
                for tg in range(4):
                    bb = 4 + tg % 2
                    if tg >= 2:
                        t.wait_ge(s_ve.h, M[("wbcp", ei, tg - 2)])
                    else:
                        G.acquire(t, [bb])
                    s_pj.inc(t.matmul(self.bk(bb), self.ones1[0:1, :],
                                      self.wrow[0:1, tg * 512:(tg + 1) * 512],
                                      start=True, stop=True))
                    M[("wbmm", ei, tg)] = s_pj.n

            @blk.vector
            def _(v, c=c, e=e):
                for tg in range(4):
                    bb = 4 + tg % 2
                    v.wait_ge(s_pj.h, M[("wbmm", ei, tg)])
                    s_ve.inc(v.tensor_copy(self.wb_sb[:, tg * 512:(tg + 1) * 512],
                                           self.bk(bb)))
                    G.release([bb], s_ve)
                    assert M[("wbcp", ei, tg)] == s_ve.n
                M[("wb", ei)] = s_ve.n
        else:
            M[("wb", ei)] = M[("wb", ei - 1)]

        M[("wrTend", c)] = s_vt.n

        if L == 0:
            self.expert_l0(c, e, ei, shared)
        else:
            self.expert_l1(c, e, ei)

    def expert_l0(self, c, e, ei, shared):
        blk, G, M = self.blk, self.G, self.M
        s_pj, s_w, s_wc = self.s_pj, self.s_w, self.s_wc
        s_ex, s_ve = self.s_ex, self.s_ve
        sh = e - 8 if shared else 0
        wcb = s_wc.n
        for j in range(11):
            M[("gucv", ei, j)] = wcb + j + 1
        M[("gucv_last", ei)] = wcb + 11
        for ob in range(4):
            M[("dncv", ei, ob)] = wcb + 11 + ob + 1
        M[("dncv_last", ei)] = wcb + 15
        pjb = s_pj.n
        for git in range(22):
            M[("gumm2", ei, git)] = pjb + git + 1
        for j in range(11):
            M[("gumm", ei, j)] = pjb + 2 * (j + 1)
        M[("gumm_last", ei)] = pjb + 22
        pjb2 = pjb + 22
        for dit in range(32):
            M[("dnmm2", ei, dit)] = pjb2 + dit + 1
        for ob in range(4):
            M[("dnmm", ei, ob)] = pjb2 + 8 * (ob + 1)
        M[("dnmm_last", ei)] = pjb2 + 32

        @blk.sync
        def _(sync, c=c, e=e):
            for j in range(11):
                if j == 0 and e == 0:
                    sync.wait_ge(s_wc.h, M[("wcv", c, 7)])
                if (ei, j) != (0, 0):
                    sync.wait_ge(s_wc.h, M[("gucv", ei, j - 1) if j > 0 else ("gucv_last", ei - 1)])
                for kc in range(8):
                    if shared:
                        gsrc = self.sg_in[kc * 128:(kc + 1) * 128,
                                          sh * 1408 + j * 128:sh * 1408 + (j + 1) * 128]
                        usrc = self.su_in[kc * 128:(kc + 1) * 128,
                                          sh * 1408 + j * 128:sh * 1408 + (j + 1) * 128]
                    else:
                        gsrc = self.egu_in[e, kc * 128:(kc + 1) * 128, j * 128:(j + 1) * 128]
                        usrc = self.egu_in[e, kc * 128:(kc + 1) * 128,
                                           1408 + j * 128:1408 + (j + 1) * 128]
                    self.dma(sync, s_w, self.gust[:, kc * 256:kc * 256 + 128], gsrc)
                    self.dma(sync, s_w, self.gust[:, kc * 256 + 128:(kc + 1) * 256], usrc)
                M[("gud", ei, j)] = s_w.n

        @blk.gpsimd
        def _(gp, c=c, e=e):
            for j in range(11):
                par = j % 2
                gp.wait_ge(s_w.h, M[("gud", ei, j)])
                if j >= 2:
                    gp.wait_ge(s_pj.h, M[("gumm", ei, j - 2)])
                elif ei > 0:
                    gp.wait_ge(s_pj.h, M[("gumm_last", ei - 1)])
                s_wc.inc(gp.tensor_copy(self.gubf[:, par * 2048:(par + 1) * 2048],
                                        self.gust[:, :]))
                assert M[("gucv", ei, j)] == s_wc.n

        @blk.tensor
        def _(t, c=c, e=e):
            ve0 = s_ve.n
            for git in range(22):
                M[("hfold", ei, git)] = ve0 + git + 1
            git = 0
            for j in range(11):
                par = j % 2
                t.wait_ge(s_wc.h, M[("gucv", ei, j)])
                if j == 0:
                    t.wait_ge(s_ve.h, M[("wb", ei)])
                for tg2 in range(2):
                    q = git % 2
                    bks = [4 * q, 4 * q + 1, 4 * q + 2, 4 * q + 3]
                    if git >= 2:
                        t.wait_ge(s_ve.h, M[("hfold", ei, git - 2)])
                    else:
                        G.acquire(t, bks)
                    for kc in range(8):
                        for t2 in range(2):
                            ins = t.matmul(
                                self.bk(bks[t2]),
                                self.gubf[:, par * 2048 + kc * 256:par * 2048 + kc * 256 + 128],
                                self.xn[:, kc * CHT + tg2 * 1024 + t2 * 512:
                                        kc * CHT + tg2 * 1024 + (t2 + 1) * 512],
                                start=(kc == 0), stop=(kc == 7))
                        for t2 in range(2):
                            ins = t.matmul(
                                self.bk(bks[2 + t2]),
                                self.gubf[:, par * 2048 + kc * 256 + 128:par * 2048 + (kc + 1) * 256],
                                self.xn[:, kc * CHT + tg2 * 1024 + t2 * 512:
                                        kc * CHT + tg2 * 1024 + (t2 + 1) * 512],
                                start=(kc == 0), stop=(kc == 7))
                    s_pj.inc(ins)
                    assert M[("gumm2", ei, git)] == s_pj.n
                    git += 1

        @blk.scalar
        def _(sc, c=c, e=e):
            git = 0
            for j in range(11):
                for tg2 in range(2):
                    q = git % 2
                    sc.wait_ge(s_pj.h, M[("gumm2", ei, git)])
                    if ei > 0 and git == 0:
                        sc.wait_ge(s_pj.h, M[("dnmm_last", ei - 1)])
                    if git >= 2:
                        sc.wait_ge(s_ve.h, M[("hfold", ei, git - 2)])
                    s_ex.inc(sc.activation(self.sgev[:, q * 1024:(q + 1) * 1024],
                                           self.ps[:, 4 * q * 512:(4 * q + 2) * 512],
                                           AF.Silu))
                    M[("silu", ei, git)] = s_ex.n
                    git += 1

        @blk.vector
        def _(v, c=c, e=e):
            git = 0
            for j in range(11):
                for tg2 in range(2):
                    q = git % 2
                    v.wait_ge(s_ex.h, M[("silu", ei, git)])
                    if git >= 1:
                        v.wait_ge(s_ve.h, M[("hfold", ei, git - 1)])
                    mm = self.s_vm.inc(v.tensor_mul(
                        self.hmid[:, :], self.sgev[:, q * 1024:(q + 1) * 1024],
                        self.ps[:, (4 * q + 2) * 512:(4 * q + 4) * 512]))
                    v.wait_ge(self.s_vm.h, mm)
                    hv = self.h_sb[:, :].rearrange("p (j t) -> p j t", j=11)[:, j, tg2 * 1024:(tg2 + 1) * 1024]
                    ins = v.tensor_mul(hv, self.hmid[:, :],
                                       self.wb_sb[:, tg2 * 1024:(tg2 + 1) * 1024])
                    s_ve.inc(ins)
                    assert M[("hfold", ei, git)] == s_ve.n
                    git += 1
            M[("hdone", ei)] = s_ve.n

        @blk.sync
        def _(sync, c=c, e=e):
            for ob in range(4):
                if (ei, ob) != (0, 0):
                    sync.wait_ge(s_wc.h, M[("dncv", ei, ob - 1) if ob > 0 else ("dncv_last", ei - 1)])
                for ic in range(11):
                    if shared:
                        src = self.sd_in[sh * 1408 + ic * 128:sh * 1408 + (ic + 1) * 128,
                                         ob * 256:(ob + 1) * 256]
                    else:
                        src = self.edn_in[e, ic * 128:(ic + 1) * 128, ob * 256:(ob + 1) * 256]
                    self.dma(sync, self.s_wd, self.dnst[:, ic * 256:(ic + 1) * 256], src)
                M[("dnd", ei, ob)] = self.s_wd.n

        @blk.gpsimd
        def _(gp, c=c, e=e):
            for ob in range(4):
                par = ob % 2
                gp.wait_ge(self.s_wd.h, M[("dnd", ei, ob)])
                if ob >= 2:
                    gp.wait_ge(s_pj.h, M[("dnmm", ei, ob - 2)])
                elif ei > 0:
                    gp.wait_ge(s_pj.h, M[("dnmm_last", ei - 1)])
                s_wc.inc(gp.tensor_copy(self.dnbf[:, par * 2816:(par + 1) * 2816],
                                        self.dnst[:, :]))
                assert M[("dncv", ei, ob)] == s_wc.n

        @blk.tensor
        def _(t, c=c, e=e):
            t.wait_ge(s_ve.h, M[("hdone", ei)])
            ve0d = s_ve.n
            for dit in range(32):
                M[("dacc", ei, dit)] = ve0d + dit + 1
            dit = 0
            for ob in range(4):
                par = ob % 2
                t.wait_ge(s_wc.h, M[("dncv", ei, ob)])
                for oc2 in range(2):
                    for tg in range(4):
                        bb = dit % 4
                        if dit >= 4:
                            t.wait_ge(s_ve.h, M[("dacc", ei, dit - 4)])
                        else:
                            G.acquire(t, [bb])
                        for ic in range(11):
                            ins = t.matmul(
                                self.bk(bb),
                                self.dnbf[:, par * 2816 + ic * 256 + oc2 * 128:
                                          par * 2816 + ic * 256 + (oc2 + 1) * 128],
                                self.h_sb[:, :].rearrange("p (j t) -> p j t", j=11)[:, ic, tg * 512:(tg + 1) * 512],
                                start=(ic == 0), stop=(ic == 10))
                        s_pj.inc(ins)
                        assert M[("dnmm2", ei, dit)] == s_pj.n
                        dit += 1

        @blk.vector
        def _(v, c=c, e=e):
            v.wait_ge(self.s_acc.h, M[("wcdone", c)])
            dit = 0
            for ob in range(4):
                for oc2 in range(2):
                    oc = ob * 2 + oc2
                    for tg in range(4):
                        bb = dit % 4
                        v.wait_ge(s_pj.h, M[("dnmm2", ei, dit)])
                        av = self.acc[:, oc * CHT + tg * 512:oc * CHT + (tg + 1) * 512]
                        ins = v.tensor_add(av, av, self.bk(bb))
                        self.s_ve.inc(ins)
                        assert M[("dacc", ei, dit)] == self.s_ve.n
                        dit += 1
            M[("eadd", ei)] = self.s_ve.n
        G.release([0, 1, 2, 3], self.s_ve)


    def expert_l1(self, c, e, ei):
        blk, G, M = self.blk, self.G, self.M
        s_pj, s_w, s_wc = self.s_pj, self.s_w, self.s_wc
        s_ex, s_ve = self.s_ex, self.s_ve

        @blk.sync
        def _(sync, c=c, e=e):
            if ei > 0:
                sync.wait_ge(self.s_ex.h, M[("hrelu_last", ei - 1)])
            self.dma(sync, self.s_wd, self.b1f[:, :], self.b1_in[e])
            self.dma(sync, self.s_wd, self.b2st[0:1, :], self.b2_in[e:e + 1, :])
            M[("bd", ei)] = self.s_wd.n

        @blk.gpsimd
        def _(gp, c=c, e=e):
            gp.wait_ge(self.s_wd.h, M[("bd", ei)])
            if ei > 0:
                gp.wait_ge(s_pj.h, M[("w2mm_last", ei - 1)])
            s_wc.inc(gp.tensor_copy(self.b2r[:, :], self.b2st[:, :]))
            M[("b2c", ei)] = s_wc.n

        for fcg in range(8):
            fi = ei * 8 + fcg

            @blk.sync
            def _(sync, c=c, e=e, fcg=fcg, fi=fi):
                if fi % 64 == 0:
                    sync.wait_ge(s_wc.h, M[("wcv", c, 7)])
                if fi > 0:
                    sync.wait_ge(s_wc.h, M[("w2cv", fi - 1)])
                for kc in range(8):
                    self.dma(sync, s_w, self.gust[:, kc * 512:(kc + 1) * 512],
                             self.w1_in[e, kc * 128:(kc + 1) * 128,
                                        fcg * 512:(fcg + 1) * 512])
                M[("w1d", fi)] = s_w.n

            @blk.gpsimd
            def _(gp, c=c, e=e, fcg=fcg, fi=fi):
                gp.wait_ge(s_w.h, M[("w1d", fi)])
                if fi >= 2:
                    gp.wait_ge(s_pj.h, M[("hmm", fi - 2)])
                s_wc.inc(gp.tensor_copy(self.gubf[:, (fi % 2) * 4096:(fi % 2 + 1) * 4096],
                                        self.gust[:, :]))
                M[("w1cv", fi)] = s_wc.n

            @blk.tensor
            def _(t, c=c, e=e, fcg=fcg, fi=fi):
                ex0 = s_ex.n
                ve0 = s_ve.n
                for hit in range(16):
                    M[("hrelu", fi, hit)] = ex0 + hit + 1
                    M[("hcf", fi, hit)] = ve0 + hit + 1
                t.wait_ge(s_wc.h, M[("w1cv", fi)])
                if fcg == 0:
                    t.wait_ge(s_ve.h, M[("wb", ei)])
                hit = 0
                for fc in range(4):
                    for tg in range(4):
                        bb = hit % 4
                        if hit >= 4:
                            t.wait_ge(s_ve.h, M[("hcf", fi, hit - 4)])
                        else:
                            G.acquire(t, [bb])
                        for kc in range(8):
                            ins = t.matmul(
                                self.bk(bb),
                                self.gubf[:, (fi % 2) * 4096 + kc * 512 + fc * 128:
                                          (fi % 2) * 4096 + kc * 512 + (fc + 1) * 128],
                                self.xn[:, kc * CHT + tg * 512:
                                        kc * CHT + (tg + 1) * 512],
                                start=(kc == 0), stop=(kc == 7))
                        s_pj.inc(ins)
                        M[("hmm2", fi, hit)] = s_pj.n
                        hit += 1
                M[("hmm", fi)] = s_pj.n

            @blk.scalar
            def _(sc, c=c, e=e, fcg=fcg, fi=fi):
                hit = 0
                for fc in range(4):
                    for tg in range(4):
                        bb = hit % 4
                        sc.wait_ge(s_pj.h, M[("hmm2", fi, hit)])
                        if fcg == 0 and hit == 0 and ei > 0:
                            sc.wait_ge(s_pj.h, M[("w2mm_last", ei - 1)])
                        if fcg > 0 and hit == 0:
                            sc.wait_ge(s_pj.h, M[("w2mm", ei, fcg - 1)])
                        hv = self.h_sb[:, :].rearrange("p (f t) -> p f t", f=4)[:, fc, tg * 512:(tg + 1) * 512]
                        s_ex.inc(sc.activation(hv, self.bk(bb), AF.Relu,
                                               bias=self.b1f[:, fcg * 4 + fc:fcg * 4 + fc + 1]))
                        assert M[("hrelu", fi, hit)] == s_ex.n
                        hit += 1
                M[("hrelu_last", ei)] = s_ex.n

            @blk.vector
            def _(v, c=c, e=e, fcg=fcg, fi=fi):
                hit = 0
                for fc in range(4):
                    for tg in range(4):
                        v.wait_ge(s_ex.h, M[("hrelu", fi, hit)])
                        hv = self.h_sb[:, :].rearrange("p (f t) -> p f t", f=4)[:, fc, tg * 512:(tg + 1) * 512]
                        s_ve.inc(v.tensor_mul(hv, hv, self.wb_sb[:, tg * 512:(tg + 1) * 512]))
                        assert M[("hcf", fi, hit)] == s_ve.n
                        hit += 1
                M[("hcfd", fi)] = s_ve.n

            @blk.sync
            def _(sync, c=c, e=e, fcg=fcg, fi=fi):
                sync.wait_ge(s_wc.h, M[("w1cv", fi)])
                for fkc in range(4):
                    self.dma(sync, s_w, self.gust[:, fkc * 1024:(fkc + 1) * 1024],
                             self.w2_in[e, fcg * 512 + fkc * 128:fcg * 512 + (fkc + 1) * 128, :])
                M[("w2d", fi)] = s_w.n

            @blk.gpsimd
            def _(gp, c=c, e=e, fcg=fcg, fi=fi):
                gp.wait_ge(s_w.h, M[("w2d", fi)])
                if fi >= 2:
                    gp.wait_ge(s_pj.h, M[("w2mmf", fi - 2)])
                s_wc.inc(gp.tensor_copy(self.dnbf[:, (fi % 2) * 4096:(fi % 2 + 1) * 4096],
                                        self.gust[:, :]))
                M[("w2cv", fi)] = s_wc.n

            @blk.tensor
            def _(t, c=c, e=e, fcg=fcg, fi=fi):
                ve0d = s_ve.n
                for dit in range(32):
                    M[("w2acc", fi, dit)] = ve0d + dit + 1
                t.wait_ge(s_wc.h, M[("w2cv", fi)])
                t.wait_ge(s_ve.h, M[("hcfd", fi)])
                if ei > 0 and fcg == 0:
                    t.wait_ge(s_wc.h, M[("b2c", ei)])
                dit = 0
                for oc in range(8):
                    for tg in range(4):
                        bb = dit % 4
                        if dit >= 4:
                            t.wait_ge(s_ve.h, M[("w2acc", fi, dit - 4)])
                        else:
                            G.acquire(t, [bb])
                        for fkc in range(4):
                            ins = t.matmul(
                                self.bk(bb),
                                self.dnbf[:, (fi % 2) * 4096 + fkc * 1024 + oc * 128:
                                          (fi % 2) * 4096 + fkc * 1024 + (oc + 1) * 128],
                                self.h_sb[:, :].rearrange("p (f t) -> p f t", f=4)[:, fkc, tg * 512:(tg + 1) * 512],
                                start=(fkc == 0), stop=(fkc == 3 and fcg > 0))
                        if fcg == 0:
                            ins = t.matmul(self.bk(bb), self.b2r[0:1, oc * 128:(oc + 1) * 128],
                                           self.ones512[0:1, :], start=False, stop=True)
                        s_pj.inc(ins)
                        M[("w2mm2", fi, dit)] = s_pj.n
                        dit += 1
                M[("w2mmf", fi)] = s_pj.n
                M[("w2mm", ei, fcg)] = s_pj.n
                if fcg == 7:
                    M[("w2mm_last", ei)] = s_pj.n

            @blk.vector
            def _(v, c=c, e=e, fcg=fcg, fi=fi):
                v.wait_ge(self.s_acc.h, M[("wcdone", c)])
                dit = 0
                for oc in range(8):
                    for tg in range(4):
                        bb = dit % 4
                        v.wait_ge(s_pj.h, M[("w2mm2", fi, dit)])
                        av = self.acc[:, oc * CHT + tg * 512:oc * CHT + (tg + 1) * 512]
                        s_ve.inc(v.tensor_add(av, av, self.bk(bb)))
                        assert M[("w2acc", fi, dit)] == s_ve.n
                        dit += 1
            G.release([0, 1, 2, 3], s_ve)
        M[("eadd", ei)] = s_ve.n

    def tail(self, c):
        L, blk, G, M = self.L, self.blk, self.G, self.M
        s_pj, s_vt, s_vm, s_sm = self.s_pj, self.s_vt, self.s_vm, self.s_sm
        s_xr, s_x1, s_ln, s_od = self.s_xr, self.s_x1, self.s_ln, self.s_od
        nexp = 10 if L == 0 else 8
        ADD_DONE = M[("eadd", c * nexp + nexp - 1)]

        x1_0p = [self.s_x1p[0].n, self.s_x1p[1].n]
        gq0 = self.s_gq.n
        sm0 = self.s_sm.n
        vt0 = s_vt.n
        ln0 = self.s_ln.n
        for tt in range(16):
            M[("x1", c, tt)] = x1_0p[tt % 2] + 16 * (tt // 2 + 1)
            M[("sq", c, tt)] = gq0 + tt + 1
            M[("lnsq", c, tt)] = sm0 + 2 * tt + 1
            M[("lnnorm", c, tt)] = sm0 + 2 * tt + 2
            if L == 0:
                M[("accT", c, tt)] = vt0 + (1 if tt == 0 else 2 * tt)
                M[("xnT", c, tt)] = vt0 + (2 * tt + 3 if tt <= 13 else (31 if tt == 14 else 32))
            else:
                M[("accT", c, tt)] = vt0 + tt + 1
            M[("xne", c, tt)] = ln0 + tt + 1
        vm0 = s_vm.n
        if L == 0:
            NVT = 9
            for tt in range(16):
                M[("x1v", c, tt)] = vm0 + NVT * tt + 1
                M[("lnr1", c, tt)] = vm0 + NVT * tt + 2
                M[("lnv", c, tt)] = vm0 + NVT * (tt + 1)
        else:
            for tt in range(16):
                M[("x1v", c, tt)] = vm0 + tt + 1
        xrm = []

        @blk.sync
        def _(sync, c=c):
            for tt in range(16):
                par = tt % 2
                if tt == 0:
                    key = ("dnmm_last", c * 10 + 9) if L == 0 else ("w2mm_last", c * 8 + 7)
                    sync.wait_ge(s_pj.h, M[key])
                if tt >= 2:
                    if L == 0:
                        sync.wait_ge(s_vm.h, M[("lnr1", c, tt - 2)])
                    else:
                        sync.wait_ge(s_vm.h, M[("x1v", c, tt - 2)])
                self.dma(sync, self.s_xp[par], self.xrst[:, par * D:(par + 1) * D],
                         self.res_in[c * CHT + tt * 128:c * CHT + (tt + 1) * 128, :])
                xrm.append(self.s_xp[par].n)

        def emit_xnT(t, c, tt):
            par = tt % 2
            bb = tt % 2
            t.wait_ge(self.s_sm.h, M[("lnnorm", c, tt)])
            if tt < 2:
                G.acquire(t, [bb])
            else:
                t.wait_ge(self.s_ln.h, M[("xne", c, tt - 2)])
            for cch in range(8):
                ins = t.transpose(self.bkb(bb, cch * 128, (cch + 1) * 128),
                                  self.xnt[:, par * D + cch * 128:par * D + (cch + 1) * 128],
                                  self.ident[:, :])
            s_vt.inc(ins)
            assert M[("xnT", c, tt)] == s_vt.n

        @blk.tensor
        def _(t, c=c):
            for tt in range(16):
                if tt == 0:
                    t.wait_ge(self.s_ve.h, ADD_DONE)
                bp = 4 + 2 * (tt % 2)
                if tt >= 2:
                    t.wait_ge(s_vm.h, M[("x1v", c, tt - 2)])
                else:
                    G.acquire(t, [bp, bp + 1])
                for oc in range(8):
                    ins = t.transpose(
                        self.ps[:, bp * 512 + oc * 128:bp * 512 + (oc + 1) * 128],
                        self.acc[:, oc * CHT + tt * 128:oc * CHT + (tt + 1) * 128],
                        self.identf[:, :])
                s_vt.inc(ins)
                assert M[("accT", c, tt)] == s_vt.n
                if L == 0 and tt >= 1:
                    emit_xnT(t, c, tt - 1)
            if L == 0:
                emit_xnT(t, c, 15)

        @blk.vector
        def _(v, c=c):
            for tt in range(16):
                par = tt % 2
                bp = 4 + 2 * (tt % 2)
                v.wait_ge(s_vt.h, M[("accT", c, tt)])
                v.wait_ge(self.s_xp[par].h, xrm[tt])
                if tt >= 2:
                    v.wait_ge(self.s_x1p[tt % 2].h, M[("x1", c, tt - 2)])
                    if L == 0:
                        v.wait_ge(self.s_gq.h, M[("sq", c, tt - 2)])
                        v.wait_ge(self.s_sm.h, M[("lnnorm", c, tt - 2)])
                mm = s_vm.inc(v.tensor_add(
                    self.x1t[:, par * D:(par + 1) * D],
                    self.xrst[:, par * D:(par + 1) * D],
                    self.ps[:, bp * 512:(bp + 2) * 512]))
                assert M[("x1v", c, tt)] == mm
                if L == 0:
                    self.tail_ln_vec(v, c, tt)
            M[("x1vd", c)] = s_vm.n
        G.release([4, 5, 6, 7], s_vm)
        M[("tailT", c)] = self.s_vt.n

        if L == 0:
            self.tail_ln(c)
        else:
            @blk.scalar
            def _(sc, c=c):
                for tt in range(16):
                    par = tt % 2
                    sc.wait_ge(s_vm.h, M[("x1v", c, tt)])
                    self.dma(sc, self.s_x1p[par],
                             self.x1_out[c * CHT + tt * 128:c * CHT + (tt + 1) * 128, :],
                             self.x1t[:, par * D:(par + 1) * D])
                    assert M[("x1", c, tt)] == self.s_x1p[par].n

    def tail_ln_vec(self, v, c, tt):
        M, lst = self.M, self.lst
        s_vm = self.s_vm
        par = tt % 2
        o = par * 12
        xv = self.x1t[:, par * D:(par + 1) * D]
        qv = self.xrst[:, par * D:(par + 1) * D]
        v.wait_ge(self.s_gq.h, M[("sq", c, tt)])
        v.reduce_sum(lst[:, o:o + 1], xv, axis=AX.X)
        mm = s_vm.inc(v.reduce_sum(lst[:, o + 1:o + 2], qv, axis=AX.X))
        assert M[("lnr1", c, tt)] == mm
        v.wait_ge(s_vm.h, mm)
        v.wait_ge(s_vm.h, s_vm.inc(v.tensor_scalar(
            lst[:, o + 2:o + 3], lst[:, o:o + 1], 1.0 / D, None, ALU.mult)))
        v.tensor_tensor(lst[:, o + 3:o + 4], lst[:, o + 2:o + 3],
                        lst[:, o + 2:o + 3], ALU.mult)
        v.wait_ge(s_vm.h, s_vm.inc(v.tensor_scalar(
            lst[:, o + 4:o + 5], lst[:, o + 1:o + 2], 1.0 / D, None, ALU.mult)))
        v.wait_ge(s_vm.h, s_vm.inc(v.tensor_tensor(
            lst[:, o + 5:o + 6], lst[:, o + 4:o + 5], lst[:, o + 3:o + 4], ALU.subtract)))
        v.wait_ge(s_vm.h, s_vm.inc(v.tensor_scalar(
            lst[:, o + 6:o + 7], lst[:, o + 5:o + 6], 1.0, 1e-5, ALU.mult, ALU.add)))
        s_vm.inc(v.reciprocal(lst[:, o + 7:o + 8], lst[:, o + 6:o + 7]))
        v.wait_ge(self.s_sm.h, M[("lnsq", c, tt)])
        v.wait_ge(s_vm.h, s_vm.inc(v.tensor_tensor(
            lst[:, o + 9:o + 10], lst[:, o + 2:o + 3], lst[:, o + 8:o + 9], ALU.mult)))
        mm = s_vm.inc(v.tensor_scalar(
            lst[:, o + 10:o + 11], lst[:, o + 9:o + 10], -1.0, None, ALU.mult))
        assert M[("lnv", c, tt)] == mm

    def tail_ln(self, c):
        blk, G, M = self.blk, self.G, self.M
        s_vm, s_sm, s_ln = self.s_vm, self.s_sm, self.s_ln
        lst = self.lst

        @blk.gpsimd
        def _(gp, c=c):
            for tt in range(16):
                par = tt % 2
                gp.wait_ge(s_vm.h, M[("x1v", c, tt)])
                if tt >= 2:
                    gp.wait_ge(s_vm.h, M[("lnr1", c, tt - 2)])
                self.s_gq.inc(gp.tensor_tensor(self.xrst[:, par * D:(par + 1) * D],
                                               self.x1t[:, par * D:(par + 1) * D],
                                               self.x1t[:, par * D:(par + 1) * D], ALU.mult))
                assert M[("sq", c, tt)] == self.s_gq.n

        def emit_xne(sc, c, tt):
            par = tt % 2
            bb = tt % 2
            sc.wait_ge(self.s_vt.h, M[("xnT", c, tt)])
            if tt >= 2:
                sc.wait_ge(self.s_odp[tt % 2].h, M[("xnd", c, tt - 2)])
            self.s_ln.inc(sc.copy(self.xne[:, par * 1024:(par + 1) * 1024],
                                  self.bkb(bb, 0, 1024)))
            assert M[("xne", c, tt)] == self.s_ln.n
            sc.wait_ge(self.s_ln.h, self.s_ln.n)
            gt = c * 16 + tt
            dst = self.xn1_out[:, :].rearrange("p (k t) -> p k t", k=8)[:, :, gt * 128:(gt + 1) * 128]
            srcv = self.xne[:, par * 1024:(par + 1) * 1024].rearrange("p (k t) -> p k t", k=8)
            self.dma(sc, self.s_odp[tt % 2], dst, srcv)
            M[("xnd", c, tt)] = self.s_odp[tt % 2].n

        @blk.scalar
        def _(sc, c=c):
            s_x1 = self.s_x1
            for tt in range(16):
                par = tt % 2
                o = par * 12
                sc.wait_ge(s_vm.h, M[("x1v", c, tt)])
                self.dma(sc, self.s_x1p[par],
                         self.x1_out[c * CHT + tt * 128:c * CHT + (tt + 1) * 128, :],
                         self.x1t[:, par * D:(par + 1) * D])
                assert M[("x1", c, tt)] == self.s_x1p[par].n
                sc.wait_ge(s_vm.h, M[("lnr1", c, tt)] + 5)
                s1 = self.s_sm.inc(sc.activation(lst[:, o + 8:o + 9],
                                                 lst[:, o + 7:o + 8], AF.Sqrt))
                assert M[("lnsq", c, tt)] == s1
                sc.wait_ge(self.s_sm.h, s1)
                sc.wait_ge(s_vm.h, M[("lnv", c, tt)])
                if tt >= 2:
                    sc.wait_ge(self.s_vt.h, M[("xnT", c, tt - 2)])
                s2 = self.s_sm.inc(sc.activation(self.xnt[:, par * D:(par + 1) * D],
                                                 self.x1t[:, par * D:(par + 1) * D], AF.Identity,
                                                 bias=lst[:, o + 10:o + 11],
                                                 scale=lst[:, o + 8:o + 9]))
                assert M[("lnnorm", c, tt)] == s2
                if tt >= 1:
                    emit_xne(sc, c, tt - 1)
            emit_xne(sc, c, 15)
        G.release([0, 1], self.s_ln)

    def epilogue(self):
        blk = self.blk
        t0p = self.s_x1p[0].n
        t1p = self.s_x1p[1].n
        o0 = self.s_odp[0].n
        o1 = self.s_odp[1].n

        @blk.sync
        def _(sync):
            sync.wait_ge(self.s_x1p[0].h, t0p)
            sync.wait_ge(self.s_x1p[1].h, t1p)
            if o0:
                sync.wait_ge(self.s_odp[0].h, o0)
            if o1:
                sync.wait_ge(self.s_odp[1].h, o1)


def build_mix(L):
    b = MixBuilder(L)
    return b.nc



ENGINES = ("sync", "scalar", "vector", "tensor", "gpsimd")


def _stage_gate(blk, sem, cnt):
    for e in ENGINES:
        @getattr(blk, e)
        def _(eng):
            eng.wait_ge(sem.h, cnt)


def _stage_done(blk, sem):
    for e in ENGINES:
        @getattr(blk, e)
        def _(eng):
            eng.sem_inc(sem.h, 1)
    sem.n += 5


def build_fused():
    root = LaunchBase("fused")
    nc = root.nc
    cst = {}
    for nm, shp, dt in (("c_ident", [128, 128], BF16), ("c_identf", [128, 128], F32),
                        ("c_negones", [1, 128], BF16), ("c_ones1", [1, 128], BF16),
                        ("c_ones512", [1, 512], BF16), ("c_ones64", [128, 2], BF16),
                        ("c_csn", [64, 2 * T], BF16), ("c_mask", [128, 2048], BF16)):
        cst[nm] = nc.dram_tensor(nm, shp, dt, kind="ExternalInput").ap()
    xn0 = nc.dram_tensor("i_xn0", [128, 8 * T], BF16, kind="Internal").ap()
    y0 = nc.dram_tensor("i_y0", [T, D], BF16, kind="Internal").ap()
    x1 = nc.dram_tensor("i_x1", [T, D], F32, kind="Internal").ap()
    xn1 = nc.dram_tensor("i_xn1", [128, 8 * T], BF16, kind="Internal").ap()
    y1 = nc.dram_tensor("i_y1", [T, D], BF16, kind="Internal").ap()
    root.blk = root.ctx.enter_context(nc.Block())
    fin = root.sem("s_fin")

    a0 = AttnBuilder(0, parent=root, io={
        "c_ident": cst["c_ident"], "c_negones": cst["c_negones"],
        "c_ones64": cst["c_ones64"], "c_csn": cst["c_csn"],
        "c_mask": cst["c_mask"], "xn_fm_o": xn0, "y_tm": y0})
    x_in = a0.x_in
    _stage_done(root.blk, fin)

    _stage_gate(root.blk, fin, fin.n)
    MixBuilder(0, parent=root, io={
        "c_ident": cst["c_ident"], "c_identf": cst["c_identf"],
        "c_ones1": cst["c_ones1"], "c_ones512": cst["c_ones512"],
        "xn_fm": xn0, "y_tm": y0, "res": x_in, "x1": x1, "xn1_fm": xn1})
    _stage_done(root.blk, fin)

    _stage_gate(root.blk, fin, fin.n)
    AttnBuilder(1, parent=root, io={
        "c_ident": cst["c_ident"], "c_negones": cst["c_negones"],
        "c_ones64": cst["c_ones64"], "xn_fm": xn1, "y_tm": y1})
    _stage_done(root.blk, fin)

    _stage_gate(root.blk, fin, fin.n)
    MixBuilder(1, parent=root, io={
        "c_ident": cst["c_ident"], "c_identf": cst["c_identf"],
        "c_ones1": cst["c_ones1"], "c_ones512": cst["c_ones512"],
        "xn_fm": xn1, "y_tm": y1, "res": x1})
    root.finish()
    return root.nc



_NC_CACHE = {}


def _get_nc(kind, L):
    key = (kind, L)
    if key not in _NC_CACHE:
        if kind == "fused":
            _NC_CACHE[key] = build_fused()
        else:
            _NC_CACHE[key] = build_attn(L) if kind == "attn" else build_mix(L)
    return _NC_CACHE[key]


def kernel(**inputs):
    if _os.environ.get("KERNEL_UNFUSED", "0") == "1":
        return _kernel_unfused(**inputs)
    consts = _host_consts()
    prep = _host_prep(inputs)
    x = np.ascontiguousarray(np.asarray(inputs["L0_attn_x"], np.float32))
    cores = list(range(NCORES))
    nc = _get_nc("fused", 0)
    maps = []
    for ci in cores:
        m = dict(consts)
        m.update({"x": x[ci],
                  "wdq": prep["l0_wdq"], "wdk": prep["l0_wdk"],
                  "wdv": prep["l0_wdv"], "up": prep["l0_up"],
                  "wc": prep["l0_wc"], "gate_wT": prep["l0_gate_wT"],
                  "eguT": prep["l0_eguT"], "edownT": prep["l0_edownT"],
                  "sgT": prep["l0_sgT"], "suT": prep["l0_suT"],
                  "sdT": prep["l0_sdT"], "segT": prep["l0_segT"],
                  "wtq": prep["l1_wtq"], "wtk": prep["l1_wtk"],
                  "wtv": prep["l1_wtv"], "bq": prep["l1_bq"],
                  "bk": prep["l1_bk"], "bv": prep["l1_bv"],
                  "wto": prep["l1_wto"], "bo": prep["l1_bo"],
                  "router_w": prep["l1_router_w"], "router_b": prep["l1_router_b"],
                  "w1": prep["l1_w1"], "b1": prep["l1_b1"],
                  "w2": prep["l1_w2"], "b2": prep["l1_b2"]})
        maps.append(m)
    r = run_bass_kernel_spmd(nc, maps, core_ids=cores)
    out = np.stack([rr["out"] for rr in r.results], axis=0)
    return out.astype(np.float32)


def _kernel_unfused(**inputs):
    consts = _host_consts()
    prep = _host_prep(inputs)
    x = np.ascontiguousarray(np.asarray(inputs["L0_attn_x"], np.float32))

    cores = list(range(NCORES))

    nc = _get_nc("attn", 0)
    maps = []
    for ci in cores:
        m = {k: consts[k] for k in ("c_ident", "c_negones", "c_ones64", "c_csn", "c_mask")}
        m.update({"x": x[ci], "wdq": prep["l0_wdq"], "wdk": prep["l0_wdk"],
                  "wdv": prep["l0_wdv"], "up": prep["l0_up"]})
        maps.append(m)
    r1 = run_bass_kernel_spmd(nc, maps, core_ids=cores)
    xn0 = [r["xn_fm_o"] for r in r1.results]
    y0 = [r["y_tm"] for r in r1.results]

    nc = _get_nc("mix", 0)
    maps = []
    for ci in cores:
        m = {k: consts[k] for k in ("c_ident", "c_identf", "c_ones1", "c_ones512")}
        m.update({"xn_fm": xn0[ci], "y_tm": y0[ci], "res": x[ci],
                  "wc": prep["l0_wc"], "gate_wT": prep["l0_gate_wT"],
                  "eguT": prep["l0_eguT"], "edownT": prep["l0_edownT"],
                  "sgT": prep["l0_sgT"], "suT": prep["l0_suT"],
                  "sdT": prep["l0_sdT"], "segT": prep["l0_segT"]})
        maps.append(m)
    r2 = run_bass_kernel_spmd(nc, maps, core_ids=cores)
    x1 = [r["x1"] for r in r2.results]
    xn1 = [r["xn1_fm"] for r in r2.results]

    nc = _get_nc("attn", 1)
    maps = []
    for ci in cores:
        m = {k: consts[k] for k in ("c_ident", "c_negones", "c_ones64")}
        m.update({"xn_fm": xn1[ci], "wtq": prep["l1_wtq"], "wtk": prep["l1_wtk"],
                  "wtv": prep["l1_wtv"], "bq": prep["l1_bq"], "bk": prep["l1_bk"],
                  "bv": prep["l1_bv"]})
        maps.append(m)
    r3 = run_bass_kernel_spmd(nc, maps, core_ids=cores)
    y1 = [r["y_tm"] for r in r3.results]

    nc = _get_nc("mix", 1)
    maps = []
    for ci in cores:
        m = {k: consts[k] for k in ("c_ident", "c_identf", "c_ones1", "c_ones512")}
        m.update({"xn_fm": xn1[ci], "y_tm": y1[ci], "res": x1[ci],
                  "wto": prep["l1_wto"], "bo": prep["l1_bo"],
                  "router_w": prep["l1_router_w"], "router_b": prep["l1_router_b"],
                  "w1": prep["l1_w1"], "b1": prep["l1_b1"],
                  "w2": prep["l1_w2"], "b2": prep["l1_b2"]})
        maps.append(m)
    r4 = run_bass_kernel_spmd(nc, maps, core_ids=cores)
    out = np.stack([r["out"] for r in r4.results], axis=0)
    return out.astype(np.float32)


# version 05
# speedup vs baseline: 1.3965x; 1.0044x over previous
import numpy as np
import ml_dtypes
from contextlib import ExitStack

import concourse.bass as bass
import concourse.mybir as mybir
from concourse.bass_utils import run_bass_kernel_spmd

F32 = mybir.dt.float32
BF16 = mybir.dt.bfloat16
AF = mybir.ActivationFunctionType
ALU = mybir.AluOpType
AX = mybir.AxisListType
BF = ml_dtypes.bfloat16

T = 4096
D = 1024
CHT = 2048
NCORES = 8
import os as _os
DBG_PAIRS = int(_os.environ.get("DBG_PAIRS", "16"))
DBG_ATT = int(_os.environ.get("DBG_ATT", "1"))
DBG_HEADS = int(_os.environ.get("DBG_HEADS", "1"))
DBG_VT = int(_os.environ.get("DBG_VT", "1"))
DBG_VUP = int(_os.environ.get("DBG_VUP", "1"))
DBG_VT_MS = int(_os.environ.get("DBG_VT_MS", "1"))
DBG_VT_T = int(_os.environ.get("DBG_VT_T", "1"))
DBG_VT_EV = int(_os.environ.get("DBG_VT_EV", "1"))
DBG_LN0 = int(_os.environ.get("DBG_LN0", "1"))
DBG_VT_N = int(_os.environ.get("DBG_VT_N", "99"))


class Sem:
    __slots__ = ("h", "n")

    def __init__(self, ctx, nc, name):
        self.h = ctx.enter_context(nc.semaphore(name))
        self.n = 0

    def inc(self, instr, k=1):
        instr.then_inc(self.h, k)
        self.n += k
        return self.n


class BankGate:

    def __init__(self):
        self.rel = {}

    def acquire(self, eng, ids):
        need = {}
        for b in ids:
            for s, cnt in self.rel.get(b, ()):
                if need.get(s, 0) < cnt:
                    need[s] = cnt
        for s, cnt in need.items():
            eng.wait_ge(s.h, cnt)
        for b in ids:
            self.rel[b] = []

    def release(self, ids, sem):
        for b in ids:
            self.rel.setdefault(b, []).append((sem, sem.n))


class LaunchBase:
    def __init__(self, name, parent=None, io=None, entry=None):
        self.parent = parent
        self.io = io or {}
        self.entry = entry or []
        if parent is None:
            self.nc = bass.Bass("TRN2", target_bir_lowering=False)
            self.sempool = {}
            self.semctx = None
        else:
            self.nc = parent.nc
            self.sempool = parent.sempool
            self.semctx = parent.ctx
        self.ctx = ExitStack()
        self.ctx.__enter__()
        if parent is None:
            self.semctx = self.ctx
        self.name = name
        self.M = {}

    def din(self, name, shape, dt=F32):
        if name in self.io:
            return self.io[name]
        return self.nc.dram_tensor(name, shape, dt, kind="ExternalInput").ap()

    def dout(self, name, shape, dt=F32):
        if name in self.io:
            return self.io[name]
        return self.nc.dram_tensor(name, shape, dt, kind="ExternalOutput").ap()

    def sb(self, name, shape, dt):
        return self.ctx.enter_context(self.nc.sbuf_tensor(self.name + "_" + name, shape, dt))

    def psum_(self, name, shape, dt=F32):
        return self.ctx.enter_context(self.nc.psum_tensor(self.name + "_" + name, shape, dt))

    def sem(self, name):
        if name not in self.sempool:
            self.sempool[name] = Sem(self.semctx, self.nc, name)
        return self.sempool[name]

    def emit_entry(self, eng):
        for s, cnt in self.entry:
            eng.wait_ge(s.h, cnt)

    def dma(self, eng, sem, dst, srcap, **kw):
        bb = self.nc.cur_bb.bb
        n0 = len(bb.instructions)
        ins = eng.dma_start(dst, srcap, **kw)
        nsub = sum(1 for i in bb.instructions[n0:]
                   if type(i).__name__ == "InstDMACopy")
        ins.then_inc(sem.h, 16)
        sem.n += 16 * max(1, nsub)
        return ins

    def finish(self):
        self.ctx.__exit__(None, None, None)
        return self.nc


def _host_consts():
    c = {}
    c["c_ident"] = np.eye(128).astype(BF)
    c["c_identf"] = np.eye(128, dtype=np.float32)
    c["c_negones"] = np.full((1, 128), -1.0, np.float32).astype(BF)
    c["c_ones1"] = np.ones((1, 128), np.float32).astype(BF)
    c["c_ones512"] = np.ones((1, 512), np.float32).astype(BF)
    o64 = np.zeros((128, 2), np.float32)
    o64[0:64, 0] = 1.0
    o64[64:128, 1] = 1.0
    c["c_ones64"] = o64.astype(BF)
    DHE = 32
    inv_freq = 1.0 / (10000.0 ** (np.arange(0, DHE, 2, dtype=np.float64) / DHE))
    freqs = np.arange(T, dtype=np.float64)[:, None] * inv_freq[None, :]
    emb = np.concatenate([freqs, freqs], axis=-1)
    cos = np.cos(emb)
    sinp = np.sin(emb) * np.where(np.arange(DHE) % 2 == 0, -1.0, 1.0)[None, :]
    csn = np.zeros((64, 2 * T), np.float64)
    csn[32:64, 0:T] = cos.T
    csn[32:64, T:2 * T] = sinp.T
    c["c_csn"] = csn.astype(BF)
    m = np.zeros((128, 4 * 512), np.float32)
    ii = np.arange(128)[:, None]
    jj = np.arange(512)[None, :]
    for r in range(4):
        m[:, r * 512:(r + 1) * 512] = (jj >= ii + 128 * r).astype(np.float32)
    c["c_mask"] = m.astype(BF)
    return c


def _host_prep(inputs):
    p = {}
    f32 = np.float32

    def T2(a):
        return np.ascontiguousarray(np.asarray(a, f32).T)

    p["l0_wdq"] = np.ascontiguousarray(np.asarray(inputs["L0_attn_Wq_down"], f32))
    p["l0_wdk"] = np.ascontiguousarray(np.asarray(inputs["L0_attn_Wk_down"], f32))
    p["l0_wdv"] = np.ascontiguousarray(np.asarray(inputs["L0_attn_Wv_down"], f32))

    def swap_cols(a):
        a = np.asarray(a, f32)
        b = np.empty_like(a)
        b[:, 0::2] = a[:, 1::2]
        b[:, 1::2] = a[:, 0::2]
        return b

    up = np.zeros((128, 640), f32)
    wqc = np.asarray(inputs["L0_attn_Wq_up_c"], f32)
    wqe = np.asarray(inputs["L0_attn_Wq_up_e"], f32)
    wkc = np.asarray(inputs["L0_attn_Wk_up_c"], f32)
    wke = np.asarray(inputs["L0_attn_Wk_up_e"], f32)
    wv = np.asarray(inputs["L0_attn_Wv_up"], f32)
    for ki, (wc_, we_) in enumerate(((wqc, wqe), (wkc, wke))):
        for hhp in range(2):
            b = ki * 256 + hhp * 128
            r = slice(64 * hhp, 64 * hhp + 64)
            up[r, b + 0:b + 32] = wc_
            up[r, b + 32:b + 64] = we_
            up[r, b + 96:b + 128] = swap_cols(we_)
    up[0:64, 512:576] = wv
    up[64:128, 576:640] = wv
    p["l0_up"] = np.ascontiguousarray(up)

    p["l0_wc"] = np.ascontiguousarray(np.asarray(inputs["L0_attn_Wc"], f32))
    p["l0_gate_wT"] = T2(inputs["L0_ffn_gate_w"])
    p["l0_eguT"] = np.ascontiguousarray(
        np.transpose(np.asarray(inputs["L0_ffn_experts_gate_up"], f32), (0, 2, 1)))
    p["l0_edownT"] = np.ascontiguousarray(
        np.transpose(np.asarray(inputs["L0_ffn_experts_down"], f32), (0, 2, 1)))
    p["l0_sgT"] = T2(inputs["L0_ffn_shared_gate_w"])
    p["l0_suT"] = T2(inputs["L0_ffn_shared_up_w"])
    p["l0_sdT"] = T2(inputs["L0_ffn_shared_down_w"])
    p["l0_segT"] = T2(inputs["L0_ffn_shared_expert_gate_w"])

    p["l1_wtq"] = T2(inputs["L1_attn_Wq"])
    p["l1_wtk"] = T2(inputs["L1_attn_Wk"])
    p["l1_wtv"] = T2(inputs["L1_attn_Wv"])
    p["l1_wto"] = T2(inputs["L1_attn_Wo"])

    def fm8(b):
        return np.ascontiguousarray(np.asarray(b, f32).reshape(8, 128).T)

    p["l1_bq"] = fm8(inputs["L1_attn_bq"])
    p["l1_bk"] = fm8(inputs["L1_attn_bk"])
    p["l1_bv"] = fm8(inputs["L1_attn_bv"])
    p["l1_bo"] = fm8(inputs["L1_attn_bo"])

    p["l1_router_w"] = np.ascontiguousarray(np.asarray(inputs["L1_ffn_router_w"], f32))
    p["l1_router_b"] = np.ascontiguousarray(np.asarray(inputs["L1_ffn_router_b"], f32).reshape(1, 8))
    p["l1_w1"] = np.ascontiguousarray(np.asarray(inputs["L1_ffn_W1"], f32))
    p["l1_w2"] = np.ascontiguousarray(np.asarray(inputs["L1_ffn_W2"], f32))
    p["l1_b1"] = np.ascontiguousarray(
        np.asarray(inputs["L1_ffn_b1"], f32).reshape(8, 32, 128).transpose(0, 2, 1))
    p["l1_b2"] = np.ascontiguousarray(np.asarray(inputs["L1_ffn_b2"], f32))
    return p



class AttnBuilder(LaunchBase):
    def __init__(self, L, parent=None, io=None, entry=None):
        super().__init__(f"attn{L}", parent=parent, io=io, entry=entry)
        self.L = L
        self.build()

    def bk(self, i, r0=0, r1=128):
        return self.ps[r0:r1, i * 512:(i + 1) * 512]

    def bkb(self, i, c0, c1, r0=0, r1=128):
        return self.psb[r0:r1, i * 1024 + c0:i * 1024 + c1]

    def mview(self, hh, a, b):
        if self.L == 0:
            return self.q_h[64:65, a:b]
        return self.q_hA[64:65, a:b] if hh == 0 else self.q_hB[0:1, a:b]

    def build(self):
        L = self.L
        B = self

        self.ident_in = B.din("c_ident", [128, 128], BF16)
        self.negones_in = B.din("c_negones", [1, 128], BF16)
        self.ones64_in = B.din("c_ones64", [128, 2], BF16)
        if L == 0:
            self.x_in = B.din("x", [T, D], F32)
            self.wq_in = B.din("wdq", [D, D], F32)
            self.wk_in = B.din("wdk", [D, D], F32)
            self.wv_in = B.din("wdv", [D, D], F32)
            self.up_in = B.din("up", [128, 640], F32)
            self.csn_in = B.din("c_csn", [64, 2 * T], BF16)
            self.mask_in = B.din("c_mask", [128, 2048], BF16)
            self.xn_out = B.dout("xn_fm_o", [128, 8 * T], BF16)
        else:
            self.xn_in = B.din("xn_fm", [128, 8 * T], BF16)
            self.wq_in = B.din("wtq", [D, D], F32)
            self.wk_in = B.din("wtk", [D, D], F32)
            self.wv_in = B.din("wtv", [D, D], F32)
            self.bq_in = B.din("bq", [128, 8], F32)
            self.bk_in = B.din("bk", [128, 8], F32)
            self.bv_in = B.din("bv", [128, 8], F32)
        self.y_out = B.dout("y_tm", [T, D], BF16)

        self.xn = B.sb("xn", [128, 8 * T], BF16)
        self.y_tm = B.sb("y_tm_s", [128, 16 * D], BF16)
        self.ident = B.sb("ident", [128, 128], BF16)
        self.negones = B.sb("negones", [1, 128], BF16)
        self.ones64 = B.sb("ones64", [128, 2], BF16)
        self.wbf = B.sb("wbf", [128, 2 * 3 * D], BF16)
        if L == 0:
            self.xst = B.sb("xst", [128, 2 * D], F32)
            self.wst = self.xst
        else:
            self.wst = B.sb("wst", [128, 2 * D], F32)
        if L == 0:
            self.xsq = B.sb("xsq", [128, 2 * D], F32)
            self.xnt = B.sb("xnt", [128, 2 * D], BF16)
            self.lst = B.sb("lst", [128, 24], F32)
            self.csn = B.sb("csn", [64, 2 * T], BF16)
            self.maskb = B.sb("maskb", [128, 2048], BF16)
            self.upf = B.sb("upf", [128, 640], F32)
            self.upb = B.sb("upb", [128, 640], BF16)
            self.qs = B.sb("qs", [128, CHT], BF16)
            self.ks = B.sb("ks", [128, T], BF16)
            self.vs = B.sb("vs", [128, T], BF16)
            self.q_h = B.sb("q_h", [128, CHT], BF16)
            self.k_h = B.sb("k_h", [128, T], BF16)
            self.rtmp = B.sb("rtmp", [64, 1024], BF16)
            self.v2f = B.sb("v2f", [128, T], BF16)
        else:
            self.bfm = B.sb("bfm", [128, 24], F32)
            self.q_hA = B.sb("q_hA", [128, CHT], BF16)
            self.q_hB = B.sb("q_hB", [128, CHT], BF16)
            self.k_hA = B.sb("k_hA", [128, T], BF16)
            self.k_hB = B.sb("k_hB", [128, T], BF16)
            self.v2f = B.sb("v2f", [128, T], BF16)
        if L == 0:
            self.sq = self.xsq.bitcast(BF16)
        else:
            self.sq = B.sb("sqb", [128, T], BF16)
        self.v_tm = B.sb("v_tm", [128, 32 * 130], BF16)
        self.PT = B.sb("PT", [128, 2 * 1024], BF16)
        self.yev = B.sb("yev", [128, 512], BF16)
        self.kst = B.sb("kst", [128, 16], F32)

        self.ps = B.psum_("ps", [128, 6 * 512], F32)
        self.psb = self.ps.bitcast(BF16)
        self.G = BankGate()

        for nm in ("s_c s_x s_gq s_ln s_xe s_vm s_sm s_w s_wc s_pj s_pe s_ua "
                   "s_vt s_ve s_s s_ex s_mk s_pv s_ye s_yd s_nr s_sq").split():
            setattr(self, nm, B.sem(nm))
        self.s_xp = [B.sem("s_xp0"), B.sem("s_xp1")]
        self.s_wp = [B.sem("s_wp0"), B.sem("s_wp1")]
        self.s_ys = B.sem("s_ys")
        self.s_vo = B.sem("s_vo")

        if self.parent is None:
            self.blk = self.ctx.enter_context(self.nc.Block())
        else:
            self.blk = self.parent.blk
        self.phase0()
        if L == 0:
            self.ln0()
        npair = 0
        for c in range(2):
            for p in range(8):
                if npair < DBG_PAIRS:
                    self.pair(c, p)
                    npair += 1
            if npair >= (c + 1) * 8:
                self.chunk_tail(c)
        self.epilogue()
        self.finish()

    def phase0(self):
        L, blk = self.L, self.blk
        s_c, s_x = self.s_c, self.s_x

        @blk.sync
        def _(sync):
            self.dma(sync, s_c, self.ident[:, :], self.ident_in)
            self.dma(sync, s_c, self.negones[:, :], self.negones_in)
            self.dma(sync, s_c, self.ones64[:, :], self.ones64_in)
            if L == 0:
                self.dma(sync, s_c, self.csn[:, :], self.csn_in)
                self.dma(sync, s_c, self.maskb[:, :], self.mask_in)
                self.dma(sync, s_c, self.upf[:, :], self.up_in)
            else:
                for i, b in enumerate((self.bq_in, self.bk_in, self.bv_in)):
                    self.dma(sync, s_c, self.bfm[:, 8 * i:8 * i + 8], b)
                for i in range(8):
                    self.dma(sync, s_x, self.xn[:, i * T:(i + 1) * T],
                             self.xn_in[:, i * T:(i + 1) * T])
        self.CONSTS = s_c.n
        self.XN0 = s_x.n
        self.UPC = self.s_wc.n

        if L == 0:
            s_wc = self.s_wc

            @blk.gpsimd
            def _(gp):
                gp.wait_ge(s_c.h, self.CONSTS)
                s_wc.inc(gp.tensor_copy(self.upb[:, :], self.upf[:, :]))
            self.UPC = s_wc.n

        @blk.gpsimd
        def _(gp):
            if L == 0:
                m0 = self.s_gq.inc(gp.memset(self.k_h[64:128, :], 0.0))
                gp.wait_ge(self.s_gq.h, m0)
                gp.memset(self.k_h[64:65, :], -1.0)
                gp.memset(self.q_h[64:128, :], 0.0)
            else:
                m0 = self.s_gq.inc(gp.memset(self.k_hA[64:128, :], 0.0))
                gp.wait_ge(self.s_gq.h, m0)
                gp.memset(self.k_hA[64:65, :], -1.0)
                m0 = self.s_gq.inc(gp.memset(self.k_hB[0:64, :], 0.0))
                gp.wait_ge(self.s_gq.h, m0)
                gp.memset(self.k_hB[0:1, :], -1.0)
                gp.memset(self.q_hA[64:128, :], 0.0)
                gp.memset(self.q_hB[0:64, :], 0.0)
            self.s_gq.inc(gp.memset(self.yev[64:128, :], 0.0))
        self.PADZ = self.s_gq.n

    def ln0(self):
        blk = self.blk
        self.LNB = self.s_ln.n
        self.XEB = self.s_xe.n
        GQ0 = self.s_gq.n
        SM0 = self.s_sm.n
        VT0 = self.s_vt.n
        if not DBG_LN0:
            self.s_xe.n += 32
            self.s_ln.n += 32

            @blk.scalar
            def _(sc):
                sc.sem_inc(self.s_xe.h, 32)
                sc.sem_inc(self.s_ln.h, 32)

            @blk.gpsimd
            def _(gp):
                self.s_gq.inc(gp.memset(self.sq[64:128, :], 0.0))
            self.SQZ = self.s_gq.n
            return
        s_x, s_gq, s_vm, s_sm, s_ln, s_vt, s_xe = (
            self.s_x, self.s_gq, self.s_vm, self.s_sm, self.s_ln, self.s_vt, self.s_xe)
        XN0 = self.XN0
        xst, xsq, xnt, lst, xn = self.xst, self.xsq, self.xnt, self.lst, self.xn
        VM0 = s_vm.n

        xm = []

        @blk.sync
        def _(sync):
            for tt in range(32):
                par = tt % 2
                if tt >= 2:
                    sync.wait_ge(s_ln.h, self.LNB + tt - 1)
                self.dma(sync, self.s_xp[par], xst[:, par * D:(par + 1) * D],
                         self.x_in[tt * 128:(tt + 1) * 128, :])
                xm.append(self.s_xp[par].n)

        @blk.gpsimd
        def _(gp):
            for tt in range(32):
                par = tt % 2
                gp.wait_ge(self.s_xp[tt % 2].h, xm[tt])
                if tt >= 2:
                    gp.wait_ge(s_vm.h, VM0 + 8 * (tt - 2) + 1)
                s_gq.inc(gp.tensor_tensor(xsq[:, par * D:(par + 1) * D],
                                          xst[:, par * D:(par + 1) * D],
                                          xst[:, par * D:(par + 1) * D], ALU.mult))
            gp.wait_ge(s_vm.h, VM0 + 8 * 31 + 1)
            s_gq.inc(gp.memset(self.sq[64:128, :], 0.0))
            self.SQZ = s_gq.n

        @blk.vector
        def _(v):
            for tt in range(32):
                par = tt % 2
                o = par * 12
                xv = xst[:, par * D:(par + 1) * D]
                qv = xsq[:, par * D:(par + 1) * D]
                v.wait_ge(self.s_xp[tt % 2].h, xm[tt])
                v.wait_ge(s_gq.h, GQ0 + tt + 1)
                v.reduce_sum(lst[:, o:o + 1], xv, axis=AX.X)
                v.wait_ge(s_vm.h, s_vm.inc(v.reduce_sum(lst[:, o + 1:o + 2], qv, axis=AX.X)))
                v.wait_ge(s_vm.h, s_vm.inc(v.tensor_scalar(
                    lst[:, o + 2:o + 3], lst[:, o:o + 1], 1.0 / D, None, ALU.mult)))
                v.tensor_tensor(lst[:, o + 3:o + 4], lst[:, o + 2:o + 3],
                                lst[:, o + 2:o + 3], ALU.mult)
                v.wait_ge(s_vm.h, s_vm.inc(v.tensor_scalar(
                    lst[:, o + 4:o + 5], lst[:, o + 1:o + 2], 1.0 / D, None, ALU.mult)))
                v.wait_ge(s_vm.h, s_vm.inc(v.tensor_tensor(
                    lst[:, o + 5:o + 6], lst[:, o + 4:o + 5], lst[:, o + 3:o + 4], ALU.subtract)))
                v.wait_ge(s_vm.h, s_vm.inc(v.tensor_scalar(
                    lst[:, o + 6:o + 7], lst[:, o + 5:o + 6], 1.0, 1e-5, ALU.mult, ALU.add)))
                s_vm.inc(v.reciprocal(lst[:, o + 7:o + 8], lst[:, o + 6:o + 7]))
                v.wait_ge(s_sm.h, SM0 + tt + 1)
                v.wait_ge(s_vm.h, s_vm.inc(v.tensor_tensor(
                    lst[:, o + 9:o + 10], lst[:, o + 2:o + 3], lst[:, o + 8:o + 9], ALU.mult)))
                v.wait_ge(s_vm.h, s_vm.inc(v.tensor_scalar(
                    lst[:, o + 10:o + 11], lst[:, o + 9:o + 10], -1.0, None, ALU.mult)))
        NV = 8

        @blk.scalar
        def _(sc):
            for tt in range(32):
                par = tt % 2
                o = par * 12
                sc.wait_ge(s_vm.h, VM0 + NV * tt + 6)
                s_sm.inc(sc.activation(lst[:, o + 8:o + 9], lst[:, o + 7:o + 8], AF.Sqrt))
                sc.wait_ge(s_sm.h, s_sm.n)
                sc.wait_ge(s_vm.h, VM0 + NV * tt + 8)
                if tt >= 2:
                    sc.wait_ge(s_vt.h, VT0 + tt - 1)
                s_ln.inc(sc.activation(xnt[:, par * D:(par + 1) * D],
                                       xst[:, par * D:(par + 1) * D], AF.Identity,
                                       bias=lst[:, o + 10:o + 11], scale=lst[:, o + 8:o + 9]))
                if tt >= 1:
                    sc.wait_ge(s_vt.h, VT0 + tt)
                    src = self.bkb(4 + (tt - 1) % 2, 0, 1024)
                    dst = xn[:, :].rearrange("p (c t) -> p c t", c=8)[:, :, (tt - 1) * 128:tt * 128]
                    s_xe.inc(sc.copy(dst, src))
            sc.wait_ge(s_vt.h, VT0 + 32)
            src = self.bkb(4 + 31 % 2, 0, 1024)
            dst = xn[:, :].rearrange("p (c t) -> p c t", c=8)[:, :, 31 * 128:32 * 128]
            s_xe.inc(sc.copy(dst, src))

        @blk.tensor
        def _(t):
            t.wait_ge(self.s_c.h, self.CONSTS)
            for tt in range(32):
                par = tt % 2
                t.wait_ge(s_ln.h, self.LNB + tt + 1)
                if tt >= 2:
                    t.wait_ge(s_xe.h, self.XEB + tt - 1)
                for cch in range(8):
                    ins = t.transpose(self.bkb(4 + tt % 2, cch * 128, (cch + 1) * 128),
                                      xnt[:, par * D + cch * 128:par * D + (cch + 1) * 128],
                                      self.ident[:, :])
                s_vt.inc(ins)

        @blk.scalar
        def _(sc):
            sc.wait_ge(s_xe.h, self.XEB + 32)
            for i in range(8):
                self.dma(sc, self.s_yd, self.xn_out[:, i * T:(i + 1) * T],
                         xn[:, i * T:(i + 1) * T])
        self.XNDMA = self.s_yd.n

    def pair(self, c, p):
        L, blk, G, M = self.L, self.blk, self.G, self.M
        kend = 2048 if (L == 0 and c == 0) else 4096
        nkt = kend // 128
        qbase = c * CHT
        it = c * 8 + p
        par = it % 2
        xn = self.xn
        s_w, s_wc, s_pj, s_pe = self.s_w, self.s_wc, self.s_pj, self.s_pe

        @blk.sync
        def _(sync):
            if L == 0 and it == 0:
                sync.wait_ge(self.s_ln.h, self.LNB + 32)
            wc0 = self.UPC
            for mi, w in enumerate((self.wk_in, self.wv_in, self.wq_in)):
                gmi = it * 3 + mi
                sp = gmi % 2
                if gmi >= 2:
                    sync.wait_ge(s_wc.h, wc0 + gmi - 1)
                for kc in range(8):
                    self.dma(sync, self.s_wp[sp],
                             self.wst[:, sp * D + kc * 128:sp * D + (kc + 1) * 128],
                             w[kc * 128:(kc + 1) * 128, p * 128:(p + 1) * 128])
                M[("wdma", gmi)] = self.s_wp[sp].n

        @blk.gpsimd
        def _(gp):
            for mi in range(3):
                gmi = it * 3 + mi
                sp = gmi % 2
                gp.wait_ge(self.s_wp[gmi % 2].h, M[("wdma", gmi)])
                if it >= 2:
                    gp.wait_ge(s_pj.h, M[("prjdone", it - 2)])
                s_wc.inc(gp.tensor_copy(
                    self.wbf[:, (par * 3 + mi) * D:(par * 3 + mi + 1) * D],
                    self.wst[:, sp * D:(sp + 1) * D]))
                M[("wcv", gmi)] = s_wc.n

        mats = [("k", kend), ("v", kend), ("q", CHT)]
        evn = []
        cum = 0
        for nm, ntok in mats:
            per = 2 if (L == 0 or nm == "v") else 4
            for g in range(ntok // 1024):
                cum += per
                evn.append(cum)

        @blk.tensor
        def _(t):
            t.wait_ge(self.s_c.h, self.CONSTS)
            if L == 0:
                t.wait_ge(self.s_xe.h, self.XEB + 32)
                t.wait_ge(s_wc.h, self.UPC)
            else:
                t.wait_ge(self.s_x.h, self.XN0)
            gi = 0
            pe0 = s_pe.n
            for mi, (nm, ntok) in enumerate(mats):
                gmi = it * 3 + mi
                t.wait_ge(s_wc.h, M[("wcv", gmi)])
                absb = qbase if nm == "q" else 0
                for g in range(ntok // 1024):
                    bks = [2 * (gi % 2), 2 * (gi % 2) + 1]
                    if gi >= 2:
                        t.wait_ge(s_pe.h, pe0 + evn[gi - 2])
                    else:
                        G.acquire(t, bks)
                    for kc in range(8):
                        for t2 in range(2):
                            ins = t.matmul(
                                self.bk(bks[t2]),
                                self.wbf[:, (par * 3 + mi) * D + kc * 128:
                                         (par * 3 + mi) * D + (kc + 1) * 128],
                                xn[:, kc * T + absb + g * 1024 + t2 * 512:
                                   kc * T + absb + g * 1024 + (t2 + 1) * 512],
                                start=(kc == 0), stop=(kc == 7))
                            if kc == 7:
                                s_pj.inc(ins)
                    M[("prj", it, mi, g)] = s_pj.n
                    gi += 1
            M[("prjdone", it)] = s_pj.n

        @blk.scalar
        def _(sc):
            gi = 0
            pe_base = s_pe.n
            for mi, (nm, ntok) in enumerate(mats):
                for g in range(ntok // 1024):
                    bks = [2 * (gi % 2), 2 * (gi % 2) + 1]
                    sc.wait_ge(s_pj.h, M[("prj", it, mi, g)])
                    for t2 in range(2):
                        col = g * 1024 + t2 * 512
                        if L == 0:
                            dst = {"k": self.ks, "v": self.vs, "q": self.qs}[nm]
                            s_pe.inc(sc.activation(dst[:, col:col + 512],
                                                   self.bk(bks[t2]), AF.Copy))
                        else:
                            bcol = {"q": 0, "k": 8, "v": 16}[nm] + p
                            bias = self.bfm[:, bcol:bcol + 1]
                            if nm == "v":
                                s_pe.inc(sc.activation(self.v2f[:, col:col + 512],
                                                       self.bk(bks[t2]), AF.Identity,
                                                       bias=bias))
                            else:
                                dA, dB = ((self.k_hA, self.k_hB) if nm == "k"
                                          else (self.q_hA, self.q_hB))
                                s_pe.inc(sc.activation(
                                    dA[0:64, col:col + 512], self.bk(bks[t2], 0, 64),
                                    AF.Identity, bias=self.bfm[0:64, bcol:bcol + 1]))
                                s_pe.inc(sc.activation(
                                    dB[64:128, col:col + 512], self.bk(bks[t2], 64, 128),
                                    AF.Identity, bias=self.bfm[64:128, bcol:bcol + 1]))
                    G.release(bks, s_pe)
                    assert s_pe.n == pe_base + evn[gi]
                    gi += 1
            M[("prjevac", it)] = s_pe.n

        if L == 0 and DBG_VUP:
            @blk.tensor
            def _(t):
                t.wait_ge(s_pe.h, M[("prjevac", it)])
                pe0v = s_pe.n
                for g in range(nkt // 4):
                    bb = 2 * (g % 2)
                    if g >= 2:
                        t.wait_ge(s_pe.h, pe0v + (g - 1))
                    else:
                        G.acquire(t, [bb])
                    i1 = t.matmul(self.bk(bb), self.upb[:, 512:640],
                                  self.vs[:, g * 512:(g + 1) * 512], start=True, stop=True)
                    s_pj.inc(i1)
                    M[("vup", it, g)] = s_pj.n

            @blk.scalar
            def _(sc):
                for g in range(nkt // 4):
                    bb = 2 * (g % 2)
                    sc.wait_ge(s_pj.h, M[("vup", it, g)])
                    s_pe.inc(sc.activation(self.v2f[:, g * 512:(g + 1) * 512],
                                           self.bk(bb), AF.Copy))
                    G.release([bb], s_pe)
                M[("vupe", it)] = s_pe.n

        s_vt, s_ve = self.s_vt, self.s_ve
        if not DBG_VT:
            M[("vtmdone", it)] = self.s_ve.n
            M[("pairpv", it)] = self.s_pv.n
            if DBG_HEADS:
                for hh in range(2):
                    self.head(c, p, hh, kend, nkt, qbase, it)
            M[("pairpv", it)] = self.s_pv.n
            return

        @blk.gpsimd
        def _(gp):
            if it >= 1:
                gp.wait_ge(self.s_pv.h, M[("pairpv", it - 1)])
            if DBG_VT_MS:
                self.s_vo.inc(gp.memset(self.v_tm[:, 0:nkt * 130], 1.0))
            else:
                self.s_vo.inc(gp.engine_nop())
            M[("vones", it)] = self.s_vo.n

        @blk.tensor
        def _(t):
            vsrc = self.v2f
            wait_m = M[("vupe", it)] if (L == 0 and DBG_VUP) else M[("prjevac", it)]
            t.wait_ge(s_pe.h, wait_m)
            pe0v2 = s_pe.n
            for g4 in range(nkt // 4):
                M[("vtm", it, g4)] = pe0v2 + 2 * (g4 + 1)
            for g4 in range(nkt // 4):
                half = g4 % 2
                if g4 >= 2:
                    t.wait_ge(s_pe.h, M[("vtm", it, g4 - 2)])
                else:
                    G.acquire(t, [4 + half])
                if DBG_VT_T:
                    for j in range(4):
                        ins = t.transpose(
                            self.bkb(4 + half, j * 128, (j + 1) * 128),
                            vsrc[:, (g4 * 4 + j) * 128:(g4 * 4 + j + 1) * 128],
                            self.ident[:, :])
                else:
                    ins = t.engine_nop()
                s_vt.inc(ins)
                M[("vT", it, g4)] = s_vt.n

        @blk.scalar
        def _(sc):
            sc.wait_ge(self.s_vo.h, M[("vones", it)])
            for g4 in range(nkt // 4):
                half = g4 % 2
                sc.wait_ge(s_vt.h, M[("vT", it, g4)])
                base = (4 + half) * 1024
                for hh in range(2):
                    if g4 < DBG_VT_N:
                        for j in range(3, -1, -1):
                            kt = g4 * 4 + j
                            ins = sc.copy(
                                self.v_tm[:, kt * 130 + hh * 65:kt * 130 + hh * 65 + 64],
                                self.psb[:, base + j * 128 + hh * 64:base + j * 128 + (hh + 1) * 64])
                    else:
                        ins = sc.copy(self.kst[:, 0:1], self.kst[:, 1:2])
                    s_pe.inc(ins)
                assert M[("vtm", it, g4)] == s_pe.n
            M[("vtmdone", it)] = s_pe.n
        G.release([4, 5], s_pe)

        for hh in range(2 if DBG_HEADS else 0):
            self.head(c, p, hh, kend, nkt, qbase, it)

        M[("pairpv", it)] = self.s_pv.n

    def head(self, c, p, hh, kend, nkt, qbase, it):
        L, blk, G, M = self.L, self.blk, self.G, self.M
        h = 2 * p + hh
        hr = 64 * hh
        s_pj, s_pe, s_ua, s_sq, s_nr = self.s_pj, self.s_pe, self.s_ua, self.s_sq, self.s_nr
        s_vm, s_sm = self.s_vm, self.s_sm

        if L == 0:
            for kind, lat, dst, ntok, absb, ubase in (
                    ("k", self.ks, self.k_h, kend, 0, 256 + 128 * hh),
                    ("q", self.qs, self.q_h, CHT, qbase, 0 + 128 * hh)):
                ntt = ntok // 512

                @blk.tensor
                def _(t, kind=kind, lat=lat, ntt=ntt, absb=absb, ubase=ubase):
                    if kind == "k":
                        t.wait_ge(s_pe.h, M[("prjevac", it)])
                        if hh == 1:
                            t.wait_ge(self.s_s.h, M[("attdone", it, 0)])
                    pe0u = s_pe.n
                    ua0 = s_ua.n
                    for g in range(ntt):
                        b0 = 2 * (g % 2)
                        b1 = b0 + 1
                        if g >= 2:
                            t.wait_ge(s_pe.h, pe0u + (g - 1))
                            t.wait_ge(s_ua.h, ua0 + 2 * (g - 1))
                        else:
                            G.acquire(t, [b0, b1])
                        rhs = lat[0:128, g * 512:(g + 1) * 512]
                        i1 = t.matmul(self.bk(b0, 0, 64), self.upb[:, ubase:ubase + 64],
                                      rhs, start=True, stop=True)
                        s_pj.inc(i1)
                        i3 = t.matmul(self.bk(b1, 0, 64),
                                      self.upb[:, ubase + 64:ubase + 128], rhs,
                                      start=True, stop=True)
                        s_pj.inc(i3)
                        M[("up", it, hh, kind, g)] = s_pj.n

                @blk.scalar
                def _(sc, kind=kind, dst=dst, ntt=ntt):
                    for g in range(ntt):
                        b0 = 2 * (g % 2)
                        sc.wait_ge(s_pj.h, M[("up", it, hh, kind, g)])
                        s_pe.inc(sc.activation(dst[0:32, g * 512:(g + 1) * 512],
                                               self.bk(b0, 0, 32), AF.Copy))
                        G.release([b0], s_pe)
                        M[("upc", it, hh, kind, g)] = s_pe.n

                @blk.vector
                def _(v, kind=kind, dst=dst, ntt=ntt, absb=absb):
                    ua0v = s_ua.n
                    for g in range(ntt):
                        b0 = 2 * (g % 2)
                        b1 = b0 + 1
                        v.wait_ge(s_pj.h, M[("up", it, hh, kind, g)])
                        v.wait_ge(s_ua.h, ua0v + 2 * g)
                        ac = absb + g * 512
                        v.tensor_mul(self.rtmp[32:64, 0:512], self.bk(b0, 32, 64),
                                     self.csn[32:64, ac:ac + 512])
                        mm = s_ua.inc(v.tensor_mul(self.rtmp[32:64, 512:1024],
                                                   self.bk(b1, 32, 64),
                                                   self.csn[32:64, T + ac:T + ac + 512]))
                        v.wait_ge(s_ua.h, mm)
                        s_ua.inc(v.tensor_add(dst[32:64, g * 512:(g + 1) * 512],
                                              self.rtmp[32:64, 0:512],
                                              self.rtmp[32:64, 512:1024]))
                        G.release([b0, b1], s_ua)
                        M[("uprope", it, hh, kind, g)] = s_ua.n
                M[(kind + "done", it, hh)] = (s_pe.n, s_ua.n)

            qsrc, ksrc = self.q_h, self.k_h
        else:
            qsrc, ksrc = ((self.q_hA, self.k_hA) if hh == 0
                          else (self.q_hB, self.k_hB))

        if L == 0:
            self.norms_k(it, hh, kend, ksrc)
            self.norms_q(it, hh, qsrc)
        else:
            if hh == 0:
                self.norms_k(it, 0, kend, ksrc)
                self.norms_k(it, 1, kend, ksrc)
                self.norms_q(it, 0, qsrc)
                self.norms_q(it, 1, qsrc)

        s_s, s_ex, s_mk, s_pv, s_ye = self.s_s, self.s_ex, self.s_mk, self.s_pv, self.s_ye

        for qg in range(4 if DBG_ATT else 0):
            qa = qbase + qg * 512
            nkt_q = min(nkt, (qa + 512) // 128) if L == 0 else nkt
            ngrp = (nkt_q + 1) // 2

            if L == 0:
                if not hasattr(self, "mk_pred"):
                    self.mk_pred = self.s_mk.n
                for j in range(4):
                    M[("mk", it, hh, qg, qa // 128 + j)] = self.mk_pred + j + 1
                self.mk_pred += 4
            if not hasattr(self, "ex_pred"):
                self.ex_pred = self.s_ex.n
            for g in range(ngrp):
                M[("exp", it, hh, qg, g)] = self.ex_pred + g + 1
            self.ex_pred += ngrp

            M[("pvprev", it, hh, qg)] = self.s_pv.n

            @blk.tensor
            def _(t, qg=qg, qa=qa, nkt_q=nkt_q, ngrp=ngrp):
                t.wait_ge(self.s_gq.h, self.PADZ)
                t.wait_ge(s_pe.h, M[("mrow", it, hh)])
                t.wait_ge(s_pe.h, M[("vtmdone", it)])
                for g in range(ngrp):
                    sgb = [2 * (g % 2), 2 * (g % 2) + 1]
                    if g >= 2:
                        t.wait_ge(self.s_ex.h, M[("exp", it, hh, qg, g - 2)])
                    else:
                        G.acquire(t, sgb)
                    for kt in (2 * g, 2 * g + 1):
                        if kt >= nkt_q:
                            continue
                        sub = kt % 2
                        vw = self.bk(sgb[sub])
                        s_s.inc(t.matmul(vw, ksrc[0:128, kt * 128:(kt + 1) * 128],
                                         qsrc[0:128, qg * 512:(qg + 1) * 512],
                                         start=True, stop=True))
                    M[("sgrp", it, hh, qg, g)] = s_s.n
                    if g >= 1:
                        self._pv_group(t, it, hh, qg, g - 1, nkt_q, qa)
                self._pv_group(t, it, hh, qg, ngrp - 1, nkt_q, qa)
                M[("pvdone", it, hh, qg)] = s_pv.n

            @blk.scalar
            def _(sc, qg=qg, qa=qa, nkt_q=nkt_q, ngrp=ngrp):
                for g in range(ngrp):
                    sgb = [2 * (g % 2), 2 * (g % 2) + 1]
                    n_in_g = 1 if (2 * g + 1 >= nkt_q) else 2
                    sc.wait_ge(s_s.h, M[("sgrp", it, hh, qg, g)])
                    if g >= 2:
                        sc.wait_ge(s_pv.h, M[("pvg", it, hh, qg, g - 2)])
                    elif ("pvprev", it, hh, qg) in M:
                        sc.wait_ge(s_pv.h, M[("pvprev", it, hh, qg)])
                    s_ex.inc(sc.activation(self.PT[:, (g % 2) * 1024:(g % 2) * 1024 + n_in_g * 512],
                                           self.ps[:, sgb[0] * 512:sgb[0] * 512 + n_in_g * 512],
                                           AF.Exp, scale=0.125))
                    assert M[("exp", it, hh, qg, g)] == s_ex.n
                    G.release(sgb, s_ex)

            if L == 0:
                @blk.gpsimd
                def _(gp, qg=qg, qa=qa, nkt_q=nkt_q):
                    for kt in range(nkt_q):
                        ktb = kt * 128
                        if ktb < qa:
                            continue
                        r = (ktb - qa) // 128
                        g = kt // 2
                        gp.wait_ge(s_ex.h, M[("exp", it, hh, qg, g)])
                        pcol = (g % 2) * 1024 + (kt % 2) * 512
                        s_mk.inc(gp.tensor_mul(self.PT[:, pcol:pcol + 512],
                                               self.PT[:, pcol:pcol + 512],
                                               self.maskb[:, r * 512:(r + 1) * 512]))
                        assert M[("mk", it, hh, qg, kt)] == s_mk.n

            @blk.scalar
            def _(sc, qg=qg):
                sc.wait_ge(s_pv.h, M[("pvdone", it, hh, qg)])
                s_ye.inc(sc.activation(self.yev[0:65, :], self.bk(4, 0, 65), AF.Copy))
                M[("yev", it, hh, qg)] = s_ye.n
                G.release([4], s_ye)

            ys0 = self.s_ys.n
            for j in range(4):
                M[("ystj", it, hh, qg, j)] = ys0 + j + 1
            M[("yst", it, hh, qg)] = ys0 + 4
            vt0y = self.s_vt.n

            @blk.tensor
            def _(t, qg=qg):
                t.wait_ge(s_ye.h, M[("yev", it, hh, qg)])
                for j in range(4):
                    if j < 2:
                        G.acquire(t, [4 + j % 2])
                    else:
                        t.wait_ge(self.s_ys.h, M[("ystj", it, hh, qg, j - 2)])
                    ins = t.transpose(self.bkb(4 + j % 2, 0, 128, 0, 128),
                                      self.yev[0:128, j * 128:(j + 1) * 128],
                                      self.ident[:, :])
                    self.s_vt.inc(ins)
                M[("yT", it, hh, qg)] = self.s_vt.n

            @blk.vector
            def _(v, qg=qg):
                if c == 1:
                    v.wait_ge(self.s_yd.h, M[("ydma", 0)])
                for j in range(4):
                    v.wait_ge(self.s_vt.h, vt0y + j + 1)
                    lv = self.kst[:, 12 + (j % 2):13 + (j % 2)]
                    mm = s_vm.inc(v.reciprocal(lv, self.bkb(4 + j % 2, 64, 65, 0, 128)))
                    v.wait_ge(s_vm.h, mm)
                    tt128 = qg * 4 + j
                    self.s_ys.inc(v.tensor_scalar(
                        self.y_tm[:, tt128 * D + h * 64:tt128 * D + (h + 1) * 64],
                        self.bkb(4 + j % 2, 0, 64, 0, 128),
                        lv, None, ALU.mult))
                    assert M[("ystj", it, hh, qg, j)] == self.s_ys.n
                G.release([4, 5], self.s_ys)

        M[("attdone", it, hh)] = self.s_s.n

    def norms_k(self, it, hh, kend, ksrc):
        L, blk, G, M = self.L, self.blk, self.G, self.M
        s_pj, s_pe, s_ua, s_sq, s_nr = self.s_pj, self.s_pe, self.s_ua, self.s_sq, self.s_nr

        @blk.scalar
        def _(sc):
            if L == 0:
                pe_m, ua_m = M[("kdone", it, hh)]
                sc.wait_ge(s_pe.h, pe_m)
                sc.wait_ge(s_ua.h, ua_m)
                s_sq.inc(sc.activation(self.sq[0:64, 0:kend], ksrc[0:64, 0:kend], AF.Square))
            elif hh == 0:
                sc.wait_ge(s_pe.h, M[("prjevac", it)])
                s_sq.inc(sc.activation(self.sq[0:64, 0:kend],
                                       self.k_hA[0:64, 0:kend], AF.Square))
                s_sq.inc(sc.activation(self.sq[64:128, 0:kend],
                                       self.k_hB[64:128, 0:kend], AF.Square))
            M[("ksq", it, hh)] = s_sq.n

        o64col = self.ones64[:, (hh if L == 1 else 0):(hh if L == 1 else 0) + 1]

        @blk.tensor
        def _(t):
            if L == 0:
                t.wait_ge(self.s_gq.h, self.SQZ)
            t.wait_ge(s_sq.h, M[("ksq", it, hh)])
            nr0 = self.s_nr.n
            for g in range(kend // 512):
                bb = 4 if g % 2 == 0 else 5
                if g >= 2:
                    t.wait_ge(self.s_nr.h, nr0 + (g - 1))
                else:
                    G.acquire(t, [bb])
                s_pj.inc(t.matmul(self.bk(bb, 0, 1), o64col,
                                  self.sq[0:128, g * 512:(g + 1) * 512],
                                  start=True, stop=True))
                M[("knmm", it, hh, g)] = s_pj.n

        @blk.vector
        def _(v):
            for g in range(kend // 512):
                bb = 4 if g % 2 == 0 else 5
                v.wait_ge(s_pj.h, M[("knmm", it, hh, g)])
                s_nr.inc(v.reduce_max(self.kst[0:1, g:g + 1], self.bk(bb, 0, 1), axis=AX.X))
                G.release([bb], s_nr)
            v.wait_ge(s_nr.h, s_nr.n)
            mm = s_nr.inc(v.reduce_max(self.kst[0:1, 8 + hh:9 + hh],
                                       self.kst[0:1, 0:kend // 512], axis=AX.X))
            v.wait_ge(s_nr.h, mm)
            s_nr.inc(v.tensor_scalar(self.kst[0:1, 10 + hh:11 + hh],
                                     self.kst[0:1, 8 + hh:9 + hh], 1.1025, None, ALU.mult))
            M[("khat", it, hh)] = s_nr.n

    def norms_q(self, it, hh, qsrc):
        L, blk, G, M = self.L, self.blk, self.G, self.M
        s_pj, s_pe, s_ua, s_sq, s_nr = self.s_pj, self.s_pe, self.s_ua, self.s_sq, self.s_nr

        @blk.scalar
        def _(sc):
            if L == 0:
                pe_m, ua_m = M[("qdone", it, hh)]
                sc.wait_ge(s_pe.h, pe_m)
                sc.wait_ge(s_ua.h, ua_m)
                sc.wait_ge(s_nr.h, M[("khat", it, hh)])
                s_sq.inc(sc.activation(self.sq[0:64, 0:CHT], qsrc[0:64, 0:CHT], AF.Square))
            elif hh == 0:
                sc.wait_ge(s_nr.h, M[("khat", it, 1)])
                s_sq.inc(sc.activation(self.sq[0:64, 0:CHT],
                                       self.q_hA[0:64, 0:CHT], AF.Square))
                s_sq.inc(sc.activation(self.sq[64:128, 0:CHT],
                                       self.q_hB[64:128, 0:CHT], AF.Square))
            M[("qsq", it, hh)] = s_sq.n

        o64col = self.ones64[:, (hh if L == 1 else 0):(hh if L == 1 else 0) + 1]

        @blk.tensor
        def _(t):
            t.wait_ge(s_sq.h, M[("qsq", it, hh)])
            pe0n = s_pe.n
            for g in range(4):
                bb = 4 if g % 2 == 0 else 5
                if g >= 2:
                    t.wait_ge(s_pe.h, pe0n + (g - 1))
                else:
                    G.acquire(t, [bb])
                s_pj.inc(t.matmul(self.bk(bb, 0, 1), o64col,
                                  self.sq[0:128, g * 512:(g + 1) * 512],
                                  start=True, stop=True))
                M[("qnmm", it, hh, g)] = s_pj.n

        @blk.scalar
        def _(sc):
            sc.wait_ge(s_nr.h, M[("khat", it, hh)])
            for g in range(4):
                bb = 4 if g % 2 == 0 else 5
                sc.wait_ge(s_pj.h, M[("qnmm", it, hh, g)])
                s_pe.inc(sc.activation(self.mview(hh, g * 512, (g + 1) * 512),
                                       self.bk(bb, 0, 1), AF.Sqrt,
                                       scale=self.kst[0:1, 10 + hh:11 + hh]))
                G.release([bb], s_pe)
            M[("mrow", it, hh)] = s_pe.n

    def _pv_group(self, t, it, hh, qg, g, nkt_q, qa):
        L, G, M = self.L, self.G, self.M
        s_pv = self.s_pv
        for kt in (2 * g, 2 * g + 1):
            if kt >= nkt_q:
                continue
            if kt == 0:
                G.acquire(t, [4])
            if L == 0 and kt * 128 >= qa:
                t.wait_ge(self.s_mk.h, M[("mk", it, hh, qg, kt)])
            else:
                t.wait_ge(self.s_ex.h, M[("exp", it, hh, qg, g)])
            pcol = (g % 2) * 1024 + (kt % 2) * 512
            ins = t.matmul(self.bk(4, 0, 65),
                           self.v_tm[:, kt * 130 + hh * 65:kt * 130 + (hh + 1) * 65],
                           self.PT[:, pcol:pcol + 512],
                           start=(kt == 0), stop=(kt == nkt_q - 1))
            s_pv.inc(ins)
        M[("pvg", it, hh, qg, g)] = s_pv.n

    def chunk_tail(self, c):
        blk, M = self.blk, self.M
        it_last = c * 8 + 7
        snap = [M[("yst", c * 8 + p, hh, qg)] for p in range(8) for hh in range(2) for qg in range(4)]
        need = max(snap)

        @blk.scalar
        def _(sc):
            sc.wait_ge(self.s_ys.h, need)
            for tt in range(16):
                self.dma(sc, self.s_yd,
                         self.y_out[c * CHT + tt * 128:c * CHT + (tt + 1) * 128, :],
                         self.y_tm[:, tt * D:(tt + 1) * D])
        M[("ydma", c)] = self.s_yd.n

    def epilogue(self):
        blk = self.blk
        total = self.s_yd.n

        @blk.sync
        def _(sync):
            sync.wait_ge(self.s_yd.h, total)


def build_attn(L):
    b = AttnBuilder(L)
    return b.nc



class MixBuilder(LaunchBase):
    def __init__(self, L, parent=None, io=None, entry=None):
        super().__init__(f"mix{L}", parent=parent, io=io, entry=entry)
        self.L = L
        self.build()

    def bk(self, i, r0=0, r1=128):
        return self.ps[r0:r1, i * 512:(i + 1) * 512]

    def bk2(self, i):
        return self.ps[:, i * 512:(i + 2) * 512]

    def bkb(self, i, c0, c1, r0=0, r1=128):
        return self.psb[r0:r1, i * 1024 + c0:i * 1024 + c1]

    def build(self):
        L = self.L
        B = self

        self.ident_in = B.din("c_ident", [128, 128], BF16)
        self.identf_in = B.din("c_identf", [128, 128], F32)
        self.ones1_in = B.din("c_ones1", [1, 128], BF16)
        self.ones512_in = B.din("c_ones512", [1, 512], BF16)
        self.xn_in = B.din("xn_fm", [128, 8 * T], BF16)
        self.y_in = B.din("y_tm", [T, D], BF16)
        self.res_in = B.din("res", [T, D], F32)
        if L == 0:
            self.wc_in = B.din("wc", [D, D], F32)
            self.gate_in = B.din("gate_wT", [D, 8], F32)
            self.egu_in = B.din("eguT", [8, D, 2816], F32)
            self.edn_in = B.din("edownT", [8, 1408, D], F32)
            self.sg_in = B.din("sgT", [D, 2816], F32)
            self.su_in = B.din("suT", [D, 2816], F32)
            self.sd_in = B.din("sdT", [2816, D], F32)
            self.seg_in = B.din("segT", [D, 1], F32)
            self.x1_out = B.dout("x1", [T, D], F32)
            self.xn1_out = B.dout("xn1_fm", [128, 8 * T], BF16)
        else:
            self.wc_in = B.din("wto", [D, D], F32)
            self.bo_in = B.din("bo", [128, 8], F32)
            self.rw_in = B.din("router_w", [D, 8], F32)
            self.rb_in = B.din("router_b", [1, 8], F32)
            self.w1_in = B.din("w1", [8, D, 4096], F32)
            self.b1_in = B.din("b1", [8, 128, 32], F32)
            self.w2_in = B.din("w2", [8, 4096, D], F32)
            self.b2_in = B.din("b2", [8, D], F32)
            self.x1_out = B.dout("out", [T, D], F32)

        self.xn = B.sb("xn", [128, 8 * CHT], BF16)
        self.acc = B.sb("acc", [128, 8 * CHT], F32)
        self.ident = B.sb("ident", [128, 128], BF16)
        self.identf = B.sb("identf", [128, 128], F32)
        self.ones1 = B.sb("ones1", [1, 128], BF16)
        self.ones512 = B.sb("ones512", [1, 512], BF16)
        if L == 0:
            self.h_sb = B.sb("h_sb", [128, 11 * CHT], BF16)
            hf = self.h_sb.bitcast(F32)
            self.yst = self.h_sb[:, 0:2048]
            self.yfm = self.h_sb[:, 2048:6144]
            self.wcbf = self.h_sb[:, 6144:14336]
            self.xnt = self.h_sb[:, 0:2048]
            self.xne = self.h_sb[:, 2048:4096]
            self.x1t = hf[:, 7168:9216]
            self.xrst = hf[:, 9216:11264]
            self.wcst = B.sb("wcst", [128, 2 * 2048], F32)
            self.gust = self.wcst
            self.gubf = B.sb("gubf", [128, 2 * 8 * 256], BF16)
            self.dnst = B.sb("dnst", [128, 11 * 256], F32)
            self.dnbf = B.sb("dnbf", [128, 2 * 11 * 256], BF16)
            self.gwbf = B.sb("gwbf", [128, 72], BF16)
            self.gwst = hf[:, 0:72]
            self.lst = B.sb("lst", [128, 24], F32)
        else:
            self.bof = B.sb("bof", [128, 8], F32)
            self.h_sb = B.sb("h_sb", [128, 4 * CHT], BF16)
            hf = self.h_sb.bitcast(F32)
            self.yst = self.h_sb[:, 0:2048]
            self.yfm = self.h_sb[:, 2048:6144]
            self.x1t = hf[:, 0:2048]
            self.xrst = hf[:, 2048:4096]
            self.gust = B.sb("gust", [128, 2 * 4096], F32)
            self.wcst = self.gust[:, 0:2048]
            self.gubf = B.sb("gubf", [128, 2 * 8 * 512], BF16)
            self.dnbf = B.sb("dnbf", [128, 2 * 4 * 1024], BF16)
            self.wcbf = self.dnbf
            self.gwbf = B.sb("gwbf", [128, 72], BF16)
            self.gwst = B.sb("gwst", [128, 72], F32)
            self.b1f = B.sb("b1f", [128, 32], F32)
            self.b2r = B.sb("b2r", [1, 1024], BF16)
            self.b2st = B.sb("b2st", [1, 1024], F32)
            self.rbr = B.sb("rbr", [1, 8], BF16)
            self.rbst = B.sb("rbst", [1, 8], F32)
        self.sgev = B.sb("sgev", [128, 2048], BF16)
        self.hmid = B.sb("hmid", [128, 1024], BF16)
        self.w_all = B.sb("w_all", [128, 160], BF16)
        self.wrow = self.sgev[0:1, 0:CHT]
        self.wb_sb = B.sb("wb_sb", [128, CHT], BF16)
        self.ls = B.sb("ls", [128, 64], F32)
        self.ps = B.psum_("ps", [128, 8 * 512], F32)
        self.psb = self.ps.bitcast(BF16)
        self.G = BankGate()

        for nm in ("s_c s_x s_y s_w s_wc s_pj s_pe s_ex s_ve s_vm s_sm s_gt "
                   "s_vt s_acc s_xr s_x1 s_ln s_od s_gq s_hh").split():
            setattr(self, nm, B.sem(nm))
        self.s_wp = [B.sem("s_wp0"), B.sem("s_wp1")]
        self.s_xp = [B.sem("s_xp0"), B.sem("s_xp1")]
        self.s_yp = [B.sem("s_yp0"), B.sem("s_yp1")]
        self.s_wd = B.sem("s_wd")
        self.s_gw = [B.sem("s_gw0"), B.sem("s_gw1")]
        self.s_x1p = [B.sem("s_x1p0"), B.sem("s_x1p1")]
        self.s_odp = [B.sem("s_odp0"), B.sem("s_odp1")]

        if self.parent is None:
            self.blk = self.ctx.enter_context(self.nc.Block())
        else:
            self.blk = self.parent.blk
        self.phase0()
        for c in range(2):
            self.wc_phase(c)
            self.gating(c)
            nexp = 10 if L == 0 else 8
            for e in range(nexp):
                self.expert(c, e)
            self.tail(c)
        self.epilogue()
        self.finish()

    def phase0(self):
        L, blk = self.L, self.blk
        s_c, s_x = self.s_c, self.s_x

        @blk.sync
        def _(sync):
            self.emit_entry(sync)
            self.dma(sync, s_c, self.ident[:, :], self.ident_in)
            self.dma(sync, s_c, self.identf[:, :], self.identf_in)
            self.dma(sync, s_c, self.ones1[:, :], self.ones1_in)
            self.dma(sync, s_c, self.ones512[:, :], self.ones512_in)
            if L == 0:
                for kc in range(8):
                    self.dma(sync, s_c, self.gwst[:, kc * 8:kc * 8 + 8],
                             self.gate_in[kc * 128:(kc + 1) * 128, :])
                    self.dma(sync, s_c, self.gwst[:, 64 + kc:65 + kc],
                             self.seg_in[kc * 128:(kc + 1) * 128, :])
            else:
                for kc in range(8):
                    self.dma(sync, s_c, self.gwst[:, kc * 8:kc * 8 + 8],
                             self.rw_in[kc * 128:(kc + 1) * 128, :])
                self.dma(sync, s_c, self.rbst[:, :], self.rb_in)
                self.dma(sync, s_c, self.bof[:, :], self.bo_in)
            for i in range(8):
                self.dma(sync, s_x, self.xn[:, i * CHT:(i + 1) * CHT],
                         self.xn_in[:, i * T:i * T + CHT])
        self.CONSTS = s_c.n
        self.XN0 = s_x.n

        @blk.gpsimd
        def _(gp):
            gp.wait_ge(s_c.h, self.CONSTS)
            ncol = 72 if L == 0 else 64
            self.s_gq.inc(gp.tensor_copy(self.gwbf[:, 0:ncol], self.gwst[:, 0:ncol]))
            if L == 1:
                self.s_gq.inc(gp.tensor_copy(self.rbr[:, :], self.rbst[:, :]))
        self.GC = self.s_gq.n

    def wc_phase(self, c):
        L, blk, G, M = self.L, self.blk, self.G, self.M
        s_y, s_wc, s_pj, s_pe, s_vt = (self.s_y, self.s_wc, self.s_pj,
                                       self.s_pe, self.s_vt)
        vt0 = s_vt.n
        pe0 = s_pe.n
        wc0 = s_wc.n
        for gt in range(16):
            M[("yT", c, gt)] = vt0 + gt + 1
            M[("yev", c, gt)] = pe0 + gt + 1
        for oc in range(8):
            M[("wcv", c, oc)] = wc0 + oc + 1

        @blk.sync
        def _(sync, c=c):
            for oc in range(8):
                sp = oc % 2
                if oc >= 2:
                    sync.wait_ge(s_wc.h, M[("wcv", c, oc - 2)])
                elif c == 1:
                    if L == 0:
                        sync.wait_ge(s_wc.h, M[("gucv_last", 9)])
                    else:
                        sync.wait_ge(s_wc.h, M[("w1cv", 63)])
                for kc in range(8):
                    self.dma(sync, self.s_wp[sp],
                             self.wcst[:, sp * D + kc * 128:sp * D + (kc + 1) * 128],
                             self.wc_in[kc * 128:(kc + 1) * 128, oc * 128:(oc + 1) * 128])
                M[("wcd", c, oc)] = self.s_wp[sp].n

        @blk.gpsimd
        def _(gp, c=c):
            gp.wait_ge(self.s_gq.h, self.GC)
            for oc in range(8):
                sp = oc % 2
                gp.wait_ge(self.s_wp[sp].h, M[("wcd", c, oc)])
                if c == 1 and oc < 2:
                    gp.wait_ge(s_pj.h, M[("wcmm", 0)])
                if L == 1 and c == 1 and oc < 2:
                    gp.wait_ge(s_pj.h, M[("w2mmf", 63)])
                self.s_wc.inc(gp.tensor_copy(
                    self.wcbf[:, oc * D:(oc + 1) * D],
                    self.wcst[:, sp * D:(sp + 1) * D]))
                assert M[("wcv", c, oc)] == self.s_wc.n

        for tg in range(4):
            @blk.sync
            def _(sync, c=c, tg=tg):
                for tt in range(4):
                    gt = tg * 4 + tt
                    par = gt % 2
                    if c == 0 and gt == 0 and L == 0:
                        sync.wait_ge(self.s_gq.h, self.GC)
                    if c == 1 and gt == 0:
                        sync.wait_ge(self.s_x1p[1].h, M[("x1", 0, 15)])
                        sync.wait_ge(self.s_x1p[0].h, M[("x1", 0, 14)])
                        sync.wait_ge(self.s_vm.h, M[("x1vd", 0)])
                        if L == 0:
                            sync.wait_ge(self.s_odp[1].h, M[("xnd", 0, 15)])
                            sync.wait_ge(self.s_odp[0].h, M[("xnd", 0, 14)])
                    if gt >= 2:
                        sync.wait_ge(s_vt.h, M[("yT", c, gt - 2)])
                    self.dma(sync, self.s_yp[par],
                             self.yst[:, par * D:(par + 1) * D],
                             self.y_in[c * CHT + gt * 128:c * CHT + (gt + 1) * 128, :])
                    M[("ydma", c, gt)] = self.s_yp[par].n

            @blk.tensor
            def _(t, c=c, tg=tg):
                t.wait_ge(self.s_c.h, self.CONSTS)
                for tt in range(4):
                    gt = tg * 4 + tt
                    par = gt % 2
                    t.wait_ge(self.s_yp[gt % 2].h, M[("ydma", c, gt)])
                    bb = 6 + gt % 2
                    if gt >= 2:
                        t.wait_ge(s_pe.h, M[("yev", c, gt - 2)])
                    else:
                        G.acquire(t, [bb])
                    for kc in range(8):
                        ins = t.transpose(self.bkb(bb, kc * 128, (kc + 1) * 128),
                                          self.yst[:, par * D + kc * 128:par * D + (kc + 1) * 128],
                                          self.ident[:, :])
                    s_vt.inc(ins)
                    assert M[("yT", c, gt)] == s_vt.n

            @blk.scalar
            def _(sc, c=c, tg=tg):
                for tt in range(4):
                    gt = tg * 4 + tt
                    bb = 6 + gt % 2
                    sc.wait_ge(s_vt.h, M[("yT", c, gt)])
                    dst = self.yfm[:, :].rearrange("p (k t) -> p k t", k=8)[:, :, tt * 128:(tt + 1) * 128]
                    src = self.bkb(bb, 0, 1024).rearrange("p (k t) -> p k t", k=8)
                    s_pe.inc(sc.copy(dst, src))
                    G.release([bb], s_pe)
                    assert M[("yev", c, gt)] == s_pe.n
                M[("yfm", c, tg)] = s_pe.n

            @blk.tensor
            def _(t, c=c, tg=tg):
                t.wait_ge(s_pe.h, M[("yfm", c, tg)])
                for ocp in range(4):
                    gidx = tg * 4 + ocp
                    bks = [2 * (gidx % 2), 2 * (gidx % 2) + 1]
                    if gidx >= 2:
                        t.wait_ge(self.s_acc.h, M[("wacc", c, gidx - 2)])
                    else:
                        G.acquire(t, bks)
                    for oc2 in range(2):
                        oc = ocp * 2 + oc2
                        t.wait_ge(s_wc.h, M[("wcv", c, oc)])
                        for kc in range(8):
                            ins = t.matmul(
                                self.bk(bks[oc2]),
                                self.wcbf[:, oc * D + kc * 128:oc * D + (kc + 1) * 128],
                                self.yfm[:, kc * 512:(kc + 1) * 512],
                                start=(kc == 0), stop=(kc == 7))
                        s_pj.inc(ins)
                    M[("wcg", c, gidx)] = s_pj.n
                    M[("wacc", c, gidx)] = self.acc_pred(2)

            @blk.scalar
            def _(sc, c=c, tg=tg):
                for ocp in range(4):
                    gidx = tg * 4 + ocp
                    bks = [2 * (gidx % 2), 2 * (gidx % 2) + 1]
                    sc.wait_ge(s_pj.h, M[("wcg", c, gidx)])
                    if c == 1 and gidx == 0:
                        sc.wait_ge(s_vt.h, M[("tailT", 0)])
                    for oc2 in range(2):
                        oc = ocp * 2 + oc2
                        dv = self.acc[:, oc * CHT + tg * 512:oc * CHT + (tg + 1) * 512]
                        if L == 1:
                            ins = sc.activation(dv, self.bk(bks[oc2]), AF.Identity,
                                                bias=self.bof[:, oc:oc + 1])
                        else:
                            ins = sc.activation(dv, self.bk(bks[oc2]), AF.Copy)
                        self.s_acc.inc(ins)
                    G.release(bks, self.s_acc)
                    assert M[("wacc", c, gidx)] == self.s_acc.n
        M[("wcmm", c)] = self.s_pj.n
        M[("wcdone", c)] = self.s_acc.n

    def acc_pred(self, k):
        if not hasattr(self, "_accp"):
            self._accp = self.s_acc.n
        self._accp += k
        return self._accp

    def gating(self, c):
        L, blk, G, M = self.L, self.blk, self.G, self.M
        s_pj, s_vm = self.s_pj, self.s_vm
        if c == 1:
            @blk.sync
            def _(sync):
                if L == 0:
                    sync.wait_ge(s_pj.h, M[("gumm_last", 9)])
                else:
                    sync.wait_ge(s_pj.h, M[("hmm", 63)])
                for i in range(8):
                    self.dma(sync, self.s_x, self.xn[:, i * CHT:(i + 1) * CHT],
                             self.xn_in[:, i * T + CHT:(i + 1) * T])
            self.XN0 = self.s_x.n
        vm0 = s_vm.n
        ex0p = self.s_ex.n
        perx = 2 if L == 0 else 1
        for tt in range(16):
            M[("lgv", c, tt)] = vm0 + 12 * tt + 4
            M[("lgw", c, tt)] = vm0 + 12 * (tt + 1)
            M[("lge", c, tt)] = ex0p + perx * (tt + 1)

        @blk.tensor
        def _(t, c=c):
            t.wait_ge(self.s_gq.h, self.GC)
            t.wait_ge(self.s_x.h, self.XN0)
            for tt in range(16):
                bb = 4 + tt % 2
                if tt >= 2:
                    t.wait_ge(s_vm.h, M[("lgv", c, tt - 2)])
                    if L == 0:
                        t.wait_ge(self.s_ex.h, M[("lge", c, tt - 2)])
                else:
                    G.acquire(t, [bb])
                for kc in range(8):
                    ins = t.matmul(self.bk(bb)[:, 0:8],
                                   self.xn[:, kc * CHT + tt * 128:
                                           kc * CHT + (tt + 1) * 128],
                                   self.gwbf[:, kc * 8:kc * 8 + 8],
                                   start=(kc == 0), stop=(kc == 7 and L == 0))
                if L == 1:
                    ins = t.matmul(self.bk(bb)[:, 0:8], self.ones1[0:1, :],
                                   self.rbr[0:1, :], start=False, stop=True)
                s_pj.inc(ins)
                if L == 0:
                    for kc in range(8):
                        ins = t.matmul(self.bk(bb)[:, 8:9],
                                       self.xn[:, kc * CHT + tt * 128:
                                               kc * CHT + (tt + 1) * 128],
                                       self.gwbf[:, 64 + kc:65 + kc],
                                       start=(kc == 0), stop=(kc == 7))
                    s_pj.inc(ins)
                M[("lgmm", c, tt)] = s_pj.n

        @blk.vector
        def _(v, c=c):
            for tt in range(16):
                bb = 4 + tt % 2
                o = (tt % 2) * 32
                v.wait_ge(s_pj.h, M[("lgmm", c, tt)])
                lsv = self.ls[:, o:o + 8]
                mm = s_vm.inc(v.tensor_copy(lsv, self.bk(bb)[:, 0:8]))
                v.wait_ge(s_vm.h, mm)
                mm = s_vm.inc(v.reduce_max(self.ls[:, o + 8:o + 9], lsv, axis=AX.X))
                v.wait_ge(s_vm.h, mm)
                mm = s_vm.inc(v.tensor_scalar(self.ls[:, o + 9:o + 10],
                                              self.ls[:, o + 8:o + 9], -1.0, None, ALU.mult))
                v.wait_ge(s_vm.h, mm)
                mm = s_vm.inc(v.tensor_scalar(self.ls[:, o + 10:o + 18], lsv,
                                              self.ls[:, o + 9:o + 10], None, ALU.add))
                assert M[("lgv", c, tt)] == mm
                if c == 1 and tt < 2:
                    v.wait_ge(self.s_vt.h, M[("wrTend", 0)])
                v.wait_ge(self.s_ex.h, M[("lge", c, tt)])
                ev = self.ls[:, o + 10:o + 18]
                t8 = self.ls[:, o + 18:o + 26]
                mm = s_vm.inc(v.tensor_scalar(t8, lsv, self.ls[:, o + 8:o + 9],
                                              -1e30, ALU.is_ge, ALU.mult))
                v.wait_ge(s_vm.h, mm)
                mm = s_vm.inc(v.tensor_tensor(t8, t8, lsv, ALU.add))
                v.wait_ge(s_vm.h, mm)
                mm = s_vm.inc(v.reduce_max(self.ls[:, o + 26:o + 27], t8, axis=AX.X))
                v.wait_ge(s_vm.h, mm)
                mm = s_vm.inc(v.tensor_scalar(t8, lsv, self.ls[:, o + 26:o + 27],
                                              None, ALU.is_ge))
                v.wait_ge(s_vm.h, mm)
                if L == 0:
                    mm = s_vm.inc(v.reduce_sum(self.ls[:, o + 27:o + 28], ev, axis=AX.X))
                    v.wait_ge(s_vm.h, mm)
                    mm = s_vm.inc(v.reciprocal(self.ls[:, o + 28:o + 29],
                                               self.ls[:, o + 27:o + 28]))
                    v.wait_ge(s_vm.h, mm)
                    mm = s_vm.inc(v.tensor_scalar(ev, ev, self.ls[:, o + 28:o + 29],
                                                  None, ALU.mult))
                    v.wait_ge(s_vm.h, mm)
                    mm = s_vm.inc(v.tensor_tensor(self.w_all[:, tt * 8:(tt + 1) * 8],
                                                  ev, t8, ALU.mult))
                else:
                    mm = s_vm.inc(v.tensor_tensor(ev, ev, t8, ALU.mult))
                    v.wait_ge(s_vm.h, mm)
                    mm = s_vm.inc(v.reduce_sum(self.ls[:, o + 27:o + 28], ev, axis=AX.X))
                    v.wait_ge(s_vm.h, mm)
                    mm = s_vm.inc(v.reciprocal(self.ls[:, o + 28:o + 29],
                                               self.ls[:, o + 27:o + 28]))
                    v.wait_ge(s_vm.h, mm)
                    mm = s_vm.inc(v.tensor_scalar(self.w_all[:, tt * 8:(tt + 1) * 8],
                                                  ev, self.ls[:, o + 28:o + 29],
                                                  None, ALU.mult))
                assert M[("lgw", c, tt)] == mm
            M[("gatedone", c)] = s_vm.n

        @blk.scalar
        def _(sc, c=c):
            for tt in range(16):
                o = (tt % 2) * 32
                sc.wait_ge(s_vm.h, M[("lgv", c, tt)])
                self.s_ex.inc(sc.activation(self.ls[:, o + 10:o + 18],
                                            self.ls[:, o + 10:o + 18], AF.Exp))
                if L == 0:
                    bb = 4 + tt % 2
                    self.s_ex.inc(sc.activation(self.w_all[:, 128 + tt:129 + tt],
                                                self.bk(bb)[:, 8:9], AF.Sigmoid))
                assert M[("lge", c, tt)] == self.s_ex.n

    def expert(self, c, e):
        L, blk, G, M = self.L, self.blk, self.G, self.M
        s_pj, s_vt, s_w, s_wc = self.s_pj, self.s_vt, self.s_w, self.s_wc
        s_ex, s_ve, s_vm = self.s_ex, self.s_ve, self.s_vm
        ei = c * (10 if L == 0 else 8) + e

        shared = (L == 0 and e >= 8)
        if not (shared and e == 9):
            @blk.tensor
            def _(t, c=c, e=e):
                t.wait_ge(s_vm.h, M[("gatedone", c)])
                for half in range(2):
                    bb = 4 + half
                    G.acquire(t, [bb])
                    for tt8 in range(8):
                        tt = half * 8 + tt8
                        col = (128 + tt) if shared else (tt * 8 + e)
                        ins = t.transpose(self.bkb(bb, tt8 * 128, (tt8 + 1) * 128, 0, 1),
                                          self.w_all[:, col:col + 1],
                                          self.ident[:, :])
                    s_vt.inc(ins)
                    M[("wrT", ei, half)] = s_vt.n

            @blk.vector
            def _(v, c=c, e=e):
                for half in range(2):
                    bb = 4 + half
                    v.wait_ge(s_vt.h, M[("wrT", ei, half)])
                    s_ve.inc(v.tensor_copy(self.wrow[0:1, half * 1024:(half + 1) * 1024],
                                           self.bkb(bb, 0, 1024, 0, 1)))
                    G.release([bb], s_ve)
                M[("wrow", ei)] = s_ve.n

            @blk.tensor
            def _(t, c=c, e=e):
                t.wait_ge(s_ve.h, M[("wrow", ei)])
                ve0w = s_ve.n
                for tg in range(4):
                    M[("wbcp", ei, tg)] = ve0w + tg + 1
                for tg in range(4):
                    bb = 4 + tg % 2
                    if tg >= 2:
                        t.wait_ge(s_ve.h, M[("wbcp", ei, tg - 2)])
                    else:
                        G.acquire(t, [bb])
                    s_pj.inc(t.matmul(self.bk(bb), self.ones1[0:1, :],
                                      self.wrow[0:1, tg * 512:(tg + 1) * 512],
                                      start=True, stop=True))
                    M[("wbmm", ei, tg)] = s_pj.n

            @blk.vector
            def _(v, c=c, e=e):
                for tg in range(4):
                    bb = 4 + tg % 2
                    v.wait_ge(s_pj.h, M[("wbmm", ei, tg)])
                    s_ve.inc(v.tensor_copy(self.wb_sb[:, tg * 512:(tg + 1) * 512],
                                           self.bk(bb)))
                    G.release([bb], s_ve)
                    assert M[("wbcp", ei, tg)] == s_ve.n
                M[("wb", ei)] = s_ve.n
        else:
            M[("wb", ei)] = M[("wb", ei - 1)]

        M[("wrTend", c)] = s_vt.n

        if L == 0:
            self.expert_l0(c, e, ei, shared)
        else:
            self.expert_l1(c, e, ei)

    def expert_l0(self, c, e, ei, shared):
        blk, G, M = self.blk, self.G, self.M
        s_pj, s_w, s_wc = self.s_pj, self.s_w, self.s_wc
        s_ex, s_ve = self.s_ex, self.s_ve
        sh = e - 8 if shared else 0
        wcb = s_wc.n
        for j in range(11):
            M[("gucv", ei, j)] = wcb + j + 1
        M[("gucv_last", ei)] = wcb + 11
        for ob in range(4):
            M[("dncv", ei, ob)] = wcb + 11 + ob + 1
        M[("dncv_last", ei)] = wcb + 15
        pjb = s_pj.n
        for git in range(22):
            M[("gumm2", ei, git)] = pjb + git + 1
        for j in range(11):
            M[("gumm", ei, j)] = pjb + 2 * (j + 1)
        M[("gumm_last", ei)] = pjb + 22
        pjb2 = pjb + 22
        for dit in range(32):
            M[("dnmm2", ei, dit)] = pjb2 + dit + 1
        for ob in range(4):
            M[("dnmm", ei, ob)] = pjb2 + 8 * (ob + 1)
        M[("dnmm_last", ei)] = pjb2 + 32

        @blk.sync
        def _(sync, c=c, e=e):
            for j in range(11):
                par = j % 2
                if j == 0 and e == 0:
                    sync.wait_ge(s_wc.h, M[("wcv", c, 7)])
                if j >= 2:
                    sync.wait_ge(s_wc.h, M[("gucv", ei, j - 2)])
                elif ei > 0:
                    sync.wait_ge(s_wc.h, M[("gucv", ei - 1, 9 + j)])
                for kc in range(8):
                    if shared:
                        gsrc = self.sg_in[kc * 128:(kc + 1) * 128,
                                          sh * 1408 + j * 128:sh * 1408 + (j + 1) * 128]
                        usrc = self.su_in[kc * 128:(kc + 1) * 128,
                                          sh * 1408 + j * 128:sh * 1408 + (j + 1) * 128]
                    else:
                        gsrc = self.egu_in[e, kc * 128:(kc + 1) * 128, j * 128:(j + 1) * 128]
                        usrc = self.egu_in[e, kc * 128:(kc + 1) * 128,
                                           1408 + j * 128:1408 + (j + 1) * 128]
                    self.dma(sync, self.s_gw[par],
                             self.gust[:, par * 2048 + kc * 256:par * 2048 + kc * 256 + 128],
                             gsrc)
                    self.dma(sync, self.s_gw[par],
                             self.gust[:, par * 2048 + kc * 256 + 128:par * 2048 + (kc + 1) * 256],
                             usrc)
                M[("gud", ei, j)] = self.s_gw[par].n

        @blk.gpsimd
        def _(gp, c=c, e=e):
            for j in range(11):
                par = j % 2
                gp.wait_ge(self.s_gw[par].h, M[("gud", ei, j)])
                if j >= 2:
                    gp.wait_ge(s_pj.h, M[("gumm", ei, j - 2)])
                elif ei > 0:
                    gp.wait_ge(s_pj.h, M[("gumm_last", ei - 1)])
                s_wc.inc(gp.tensor_copy(self.gubf[:, par * 2048:(par + 1) * 2048],
                                        self.gust[:, par * 2048:(par + 1) * 2048]))
                assert M[("gucv", ei, j)] == s_wc.n

        @blk.tensor
        def _(t, c=c, e=e):
            ve0 = s_ve.n
            for git in range(22):
                M[("hfold", ei, git)] = ve0 + git + 1
            git = 0
            for j in range(11):
                par = j % 2
                t.wait_ge(s_wc.h, M[("gucv", ei, j)])
                if j == 0:
                    t.wait_ge(s_ve.h, M[("wb", ei)])
                for tg2 in range(2):
                    q = git % 2
                    bks = [4 * q, 4 * q + 1, 4 * q + 2, 4 * q + 3]
                    if git >= 2:
                        t.wait_ge(s_ve.h, M[("hfold", ei, git - 2)])
                    else:
                        G.acquire(t, bks)
                    for kc in range(8):
                        for t2 in range(2):
                            ins = t.matmul(
                                self.bk(bks[t2]),
                                self.gubf[:, par * 2048 + kc * 256:par * 2048 + kc * 256 + 128],
                                self.xn[:, kc * CHT + tg2 * 1024 + t2 * 512:
                                        kc * CHT + tg2 * 1024 + (t2 + 1) * 512],
                                start=(kc == 0), stop=(kc == 7))
                        for t2 in range(2):
                            ins = t.matmul(
                                self.bk(bks[2 + t2]),
                                self.gubf[:, par * 2048 + kc * 256 + 128:par * 2048 + (kc + 1) * 256],
                                self.xn[:, kc * CHT + tg2 * 1024 + t2 * 512:
                                        kc * CHT + tg2 * 1024 + (t2 + 1) * 512],
                                start=(kc == 0), stop=(kc == 7))
                    s_pj.inc(ins)
                    assert M[("gumm2", ei, git)] == s_pj.n
                    git += 1

        @blk.scalar
        def _(sc, c=c, e=e):
            git = 0
            for j in range(11):
                for tg2 in range(2):
                    q = git % 2
                    sc.wait_ge(s_pj.h, M[("gumm2", ei, git)])
                    if ei > 0 and git == 0:
                        sc.wait_ge(s_pj.h, M[("dnmm_last", ei - 1)])
                    if git >= 2:
                        sc.wait_ge(s_ve.h, M[("hfold", ei, git - 2)])
                    s_ex.inc(sc.activation(self.sgev[:, q * 1024:(q + 1) * 1024],
                                           self.ps[:, 4 * q * 512:(4 * q + 2) * 512],
                                           AF.Silu))
                    M[("silu", ei, git)] = s_ex.n
                    git += 1

        @blk.vector
        def _(v, c=c, e=e):
            git = 0
            for j in range(11):
                for tg2 in range(2):
                    q = git % 2
                    v.wait_ge(s_ex.h, M[("silu", ei, git)])
                    if git >= 1:
                        v.wait_ge(s_ve.h, M[("hfold", ei, git - 1)])
                    mm = self.s_vm.inc(v.tensor_mul(
                        self.hmid[:, :], self.sgev[:, q * 1024:(q + 1) * 1024],
                        self.ps[:, (4 * q + 2) * 512:(4 * q + 4) * 512]))
                    v.wait_ge(self.s_vm.h, mm)
                    hv = self.h_sb[:, :].rearrange("p (j t) -> p j t", j=11)[:, j, tg2 * 1024:(tg2 + 1) * 1024]
                    ins = v.tensor_mul(hv, self.hmid[:, :],
                                       self.wb_sb[:, tg2 * 1024:(tg2 + 1) * 1024])
                    s_ve.inc(ins)
                    assert M[("hfold", ei, git)] == s_ve.n
                    git += 1
            M[("hdone", ei)] = s_ve.n

        @blk.sync
        def _(sync, c=c, e=e):
            for ob in range(4):
                if (ei, ob) != (0, 0):
                    sync.wait_ge(s_wc.h, M[("dncv", ei, ob - 1) if ob > 0 else ("dncv_last", ei - 1)])
                for ic in range(11):
                    if shared:
                        src = self.sd_in[sh * 1408 + ic * 128:sh * 1408 + (ic + 1) * 128,
                                         ob * 256:(ob + 1) * 256]
                    else:
                        src = self.edn_in[e, ic * 128:(ic + 1) * 128, ob * 256:(ob + 1) * 256]
                    self.dma(sync, self.s_wd, self.dnst[:, ic * 256:(ic + 1) * 256], src)
                M[("dnd", ei, ob)] = self.s_wd.n

        @blk.gpsimd
        def _(gp, c=c, e=e):
            for ob in range(4):
                par = ob % 2
                gp.wait_ge(self.s_wd.h, M[("dnd", ei, ob)])
                if ob >= 2:
                    gp.wait_ge(s_pj.h, M[("dnmm", ei, ob - 2)])
                elif ei > 0:
                    gp.wait_ge(s_pj.h, M[("dnmm_last", ei - 1)])
                s_wc.inc(gp.tensor_copy(self.dnbf[:, par * 2816:(par + 1) * 2816],
                                        self.dnst[:, :]))
                assert M[("dncv", ei, ob)] == s_wc.n

        @blk.tensor
        def _(t, c=c, e=e):
            t.wait_ge(s_ve.h, M[("hdone", ei)])
            ve0d = s_ve.n
            for dit in range(32):
                M[("dacc", ei, dit)] = ve0d + dit + 1
            dit = 0
            for ob in range(4):
                par = ob % 2
                t.wait_ge(s_wc.h, M[("dncv", ei, ob)])
                for oc2 in range(2):
                    for tg in range(4):
                        bb = dit % 4
                        if dit >= 4:
                            t.wait_ge(s_ve.h, M[("dacc", ei, dit - 4)])
                        else:
                            G.acquire(t, [bb])
                        for ic in range(11):
                            ins = t.matmul(
                                self.bk(bb),
                                self.dnbf[:, par * 2816 + ic * 256 + oc2 * 128:
                                          par * 2816 + ic * 256 + (oc2 + 1) * 128],
                                self.h_sb[:, :].rearrange("p (j t) -> p j t", j=11)[:, ic, tg * 512:(tg + 1) * 512],
                                start=(ic == 0), stop=(ic == 10))
                        s_pj.inc(ins)
                        assert M[("dnmm2", ei, dit)] == s_pj.n
                        dit += 1

        @blk.vector
        def _(v, c=c, e=e):
            v.wait_ge(self.s_acc.h, M[("wcdone", c)])
            dit = 0
            for ob in range(4):
                for oc2 in range(2):
                    oc = ob * 2 + oc2
                    for tg in range(4):
                        bb = dit % 4
                        v.wait_ge(s_pj.h, M[("dnmm2", ei, dit)])
                        av = self.acc[:, oc * CHT + tg * 512:oc * CHT + (tg + 1) * 512]
                        ins = v.tensor_add(av, av, self.bk(bb))
                        self.s_ve.inc(ins)
                        assert M[("dacc", ei, dit)] == self.s_ve.n
                        dit += 1
            M[("eadd", ei)] = self.s_ve.n
        G.release([0, 1, 2, 3], self.s_ve)


    def expert_l1(self, c, e, ei):
        blk, G, M = self.blk, self.G, self.M
        s_pj, s_w, s_wc = self.s_pj, self.s_w, self.s_wc
        s_ex, s_ve = self.s_ex, self.s_ve

        @blk.sync
        def _(sync, c=c, e=e):
            if ei > 0:
                sync.wait_ge(self.s_ex.h, M[("hrelu_last", ei - 1)])
            self.dma(sync, self.s_wd, self.b1f[:, :], self.b1_in[e])
            self.dma(sync, self.s_wd, self.b2st[0:1, :], self.b2_in[e:e + 1, :])
            M[("bd", ei)] = self.s_wd.n

        @blk.gpsimd
        def _(gp, c=c, e=e):
            gp.wait_ge(self.s_wd.h, M[("bd", ei)])
            if ei > 0:
                gp.wait_ge(s_pj.h, M[("w2mm_last", ei - 1)])
            s_wc.inc(gp.tensor_copy(self.b2r[:, :], self.b2st[:, :]))
            M[("b2c", ei)] = s_wc.n

        for fcg in range(8):
            fi = ei * 8 + fcg

            @blk.sync
            def _(sync, c=c, e=e, fcg=fcg, fi=fi):
                if fi % 64 == 0:
                    sync.wait_ge(s_wc.h, M[("wcv", c, 7)])
                if fi > 0:
                    sync.wait_ge(s_wc.h, M[("w1cv", fi - 1)])
                for kc in range(8):
                    self.dma(sync, self.s_gw[0], self.gust[:, kc * 512:(kc + 1) * 512],
                             self.w1_in[e, kc * 128:(kc + 1) * 128,
                                        fcg * 512:(fcg + 1) * 512])
                M[("w1d", fi)] = self.s_gw[0].n

            @blk.gpsimd
            def _(gp, c=c, e=e, fcg=fcg, fi=fi):
                gp.wait_ge(self.s_gw[0].h, M[("w1d", fi)])
                if fi >= 2:
                    gp.wait_ge(s_pj.h, M[("hmm", fi - 2)])
                s_wc.inc(gp.tensor_copy(self.gubf[:, (fi % 2) * 4096:(fi % 2 + 1) * 4096],
                                        self.gust[:, 0:4096]))
                M[("w1cv", fi)] = s_wc.n

            @blk.tensor
            def _(t, c=c, e=e, fcg=fcg, fi=fi):
                ex0 = s_ex.n
                ve0 = s_ve.n
                for hit in range(16):
                    M[("hrelu", fi, hit)] = ex0 + hit + 1
                    M[("hcf", fi, hit)] = ve0 + hit + 1
                t.wait_ge(s_wc.h, M[("w1cv", fi)])
                if fcg == 0:
                    t.wait_ge(s_ve.h, M[("wb", ei)])
                hit = 0
                for fc in range(4):
                    for tg in range(4):
                        bb = hit % 4
                        if hit >= 4:
                            t.wait_ge(s_ve.h, M[("hcf", fi, hit - 4)])
                        else:
                            G.acquire(t, [bb])
                        for kc in range(8):
                            ins = t.matmul(
                                self.bk(bb),
                                self.gubf[:, (fi % 2) * 4096 + kc * 512 + fc * 128:
                                          (fi % 2) * 4096 + kc * 512 + (fc + 1) * 128],
                                self.xn[:, kc * CHT + tg * 512:
                                        kc * CHT + (tg + 1) * 512],
                                start=(kc == 0), stop=(kc == 7))
                        s_pj.inc(ins)
                        M[("hmm2", fi, hit)] = s_pj.n
                        hit += 1
                M[("hmm", fi)] = s_pj.n

            @blk.scalar
            def _(sc, c=c, e=e, fcg=fcg, fi=fi):
                hit = 0
                for fc in range(4):
                    for tg in range(4):
                        bb = hit % 4
                        sc.wait_ge(s_pj.h, M[("hmm2", fi, hit)])
                        if fcg == 0 and hit == 0 and ei > 0:
                            sc.wait_ge(s_pj.h, M[("w2mm_last", ei - 1)])
                        if fcg > 0 and hit == 0:
                            sc.wait_ge(s_pj.h, M[("w2mm", ei, fcg - 1)])
                        hv = self.h_sb[:, :].rearrange("p (f t) -> p f t", f=4)[:, fc, tg * 512:(tg + 1) * 512]
                        s_ex.inc(sc.activation(hv, self.bk(bb), AF.Relu,
                                               bias=self.b1f[:, fcg * 4 + fc:fcg * 4 + fc + 1]))
                        assert M[("hrelu", fi, hit)] == s_ex.n
                        hit += 1
                M[("hrelu_last", ei)] = s_ex.n

            @blk.vector
            def _(v, c=c, e=e, fcg=fcg, fi=fi):
                hit = 0
                for fc in range(4):
                    for tg in range(4):
                        v.wait_ge(s_ex.h, M[("hrelu", fi, hit)])
                        hv = self.h_sb[:, :].rearrange("p (f t) -> p f t", f=4)[:, fc, tg * 512:(tg + 1) * 512]
                        s_ve.inc(v.tensor_mul(hv, hv, self.wb_sb[:, tg * 512:(tg + 1) * 512]))
                        assert M[("hcf", fi, hit)] == s_ve.n
                        hit += 1
                M[("hcfd", fi)] = s_ve.n

            @blk.sync
            def _(sync, c=c, e=e, fcg=fcg, fi=fi):
                if fi > 0:
                    sync.wait_ge(s_wc.h, M[("w2cv", fi - 1)])
                for fkc in range(4):
                    self.dma(sync, self.s_gw[1],
                             self.gust[:, 4096 + fkc * 1024:4096 + (fkc + 1) * 1024],
                             self.w2_in[e, fcg * 512 + fkc * 128:fcg * 512 + (fkc + 1) * 128, :])
                M[("w2d", fi)] = self.s_gw[1].n

            @blk.gpsimd
            def _(gp, c=c, e=e, fcg=fcg, fi=fi):
                gp.wait_ge(self.s_gw[1].h, M[("w2d", fi)])
                if fi % 64 <= 1:
                    gp.wait_ge(s_pj.h, M[("wcmm", c)])
                if fi >= 2:
                    gp.wait_ge(s_pj.h, M[("w2mmf", fi - 2)])
                s_wc.inc(gp.tensor_copy(self.dnbf[:, (fi % 2) * 4096:(fi % 2 + 1) * 4096],
                                        self.gust[:, 4096:8192]))
                M[("w2cv", fi)] = s_wc.n

            @blk.tensor
            def _(t, c=c, e=e, fcg=fcg, fi=fi):
                ve0d = s_ve.n
                for dit in range(32):
                    M[("w2acc", fi, dit)] = ve0d + dit + 1
                t.wait_ge(s_wc.h, M[("w2cv", fi)])
                t.wait_ge(s_ve.h, M[("hcfd", fi)])
                if ei > 0 and fcg == 0:
                    t.wait_ge(s_wc.h, M[("b2c", ei)])
                dit = 0
                for oc in range(8):
                    for tg in range(4):
                        bb = dit % 4
                        if dit >= 4:
                            t.wait_ge(s_ve.h, M[("w2acc", fi, dit - 4)])
                        else:
                            G.acquire(t, [bb])
                        for fkc in range(4):
                            ins = t.matmul(
                                self.bk(bb),
                                self.dnbf[:, (fi % 2) * 4096 + fkc * 1024 + oc * 128:
                                          (fi % 2) * 4096 + fkc * 1024 + (oc + 1) * 128],
                                self.h_sb[:, :].rearrange("p (f t) -> p f t", f=4)[:, fkc, tg * 512:(tg + 1) * 512],
                                start=(fkc == 0), stop=(fkc == 3 and fcg > 0))
                        if fcg == 0:
                            ins = t.matmul(self.bk(bb), self.b2r[0:1, oc * 128:(oc + 1) * 128],
                                           self.ones512[0:1, :], start=False, stop=True)
                        s_pj.inc(ins)
                        M[("w2mm2", fi, dit)] = s_pj.n
                        dit += 1
                M[("w2mmf", fi)] = s_pj.n
                M[("w2mm", ei, fcg)] = s_pj.n
                if fcg == 7:
                    M[("w2mm_last", ei)] = s_pj.n

            @blk.vector
            def _(v, c=c, e=e, fcg=fcg, fi=fi):
                v.wait_ge(self.s_acc.h, M[("wcdone", c)])
                dit = 0
                for oc in range(8):
                    for tg in range(4):
                        bb = dit % 4
                        v.wait_ge(s_pj.h, M[("w2mm2", fi, dit)])
                        av = self.acc[:, oc * CHT + tg * 512:oc * CHT + (tg + 1) * 512]
                        s_ve.inc(v.tensor_add(av, av, self.bk(bb)))
                        assert M[("w2acc", fi, dit)] == s_ve.n
                        dit += 1
            G.release([0, 1, 2, 3], s_ve)
        M[("eadd", ei)] = s_ve.n

    def tail(self, c):
        L, blk, G, M = self.L, self.blk, self.G, self.M
        s_pj, s_vt, s_vm, s_sm = self.s_pj, self.s_vt, self.s_vm, self.s_sm
        s_xr, s_x1, s_ln, s_od = self.s_xr, self.s_x1, self.s_ln, self.s_od
        nexp = 10 if L == 0 else 8
        ADD_DONE = M[("eadd", c * nexp + nexp - 1)]

        x1_0p = [self.s_x1p[0].n, self.s_x1p[1].n]
        gq0 = self.s_gq.n
        sm0 = self.s_sm.n
        vt0 = s_vt.n
        ln0 = self.s_ln.n
        for tt in range(16):
            M[("x1", c, tt)] = x1_0p[tt % 2] + 16 * (tt // 2 + 1)
            M[("sq", c, tt)] = gq0 + tt + 1
            M[("lnsq", c, tt)] = sm0 + 2 * tt + 1
            M[("lnnorm", c, tt)] = sm0 + 2 * tt + 2
            if L == 0:
                M[("accT", c, tt)] = vt0 + (1 if tt == 0 else 2 * tt)
                M[("xnT", c, tt)] = vt0 + (2 * tt + 3 if tt <= 13 else (31 if tt == 14 else 32))
            else:
                M[("accT", c, tt)] = vt0 + tt + 1
            M[("xne", c, tt)] = ln0 + tt + 1
        vm0 = s_vm.n
        if L == 0:
            NVT = 9
            for tt in range(16):
                M[("x1v", c, tt)] = vm0 + NVT * tt + 1
                M[("lnr1", c, tt)] = vm0 + NVT * tt + 2
                M[("lnv", c, tt)] = vm0 + NVT * (tt + 1)
        else:
            for tt in range(16):
                M[("x1v", c, tt)] = vm0 + tt + 1
        xrm = []

        @blk.sync
        def _(sync, c=c):
            for tt in range(16):
                par = tt % 2
                if tt == 0:
                    key = ("dnmm_last", c * 10 + 9) if L == 0 else ("w2mm_last", c * 8 + 7)
                    sync.wait_ge(s_pj.h, M[key])
                if tt >= 2:
                    if L == 0:
                        sync.wait_ge(s_vm.h, M[("lnr1", c, tt - 2)])
                    else:
                        sync.wait_ge(s_vm.h, M[("x1v", c, tt - 2)])
                self.dma(sync, self.s_xp[par], self.xrst[:, par * D:(par + 1) * D],
                         self.res_in[c * CHT + tt * 128:c * CHT + (tt + 1) * 128, :])
                xrm.append(self.s_xp[par].n)

        def emit_xnT(t, c, tt):
            par = tt % 2
            bb = tt % 2
            t.wait_ge(self.s_sm.h, M[("lnnorm", c, tt)])
            if tt < 2:
                G.acquire(t, [bb])
            else:
                t.wait_ge(self.s_ln.h, M[("xne", c, tt - 2)])
            for cch in range(8):
                ins = t.transpose(self.bkb(bb, cch * 128, (cch + 1) * 128),
                                  self.xnt[:, par * D + cch * 128:par * D + (cch + 1) * 128],
                                  self.ident[:, :])
            s_vt.inc(ins)
            assert M[("xnT", c, tt)] == s_vt.n

        @blk.tensor
        def _(t, c=c):
            for tt in range(16):
                if tt == 0:
                    t.wait_ge(self.s_ve.h, ADD_DONE)
                bp = 4 + 2 * (tt % 2)
                if tt >= 2:
                    t.wait_ge(s_vm.h, M[("x1v", c, tt - 2)])
                else:
                    G.acquire(t, [bp, bp + 1])
                for oc in range(8):
                    ins = t.transpose(
                        self.ps[:, bp * 512 + oc * 128:bp * 512 + (oc + 1) * 128],
                        self.acc[:, oc * CHT + tt * 128:oc * CHT + (tt + 1) * 128],
                        self.identf[:, :])
                s_vt.inc(ins)
                assert M[("accT", c, tt)] == s_vt.n
                if L == 0 and tt >= 1:
                    emit_xnT(t, c, tt - 1)
            if L == 0:
                emit_xnT(t, c, 15)

        @blk.vector
        def _(v, c=c):
            for tt in range(16):
                par = tt % 2
                bp = 4 + 2 * (tt % 2)
                v.wait_ge(s_vt.h, M[("accT", c, tt)])
                v.wait_ge(self.s_xp[par].h, xrm[tt])
                if tt >= 2:
                    v.wait_ge(self.s_x1p[tt % 2].h, M[("x1", c, tt - 2)])
                    if L == 0:
                        v.wait_ge(self.s_gq.h, M[("sq", c, tt - 2)])
                        v.wait_ge(self.s_sm.h, M[("lnnorm", c, tt - 2)])
                mm = s_vm.inc(v.tensor_add(
                    self.x1t[:, par * D:(par + 1) * D],
                    self.xrst[:, par * D:(par + 1) * D],
                    self.ps[:, bp * 512:(bp + 2) * 512]))
                assert M[("x1v", c, tt)] == mm
                if L == 0:
                    self.tail_ln_vec(v, c, tt)
            M[("x1vd", c)] = s_vm.n
        G.release([4, 5, 6, 7], s_vm)
        M[("tailT", c)] = self.s_vt.n

        if L == 0:
            self.tail_ln(c)
        else:
            @blk.scalar
            def _(sc, c=c):
                for tt in range(16):
                    par = tt % 2
                    sc.wait_ge(s_vm.h, M[("x1v", c, tt)])
                    self.dma(sc, self.s_x1p[par],
                             self.x1_out[c * CHT + tt * 128:c * CHT + (tt + 1) * 128, :],
                             self.x1t[:, par * D:(par + 1) * D])
                    assert M[("x1", c, tt)] == self.s_x1p[par].n

    def tail_ln_vec(self, v, c, tt):
        M, lst = self.M, self.lst
        s_vm = self.s_vm
        par = tt % 2
        o = par * 12
        xv = self.x1t[:, par * D:(par + 1) * D]
        qv = self.xrst[:, par * D:(par + 1) * D]
        v.wait_ge(self.s_gq.h, M[("sq", c, tt)])
        v.reduce_sum(lst[:, o:o + 1], xv, axis=AX.X)
        mm = s_vm.inc(v.reduce_sum(lst[:, o + 1:o + 2], qv, axis=AX.X))
        assert M[("lnr1", c, tt)] == mm
        v.wait_ge(s_vm.h, mm)
        v.wait_ge(s_vm.h, s_vm.inc(v.tensor_scalar(
            lst[:, o + 2:o + 3], lst[:, o:o + 1], 1.0 / D, None, ALU.mult)))
        v.tensor_tensor(lst[:, o + 3:o + 4], lst[:, o + 2:o + 3],
                        lst[:, o + 2:o + 3], ALU.mult)
        v.wait_ge(s_vm.h, s_vm.inc(v.tensor_scalar(
            lst[:, o + 4:o + 5], lst[:, o + 1:o + 2], 1.0 / D, None, ALU.mult)))
        v.wait_ge(s_vm.h, s_vm.inc(v.tensor_tensor(
            lst[:, o + 5:o + 6], lst[:, o + 4:o + 5], lst[:, o + 3:o + 4], ALU.subtract)))
        v.wait_ge(s_vm.h, s_vm.inc(v.tensor_scalar(
            lst[:, o + 6:o + 7], lst[:, o + 5:o + 6], 1.0, 1e-5, ALU.mult, ALU.add)))
        s_vm.inc(v.reciprocal(lst[:, o + 7:o + 8], lst[:, o + 6:o + 7]))
        v.wait_ge(self.s_sm.h, M[("lnsq", c, tt)])
        v.wait_ge(s_vm.h, s_vm.inc(v.tensor_tensor(
            lst[:, o + 9:o + 10], lst[:, o + 2:o + 3], lst[:, o + 8:o + 9], ALU.mult)))
        mm = s_vm.inc(v.tensor_scalar(
            lst[:, o + 10:o + 11], lst[:, o + 9:o + 10], -1.0, None, ALU.mult))
        assert M[("lnv", c, tt)] == mm

    def tail_ln(self, c):
        blk, G, M = self.blk, self.G, self.M
        s_vm, s_sm, s_ln = self.s_vm, self.s_sm, self.s_ln
        lst = self.lst

        @blk.gpsimd
        def _(gp, c=c):
            for tt in range(16):
                par = tt % 2
                gp.wait_ge(s_vm.h, M[("x1v", c, tt)])
                if tt >= 2:
                    gp.wait_ge(s_vm.h, M[("lnr1", c, tt - 2)])
                self.s_gq.inc(gp.tensor_tensor(self.xrst[:, par * D:(par + 1) * D],
                                               self.x1t[:, par * D:(par + 1) * D],
                                               self.x1t[:, par * D:(par + 1) * D], ALU.mult))
                assert M[("sq", c, tt)] == self.s_gq.n

        def emit_xne(sc, c, tt):
            par = tt % 2
            bb = tt % 2
            sc.wait_ge(self.s_vt.h, M[("xnT", c, tt)])
            if tt >= 2:
                sc.wait_ge(self.s_odp[tt % 2].h, M[("xnd", c, tt - 2)])
            self.s_ln.inc(sc.copy(self.xne[:, par * 1024:(par + 1) * 1024],
                                  self.bkb(bb, 0, 1024)))
            assert M[("xne", c, tt)] == self.s_ln.n
            sc.wait_ge(self.s_ln.h, self.s_ln.n)
            gt = c * 16 + tt
            dst = self.xn1_out[:, :].rearrange("p (k t) -> p k t", k=8)[:, :, gt * 128:(gt + 1) * 128]
            srcv = self.xne[:, par * 1024:(par + 1) * 1024].rearrange("p (k t) -> p k t", k=8)
            self.dma(sc, self.s_odp[tt % 2], dst, srcv)
            M[("xnd", c, tt)] = self.s_odp[tt % 2].n

        @blk.scalar
        def _(sc, c=c):
            s_x1 = self.s_x1
            for tt in range(16):
                par = tt % 2
                o = par * 12
                sc.wait_ge(s_vm.h, M[("x1v", c, tt)])
                self.dma(sc, self.s_x1p[par],
                         self.x1_out[c * CHT + tt * 128:c * CHT + (tt + 1) * 128, :],
                         self.x1t[:, par * D:(par + 1) * D])
                assert M[("x1", c, tt)] == self.s_x1p[par].n
                sc.wait_ge(s_vm.h, M[("lnr1", c, tt)] + 5)
                s1 = self.s_sm.inc(sc.activation(lst[:, o + 8:o + 9],
                                                 lst[:, o + 7:o + 8], AF.Sqrt))
                assert M[("lnsq", c, tt)] == s1
                sc.wait_ge(self.s_sm.h, s1)
                sc.wait_ge(s_vm.h, M[("lnv", c, tt)])
                if tt >= 2:
                    sc.wait_ge(self.s_vt.h, M[("xnT", c, tt - 2)])
                s2 = self.s_sm.inc(sc.activation(self.xnt[:, par * D:(par + 1) * D],
                                                 self.x1t[:, par * D:(par + 1) * D], AF.Identity,
                                                 bias=lst[:, o + 10:o + 11],
                                                 scale=lst[:, o + 8:o + 9]))
                assert M[("lnnorm", c, tt)] == s2
                if tt >= 1:
                    emit_xne(sc, c, tt - 1)
            emit_xne(sc, c, 15)
        G.release([0, 1], self.s_ln)

    def epilogue(self):
        blk = self.blk
        t0p = self.s_x1p[0].n
        t1p = self.s_x1p[1].n
        o0 = self.s_odp[0].n
        o1 = self.s_odp[1].n

        @blk.sync
        def _(sync):
            sync.wait_ge(self.s_x1p[0].h, t0p)
            sync.wait_ge(self.s_x1p[1].h, t1p)
            if o0:
                sync.wait_ge(self.s_odp[0].h, o0)
            if o1:
                sync.wait_ge(self.s_odp[1].h, o1)


def build_mix(L):
    b = MixBuilder(L)
    return b.nc



ENGINES = ("sync", "scalar", "vector", "tensor", "gpsimd")


def _stage_gate(blk, sem, cnt):
    for e in ENGINES:
        @getattr(blk, e)
        def _(eng):
            eng.wait_ge(sem.h, cnt)


def _stage_done(blk, sem):
    for e in ENGINES:
        @getattr(blk, e)
        def _(eng):
            eng.sem_inc(sem.h, 1)
    sem.n += 5


def build_fused():
    root = LaunchBase("fused")
    nc = root.nc
    cst = {}
    for nm, shp, dt in (("c_ident", [128, 128], BF16), ("c_identf", [128, 128], F32),
                        ("c_negones", [1, 128], BF16), ("c_ones1", [1, 128], BF16),
                        ("c_ones512", [1, 512], BF16), ("c_ones64", [128, 2], BF16),
                        ("c_csn", [64, 2 * T], BF16), ("c_mask", [128, 2048], BF16)):
        cst[nm] = nc.dram_tensor(nm, shp, dt, kind="ExternalInput").ap()
    xn0 = nc.dram_tensor("i_xn0", [128, 8 * T], BF16, kind="Internal").ap()
    y0 = nc.dram_tensor("i_y0", [T, D], BF16, kind="Internal").ap()
    x1 = nc.dram_tensor("i_x1", [T, D], F32, kind="Internal").ap()
    xn1 = nc.dram_tensor("i_xn1", [128, 8 * T], BF16, kind="Internal").ap()
    y1 = nc.dram_tensor("i_y1", [T, D], BF16, kind="Internal").ap()
    root.blk = root.ctx.enter_context(nc.Block())
    fin = root.sem("s_fin")

    a0 = AttnBuilder(0, parent=root, io={
        "c_ident": cst["c_ident"], "c_negones": cst["c_negones"],
        "c_ones64": cst["c_ones64"], "c_csn": cst["c_csn"],
        "c_mask": cst["c_mask"], "xn_fm_o": xn0, "y_tm": y0})
    x_in = a0.x_in
    _stage_done(root.blk, fin)

    _stage_gate(root.blk, fin, fin.n)
    MixBuilder(0, parent=root, io={
        "c_ident": cst["c_ident"], "c_identf": cst["c_identf"],
        "c_ones1": cst["c_ones1"], "c_ones512": cst["c_ones512"],
        "xn_fm": xn0, "y_tm": y0, "res": x_in, "x1": x1, "xn1_fm": xn1})
    _stage_done(root.blk, fin)

    _stage_gate(root.blk, fin, fin.n)
    AttnBuilder(1, parent=root, io={
        "c_ident": cst["c_ident"], "c_negones": cst["c_negones"],
        "c_ones64": cst["c_ones64"], "xn_fm": xn1, "y_tm": y1})
    _stage_done(root.blk, fin)

    _stage_gate(root.blk, fin, fin.n)
    MixBuilder(1, parent=root, io={
        "c_ident": cst["c_ident"], "c_identf": cst["c_identf"],
        "c_ones1": cst["c_ones1"], "c_ones512": cst["c_ones512"],
        "xn_fm": xn1, "y_tm": y1, "res": x1})
    root.finish()
    return root.nc



_NC_CACHE = {}


def _get_nc(kind, L):
    key = (kind, L)
    if key not in _NC_CACHE:
        if kind == "fused":
            _NC_CACHE[key] = build_fused()
        else:
            _NC_CACHE[key] = build_attn(L) if kind == "attn" else build_mix(L)
    return _NC_CACHE[key]


def kernel(**inputs):
    if _os.environ.get("KERNEL_UNFUSED", "0") == "1":
        return _kernel_unfused(**inputs)
    consts = _host_consts()
    prep = _host_prep(inputs)
    x = np.ascontiguousarray(np.asarray(inputs["L0_attn_x"], np.float32))
    cores = list(range(NCORES))
    nc = _get_nc("fused", 0)
    maps = []
    for ci in cores:
        m = dict(consts)
        m.update({"x": x[ci],
                  "wdq": prep["l0_wdq"], "wdk": prep["l0_wdk"],
                  "wdv": prep["l0_wdv"], "up": prep["l0_up"],
                  "wc": prep["l0_wc"], "gate_wT": prep["l0_gate_wT"],
                  "eguT": prep["l0_eguT"], "edownT": prep["l0_edownT"],
                  "sgT": prep["l0_sgT"], "suT": prep["l0_suT"],
                  "sdT": prep["l0_sdT"], "segT": prep["l0_segT"],
                  "wtq": prep["l1_wtq"], "wtk": prep["l1_wtk"],
                  "wtv": prep["l1_wtv"], "bq": prep["l1_bq"],
                  "bk": prep["l1_bk"], "bv": prep["l1_bv"],
                  "wto": prep["l1_wto"], "bo": prep["l1_bo"],
                  "router_w": prep["l1_router_w"], "router_b": prep["l1_router_b"],
                  "w1": prep["l1_w1"], "b1": prep["l1_b1"],
                  "w2": prep["l1_w2"], "b2": prep["l1_b2"]})
        maps.append(m)
    r = run_bass_kernel_spmd(nc, maps, core_ids=cores)
    out = np.stack([rr["out"] for rr in r.results], axis=0)
    return out.astype(np.float32)


def _kernel_unfused(**inputs):
    consts = _host_consts()
    prep = _host_prep(inputs)
    x = np.ascontiguousarray(np.asarray(inputs["L0_attn_x"], np.float32))

    cores = list(range(NCORES))

    nc = _get_nc("attn", 0)
    maps = []
    for ci in cores:
        m = {k: consts[k] for k in ("c_ident", "c_negones", "c_ones64", "c_csn", "c_mask")}
        m.update({"x": x[ci], "wdq": prep["l0_wdq"], "wdk": prep["l0_wdk"],
                  "wdv": prep["l0_wdv"], "up": prep["l0_up"]})
        maps.append(m)
    r1 = run_bass_kernel_spmd(nc, maps, core_ids=cores)
    xn0 = [r["xn_fm_o"] for r in r1.results]
    y0 = [r["y_tm"] for r in r1.results]

    nc = _get_nc("mix", 0)
    maps = []
    for ci in cores:
        m = {k: consts[k] for k in ("c_ident", "c_identf", "c_ones1", "c_ones512")}
        m.update({"xn_fm": xn0[ci], "y_tm": y0[ci], "res": x[ci],
                  "wc": prep["l0_wc"], "gate_wT": prep["l0_gate_wT"],
                  "eguT": prep["l0_eguT"], "edownT": prep["l0_edownT"],
                  "sgT": prep["l0_sgT"], "suT": prep["l0_suT"],
                  "sdT": prep["l0_sdT"], "segT": prep["l0_segT"]})
        maps.append(m)
    r2 = run_bass_kernel_spmd(nc, maps, core_ids=cores)
    x1 = [r["x1"] for r in r2.results]
    xn1 = [r["xn1_fm"] for r in r2.results]

    nc = _get_nc("attn", 1)
    maps = []
    for ci in cores:
        m = {k: consts[k] for k in ("c_ident", "c_negones", "c_ones64")}
        m.update({"xn_fm": xn1[ci], "wtq": prep["l1_wtq"], "wtk": prep["l1_wtk"],
                  "wtv": prep["l1_wtv"], "bq": prep["l1_bq"], "bk": prep["l1_bk"],
                  "bv": prep["l1_bv"]})
        maps.append(m)
    r3 = run_bass_kernel_spmd(nc, maps, core_ids=cores)
    y1 = [r["y_tm"] for r in r3.results]

    nc = _get_nc("mix", 1)
    maps = []
    for ci in cores:
        m = {k: consts[k] for k in ("c_ident", "c_identf", "c_ones1", "c_ones512")}
        m.update({"xn_fm": xn1[ci], "y_tm": y1[ci], "res": x1[ci],
                  "wto": prep["l1_wto"], "bo": prep["l1_bo"],
                  "router_w": prep["l1_router_w"], "router_b": prep["l1_router_b"],
                  "w1": prep["l1_w1"], "b1": prep["l1_b1"],
                  "w2": prep["l1_w2"], "b2": prep["l1_b2"]})
        maps.append(m)
    r4 = run_bass_kernel_spmd(nc, maps, core_ids=cores)
    out = np.stack([r["out"] for r in r4.results], axis=0)
    return out.astype(np.float32)
